# version 05
# baseline (speedup 1.0000x reference)
import math
import os
from contextlib import ExitStack

import numpy as np
import concourse.bass as bass
import concourse.mybir as mybir
from concourse.bass_utils import run_bass_kernel_spmd

F32 = mybir.dt.float32
BF16 = mybir.dt.bfloat16
AF = mybir.ActivationFunctionType
OP = mybir.AluOpType

T = 4096
D = 1024
NC8 = 8
LT = 512
NLT = T // LT
H = 16
DH = 64
EPS = 1e-5

ENGINES = ("sync", "tensor", "scalar", "vector", "gpsimd")


class Sem:
    def __init__(self, stack, nc, name):
        self.h = nc.alloc_semaphore(name)
        self.n = 0


class DmaQ:

    def __init__(self, S, eng, tag, n=3):
        self.S = S
        self.eng = eng
        self.sems = [S.sem(f"{tag}{i}") for i in range(n)]
        self.n = n
        self.w = 0

    def wave(self):
        s = self.sems[self.w % self.n]
        if self.w >= self.n:
            self.S.wait(self.eng, s)
        self.w += 1
        return s


AUTO_SERIAL = ("vector", "scalar", "gpsimd")


class Sched:

    def __init__(self, nc, stack, name):
        self.nc = nc
        self.stack = stack
        self.name = name
        self.ops = {e: [] for e in ENGINES}
        self._semid = 0
        self.sems = []
        self.eng = {e: self.sem(e[:2]) for e in ("tensor", "scalar", "vector", "gpsimd")}

    def sem(self, tag=""):
        self._semid += 1
        s = Sem(self.stack, self.nc, f"{self.name}_s{self._semid}{tag}")
        self.sems.append(s)
        return s

    def wait(self, eng, sem, val=None):
        v = sem.n if val is None else val
        if v > 0:
            self.ops[eng].append(lambda e, _h=sem.h, _v=v: e.wait_ge(_h, _v))

    def op(self, eng, fn):
        s = self.eng[eng]
        if eng in AUTO_SERIAL and s.n > 0:
            self.wait(eng, s)
        self.ops[eng].append(lambda e, _f=fn, _h=s.h: _f(e).then_inc(_h, 1))
        s.n += 1
        return s.n

    def dmaq(self, eng, tag, n=3):
        return DmaQ(self, eng, tag, n)

    def dma(self, eng, sem, out, in_, **kw):
        self.ops[eng].append(lambda e, _o=out, _i=in_, _h=sem.h, _kw=kw: e.dma_start(
            out=_o, in_=_i, **_kw).then_inc(_h, 16))
        sem.n += 16

    def barrier(self, engines=ENGINES):
        for e in engines:
            for s in self.sems:
                if not (e in self.eng and self.eng[e] is s):
                    self.wait(e, s)

    def flush(self):
        with self.nc.Block() as block:
            for e in ENGINES:
                if self.ops[e]:
                    def body(eng, _l=tuple(self.ops[e])):
                        for f in _l:
                            f(eng)
                    getattr(block, e)(body)
        self.ops = {e: [] for e in ENGINES}


DRAM_KINDS = {}
CROSS = {}


def declare_dram(nc, name, shape, dtype, kind):
    DRAM_KINDS[name] = kind
    return nc.dram_tensor(name, list(shape), dtype, kind=kind).ap()


WEIGHTS = {
    "L0_attn_Wqkv_w": (3072, 1024), "L0_attn_Wqkv_b": (1, 3072),
    "L0_attn_out_w": (1024, 1024), "L0_attn_out_b": (1, 1024),
    "L0_ffn_gate_W": (8, 1024), "L0_ffn_expert_W": (8, 1024, 1024),
    "L0_ffn_out_W": (1024, 1024),
    "L1_attn_ln1_g": (1, 1024), "L1_attn_ln1_b": (1, 1024),
    "L1_attn_ln2_g": (1, 1024), "L1_attn_ln2_b": (1, 1024),
    "L1_attn_tm_k": (1, 1024), "L1_attn_tm_v": (1, 1024), "L1_attn_tm_r": (1, 1024),
    "L1_attn_time_decay": (1, 1024), "L1_attn_time_first": (1, 1024),
    "L1_attn_Wk": (1024, 1024), "L1_attn_Wv": (1024, 1024),
    "L1_attn_Wr": (1024, 1024), "L1_attn_Wo": (1024, 1024),
    "L1_attn_fm_k": (1, 1024), "L1_attn_fm_r": (1, 1024),
    "L1_attn_Fk": (4096, 1024), "L1_attn_Fr": (1024, 1024), "L1_attn_Fv": (1024, 4096),
    "L1_ffn_gate_w": (16, 1024),
    "L1_ffn_w1": (16, 1024, 512), "L1_ffn_w2": (16, 512, 1024), "L1_ffn_w3": (16, 1024, 512),
    "L1_ffn_ws1": (1024, 1024), "L1_ffn_ws2": (1024, 1024), "L1_ffn_ws3": (1024, 1024),
}

INTERMEDIATES = {
    "x0": ((D, T), F32, 0), "xb0": ((D, T), BF16, 0),
    "wqkv_bf": ((3072, 1024), BF16, 0),
    "outw_bf": ((1024, 1024), BF16, 0),
    "expw0_bf": ((1024, 1024), BF16, 0),
    "expw1_bf": ((1024, 1024), BF16, 0),
    "outW_bf": ((1024, 1024), BF16, 0),
    "wk_bf": ((1024, 1024), BF16, 0), "wv_bf": ((1024, 1024), BF16, 0),
    "wr_bf": ((1024, 1024), BF16, 0), "wo_bf": ((1024, 1024), BF16, 0),
    "fk_bf": ((4096, 1024), BF16, 0), "fr_bf": ((1024, 1024), BF16, 0),
    "fv_bf": ((1024, 4096), BF16, 0),
    "x1": ((D, T), F32, 1), "xb1": ((D, T), BF16, 1),
    "x2": ((D, T), F32, 2), "xb2": ((D, T), BF16, 2),
    "kTf": ((D, T), F32, 3), "vTf": ((D, T), F32, 3), "srT": ((D, T), BF16, 4),
    "xrT": ((D, T), BF16, 3),
    "xm": ((D, T), F32, 4), "xmb": ((D, T), BF16, 4),
    "x3": ((D, T), F32, 5), "xb3": ((D, T), BF16, 5),
    "gkb": ((D, T), BF16, 5), "grb": ((D, T), BF16, 5),
    "kvp": ((D, T), F32, 5), "kvT": ((D, T), F32, 5),
    "ctr": ((16, T), BF16, 6), "ash": ((D, T), BF16, 6),
    "a_all": ((16, 512, T), BF16, 7),
    "ypart": ((D, T), F32, 8),
}

PHASE_READS = {
    1: ["x0", "xb0", "wqkv_bf", "outw_bf"],
    2: ["x1", "xb1", "expw0_bf", "expw1_bf", "outW_bf"],
    3: ["x2", "xb2", "wk_bf", "wv_bf", "wr_bf"],
    4: ["x2", "kTf", "vTf", "xrT", "srT", "wr_bf", "wo_bf"],
    5: ["x2", "xm", "fk_bf", "fr_bf", "fv_bf", "gkb", "grb", "kvp", "kvT"],
    6: ["x3", "xb3"],
    7: ["xb3", "ctr"],
    8: ["x3", "a_all", "ash", "ypart"],
}



def make_identity(S, stack, dtype, tag):
    nc = S.nc
    ones = stack.enter_context(nc.sbuf_tensor(f"{S.name}_idsrc_{tag}", [128, 128], dtype))
    ident = stack.enter_context(nc.sbuf_tensor(f"{S.name}_ident_{tag}", [128, 128], dtype))
    S.op("gpsimd", lambda g: g.memset(ones[:], 1.0))
    S.op("gpsimd", lambda g: g.affine_select(
        ident[:], ones[:], pattern=[[-1, 128]], compare_op=OP.is_equal, fill=0.0,
        base=0, channel_multiplier=1))
    return ident, S.eng["gpsimd"].n


def load_vecs_fm(S, stack, specs, identity, id_ready, psum, ldq):
    nc = S.nc
    nmax = max(n for _, _, n in specs)
    row = stack.enter_context(nc.sbuf_tensor(f"{S.name}_vrow", [1, nmax], F32))
    outs = {}
    col = 0
    S.wait("tensor", S.eng["gpsimd"], id_ready)
    S.wait("tensor", S.eng["scalar"])
    for key, ap, n in specs:
        nch = n // 128
        w = ldq.wave()
        S.wait("sync", S.eng["tensor"])
        S.dma(ldq.eng, w, row[0:1, 0:n], ap[0:1, :])
        S.wait("tensor", w)
        for c in range(nch):
            S.op("tensor", lambda t, _c=c, _col=col: t.transpose(
                psum[:, _col + _c:_col + _c + 1], row[0:1, _c * 128:(_c + 1) * 128],
                identity[0:1, 0:1]))
        outs[key] = stack.enter_context(nc.sbuf_tensor(f"{S.name}_vfm_{key}", [128, nch], F32))
        S.wait("scalar", S.eng["tensor"])
        S.op("scalar", lambda s, _k=key, _n=nch, _col=col: s.activation(
            outs[_k][:, 0:_n], psum[:, _col:_col + _n], AF.Copy))
        col += nch
    return outs


def make_ln_scratch(S, stack):
    nc = S.nc
    scr = {
        "rb": stack.enter_context(nc.sbuf_tensor(f"{S.name}_lnrb", [128, LT], BF16)),
        "sq": stack.enter_context(nc.sbuf_tensor(f"{S.name}_lnsq", [128, LT], BF16)),
        "mrow": stack.enter_context(nc.sbuf_tensor(f"{S.name}_lnmr", [1, LT], BF16)),
        "rrow": stack.enter_context(nc.sbuf_tensor(f"{S.name}_lnrr", [1, LT], BF16)),
        "rows": stack.enter_context(nc.sbuf_tensor(f"{S.name}_lnrows", [1, 4 * LT], F32)),
        "epsr": stack.enter_context(nc.sbuf_tensor(f"{S.name}_lneps", [1, 1], F32)),
    }
    S.op("vector", lambda v: v.memset(scr["epsr"][:, :], EPS))
    return scr


def emit_ln(S, scr, r_tiles, onesc, onesr, bsum, bsumsq, bmean, brstd,
            out_f32, out_b16, g_vec=None, b_vec=None):
    rb, sq, mrow, rrow, rows = scr["rb"], scr["sq"], scr["mrow"], scr["rrow"], scr["rows"]
    PE, SC, VE = S.eng["tensor"], S.eng["scalar"], S.eng["vector"]
    for c in range(NC8):
        S.wait("vector", PE)
        S.op("vector", lambda v, _c=c: v.tensor_copy(rb[:, :], r_tiles[_c][:, :]))
        S.wait("scalar", PE)
        S.op("scalar", lambda s, _c=c: s.activation(sq[:, :], r_tiles[_c][:, :], AF.Square))
        S.wait("tensor", VE)
        S.wait("tensor", SC)
        S.op("tensor", lambda t, _c=c: t.matmul(
            bsum[0:1, :], onesc[:, 0:1], rb[:, :], start=(_c == 0), stop=(_c == NC8 - 1)))
        S.op("tensor", lambda t, _c=c: t.matmul(
            bsumsq[0:1, :], onesc[:, 0:1], sq[:, :], start=(_c == 0), stop=(_c == NC8 - 1)))
    mean = rows[0:1, 0:LT]
    msq = rows[0:1, LT:2 * LT]
    var = rows[0:1, 2 * LT:3 * LT]
    rstd = rows[0:1, 3 * LT:4 * LT]
    S.wait("vector", PE)
    S.op("vector", lambda v: v.tensor_scalar(mean, bsum[0:1, :], 1.0 / D, None, OP.mult))
    S.op("vector", lambda v: v.tensor_tensor(msq, mean, mean, OP.mult))
    S.op("vector", lambda v: v.scalar_tensor_tensor(
        var, bsumsq[0:1, :], 1.0 / D, msq, OP.mult, OP.subtract))
    S.wait("scalar", VE)
    S.op("scalar", lambda s: s.activation(var, var, AF.Ln, bias=scr["epsr"][0:1, 0:1], scale=1.0))
    S.op("scalar", lambda s: s.activation(rstd, var, AF.Exp, scale=-0.5))
    S.wait("vector", SC)
    S.op("vector", lambda v: v.tensor_copy(mrow[:, :], mean))
    S.op("vector", lambda v: v.tensor_copy(rrow[:, :], rstd))
    S.wait("tensor", VE)
    S.op("tensor", lambda t: t.matmul(bmean[:, :], onesr[0:1, :], mrow[:, :], start=True, stop=True))
    S.op("tensor", lambda t: t.matmul(brstd[:, :], onesr[0:1, :], rrow[:, :], start=True, stop=True))
    S.wait("vector", PE)
    for c in range(NC8):
        S.op("vector", lambda v, _c=c: v.tensor_tensor(out_f32[_c][:, :], r_tiles[_c][:, :], bmean[:, :], OP.subtract))
        if g_vec is not None:
            S.op("vector", lambda v, _c=c: v.tensor_tensor(out_f32[_c][:, :], out_f32[_c][:, :], brstd[:, :], OP.mult))
            S.op("vector", lambda v, _c=c: v.tensor_scalar(
                out_f32[_c][:, :], out_f32[_c][:, :], g_vec[:, _c:_c + 1], b_vec[:, _c:_c + 1], OP.mult, OP.add))
        else:
            S.op("vector", lambda v, _c=c: v.tensor_tensor(out_f32[_c][:, :], out_f32[_c][:, :], brstd[:, :], OP.mult))
        if out_b16 is not None:
            S.op("vector", lambda v, _c=c: v.tensor_copy(out_b16[_c][:, :], out_f32[_c][:, :]))



def phase0(nc, A):
    with ExitStack() as stack:
        S = Sched(nc, stack, "p0")
        gq = S.dmaq("gpsimd", "gq", 3)
        cq = S.dmaq("gpsimd", "cq", 1)
        cast = cq.wave()
        casts = [
            ("L0_attn_Wqkv_w", "wqkv_bf"), ("L0_attn_out_w", "outw_bf"),
            ("L0_ffn_out_W", "outW_bf"),
            ("L1_attn_Wk", "wk_bf"), ("L1_attn_Wv", "wv_bf"),
            ("L1_attn_Wr", "wr_bf"), ("L1_attn_Wo", "wo_bf"),
            ("L1_attn_Fk", "fk_bf"), ("L1_attn_Fr", "fr_bf"), ("L1_attn_Fv", "fv_bf"),
        ]
        for src, dst in casts:
            S.dma("gpsimd", cast, A[dst][:, :], A[src][:, :])
        S.dma("gpsimd", cast, A["expw0_bf"][:, :], A["L0_ffn_expert_W"][0, :, :])
        S.dma("gpsimd", cast, A["expw1_bf"][:, :], A["L0_ffn_expert_W"][1, :, :])
        CROSS["cast"] = (cast, cast.n)

        identf, idrdy = make_identity(S, stack, F32, "f")
        xin_s = [stack.enter_context(nc.sbuf_tensor(f"p0_xin{p}", [128, 1024], F32)) for p in range(2)]
        stg = [stack.enter_context(nc.sbuf_tensor(f"p0_stg{c}", [128, 512], F32)) for c in range(NC8)]
        stgb = [stack.enter_context(nc.sbuf_tensor(f"p0_stgb{c}", [128, 512], BF16)) for c in range(NC8)]
        pt = [stack.enter_context(nc.psum_tensor(f"p0_pt{i}", [128, 512], F32)) for i in range(2)]
        lq = S.dmaq("sync", "lq", 4)
        PE, SC, VE, PL = S.eng["tensor"], S.eng["scalar"], S.eng["vector"], S.eng["gpsimd"]
        S.wait("tensor", PL, idrdy)
        rel = {}

        def wait_rel(eng, key):
            for sem, cnt in rel.get(key, ()):
                S.wait(eng, sem, cnt)

        stw = None
        for grp in range(8):
            for j in range(4):
                i = grp * 4 + j
                xin = xin_s[i % 2]
                wait_rel("sync", ("xin", i % 2))
                w = lq.wave()
                S.dma("sync", w, xin[:, :], A["x_in"][i * 128:(i + 1) * 128, :])
                S.wait("tensor", w)
                if grp > 0 and j == 0:
                    S.wait("scalar", stw)
                    S.wait("scalar", VE)
                for c in range(NC8):
                    reg = ((c % 2) * 2 + (j % 2))
                    bank = pt[reg // 2]
                    r0 = (reg % 2) * 256
                    wait_rel("tensor", ("pt", reg))
                    S.wait("tensor", SC)
                    S.op("tensor", lambda t, _c=c, _b=bank, _r=r0, _x=xin: t.transpose(
                        _b[:, _r:_r + 128], _x[:, _c * 128:(_c + 1) * 128], identf[:]))
                    S.wait("scalar", PE)
                    S.op("scalar", lambda s, _c=c, _j=j, _b=bank, _r=r0: s.activation(
                        stg[_c][:, _j * 128:(_j + 1) * 128], _b[:, _r:_r + 128], AF.Copy))
                    rel[("pt", reg)] = ((SC, SC.n),)
                rel[("xin", i % 2)] = ((PE, PE.n),)
            S.wait("vector", SC)
            if stw is not None:
                S.wait("vector", stw)
            for c in range(NC8):
                S.op("vector", lambda v, _c=c: v.tensor_copy(stgb[_c][:, :], stg[_c][:, :]))
            S.wait("gpsimd", SC)
            S.wait("gpsimd", VE)
            stw = gq.wave()
            for c in range(NC8):
                S.dma("gpsimd", stw, A["x0"][c * 128:(c + 1) * 128, grp * 512:(grp + 1) * 512], stg[c][:, :])
                S.dma("gpsimd", stw, A["xb0"][c * 128:(c + 1) * 128, grp * 512:(grp + 1) * 512], stgb[c][:, :])
        for s_ in gq.sems:
            S.wait("gpsimd", s_)
        S.flush()



def phase1(nc, A):
    with ExitStack() as stack:
        S = Sched(nc, stack, "p1")
        wqkvT = stack.enter_context(nc.sbuf_tensor("p1_wqkvT", [128, NC8 * 3072], BF16))
        outwT = stack.enter_context(nc.sbuf_tensor("p1_outwT", [128, NC8 * 1024], BF16))
        xb = [stack.enter_context(nc.sbuf_tensor(f"p1_xb{c}", [128, LT], BF16)) for c in range(NC8)]
        x0t = [stack.enter_context(nc.sbuf_tensor(f"p1_x0t{c}", [128, LT], F32)) for c in range(NC8)]
        qT = stack.enter_context(nc.sbuf_tensor("p1_qT", [128, NC8 * LT], BF16))
        kT = stack.enter_context(nc.sbuf_tensor("p1_kT", [128, NC8 * LT], BF16))
        ktok = [stack.enter_context(nc.sbuf_tensor(f"p1_ktok{ts}", [128, 1024], BF16)) for ts in range(4)]
        vtok = [stack.enter_context(nc.sbuf_tensor(f"p1_vtok{ts}", [128, 1024], BF16)) for ts in range(4)]
        sT = [stack.enter_context(nc.sbuf_tensor(f"p1_sT{i}", [128, LT], BF16)) for i in range(4)]
        yT = [stack.enter_context(nc.sbuf_tensor(f"p1_yT{c}", [128, LT], BF16)) for c in range(NC8)]
        msb = stack.enter_context(nc.sbuf_tensor("p1_msb", [128, 512], BF16))
        dlt = [stack.enter_context(nc.sbuf_tensor(f"p1_dlt{c}", [128, LT], F32)) for c in range(NC8)]
        x1t = dlt
        xb1t = [stack.enter_context(nc.sbuf_tensor(f"p1_xb1t{c}", [128, LT], BF16)) for c in range(NC8)]
        trisrc = stack.enter_context(nc.sbuf_tensor("p1_trisrc", [128, 128], F32))
        tri = stack.enter_context(nc.sbuf_tensor("p1_tri", [128, 128], F32))
        onesc = stack.enter_context(nc.sbuf_tensor("p1_onesc", [128, 1], BF16))
        onesr = stack.enter_context(nc.sbuf_tensor("p1_onesr", [1, 128], BF16))
        ones_lt = stack.enter_context(nc.sbuf_tensor("p1_oneslt", [1, LT], BF16))
        lnscr = make_ln_scratch(S, stack)
        bqkv_row = stack.enter_context(nc.sbuf_tensor("p1_bqkvr", [1, 3072], BF16))
        bout_row = stack.enter_context(nc.sbuf_tensor("p1_boutr", [1, 1024], BF16))
        bk = [stack.enter_context(nc.psum_tensor(f"p1_bk{i}", [128, LT], F32)) for i in range(6)]
        py = stack.enter_context(nc.psum_tensor("p1_py", [128, LT], F32))
        pm = stack.enter_context(nc.psum_tensor("p1_pm", [128, 512], F32))

        lq = S.dmaq("sync", "lq", 4)
        gq = S.dmaq("gpsimd", "gq", 3)
        PE, SC, VE, PL = S.eng["tensor"], S.eng["scalar"], S.eng["vector"], S.eng["gpsimd"]
        rel = {}

        def wait_rel(eng, key):
            for sem, cnt in rel.get(key, ()):
                S.wait(eng, sem, cnt)

        ldp = gq.wave()
        S.dma("gpsimd", ldp, bqkv_row[0:1, :], A["L0_attn_Wqkv_b"][0:1, :])
        S.dma("gpsimd", ldp, bout_row[0:1, :], A["L0_attn_out_b"][0:1, :])
        if "cast" in CROSS:
            S.wait("sync", CROSS["cast"][0], CROSS["cast"][1])
        wld = lq.wave()
        for c in range(NC8):
            S.dma("sync", wld, wqkvT[:, c * 3072:(c + 1) * 3072], A["wqkv_bf"][:, c * 128:(c + 1) * 128], transpose=True)
            S.dma("sync", wld, outwT[:, c * 1024:(c + 1) * 1024], A["outw_bf"][:, c * 128:(c + 1) * 128], transpose=True)
        S.op("gpsimd", lambda g: g.memset(trisrc[:], 1.0))
        S.op("gpsimd", lambda g: g.affine_select(
            tri[:], trisrc[:], pattern=[[1, 128]], compare_op=OP.is_ge, fill=0.0,
            base=0, channel_multiplier=-1))
        S.op("vector", lambda v: v.memset(onesc[:], 1.0))
        S.op("vector", lambda v: v.memset(onesr[:], 1.0))
        S.op("vector", lambda v: v.memset(ones_lt[:], 1.0))
        S.op("vector", lambda v: v.memset(msb[:], 0.0))

        for lt in range(NLT):
            tw = lq.wave()
            wait_rel("sync", "xtiles")
            for c in range(NC8):
                S.dma("sync", tw, xb[c][:, :], A["xb0"][c * 128:(c + 1) * 128, lt * LT:(lt + 1) * LT])
                S.dma("sync", tw, x0t[c][:, :], A["x0"][c * 128:(c + 1) * 128, lt * LT:(lt + 1) * LT])
            S.wait("tensor", tw)
            if lt == 0:
                S.wait("tensor", wld)
                S.wait("tensor", ldp)
                S.wait("tensor", PL)
                S.wait("tensor", VE)
            for oc in range(16):
                bank = bk[oc % 3]
                wait_rel("tensor", ("bk", oc % 3))
                for kc in range(NC8):
                    S.op("tensor", lambda t, _oc=oc, _kc=kc, _b=bank: t.matmul(
                        _b[:, :], wqkvT[:, _kc * 3072 + _oc * 128:_kc * 3072 + (_oc + 1) * 128],
                        xb[_kc][:, :], start=(_kc == 0), stop=False))
                S.op("tensor", lambda t, _oc=oc, _b=bank: t.matmul(
                    _b[:, :], bqkv_row[0:1, _oc * 128:(_oc + 1) * 128], ones_lt[0:1, :],
                    start=False, stop=True))
                dst, cc, scl = (qT, oc, 1.0) if oc < 8 else (kT, oc - 8, 0.125)
                S.wait("scalar", PE)
                wait_rel("scalar", ("qkT", 0))
                S.op("scalar", lambda s, _b=bank, _d=dst, _cc=cc, _sc=scl: s.activation(
                    _d[:, _cc * LT:(_cc + 1) * LT], _b[:, :], AF.Identity, scale=_sc))
                rel[("bk", oc % 3)] = ((SC, SC.n),)
            for ts in range(4):
                for part in range(2):
                    for half in range(2):
                        i = (ts * 4 + part * 2 + half)
                        bank = bk[i % 3]
                        dout0 = 1024 + part * 1024 + half * 512
                        wait_rel("tensor", ("bk", i % 3))
                        for kc in range(NC8):
                            S.op("tensor", lambda t, _ts=ts, _kc=kc, _b=bank, _d0=dout0: t.matmul(
                                _b[:, :], xb[_kc][:, _ts * 128:(_ts + 1) * 128],
                                wqkvT[:, _kc * 3072 + _d0:_kc * 3072 + _d0 + 512],
                                start=(_kc == 0), stop=False))
                        S.op("tensor", lambda t, _b=bank, _d0=dout0: t.matmul(
                            _b[:, :], onesr[0:1, 0:128], bqkv_row[0:1, _d0:_d0 + 512],
                            start=False, stop=True))
                        dstt = ktok if part == 0 else vtok
                        scl = 0.125 if part == 0 else 1.0
                        S.wait("scalar", PE)
                        wait_rel("scalar", ("tok", part, ts))
                        S.op("scalar", lambda s, _b=bank, _d=dstt, _ts=ts, _h=half, _sc=scl: s.activation(
                            _d[_ts][:, _h * 512:(_h + 1) * 512], _b[:, :], AF.Identity, scale=_sc))
                        rel[("bk", i % 3)] = ((SC, SC.n),)
            rel["xtiles"] = ((PE, PE.n), (VE, VE.n))
            if lt > 0:
                S.wait("scalar", PE)
                S.op("scalar", lambda s: s.activation(msb[:, :], pm[:, :], AF.Copy))
            msb_ready = (SC, SC.n)
            for h in range(H):
                hc = h // 2
                hb = (h % 2) * 64
                st_ready = {}
                for ms in range(4):
                    n = LT - ms * 128
                    bi = (h * 4 + ms) % 6
                    bank = bk[bi]
                    wait_rel("tensor", ("bk", bi))
                    S.op("tensor", lambda t, _hc=hc, _hb=hb, _ms=ms, _n=n, _b=bank: t.matmul(
                        _b[:, 0:_n],
                        kT[_hb:_hb + 64, _hc * LT + _ms * 128:_hc * LT + (_ms + 1) * 128],
                        qT[_hb:_hb + 64, _hc * LT + _ms * 128:_hc * LT + _ms * 128 + _n],
                        start=True, stop=True))
                    S.wait("vector", PE)
                    S.op("vector", lambda v, _ms=ms, _b=bank: v.tensor_tensor(
                        sT[_ms][:, 0:128], _b[:, 0:128], tri[:, :], OP.mult))
                    if n > 128:
                        S.wait("scalar", PE)
                        S.wait("scalar", VE)
                        S.op("scalar", lambda s, _ms=ms, _n=n, _b=bank: s.activation(
                            sT[_ms][:, 128:_n], _b[:, 128:_n], AF.Copy))
                        st_ready[ms] = ((VE, VE.n), (SC, SC.n))
                        rel[("bk", bi)] = ((VE, VE.n), (SC, SC.n))
                    else:
                        st_ready[ms] = ((VE, VE.n),)
                        rel[("bk", bi)] = ((VE, VE.n),)
                wait_rel("tensor", "py")
                S.wait("tensor", msb_ready[0], msb_ready[1])
                S.op("tensor", lambda t, _hb=hb, _hc=hc: t.matmul(
                    py[_hb:_hb + 64, :], msb[_hb:_hb + 64, _hc * 64:_hc * 64 + 64],
                    qT[_hb:_hb + 64, _hc * LT:(_hc + 1) * LT], start=True, stop=False))
                for ms in range(4):
                    n = LT - ms * 128
                    for sem, cnt in st_ready[ms]:
                        S.wait("tensor", sem, cnt)
                    S.op("tensor", lambda t, _hb=hb, _h=h, _ms=ms, _n=n: t.matmul(
                        py[_hb:_hb + 64, _ms * 128:_ms * 128 + _n],
                        vtok[_ms][:, _h * 64:(_h + 1) * 64],
                        sT[_ms][:, 0:_n], start=False, stop=(_ms == 3)))
                if h % 2 == 1:
                    S.wait("scalar", PE)
                    wait_rel("scalar", ("yT", hc))
                    S.op("scalar", lambda s, _hc=hc: s.activation(
                        yT[_hc][:, :], py[:, :], AF.Copy))
                    rel["py"] = ((SC, SC.n),)
            for h in range(H):
                hc = h // 2
                hb = (h % 2) * 64
                for ts in range(4):
                    first = (lt == 0 and h in (0, 1) and ts == 0)
                    S.op("tensor", lambda t, _h=h, _hb=hb, _hc=hc, _ts=ts, _f=first: t.matmul(
                        pm[_hb:_hb + 64, _hc * 64:_hc * 64 + 64],
                        ktok[_ts][:, _h * 64:(_h + 1) * 64],
                        vtok[_ts][:, _h * 64:(_h + 1) * 64],
                        start=_f, stop=False, skip_group_check=True))
            for part in range(2):
                for ts in range(4):
                    rel[("tok", part, ts)] = ((PE, PE.n),)
            S.wait("tensor", SC)
            for oc in range(NC8):
                bank = bk[3 + oc % 2]
                wait_rel("tensor", ("bk", 3 + oc % 2))
                for kc in range(NC8):
                    S.op("tensor", lambda t, _oc=oc, _kc=kc, _b=bank: t.matmul(
                        _b[:, :], outwT[:, _kc * 1024 + _oc * 128:_kc * 1024 + (_oc + 1) * 128],
                        yT[_kc][:, :], start=(_kc == 0), stop=False))
                S.op("tensor", lambda t, _oc=oc, _b=bank: t.matmul(
                    _b[:, :], bout_row[0:1, _oc * 128:(_oc + 1) * 128], ones_lt[0:1, :],
                    start=False, stop=True))
                rel[("yT", oc)] = ((PE, PE.n),)
                S.wait("scalar", PE)
                wait_rel("scalar", "stores")
                wait_rel("scalar", ("dlt", oc))
                S.op("scalar", lambda s, _oc=oc, _b=bank: s.activation(
                    dlt[_oc][:, :], _b[:, :], AF.Copy))
                rel[("bk", 3 + oc % 2)] = ((SC, SC.n),)
            S.wait("vector", SC)
            for c in range(NC8):
                S.op("vector", lambda v, _c=c: v.tensor_tensor(
                    dlt[_c][:, :], dlt[_c][:, :], x0t[_c][:, :], OP.add))
            rel["xtiles"] = ((PE, rel["xtiles"][0][1]), (VE, VE.n))
            S.wait("scalar", VE)
            emit_ln(S, lnscr, dlt, onesc, onesr, bk[3], bk[4], bk[4], bk[3],
                    x1t, xb1t)
            rel[("bk", 3)] = ((VE, VE.n),)
            rel[("bk", 4)] = ((VE, VE.n),)
            S.wait("gpsimd", VE)
            sw = gq.wave()
            for c in range(NC8):
                S.dma("gpsimd", sw, A["x1"][c * 128:(c + 1) * 128, lt * LT:(lt + 1) * LT], x1t[c][:, :])
                S.dma("gpsimd", sw, A["xb1"][c * 128:(c + 1) * 128, lt * LT:(lt + 1) * LT], xb1t[c][:, :])
            rel["stores"] = ((sw, sw.n),)
            for c in range(NC8):
                rel[("dlt", c)] = ((sw, sw.n), (VE, VE.n))
            rel[("qkT", 0)] = ((PE, PE.n),)
        for s_ in gq.sems:
            S.wait("gpsimd", s_)
        S.flush()



def phase2(nc, A):
    with ExitStack() as stack:
        S = Sched(nc, stack, "p2")
        w0T = stack.enter_context(nc.sbuf_tensor("p2_w0T", [128, NC8 * 1024], BF16))
        w1T = stack.enter_context(nc.sbuf_tensor("p2_w1T", [128, NC8 * 1024], BF16))
        outWT = stack.enter_context(nc.sbuf_tensor("p2_outWT", [128, NC8 * 1024], BF16))
        gw = stack.enter_context(nc.sbuf_tensor("p2_gw", [8, 1024], F32))
        gwT = stack.enter_context(nc.sbuf_tensor("p2_gwT", [128, NC8 * 8], F32))
        xb = [stack.enter_context(nc.sbuf_tensor(f"p2_xb{c}", [128, LT], BF16)) for c in range(NC8)]
        x1f = [stack.enter_context(nc.sbuf_tensor(f"p2_x1f{c}", [128, LT], F32)) for c in range(NC8)]
        glog_s = stack.enter_context(nc.sbuf_tensor("p2_glog", [8, LT], F32))
        ptok_s = stack.enter_context(nc.sbuf_tensor("p2_ptok", [128, 32], F32))
        e8 = stack.enter_context(nc.sbuf_tensor("p2_e8", [128, 8], F32))
        p8_ = stack.enter_context(nc.sbuf_tensor("p2_p8", [128, 8], F32))
        g8 = stack.enter_context(nc.sbuf_tensor("p2_g8", [128, 8], F32))
        gs = stack.enter_context(nc.sbuf_tensor("p2_gs", [128, 8], F32))
        s01 = stack.enter_context(nc.sbuf_tensor("p2_s01", [128, 8], F32))
        st2row = stack.enter_context(nc.sbuf_tensor("p2_st2r", [2, LT], BF16))
        sel0 = stack.enter_context(nc.sbuf_tensor("p2_sel0", [2, 128], BF16))
        sel1 = stack.enter_context(nc.sbuf_tensor("p2_sel1", [2, 128], BF16))
        ones2 = stack.enter_context(nc.sbuf_tensor("p2_ones2", [2, 128], BF16))
        h0s = [stack.enter_context(nc.sbuf_tensor(f"p2_h0s{c}", [128, LT], F32)) for c in range(NC8)]
        h1s = [stack.enter_context(nc.sbuf_tensor(f"p2_h1s{c}", [128, LT], F32)) for c in range(NC8)]
        f0 = stack.enter_context(nc.sbuf_tensor("p2_f0", [128, LT], F32))
        finT = [stack.enter_context(nc.sbuf_tensor(f"p2_fin{c}", [128, LT], BF16)) for c in range(NC8)]
        dlt = [stack.enter_context(nc.sbuf_tensor(f"p2_dlt{c}", [128, LT], F32)) for c in range(NC8)]
        x2t = dlt
        xb2t = [stack.enter_context(nc.sbuf_tensor(f"p2_xb2t{c}", [128, LT], BF16)) for c in range(NC8)]
        onesc = stack.enter_context(nc.sbuf_tensor("p2_onesc", [128, 1], BF16))
        onesr = stack.enter_context(nc.sbuf_tensor("p2_onesr", [1, 128], BF16))
        lnscr = make_ln_scratch(S, stack)
        identf, idrdyp = make_identity(S, stack, F32, "f")

        psc = [stack.enter_context(nc.psum_tensor(f"p2_psc{i}", [128, LT], F32)) for i in range(2)]
        pbc = [stack.enter_context(nc.psum_tensor(f"p2_pbc{i}", [128, LT], F32)) for i in range(2)]
        pst = [stack.enter_context(nc.psum_tensor(f"p2_pst{i}", [128, LT], F32)) for i in range(3)]
        pl = stack.enter_context(nc.psum_tensor("p2_pl", [128, LT], F32))

        lq = S.dmaq("sync", "lq", 4)
        gq = S.dmaq("gpsimd", "gq", 3)
        PE, SC, VE, PL = S.eng["tensor"], S.eng["scalar"], S.eng["vector"], S.eng["gpsimd"]
        rel = {}

        def wait_rel(eng, key):
            for sem, cnt in rel.get(key, ()):
                S.wait(eng, sem, cnt)

        s01_rel = 0
        if "cast" in CROSS:
            S.wait("sync", CROSS["cast"][0], CROSS["cast"][1])
        wld = lq.wave()
        for c in range(NC8):
            S.dma("sync", wld, w0T[:, c * 1024:(c + 1) * 1024], A["expw0_bf"][:, c * 128:(c + 1) * 128], transpose=True)
            S.dma("sync", wld, w1T[:, c * 1024:(c + 1) * 1024], A["expw1_bf"][:, c * 128:(c + 1) * 128], transpose=True)
            S.dma("sync", wld, outWT[:, c * 1024:(c + 1) * 1024], A["outW_bf"][:, c * 128:(c + 1) * 128], transpose=True)
        S.dma("sync", wld, gw[:, :], A["L0_ffn_gate_W"][:, :])
        S.wait("tensor", wld)
        S.wait("tensor", PL, idrdyp)
        for c in range(NC8):
            S.op("tensor", lambda t, _c=c: t.transpose(
                pl[:, _c * 8:(_c + 1) * 8], gw[0:8, _c * 128:(_c + 1) * 128], identf[0:8, 0:8]))
        S.wait("scalar", PE)
        S.op("scalar", lambda s: s.activation(gwT[:, 0:NC8 * 8], pl[:, 0:NC8 * 8], AF.Copy))
        S.op("gpsimd", lambda g: g.memset(ones2[:, :], 1.0))
        S.op("gpsimd", lambda g: g.affine_select(
            sel0[:, :], ones2[:, :], pattern=[[0, 128]], compare_op=OP.is_equal, fill=0.0,
            base=0, channel_multiplier=1))
        S.op("gpsimd", lambda g: g.affine_select(
            sel1[:, :], ones2[:, :], pattern=[[0, 128]], compare_op=OP.is_equal, fill=0.0,
            base=-1, channel_multiplier=1))
        S.op("vector", lambda v: v.memset(onesc[:], 1.0))
        S.op("vector", lambda v: v.memset(onesr[:], 1.0))

        for lt in range(NLT):
            tw = lq.wave()
            wait_rel("sync", "xtiles")
            for c in range(NC8):
                S.dma("sync", tw, xb[c][:, :], A["xb1"][c * 128:(c + 1) * 128, lt * LT:(lt + 1) * LT])
                S.dma("sync", tw, x1f[c][:, :], A["x1"][c * 128:(c + 1) * 128, lt * LT:(lt + 1) * LT])
            S.wait("tensor", tw)
            if lt == 0:
                S.wait("tensor", SC)
                S.wait("tensor", PL)
                S.wait("tensor", VE)
            wait_rel("tensor", "pst0")
            for kc in range(NC8):
                S.op("tensor", lambda t, _kc=kc: t.matmul(
                    pst[0][0:8, :], gwT[:, _kc * 8:(_kc + 1) * 8], x1f[_kc][:, :],
                    start=(_kc == 0), stop=(_kc == NC8 - 1)))
            S.wait("scalar", PE)
            wait_rel("scalar", "glog")
            S.op("scalar", lambda s: s.activation(glog_s[:, :], pst[0][0:8, :], AF.Copy))
            S.wait("tensor", SC)
            wait_rel("tensor", "pst1")
            for j in range(4):
                S.op("tensor", lambda t, _j=j: t.transpose(
                    pst[1][:, _j * 8:(_j + 1) * 8], glog_s[0:8, _j * 128:(_j + 1) * 128], identf[0:8, 0:8]))
            S.wait("scalar", PE)
            wait_rel("scalar", "ptok")
            S.op("scalar", lambda s: s.activation(ptok_s[:, 0:32], pst[1][:, 0:32], AF.Copy))
            rel["glog"] = ((PE, PE.n),)
            rel["pst0"] = ((SC, SC.n),)
            rel["pst1"] = ((SC, SC.n),)
            for oc in range(NC8):
                pair = (psc[0], psc[1]) if oc % 2 == 0 else (pst[0], pst[1])
                key = ("xp", oc % 2)
                wait_rel("tensor", key)
                if oc % 2 == 1:
                    wait_rel("tensor", "pst0")
                    wait_rel("tensor", "pst1")
                for kc in range(NC8):
                    S.op("tensor", lambda t, _oc=oc, _kc=kc, _b=pair[0]: t.matmul(
                        _b[:, :], w0T[:, _kc * 1024 + _oc * 128:_kc * 1024 + (_oc + 1) * 128],
                        xb[_kc][:, :], start=(_kc == 0), stop=(_kc == NC8 - 1)))
                for kc in range(NC8):
                    S.op("tensor", lambda t, _oc=oc, _kc=kc, _b=pair[1]: t.matmul(
                        _b[:, :], w1T[:, _kc * 1024 + _oc * 128:_kc * 1024 + (_oc + 1) * 128],
                        xb[_kc][:, :], start=(_kc == 0), stop=(_kc == NC8 - 1)))
                S.wait("scalar", PE)
                wait_rel("scalar", ("hs", oc))
                S.op("scalar", lambda s, _oc=oc, _b=pair[0]: s.activation(h0s[_oc][:, :], _b[:, :], AF.Copy))
                S.op("scalar", lambda s, _oc=oc, _b=pair[1]: s.activation(h1s[_oc][:, :], _b[:, :], AF.Copy))
                rel[key] = ((SC, SC.n),)
                if oc % 2 == 1:
                    rel["pst0"] = ((SC, SC.n),)
                    rel["pst1"] = ((SC, SC.n),)
            rel["xtiles"] = ((PE, PE.n), (VE, VE.n))
            for j in range(4):
                L = ptok_s[:, j * 8:(j + 1) * 8]
                S.wait("vector", SC)
                S.wait("vector", PE, s01_rel)
                S.op("vector", lambda v, _L=L: v.reduce_max(gs[:, 0:1], _L, axis=mybir.AxisListType.X))
                S.op("vector", lambda v: v.tensor_scalar(gs[:, 1:2], gs[:, 0:1], -1.0, None, OP.mult))
                S.wait("scalar", VE)
                S.op("scalar", lambda s, _L=L: s.activation(e8[:, :], _L, AF.Exp, bias=gs[:, 1:2]))
                S.wait("vector", SC)
                S.op("vector", lambda v: v.reduce_sum(gs[:, 2:3], e8[:, :], axis=mybir.AxisListType.X))
                S.op("vector", lambda v: v.reciprocal(gs[:, 3:4], gs[:, 2:3]))
                S.op("vector", lambda v: v.tensor_scalar(p8_[:, :], e8[:, :], gs[:, 3:4], None, OP.mult))
                S.op("vector", lambda v: v.tensor_scalar(g8[:, :], p8_[:, :], 0.3, None, OP.is_gt))
                S.op("vector", lambda v: v.reduce_sum(gs[:, 4:5], g8[:, :], axis=mybir.AxisListType.X))
                S.op("vector", lambda v: v.tensor_scalar(gs[:, 5:6], gs[:, 4:5], -1.0, 8.0, OP.mult, OP.add))
                S.op("vector", lambda v, _j=j: v.tensor_tensor(s01[:, 2 * _j:2 * _j + 1], gs[:, 5:6], p8_[:, 0:1], OP.mult))
                S.op("vector", lambda v, _j=j: v.tensor_tensor(s01[:, 2 * _j + 1:2 * _j + 2], gs[:, 4:5], p8_[:, 1:2], OP.mult))
            S.wait("tensor", VE)
            wait_rel("tensor", "pst2")
            for j in range(4):
                S.op("tensor", lambda t, _j=j: t.transpose(
                    pst[2][0:2, _j * 128:(_j + 1) * 128], s01[:, 2 * _j:2 * _j + 2], identf[:, :]))
            s01_rel = PE.n
            rel["ptok"] = ((VE, VE.n),)
            S.wait("scalar", PE)
            wait_rel("scalar", "st2row")
            S.op("scalar", lambda s: s.activation(st2row[:, :], pst[2][0:2, :], AF.Copy))
            S.wait("tensor", SC)
            wait_rel("tensor", "pbc")
            S.op("tensor", lambda t: t.matmul(pbc[0][:, :], sel0[:, :], st2row[:, :], start=True, stop=True))
            S.op("tensor", lambda t: t.matmul(pbc[1][:, :], sel1[:, :], st2row[:, :], start=True, stop=True))
            rel["st2row"] = ((PE, PE.n),)
            rel["pst2"] = ((PE, PE.n),)
            bc_done = PE.n
            S.wait("vector", PE, bc_done)
            S.wait("vector", SC)
            for oc in range(NC8):
                S.op("vector", lambda v, _oc=oc: v.tensor_tensor(f0[:, :], h0s[_oc][:, :], pbc[0][:, :], OP.mult))
                S.op("vector", lambda v, _oc=oc: v.tensor_tensor(finT[_oc][:, :], h1s[_oc][:, :], pbc[1][:, :], OP.mult))
                S.op("vector", lambda v, _oc=oc: v.tensor_tensor(finT[_oc][:, :], finT[_oc][:, :], f0[:, :], OP.add))
                rel[("hs", oc)] = ((VE, VE.n),)
            rel["pbc"] = ((VE, VE.n),)
            S.wait("tensor", VE)
            for oc in range(NC8):
                bank = psc[oc % 2]
                wait_rel("tensor", ("xp", 0))
                for kc in range(NC8):
                    S.op("tensor", lambda t, _oc=oc, _kc=kc, _b=bank: t.matmul(
                        _b[:, :], outWT[:, _kc * 1024 + _oc * 128:_kc * 1024 + (_oc + 1) * 128],
                        finT[_kc][:, :], start=(_kc == 0), stop=(_kc == NC8 - 1)))
                S.wait("scalar", PE)
                wait_rel("scalar", ("dlt", oc))
                S.op("scalar", lambda s, _oc=oc, _b=bank: s.activation(
                    dlt[_oc][:, :], _b[:, :], AF.Copy))
                rel[("xp", 0)] = ((SC, SC.n),)
            S.wait("vector", SC)
            for c in range(NC8):
                S.op("vector", lambda v, _c=c: v.tensor_tensor(
                    dlt[_c][:, :], dlt[_c][:, :], x1f[_c][:, :], OP.add))
            rel["xtiles"] = ((PE, rel["xtiles"][0][1]), (VE, VE.n))
            S.wait("scalar", VE)
            emit_ln(S, lnscr, dlt, onesc, onesr, pl, pbc[0], pbc[0], pl,
                    x2t, xb2t)
            rel["pbc"] = ((VE, VE.n),)
            S.wait("gpsimd", VE)
            sw = gq.wave()
            for c in range(NC8):
                S.dma("gpsimd", sw, A["x2"][c * 128:(c + 1) * 128, lt * LT:(lt + 1) * LT], x2t[c][:, :])
                S.dma("gpsimd", sw, A["xb2"][c * 128:(c + 1) * 128, lt * LT:(lt + 1) * LT], xb2t[c][:, :])
            for c in range(NC8):
                rel[("dlt", c)] = ((sw, sw.n), (VE, VE.n))
            rel["pst0"] = ((PE, PE.n),)
        for s_ in gq.sems:
            S.wait("gpsimd", s_)
        S.flush()



def phase3(nc, A):
    with ExitStack() as stack:
        S = Sched(nc, stack, "p3")
        wkT = stack.enter_context(nc.sbuf_tensor("p3_wkT", [128, NC8 * 1024], BF16))
        wvT = stack.enter_context(nc.sbuf_tensor("p3_wvT", [128, NC8 * 1024], BF16))
        x2f = [stack.enter_context(nc.sbuf_tensor(f"p3_x2f{c}", [128, LT], F32)) for c in range(NC8)]
        hbuf = [stack.enter_context(nc.sbuf_tensor(f"p3_h{c}", [128, LT + 1], F32)) for c in range(NC8)]
        dscr = stack.enter_context(nc.sbuf_tensor("p3_dscr", [128, LT], F32))
        xkb = [stack.enter_context(nc.sbuf_tensor(f"p3_xkb{c}", [128, LT], BF16)) for c in range(NC8)]
        xvb = [stack.enter_context(nc.sbuf_tensor(f"p3_xvb{c}", [128, LT], BF16)) for c in range(NC8)]
        xrb = [stack.enter_context(nc.sbuf_tensor(f"p3_xrb{c}", [128, LT], BF16)) for c in range(NC8)]
        kst = [stack.enter_context(nc.sbuf_tensor(f"p3_kst{c}", [128, LT], F32)) for c in range(NC8)]
        vst = [stack.enter_context(nc.sbuf_tensor(f"p3_vst{c}", [128, LT], F32)) for c in range(NC8)]
        onesc = stack.enter_context(nc.sbuf_tensor("p3_onesc", [128, 1], BF16))
        onesr = stack.enter_context(nc.sbuf_tensor("p3_onesr", [1, 128], BF16))
        lnscr = make_ln_scratch(S, stack)
        identf, idrdy = make_identity(S, stack, F32, "f")
        psc = [stack.enter_context(nc.psum_tensor(f"p3_psc{i}", [128, LT], F32)) for i in range(6)]
        pl = [stack.enter_context(nc.psum_tensor(f"p3_pl{i}", [128, LT], F32)) for i in range(2)]

        lq = S.dmaq("sync", "lq", 4)
        gq = S.dmaq("gpsimd", "gq", 3)
        PE, SC, VE, PL = S.eng["tensor"], S.eng["scalar"], S.eng["vector"], S.eng["gpsimd"]
        rel = {}

        def wait_rel(eng, key):
            for sem, cnt in rel.get(key, ()):
                S.wait(eng, sem, cnt)

        if "cast" in CROSS:
            S.wait("sync", CROSS["cast"][0], CROSS["cast"][1])
        wld = lq.wave()
        for c in range(NC8):
            S.dma("sync", wld, wkT[:, c * 1024:(c + 1) * 1024], A["wk_bf"][:, c * 128:(c + 1) * 128], transpose=True)
            S.dma("sync", wld, wvT[:, c * 1024:(c + 1) * 1024], A["wv_bf"][:, c * 128:(c + 1) * 128], transpose=True)
        vecs = load_vecs_fm(S, stack, [
            ("l1g", A["L1_attn_ln1_g"], D), ("l1b", A["L1_attn_ln1_b"], D),
            ("tmk", A["L1_attn_tm_k"], D), ("tmv", A["L1_attn_tm_v"], D),
            ("tmr", A["L1_attn_tm_r"], D)], identf, idrdy, pl[0], lq)
        ln1g, ln1b = vecs["l1g"], vecs["l1b"]
        tmk, tmv, tmr = vecs["tmk"], vecs["tmv"], vecs["tmr"]
        S.op("vector", lambda v: v.memset(onesc[:], 1.0))
        S.op("vector", lambda v: v.memset(onesr[:], 1.0))
        for c in range(NC8):
            S.op("vector", lambda v, _c=c: v.memset(hbuf[_c][:, 0:1], 0.0))

        for lt in range(NLT):
            tw = lq.wave()
            wait_rel("sync", "x2f")
            for c in range(NC8):
                S.dma("sync", tw, x2f[c][:, :], A["x2"][c * 128:(c + 1) * 128, lt * LT:(lt + 1) * LT])
            S.wait("tensor", tw)
            S.wait("scalar", tw)
            S.wait("vector", tw)
            if lt == 0:
                S.wait("tensor", wld)
                S.wait("tensor", SC)
            if lt > 0:
                for c in range(NC8):
                    S.op("vector", lambda v, _c=c: v.tensor_copy(hbuf[_c][:, 0:1], hbuf[_c][:, LT:LT + 1]))
            emit_ln(S, lnscr, x2f, onesc, onesr, pl[0], pl[1], pl[1], pl[0],
                    [hbuf[c][:, 1:LT + 1] for c in range(NC8)], None, g_vec=ln1g, b_vec=ln1b)
            rel["x2f"] = ((VE, VE.n),)
            wait_rel("vector", "mixes")
            wait_rel("vector", ("st", 0))
            for c in range(NC8):
                h_ = hbuf[c][:, 1:LT + 1]
                hh = hbuf[c][:, 0:LT]
                S.op("vector", lambda v, _h=h_, _hh=hh: v.tensor_tensor(dscr[:, :], _h, _hh, OP.subtract))
                S.op("vector", lambda v, _c=c, _hh=hh: v.scalar_tensor_tensor(
                    xkb[_c][:, :], dscr[:, :], tmk[:, _c:_c + 1], _hh, OP.mult, OP.add))
                S.op("vector", lambda v, _c=c, _hh=hh: v.scalar_tensor_tensor(
                    xvb[_c][:, :], dscr[:, :], tmv[:, _c:_c + 1], _hh, OP.mult, OP.add))
                S.op("vector", lambda v, _c=c, _hh=hh: v.scalar_tensor_tensor(
                    xrb[_c][:, :], dscr[:, :], tmr[:, _c:_c + 1], _hh, OP.mult, OP.add))
            S.wait("tensor", VE)
            for oc in range(NC8):
                bk = psc[0 + oc % 2]
                bv = psc[2 + oc % 2]
                for i in (0 + oc % 2, 2 + oc % 2):
                    wait_rel("tensor", ("psc", i))
                for kc in range(NC8):
                    S.op("tensor", lambda t, _oc=oc, _kc=kc, _b=bk: t.matmul(
                        _b[:, :], wkT[:, _kc * 1024 + _oc * 128:_kc * 1024 + (_oc + 1) * 128],
                        xkb[_kc][:, :], start=(_kc == 0), stop=(_kc == NC8 - 1)))
                for kc in range(NC8):
                    S.op("tensor", lambda t, _oc=oc, _kc=kc, _b=bv: t.matmul(
                        _b[:, :], wvT[:, _kc * 1024 + _oc * 128:_kc * 1024 + (_oc + 1) * 128],
                        xvb[_kc][:, :], start=(_kc == 0), stop=(_kc == NC8 - 1)))
                S.wait("scalar", PE)
                wait_rel("scalar", ("st", oc))
                S.op("scalar", lambda s, _oc=oc, _b=bk: s.activation(kst[_oc][:, :], _b[:, :], AF.Copy))
                rel[("psc", 0 + oc % 2)] = ((SC, SC.n),)
                S.op("scalar", lambda s, _oc=oc, _b=bv: s.activation(vst[_oc][:, :], _b[:, :], AF.Copy))
                rel[("psc", 2 + oc % 2)] = ((SC, SC.n),)
            rel["mixes"] = ((PE, PE.n), (VE, VE.n))
            S.wait("gpsimd", SC)
            S.wait("gpsimd", VE)
            sw = gq.wave()
            for c in range(NC8):
                S.dma("gpsimd", sw, A["kTf"][c * 128:(c + 1) * 128, lt * LT:(lt + 1) * LT], kst[c][:, :])
                S.dma("gpsimd", sw, A["vTf"][c * 128:(c + 1) * 128, lt * LT:(lt + 1) * LT], vst[c][:, :])
                S.dma("gpsimd", sw, A["xrT"][c * 128:(c + 1) * 128, lt * LT:(lt + 1) * LT], xrb[c][:, :])
            for c in range(NC8):
                rel[("st", c)] = ((sw, sw.n),)
        for s_ in gq.sems:
            S.wait("gpsimd", s_)
        S.flush()


H2 = T // 2


def phase4(nc, A):
    with ExitStack() as stack:
        S = Sched(nc, stack, "p4")
        woT = stack.enter_context(nc.sbuf_tensor("p4_woT", [128, NC8 * 1024], BF16))
        wrT = stack.enter_context(nc.sbuf_tensor("p4_wrT", [128, NC8 * 1024], BF16))
        yT = [stack.enter_context(nc.sbuf_tensor(f"p4_y{c}", [128, T], BF16)) for c in range(NC8)]
        kbb = [stack.enter_context(nc.sbuf_tensor(f"p4_kb{p}", [128, H2], F32)) for p in range(2)]
        vb = [stack.enter_context(nc.sbuf_tensor(f"p4_vb{p}", [128, H2], F32)) for p in range(2)]
        ekb = stack.enter_context(nc.sbuf_tensor("p4_ekb", [128, H2], F32))
        ekvb = stack.enter_context(nc.sbuf_tensor("p4_ekvb", [128, H2], F32))
        lnsc = stack.enter_context(nc.sbuf_tensor("p4_lnsc", [128, H2], F32))
        Nb = stack.enter_context(nc.sbuf_tensor("p4_Nb", [128, H2 + 1], F32))
        Db = stack.enter_context(nc.sbuf_tensor("p4_Db", [128, H2 + 1], F32))
        xrb = [stack.enter_context(nc.sbuf_tensor(f"p4_xrb{c}", [128, LT], BF16)) for c in range(NC8)]
        srt = [stack.enter_context(nc.sbuf_tensor(f"p4_srt{c}", [128, LT], BF16)) for c in range(NC8)]
        srl = xrb
        rwk = srt
        x2f = [stack.enter_context(nc.sbuf_tensor(f"p4_x2f{c}", [128, LT], F32)) for c in range(NC8)]
        ew = stack.enter_context(nc.sbuf_tensor("p4_ew", [128, NC8], F32))
        eu = stack.enter_context(nc.sbuf_tensor("p4_eu", [128, NC8], F32))
        identf, idrdy = make_identity(S, stack, F32, "f")
        pvec = stack.enter_context(nc.psum_tensor("p4_pvec", [128, LT], F32))
        psc = [stack.enter_context(nc.psum_tensor(f"p4_psc{i}", [128, LT], F32)) for i in range(4)]

        lq = S.dmaq("sync", "lq", 4)
        gq = S.dmaq("gpsimd", "gq", 3)
        PE, SC, VE, PL = S.eng["tensor"], S.eng["scalar"], S.eng["vector"], S.eng["gpsimd"]
        rel = {}

        def wait_rel(eng, key):
            for sem, cnt in rel.get(key, ()):
                S.wait(eng, sem, cnt)

        if "cast" in CROSS:
            S.wait("sync", CROSS["cast"][0], CROSS["cast"][1])
        wld = lq.wave()
        for c in range(NC8):
            S.dma("sync", wld, woT[:, c * 1024:(c + 1) * 1024], A["wo_bf"][:, c * 128:(c + 1) * 128], transpose=True)
            S.dma("sync", wld, wrT[:, c * 1024:(c + 1) * 1024], A["wr_bf"][:, c * 128:(c + 1) * 128], transpose=True)
        vecs = load_vecs_fm(S, stack, [
            ("td", A["L1_attn_time_decay"], D), ("tf", A["L1_attn_time_first"], D)],
            identf, idrdy, pvec, lq)
        td, tf = vecs["td"], vecs["tf"]
        S.op("scalar", lambda s: s.activation(ew[:, :], td[:, :], AF.Exp))
        S.op("scalar", lambda s: s.activation(ew[:, :], ew[:, :], AF.Exp, scale=-1.0))
        S.op("scalar", lambda s: s.activation(eu[:, :], tf[:, :], AF.Exp))

        sw0 = lq.wave()
        S.dma("sync", sw0, kbb[0][:, :], A["kTf"][0:128, 0:H2])
        S.dma("sync", sw0, vb[0][:, :], A["vTf"][0:128, 0:H2])
        scan_waves = {0: sw0}

        def emit_rtile(lt):
            tw = lq.wave()
            wait_rel("sync", "xrb")
            for c in range(NC8):
                S.dma("sync", tw, xrb[c][:, :], A["xrT"][c * 128:(c + 1) * 128, lt * LT:(lt + 1) * LT])
            S.wait("tensor", tw)
            if lt == 0:
                S.wait("tensor", wld)
            for oc in range(NC8):
                bank = psc[oc % 2]
                wait_rel("tensor", ("psc", oc % 2))
                for kc in range(NC8):
                    S.op("tensor", lambda t, _oc=oc, _kc=kc, _b=bank: t.matmul(
                        _b[:, :], wrT[:, _kc * 1024 + _oc * 128:_kc * 1024 + (_oc + 1) * 128],
                        xrb[_kc][:, :], start=(_kc == 0), stop=(_kc == NC8 - 1)))
                S.wait("scalar", PE)
                wait_rel("scalar", ("srt", oc))
                S.op("scalar", lambda s, _oc=oc, _b=bank: s.activation(srt[_oc][:, :], _b[:, :], AF.Sigmoid))
                rel[("psc", oc % 2)] = ((SC, SC.n),)
            rel["xrb"] = ((PE, PE.n),)
            S.wait("gpsimd", SC)
            rw = gq.wave()
            for c in range(NC8):
                S.dma("gpsimd", rw, A["srT"][c * 128:(c + 1) * 128, lt * LT:(lt + 1) * LT], srt[c][:, :])
            for c in range(NC8):
                rel[("srt", c)] = ((rw, rw.n),)

        for it in range(16):
            ch, half = it // 2, it % 2
            t0 = half * H2
            par = it % 2
            w = scan_waves[it]
            if it + 1 < 16:
                nch, nhalf = (it + 1) // 2, (it + 1) % 2
                nw = lq.wave()
                scan_waves[it + 1] = nw
                wait_rel("sync", ("kv", (it + 1) % 2))
                S.dma("sync", nw, kbb[(it + 1) % 2][:, :],
                      A["kTf"][nch * 128:(nch + 1) * 128, nhalf * H2:(nhalf + 1) * H2])
                S.dma("sync", nw, vb[(it + 1) % 2][:, :],
                      A["vTf"][nch * 128:(nch + 1) * 128, nhalf * H2:(nhalf + 1) * H2])
            if it % 2 == 0 and it // 2 < NLT:
                emit_rtile(it // 2)
            S.wait("scalar", w)
            S.wait("scalar", VE)
            S.op("scalar", lambda s, _p=par: s.activation(ekb[:, :], kbb[_p][:, :], AF.Exp))
            S.wait("vector", SC)
            S.op("vector", lambda v, _p=par: v.tensor_tensor(ekvb[:, :], ekb[:, :], vb[_p][:, :], OP.mult))
            rel[("kv", par)] = ((VE, VE.n),)
            if half == 0:
                S.op("vector", lambda v: v.memset(Nb[:, 0:1], 0.0))
                S.op("vector", lambda v: v.memset(Db[:, 0:1], 0.0))
            else:
                S.op("vector", lambda v: v.tensor_copy(Nb[:, 0:1], Nb[:, H2:H2 + 1]))
                S.op("vector", lambda v: v.tensor_copy(Db[:, 0:1], Db[:, H2:H2 + 1]))
            ewb = ew[:, ch:ch + 1].broadcast_to((128, H2))
            S.op("vector", lambda v, _e=ewb: v.tensor_tensor_scan(
                Nb[:, 1:H2 + 1], _e, ekvb[:, :], Nb[:, 0:1], OP.mult, OP.add))
            S.op("vector", lambda v, _e=ewb: v.tensor_tensor_scan(
                Db[:, 1:H2 + 1], _e, ekb[:, :], Db[:, 0:1], OP.mult, OP.add))
            S.op("vector", lambda v, _ch=ch: v.scalar_tensor_tensor(
                ekvb[:, :], ekvb[:, :], eu[:, _ch:_ch + 1], Nb[:, 0:H2], OP.mult, OP.add))
            S.op("vector", lambda v, _ch=ch: v.scalar_tensor_tensor(
                ekb[:, :], ekb[:, :], eu[:, _ch:_ch + 1], Db[:, 0:H2], OP.mult, OP.add))
            S.wait("scalar", VE)
            S.op("scalar", lambda s: s.activation(lnsc[:, :], ekb[:, :], AF.Ln))
            S.op("scalar", lambda s: s.activation(ekb[:, :], lnsc[:, :], AF.Exp, scale=-1.0))
            S.wait("vector", SC)
            S.op("vector", lambda v, _ch=ch, _t0=t0: v.tensor_tensor(
                yT[_ch][:, _t0:_t0 + H2], ekvb[:, :], ekb[:, :], OP.mult))

        for lt in range(NLT):
            tw = lq.wave()
            wait_rel("sync", "wotile")
            wait_rel("sync", "xrb")
            for c in range(NC8):
                S.dma("sync", tw, x2f[c][:, :], A["x2"][c * 128:(c + 1) * 128, lt * LT:(lt + 1) * LT])
                S.dma("sync", tw, srl[c][:, :], A["srT"][c * 128:(c + 1) * 128, lt * LT:(lt + 1) * LT])
            if lt == 0:
                for s_ in gq.sems:
                    S.wait("sync", s_)
            S.wait("vector", tw)
            S.wait("vector", VE)
            if lt == 0:
                for s_ in gq.sems:
                    S.wait("vector", s_)
            for c in range(NC8):
                S.op("vector", lambda v, _c=c, _lt=lt: v.tensor_tensor(
                    rwk[_c][:, :], yT[_c][:, _lt * LT:(_lt + 1) * LT], srl[_c][:, :], OP.mult))
            S.wait("tensor", VE)
            for oc in range(NC8):
                bank = psc[2 + oc % 2]
                wait_rel("tensor", ("pwo", oc % 2))
                for kc in range(NC8):
                    S.op("tensor", lambda t, _oc=oc, _kc=kc, _b=bank: t.matmul(
                        _b[:, :], woT[:, _kc * 1024 + _oc * 128:_kc * 1024 + (_oc + 1) * 128],
                        rwk[_kc][:, :], start=(_kc == 0), stop=(_kc == NC8 - 1)))
                S.wait("vector", PE)
                S.op("vector", lambda v, _oc=oc, _b=bank: v.tensor_tensor(
                    x2f[_oc][:, :], x2f[_oc][:, :], _b[:, :], OP.add))
                rel[("pwo", oc % 2)] = ((VE, VE.n),)
            S.wait("gpsimd", VE)
            sw = gq.wave()
            for c in range(NC8):
                S.dma("gpsimd", sw, A["xm"][c * 128:(c + 1) * 128, lt * LT:(lt + 1) * LT], x2f[c][:, :])
            rel["wotile"] = ((VE, VE.n), (PE, PE.n), (sw, sw.n))
        for s_ in gq.sems:
            S.wait("gpsimd", s_)
        S.flush()



def phase5(nc, A):
    with ExitStack() as stack:
        S = Sched(nc, stack, "p5")
        fkTh = stack.enter_context(nc.sbuf_tensor("p5_fkTh", [128, NC8 * 2048], BF16))
        fvTh = stack.enter_context(nc.sbuf_tensor("p5_fvTh", [128, 16 * 1024], BF16))
        frT = stack.enter_context(nc.sbuf_tensor("p5_frT", [128, NC8 * 1024], BF16))
        xmf = [stack.enter_context(nc.sbuf_tensor(f"p5_xmf{c}", [128, LT], F32)) for c in range(NC8)]
        x2f = [stack.enter_context(nc.sbuf_tensor(f"p5_x2f{c}", [128, LT], F32)) for c in range(NC8)]
        gbuf = [stack.enter_context(nc.sbuf_tensor(f"p5_g{c}", [128, LT + 1], F32)) for c in range(NC8)]
        dscr = stack.enter_context(nc.sbuf_tensor("p5_dscr", [128, LT], F32))
        gkb = [stack.enter_context(nc.sbuf_tensor(f"p5_gkb{c}", [128, LT], BF16)) for c in range(NC8)]
        gkp = [gkb, [stack.enter_context(nc.sbuf_tensor(f"p5_gk2{c}", [128, LT], BF16)) for c in range(NC8)]]
        grb = [stack.enter_context(nc.sbuf_tensor(f"p5_grb{c}", [128, LT], BF16)) for c in range(NC8)]
        kfT = [stack.enter_context(nc.sbuf_tensor(f"p5_kf{i}", [128, LT], BF16)) for i in range(16)]
        relsc = stack.enter_context(nc.sbuf_tensor("p5_rel", [128, LT], F32))
        kvs = [stack.enter_context(nc.sbuf_tensor(f"p5_kvs{c}", [128, LT], F32)) for c in range(NC8)]
        kvp_t = x2f
        sig = relsc
        dlt = kvs
        x3t = dlt
        xb3t = gkb
        onesc = stack.enter_context(nc.sbuf_tensor("p5_onesc", [128, 1], BF16))
        onesr = stack.enter_context(nc.sbuf_tensor("p5_onesr", [1, 128], BF16))
        lnscr = make_ln_scratch(S, stack)
        identf, idrdy = make_identity(S, stack, F32, "f")
        pk = [stack.enter_context(nc.psum_tensor(f"p5_pk{i}", [128, LT], F32)) for i in range(2)]
        pkv = [stack.enter_context(nc.psum_tensor(f"p5_pkv{i}", [128, LT], F32)) for i in range(2)]
        pvec = stack.enter_context(nc.psum_tensor("p5_pvec", [128, LT], F32))
        pl = [stack.enter_context(nc.psum_tensor(f"p5_pl{i}", [128, LT], F32)) for i in range(2)]

        lq = S.dmaq("sync", "lq", 4)
        gq = S.dmaq("gpsimd", "gq", 4)
        PE, SC, VE, PL = S.eng["tensor"], S.eng["scalar"], S.eng["vector"], S.eng["gpsimd"]
        rel = {}

        def wait_rel(eng, key):
            for sem, cnt in rel.get(key, ()):
                S.wait(eng, sem, cnt)

        vecs = load_vecs_fm(S, stack, [
            ("l2g", A["L1_attn_ln2_g"], D), ("l2b", A["L1_attn_ln2_b"], D),
            ("fmk", A["L1_attn_fm_k"], D), ("fmr", A["L1_attn_fm_r"], D)],
            identf, idrdy, pvec, lq)
        ln2g, ln2b, fmk, fmr = vecs["l2g"], vecs["l2b"], vecs["fmk"], vecs["fmr"]
        S.op("vector", lambda v: v.memset(onesc[:], 1.0))
        S.op("vector", lambda v: v.memset(onesr[:], 1.0))
        for c in range(NC8):
            S.op("vector", lambda v, _c=c: v.memset(gbuf[_c][:, 0:1], 0.0))
        wld = {}

        def load_dfh_weights(dfh):
            if "cast" in CROSS:
                S.wait("sync", CROSS["cast"][0], CROSS["cast"][1])
            w = lq.wave()
            wld[dfh] = w
            for c in range(NC8):
                S.dma("sync", w, fkTh[:, c * 2048:(c + 1) * 2048],
                      A["fk_bf"][dfh * 2048:(dfh + 1) * 2048, c * 128:(c + 1) * 128], transpose=True)
            for d in range(16):
                S.dma("sync", w, fvTh[:, d * 1024:(d + 1) * 1024],
                      A["fv_bf"][:, dfh * 2048 + d * 128:dfh * 2048 + (d + 1) * 128], transpose=True)

        P5SEC = int(os.environ.get("P5SEC", "4"))
        kfT_rel = 0
        load_dfh_weights(0)

        def emit_a0(lt):
            tw = lq.wave()
            wait_rel("sync", "xmf")
            for c in range(NC8):
                S.dma("sync", tw, xmf[c][:, :], A["xm"][c * 128:(c + 1) * 128, lt * LT:(lt + 1) * LT])
            S.wait("vector", tw)
            S.wait("scalar", tw)
            if lt > 0:
                for c in range(NC8):
                    S.op("vector", lambda v, _c=c: v.tensor_copy(gbuf[_c][:, 0:1], gbuf[_c][:, LT:LT + 1]))
            emit_ln(S, lnscr, xmf, onesc, onesr, pl[0], pl[1], pl[1], pl[0],
                    [gbuf[c][:, 1:LT + 1] for c in range(NC8)], None, g_vec=ln2g, b_vec=ln2b)
            rel["xmf"] = ((VE, VE.n),)
            par = lt % 2
            wait_rel("vector", ("gmix", par))
            wait_rel("vector", "grst")
            for c in range(NC8):
                g_ = gbuf[c][:, 1:LT + 1]
                gg = gbuf[c][:, 0:LT]
                S.op("vector", lambda v, _g=g_, _gg=gg: v.tensor_tensor(dscr[:, :], _g, _gg, OP.subtract))
                S.op("vector", lambda v, _c=c, _gg=gg, _p=par: v.scalar_tensor_tensor(
                    gkp[_p][_c][:, :], dscr[:, :], fmk[:, _c:_c + 1], _gg, OP.mult, OP.add))
                S.op("vector", lambda v, _c=c, _gg=gg: v.scalar_tensor_tensor(
                    grb[_c][:, :], dscr[:, :], fmr[:, _c:_c + 1], _gg, OP.mult, OP.add))
            S.wait("gpsimd", VE)
            mw = gq.wave()
            for c in range(NC8):
                S.dma("gpsimd", mw, A["gkb"][c * 128:(c + 1) * 128, lt * LT:(lt + 1) * LT], gkp[par][c][:, :])
                S.dma("gpsimd", mw, A["grb"][c * 128:(c + 1) * 128, lt * LT:(lt + 1) * LT], grb[c][:, :])
            rel["grst"] = ((mw, mw.n),)
            rel[("a0", lt)] = ((VE, VE.n),)

        def emit_kfkv(lt, dfh, gk_tiles):
            nonlocal kfT_rel
            for dfc in range(16):
                bank = pk[dfc % 2]
                wait_rel("tensor", ("pk", dfc % 2))
                for kc in range(NC8):
                    S.op("tensor", lambda t, _d=dfc, _kc=kc, _b=bank, _g=gk_tiles: t.matmul(
                        _b[:, :], fkTh[:, _kc * 2048 + _d * 128:_kc * 2048 + (_d + 1) * 128],
                        _g[_kc][:, :], start=(_kc == 0), stop=(_kc == NC8 - 1)))
                S.wait("scalar", PE)
                S.wait("scalar", PE, kfT_rel)
                S.op("scalar", lambda s, _b=bank: s.activation(relsc[:, :], _b[:, :], AF.Relu))
                rel[("pk", dfc % 2)] = ((SC, SC.n),)
                S.op("scalar", lambda s, _d=dfc: s.activation(kfT[_d][:, :], relsc[:, :], AF.Square))
            for oc in range(NC8):
                bank = pkv[oc % 2]
                wait_rel("tensor", ("pkv", oc % 2))
                S.wait("tensor", SC)
                for dfc in range(16):
                    S.op("tensor", lambda t, _oc=oc, _d=dfc, _b=bank: t.matmul(
                        _b[:, :], fvTh[:, _d * 1024 + _oc * 128:_d * 1024 + (_oc + 1) * 128],
                        kfT[_d][:, :], start=(_d == 0), stop=(_d == 15)))
                S.wait("scalar", PE)
                wait_rel("scalar", ("kvs", oc))
                S.op("scalar", lambda s, _oc=oc, _b=bank: s.activation(kvs[_oc][:, :], _b[:, :], AF.Copy))
                rel[("pkv", oc % 2)] = ((SC, SC.n),)
                if dfh == 1:
                    S.wait("vector", SC)
                    S.op("vector", lambda v, _oc=oc: v.tensor_tensor(
                        kvs[_oc][:, :], kvs[_oc][:, :], kvp_t[_oc][:, :], OP.add))
            kfT_rel = PE.n
            if dfh == 1:
                rel["kvp"] = ((VE, VE.n),)
            S.wait("gpsimd", VE if dfh == 1 else SC)
            kw = gq.wave()
            dstn = "kvp" if dfh == 0 else "kvT"
            for c in range(NC8):
                S.dma("gpsimd", kw, A[dstn][c * 128:(c + 1) * 128, lt * LT:(lt + 1) * LT], kvs[c][:, :])
            for c in range(NC8):
                rel[("kvs", c)] = ((kw, kw.n),)

        for lt in range(NLT + 1):
            if lt >= 1:
                plt = lt - 1
                par = plt % 2
                for sem, cnt in rel.get(("a0", plt), ()):
                    S.wait("tensor", sem, cnt)
                if plt == 0:
                    S.wait("tensor", wld[0])
                emit_kfkv(plt, 0, gkp[par])
                rel[("gmix", par)] = ((PE, PE.n),)
            if lt < NLT:
                emit_a0(lt)
        S.wait("sync", PE)
        load_dfh_weights(1)
        for lt in range(NLT):
            tw = lq.wave()
            wait_rel("sync", "gki")
            for c in range(NC8):
                S.dma("sync", tw, gkb[c][:, :], A["gkb"][c * 128:(c + 1) * 128, lt * LT:(lt + 1) * LT])
            wait_rel("sync", "kvp")
            for c in range(NC8):
                S.dma("sync", tw, kvp_t[c][:, :], A["kvp"][c * 128:(c + 1) * 128, lt * LT:(lt + 1) * LT])
            S.wait("tensor", tw)
            S.wait("vector", tw)
            if lt == 0:
                S.wait("tensor", wld[1])
                for s_ in gq.sems:
                    S.wait("sync", s_)
            emit_kfkv(lt, 1, gkb)
            rel["gki"] = ((PE, PE.n),)
        for s_ in gq.sems:
            S.wait("sync", s_)
        if "cast" in CROSS:
            S.wait("sync", CROSS["cast"][0], CROSS["cast"][1])
        wldc = lq.wave()
        S.wait("sync", PE)
        for c in range(NC8):
            S.dma("sync", wldc, frT[:, c * 1024:(c + 1) * 1024], A["fr_bf"][:, c * 128:(c + 1) * 128], transpose=True)
        rel.pop("gki", None)
        for lt in range(NLT if P5SEC >= 4 else 0):
            tw = lq.wave()
            wait_rel("sync", "ctile")
            wait_rel("sync", "xmf")
            for c in range(NC8):
                S.dma("sync", tw, grb[c][:, :], A["grb"][c * 128:(c + 1) * 128, lt * LT:(lt + 1) * LT])
                S.dma("sync", tw, kvs[c][:, :], A["kvT"][c * 128:(c + 1) * 128, lt * LT:(lt + 1) * LT])
                S.dma("sync", tw, xmf[c][:, :], A["xm"][c * 128:(c + 1) * 128, lt * LT:(lt + 1) * LT])
                S.dma("sync", tw, x2f[c][:, :], A["x2"][c * 128:(c + 1) * 128, lt * LT:(lt + 1) * LT])
            S.wait("tensor", tw)
            if lt == 0:
                S.wait("tensor", wldc)
            S.wait("vector", tw)
            S.wait("scalar", tw)
            for oc in range(NC8):
                bank = pk[oc % 2]
                wait_rel("tensor", ("pk", oc % 2))
                for kc in range(NC8):
                    S.op("tensor", lambda t, _oc=oc, _kc=kc, _b=bank: t.matmul(
                        _b[:, :], frT[:, _kc * 1024 + _oc * 128:_kc * 1024 + (_oc + 1) * 128],
                        grb[_kc][:, :], start=(_kc == 0), stop=(_kc == NC8 - 1)))
                S.wait("scalar", PE)
                S.wait("scalar", VE)
                S.op("scalar", lambda s, _b=bank: s.activation(sig[:, :], _b[:, :], AF.Sigmoid))
                rel[("pk", oc % 2)] = ((SC, SC.n),)
                S.wait("vector", SC)
                S.op("vector", lambda v, _oc=oc: v.tensor_tensor(sig[:, :], sig[:, :], kvs[_oc][:, :], OP.mult))
                S.op("vector", lambda v, _oc=oc: v.tensor_tensor(dlt[_oc][:, :], x2f[_oc][:, :], xmf[_oc][:, :], OP.add))
                S.op("vector", lambda v, _oc=oc: v.tensor_tensor(dlt[_oc][:, :], dlt[_oc][:, :], sig[:, :], OP.add))
            S.wait("scalar", VE)
            emit_ln(S, lnscr, dlt, onesc, onesr, pl[0], pl[1], pvec, pl[0],
                    x3t, xb3t)
            S.wait("gpsimd", VE)
            sw = gq.wave()
            for c in range(NC8):
                S.dma("gpsimd", sw, A["x3"][c * 128:(c + 1) * 128, lt * LT:(lt + 1) * LT], x3t[c][:, :])
                S.dma("gpsimd", sw, A["xb3"][c * 128:(c + 1) * 128, lt * LT:(lt + 1) * LT], xb3t[c][:, :])
            rel["ctile"] = ((sw, sw.n), (VE, VE.n))
        for s_ in gq.sems:
            S.wait("gpsimd", s_)
        S.flush()



def phase6(nc, A):
    with ExitStack() as stack:
        S = Sched(nc, stack, "p6")
        ws1b = stack.enter_context(nc.sbuf_tensor("p6_ws1", [128, NC8 * 1024], BF16))
        ws3b = stack.enter_context(nc.sbuf_tensor("p6_ws3", [128, NC8 * 1024], BF16))
        gw = stack.enter_context(nc.sbuf_tensor("p6_gw", [16, 1024], F32))
        gwT = stack.enter_context(nc.sbuf_tensor("p6_gwT", [128, NC8 * 16], F32))
        x3f = [stack.enter_context(nc.sbuf_tensor(f"p6_x3f{c}", [128, LT], F32)) for c in range(NC8)]
        xb3 = [stack.enter_context(nc.sbuf_tensor(f"p6_xb3{c}", [128, LT], BF16)) for c in range(NC8)]
        glog_s = stack.enter_context(nc.sbuf_tensor("p6_glog", [16, LT], F32))
        ptok = stack.enter_context(nc.sbuf_tensor("p6_ptok", [128, 64], F32))
        e16 = stack.enter_context(nc.sbuf_tensor("p6_e16", [128, 16], F32))
        p16 = stack.enter_context(nc.sbuf_tensor("p6_p16", [128, 16], F32))
        m16 = stack.enter_context(nc.sbuf_tensor("p6_m16", [128, 16], F32))
        cur = stack.enter_context(nc.sbuf_tensor("p6_cur", [128, 16], F32))
        gm16 = stack.enter_context(nc.sbuf_tensor("p6_gm16", [128, 16], F32))
        gsc = stack.enter_context(nc.sbuf_tensor("p6_gsc", [128, 4], F32))
        gs = stack.enter_context(nc.sbuf_tensor("p6_gs", [128, 8], F32))
        ctok = stack.enter_context(nc.sbuf_tensor("p6_ctok", [128, 64], F32))
        ctr_s = stack.enter_context(nc.sbuf_tensor("p6_ctr", [16, LT], BF16))
        s1f = stack.enter_context(nc.sbuf_tensor("p6_s1f", [128, LT], F32))
        ashb = [stack.enter_context(nc.sbuf_tensor(f"p6_ash{c}", [128, LT], BF16)) for c in range(NC8)]
        identf, idrdy = make_identity(S, stack, F32, "f")
        pg = [stack.enter_context(nc.psum_tensor(f"p6_pg{i}", [128, LT], F32)) for i in range(2)]
        ph = [stack.enter_context(nc.psum_tensor(f"p6_ph{i}", [128, LT], F32)) for i in range(4)]
        pt2 = stack.enter_context(nc.psum_tensor("p6_pt2", [128, LT], F32))

        lq = S.dmaq("sync", "lq", 3)
        gq = S.dmaq("gpsimd", "gq", 3)
        PE, SC, VE, PL = S.eng["tensor"], S.eng["scalar"], S.eng["vector"], S.eng["gpsimd"]

        wldp = gq.wave()
        for c in range(NC8):
            S.dma("gpsimd", wldp, ws1b[:, c * 1024:(c + 1) * 1024], A["L1_ffn_ws1"][c * 128:(c + 1) * 128, :])
            S.dma("gpsimd", wldp, ws3b[:, c * 1024:(c + 1) * 1024], A["L1_ffn_ws3"][c * 128:(c + 1) * 128, :])
        wld = lq.wave()
        S.dma("sync", wld, gw[:, :], A["L1_ffn_gate_w"][:, :])
        S.wait("tensor", wld)
        S.wait("tensor", PL, idrdy)
        for c in range(NC8):
            S.op("tensor", lambda t, _c=c: t.transpose(
                pt2[:, _c * 16:(_c + 1) * 16], gw[0:16, _c * 128:(_c + 1) * 128], identf[0:16, 0:16]))
        S.wait("scalar", PE)
        S.op("scalar", lambda s: s.activation(gwT[:, 0:NC8 * 16], pt2[:, 0:NC8 * 16], AF.Copy))
        s1f_rel = 0
        ct_rel = 0

        rel = {}

        def wait_rel(eng, key):
            for sem, cnt in rel.get(key, ()):
                S.wait(eng, sem, cnt)

        for lt in range(NLT):
            tw = lq.wave()
            wait_rel("sync", "xtiles")
            for c in range(NC8):
                S.dma("sync", tw, x3f[c][:, :], A["x3"][c * 128:(c + 1) * 128, lt * LT:(lt + 1) * LT])
                S.dma("sync", tw, xb3[c][:, :], A["xb3"][c * 128:(c + 1) * 128, lt * LT:(lt + 1) * LT])
            S.wait("tensor", tw)
            if lt == 0:
                S.wait("tensor", wldp)
                S.wait("tensor", SC)
                S.wait("tensor", VE)
            wait_rel("tensor", "pt2")
            for kc in range(NC8):
                S.op("tensor", lambda t, _kc=kc: t.matmul(
                    pt2[0:16, :], gwT[:, _kc * 16:(_kc + 1) * 16], x3f[_kc][:, :],
                    start=(_kc == 0), stop=(_kc == NC8 - 1)))
            S.wait("scalar", PE)
            wait_rel("scalar", "glog")
            S.op("scalar", lambda s: s.activation(glog_s[:, :], pt2[0:16, :], AF.Copy))
            S.wait("tensor", SC)
            for j in range(4):
                S.op("tensor", lambda t, _j=j: t.transpose(
                    pt2[:, _j * 16:(_j + 1) * 16], glog_s[0:16, _j * 128:(_j + 1) * 128], identf[0:16, 0:16]))
            S.wait("scalar", PE)
            wait_rel("scalar", "ptok")
            S.op("scalar", lambda s: s.activation(ptok[:, 0:64], pt2[:, 0:64], AF.Copy))
            rel["glog"] = ((PE, PE.n),)
            rel["pt2"] = ((SC, SC.n),)
            for oc in range(NC8):
                b1 = ph[oc % 2]
                b3 = ph[2 + oc % 2]
                wait_rel("tensor", ("ph", oc % 2))
                for kc in range(NC8):
                    S.op("tensor", lambda t, _oc=oc, _kc=kc, _b=b1: t.matmul(
                        _b[:, :], ws1b[:, _kc * 1024 + _oc * 128:_kc * 1024 + (_oc + 1) * 128],
                        xb3[_kc][:, :], start=(_kc == 0), stop=(_kc == NC8 - 1)))
                for kc in range(NC8):
                    S.op("tensor", lambda t, _oc=oc, _kc=kc, _b=b3: t.matmul(
                        _b[:, :], ws3b[:, _kc * 1024 + _oc * 128:_kc * 1024 + (_oc + 1) * 128],
                        xb3[_kc][:, :], start=(_kc == 0), stop=(_kc == NC8 - 1)))
                S.wait("scalar", PE)
                S.wait("scalar", VE, s1f_rel)
                S.op("scalar", lambda s, _b=b1: s.activation(s1f[:, :], _b[:, :], AF.Silu))
                S.wait("vector", SC)
                wait_rel("vector", ("ashb", oc))
                S.op("vector", lambda v, _oc=oc, _b=b3: v.tensor_tensor(
                    ashb[_oc][:, :], s1f[:, :], _b[:, :], OP.mult))
                s1f_rel = VE.n
                rel[("ph", oc % 2)] = ((VE, VE.n), (SC, SC.n))
            rel["xtiles"] = ((PE, PE.n),)
            for j in range(4):
                L = ptok[:, j * 16:(j + 1) * 16]
                S.wait("vector", SC)
                S.wait("vector", PE, ct_rel)
                S.op("vector", lambda v, _L=L: v.reduce_max(gs[:, 0:1], _L, axis=mybir.AxisListType.X))
                S.op("vector", lambda v: v.tensor_scalar(gs[:, 1:2], gs[:, 0:1], -1.0, None, OP.mult))
                S.wait("scalar", VE)
                S.op("scalar", lambda s, _L=L: s.activation(e16[:, :], _L, AF.Exp, bias=gs[:, 1:2]))
                S.wait("vector", SC)
                S.op("vector", lambda v: v.reduce_sum(gs[:, 2:3], e16[:, :], axis=mybir.AxisListType.X))
                S.op("vector", lambda v: v.reciprocal(gs[:, 3:4], gs[:, 2:3]))
                S.op("vector", lambda v: v.tensor_scalar(p16[:, :], e16[:, :], gs[:, 3:4], None, OP.mult))
                S.op("vector", lambda v: v.reduce_max(
                    gsc[:, :], p16[:, :].rearrange("p (g e) -> p g e", g=4), axis=mybir.AxisListType.X))
                S.op("vector", lambda v: v.reduce_max(gs[:, 4:5], gsc[:, :], axis=mybir.AxisListType.X))
                S.op("vector", lambda v: v.tensor_scalar(cur[:, 0:4], gsc[:, :], gs[:, 4:5], None, OP.is_ge))
                S.op("vector", lambda v: v.scalar_tensor_tensor(
                    cur[:, 0:4], cur[:, 0:4], -1e30, gsc[:, :], OP.mult, OP.add))
                S.op("vector", lambda v: v.reduce_max(gs[:, 5:6], cur[:, 0:4], axis=mybir.AxisListType.X))
                S.op("vector", lambda v: v.tensor_scalar(gsc[:, :], gsc[:, :], gs[:, 5:6], None, OP.is_ge))
                S.op("vector", lambda v: v.tensor_copy(
                    gm16[:, :].rearrange("p (g e) -> p g e", g=4),
                    gsc[:, :].broadcast_to((128, 4, 4))))
                S.op("vector", lambda v: v.tensor_tensor(m16[:, :], p16[:, :], gm16[:, :], OP.mult))
                S.op("vector", lambda v: v.tensor_copy(cur[:, :], m16[:, :]))
                for it in range(3):
                    S.op("vector", lambda v: v.reduce_max(gs[:, 6:7], cur[:, :], axis=mybir.AxisListType.X))
                    S.op("vector", lambda v: v.tensor_scalar(e16[:, :], cur[:, :], gs[:, 6:7], None, OP.is_ge))
                    S.op("vector", lambda v: v.scalar_tensor_tensor(
                        cur[:, :], e16[:, :], -1e30, cur[:, :], OP.mult, OP.add))
                S.op("vector", lambda v: v.reduce_max(gs[:, 6:7], cur[:, :], axis=mybir.AxisListType.X))
                S.op("vector", lambda v: v.tensor_scalar(e16[:, :], m16[:, :], gs[:, 6:7], None, OP.is_ge))
                S.op("vector", lambda v, _j=j: v.tensor_tensor(ctok[:, _j * 16:(_j + 1) * 16], p16[:, :], e16[:, :], OP.mult))
            S.wait("tensor", VE)
            wait_rel("tensor", "pt2")
            for j in range(4):
                S.op("tensor", lambda t, _j=j: t.transpose(
                    pt2[0:16, _j * 128:(_j + 1) * 128], ctok[:, _j * 16:(_j + 1) * 16], identf[:, :]))
            ct_rel = PE.n
            rel["ptok"] = ((VE, VE.n),)
            S.wait("scalar", PE)
            wait_rel("scalar", "ctr")
            S.op("scalar", lambda s: s.activation(ctr_s[:, :], pt2[0:16, :], AF.Copy))
            rel["pt2"] = ((SC, SC.n),)
            S.wait("gpsimd", SC)
            cw = gq.wave()
            S.dma("gpsimd", cw, A["ctr"][:, lt * LT:(lt + 1) * LT], ctr_s[:, :])
            rel["ctr"] = ((cw, cw.n),)
            S.wait("gpsimd", VE)
            aw = gq.wave()
            for c in range(NC8):
                S.dma("gpsimd", aw, A["ash"][c * 128:(c + 1) * 128, lt * LT:(lt + 1) * LT], ashb[c][:, :])
            for c in range(NC8):
                rel[("ashb", c)] = ((aw, aw.n),)
        for s_ in gq.sems:
            S.wait("gpsimd", s_)
        S.flush()



def phase7(nc, A):
    with ExitStack() as stack:
        S = Sched(nc, stack, "p7")
        w1b = [stack.enter_context(nc.sbuf_tensor(f"p7_w1_{p}", [128, NC8 * 512], BF16)) for p in range(2)]
        w3b = [stack.enter_context(nc.sbuf_tensor(f"p7_w3_{p}", [128, NC8 * 512], BF16)) for p in range(2)]
        xb3 = [stack.enter_context(nc.sbuf_tensor(f"p7_xb3{c}", [128, T], BF16)) for c in range(NC8)]
        ctr_s = stack.enter_context(nc.sbuf_tensor("p7_ctr", [16, T], BF16))
        sel = [stack.enter_context(nc.sbuf_tensor(f"p7_sel{e}", [16, 128], BF16)) for e in range(16)]
        ones16 = stack.enter_context(nc.sbuf_tensor("p7_ones16", [16, 128], BF16))
        s1f = stack.enter_context(nc.sbuf_tensor("p7_s1f", [128, LT], F32))
        af = stack.enter_context(nc.sbuf_tensor("p7_af", [128, LT], F32))
        ab = [[stack.enter_context(nc.sbuf_tensor(f"p7_ab{p}_{i}", [128, LT], BF16)) for i in range(4)]
              for p in range(2)]
        ph = [stack.enter_context(nc.psum_tensor(f"p7_ph{i}", [128, LT], F32)) for i in range(4)]
        pc = [stack.enter_context(nc.psum_tensor(f"p7_pc{i}", [128, LT], F32)) for i in range(2)]

        lq = S.dmaq("sync", "lq", 3)
        gq = S.dmaq("gpsimd", "gq", 4)
        PE, SC, VE, PL = S.eng["tensor"], S.eng["scalar"], S.eng["vector"], S.eng["gpsimd"]

        rel = {}

        def wait_rel(eng, key):
            for sem, cnt in rel.get(key, ()):
                S.wait(eng, sem, cnt)

        S.op("gpsimd", lambda g: g.memset(ones16[:, :], 1.0))
        for e in range(16):
            S.op("gpsimd", lambda g, _e=e: g.affine_select(
                sel[_e][:, :], ones16[:, :], pattern=[[0, 128]], compare_op=OP.is_equal,
                fill=0.0, base=-_e, channel_multiplier=1))
        cld = lq.wave()
        S.dma("sync", cld, ctr_s[:, :], A["ctr"][:, :])
        for c in range(NC8):
            S.dma("sync", cld, xb3[c][:, :], A["xb3"][c * 128:(c + 1) * 128, :])
        ew = {0: gq.wave()}
        for kc in range(NC8):
            S.dma("gpsimd", ew[0], w1b[0][:, kc * 512:(kc + 1) * 512], A["L1_ffn_w1"][0, kc * 128:(kc + 1) * 128, :])
            S.dma("gpsimd", ew[0], w3b[0][:, kc * 512:(kc + 1) * 512], A["L1_ffn_w3"][0, kc * 128:(kc + 1) * 128, :])

        wrel = {}
        s1f_rel = 0
        for e in range(16):
            if e + 1 < 16:
                nset = (e + 1) % 2
                w = gq.wave()
                ew[e + 1] = w
                if e >= 1:
                    S.wait("gpsimd", PE, wrel[e - 1])
                for kc in range(NC8):
                    S.dma("gpsimd", w, w1b[nset][:, kc * 512:(kc + 1) * 512],
                          A["L1_ffn_w1"][e + 1, kc * 128:(kc + 1) * 128, :])
                    S.dma("gpsimd", w, w3b[nset][:, kc * 512:(kc + 1) * 512],
                          A["L1_ffn_w3"][e + 1, kc * 128:(kc + 1) * 128, :])
            wset = e % 2
            for lt in range(NLT):
                glob = e * NLT + lt
                par = glob % 2
                if e == 0 and lt == 0:
                    S.wait("tensor", cld)
                    S.wait("tensor", PL)
                if lt == 0:
                    S.wait("tensor", ew[e])
                pcb = pc[par]
                wait_rel("tensor", ("pc", par))
                S.op("tensor", lambda t, _e=e, _lt=lt, _b=pcb: t.matmul(
                    _b[:, :], sel[_e][:, :], ctr_s[:, _lt * LT:(_lt + 1) * LT], start=True, stop=True))
                for ic in range(4):
                    b1 = ph[ic % 2]
                    b3 = ph[2 + ic % 2]
                    wait_rel("tensor", ("ph", ic % 2))
                    for kc in range(NC8):
                        S.op("tensor", lambda t, _ic=ic, _kc=kc, _b=b1, _lt=lt, _w=wset: t.matmul(
                            _b[:, :], w1b[_w][:, _kc * 512 + _ic * 128:_kc * 512 + (_ic + 1) * 128],
                            xb3[_kc][:, _lt * LT:(_lt + 1) * LT], start=(_kc == 0), stop=(_kc == NC8 - 1)))
                    for kc in range(NC8):
                        S.op("tensor", lambda t, _ic=ic, _kc=kc, _b=b3, _lt=lt, _w=wset: t.matmul(
                            _b[:, :], w3b[_w][:, _kc * 512 + _ic * 128:_kc * 512 + (_ic + 1) * 128],
                            xb3[_kc][:, _lt * LT:(_lt + 1) * LT], start=(_kc == 0), stop=(_kc == NC8 - 1)))
                    S.wait("scalar", PE)
                    S.wait("scalar", VE, s1f_rel)
                    S.op("scalar", lambda s, _b=b1: s.activation(s1f[:, :], _b[:, :], AF.Silu))
                    S.wait("vector", SC)
                    S.op("vector", lambda v, _b=b3: v.tensor_tensor(af[:, :], s1f[:, :], _b[:, :], OP.mult))
                    s1f_rel = VE.n
                    if ("ab", par) in rel:
                        wait_rel("vector", ("ab", par))
                        rel.pop(("ab", par))
                    S.op("vector", lambda v, _ic=ic, _p=par, _pc=pcb: v.tensor_tensor(
                        ab[_p][_ic][:, :], af[:, :], _pc[:, :], OP.mult))
                    rel[("ph", ic % 2)] = ((VE, VE.n), (SC, SC.n))
                rel[("pc", par)] = ((VE, VE.n),)
                S.wait("gpsimd", VE)
                aw = gq.wave()
                for ic in range(4):
                    S.dma("gpsimd", aw, A["a_all"][e, ic * 128:(ic + 1) * 128, lt * LT:(lt + 1) * LT],
                          ab[par][ic][:, :])
                rel[("ab", par)] = ((aw, aw.n),)
            wrel[e] = PE.n
        for s_ in gq.sems:
            S.wait("gpsimd", s_)
        S.flush()



def phase8(nc, A):
    with ExitStack() as stack:
        S = Sched(nc, stack, "p8")
        w2b = stack.enter_context(nc.sbuf_tensor("p8_w2", [128, 32 * 1024], BF16))
        ws2b = stack.enter_context(nc.sbuf_tensor("p8_ws2", [128, NC8 * 1024], BF16))
        at = [[stack.enter_context(nc.sbuf_tensor(f"p8_a{e}_{i}", [128, LT], BF16)) for i in range(4)]
              for e in range(8)]
        ashb = [stack.enter_context(nc.sbuf_tensor(f"p8_ash{c}", [128, LT], BF16)) for c in range(NC8)]
        x3f = [stack.enter_context(nc.sbuf_tensor(f"p8_x3f{c}", [128, LT], F32)) for c in range(NC8)]
        ypt = [stack.enter_context(nc.sbuf_tensor(f"p8_ypt{c}", [128, LT], F32)) for c in range(NC8)]
        dlt = [stack.enter_context(nc.sbuf_tensor(f"p8_dlt{c}", [128, LT], F32)) for c in range(NC8)]
        x4t = dlt
        stg = stack.enter_context(nc.sbuf_tensor("p8_stg", [128, 1024], F32))
        onesc = stack.enter_context(nc.sbuf_tensor("p8_onesc", [128, 1], BF16))
        onesr = stack.enter_context(nc.sbuf_tensor("p8_onesr", [1, 128], BF16))
        lnscr = make_ln_scratch(S, stack)
        identf, idrdy = make_identity(S, stack, F32, "f")
        py = [stack.enter_context(nc.psum_tensor(f"p8_py{i}", [128, LT], F32)) for i in range(3)]
        ptr = [stack.enter_context(nc.psum_tensor(f"p8_ptr{i}", [128, LT], F32)) for i in range(2)]
        pl = [stack.enter_context(nc.psum_tensor(f"p8_pl{i}", [128, LT], F32)) for i in range(3)]

        lq = S.dmaq("sync", "lq", 4)
        gq = S.dmaq("gpsimd", "gq", 4)
        PE, SC, VE, PL = S.eng["tensor"], S.eng["scalar"], S.eng["vector"], S.eng["gpsimd"]
        rel = {}

        def wait_rel(eng, key):
            for sem, cnt in rel.get(key, ()):
                S.wait(eng, sem, cnt)

        S.op("vector", lambda v: v.memset(onesc[:], 1.0))
        S.op("vector", lambda v: v.memset(onesr[:], 1.0))

        for half in range(2):
            if half == 1:
                for s_ in gq.sems:
                    S.wait("sync", s_)
            ww = gq.wave()
            if half == 1:
                S.wait("gpsimd", PE)
            for eg in range(8):
                e = half * 8 + eg
                for ic in range(4):
                    S.dma("gpsimd", ww, w2b[:, (eg * 4 + ic) * 1024:(eg * 4 + ic + 1) * 1024],
                          A["L1_ffn_w2"][e, ic * 128:(ic + 1) * 128, :])
            if half == 0:
                for c in range(NC8):
                    S.dma("gpsimd", ww, ws2b[:, c * 1024:(c + 1) * 1024],
                          A["L1_ffn_ws2"][c * 128:(c + 1) * 128, :])
            for lt in range(NLT):
                tw = lq.wave()
                wait_rel("sync", "atiles")
                for eg in range(8):
                    e = half * 8 + eg
                    for ic in range(4):
                        S.dma("sync", tw, at[eg][ic][:, :],
                              A["a_all"][e, ic * 128:(ic + 1) * 128, lt * LT:(lt + 1) * LT])
                if half == 0:
                    wait_rel("sync", "ashb")
                    for c in range(NC8):
                        S.dma("sync", tw, ashb[c][:, :], A["ash"][c * 128:(c + 1) * 128, lt * LT:(lt + 1) * LT])
                else:
                    wait_rel("sync", "yxt")
                    for c in range(NC8):
                        S.dma("sync", tw, ypt[c][:, :], A["ypart"][c * 128:(c + 1) * 128, lt * LT:(lt + 1) * LT])
                        S.dma("sync", tw, x3f[c][:, :], A["x3"][c * 128:(c + 1) * 128, lt * LT:(lt + 1) * LT])
                S.wait("tensor", tw)
                if lt == 0:
                    S.wait("tensor", ww)
                    S.wait("tensor", PL, idrdy)
                    S.wait("tensor", VE)
                for oc in range(NC8):
                    bank = py[oc % 3]
                    wait_rel("tensor", ("py", oc % 3))
                    first = True
                    for eg in range(8):
                        for ic in range(4):
                            last = (half == 1 and eg == 7 and ic == 3)
                            S.op("tensor", lambda t, _eg=eg, _ic=ic, _oc=oc, _b=bank, _f=first, _l=last: t.matmul(
                                _b[:, :], w2b[:, (_eg * 4 + _ic) * 1024 + _oc * 128:(_eg * 4 + _ic) * 1024 + (_oc + 1) * 128],
                                at[_eg][_ic][:, :], start=_f, stop=_l))
                            first = False
                    if half == 0:
                        for kc in range(NC8):
                            S.op("tensor", lambda t, _kc=kc, _oc=oc, _b=bank: t.matmul(
                                _b[:, :], ws2b[:, _kc * 1024 + _oc * 128:_kc * 1024 + (_oc + 1) * 128],
                                ashb[_kc][:, :], start=False, stop=(_kc == NC8 - 1)))
                    S.wait("scalar", PE)
                    if half == 0:
                        wait_rel("scalar", ("ypt", oc))
                        S.op("scalar", lambda s, _oc=oc, _b=bank: s.activation(ypt[_oc][:, :], _b[:, :], AF.Copy))
                    else:
                        wait_rel("scalar", ("dlt", oc))
                        S.op("scalar", lambda s, _oc=oc, _b=bank: s.activation(dlt[_oc][:, :], _b[:, :], AF.Copy))
                    rel[("py", oc % 3)] = ((SC, SC.n),)
                rel["atiles"] = ((PE, PE.n),)
                if half == 0:
                    rel["ashb"] = ((PE, PE.n),)
                    S.wait("gpsimd", SC)
                    yw = gq.wave()
                    for c in range(NC8):
                        S.dma("gpsimd", yw, A["ypart"][c * 128:(c + 1) * 128, lt * LT:(lt + 1) * LT], ypt[c][:, :])
                    for c in range(NC8):
                        rel[("ypt", c)] = ((yw, yw.n),)
                else:
                    S.wait("vector", SC)
                    for c in range(NC8):
                        S.op("vector", lambda v, _c=c: v.tensor_tensor(dlt[_c][:, :], dlt[_c][:, :], ypt[_c][:, :], OP.add))
                        S.op("vector", lambda v, _c=c: v.tensor_tensor(dlt[_c][:, :], dlt[_c][:, :], x3f[_c][:, :], OP.add))
                    rel["yxt"] = ((VE, VE.n),)
                    S.wait("scalar", VE)
                    emit_ln(S, lnscr, dlt, onesc, onesr, pl[0], pl[1], pl[2], pl[0],
                            x4t, None)
                    S.wait("tensor", VE)
                    for ts in range(4):
                        for c in range(NC8):
                            bank = ptr[c % 2]
                            wait_rel("tensor", ("ptr", c % 2))
                            S.wait("tensor", SC)
                            S.op("tensor", lambda t, _c=c, _ts=ts, _b=bank: t.transpose(
                                _b[:, 0:128], x4t[_c][:, _ts * 128:(_ts + 1) * 128], identf[:, :]))
                            S.wait("scalar", PE)
                            wait_rel("scalar", "stg")
                            S.op("scalar", lambda s, _c=c, _b=bank: s.activation(
                                stg[:, _c * 128:(_c + 1) * 128], _b[:, 0:128], AF.Copy))
                            rel[("ptr", c % 2)] = ((SC, SC.n),)
                        S.wait("gpsimd", SC)
                        ow = gq.wave()
                        S.dma("gpsimd", ow, A["out"][lt * LT + ts * 128:lt * LT + (ts + 1) * 128, :], stg[:, :])
                        rel["stg"] = ((ow, ow.n),)
                    for c in range(NC8):
                        rel[("dlt", c)] = ((PE, PE.n),)
        for s_ in gq.sems:
            S.wait("gpsimd", s_)
        S.flush()


PHASE_FNS = {0: phase0, 1: phase1, 2: phase2, 3: phase3, 4: phase4, 5: phase5, 6: phase6, 7: phase7, 8: phase8}


def build(phases):
    CROSS.clear()
    DRAM_KINDS.clear()
    nc = bass.Bass("TRN2", target_bir_lowering=False)
    present = set(phases)
    A = {}
    A["x_in"] = declare_dram(nc, "x_in", (T, D), F32, "ExternalInput")
    for name, shp in WEIGHTS.items():
        A[name] = declare_dram(nc, name, shp, F32, "ExternalInput")
    A["out"] = declare_dram(nc, "out", (T, D), F32, "ExternalOutput")
    for name, (shp, dt, prod) in INTERMEDIATES.items():
        readers = [p for p in present if name in PHASE_READS.get(p, [])]
        written = prod in present
        if written and readers:
            kind = "Internal"
        elif written:
            kind = "ExternalOutput"
        elif readers:
            kind = "ExternalInput"
        else:
            continue
        A[name] = declare_dram(nc, name, shp, dt, kind)
    for p in phases:
        PHASE_FNS[p](nc, A)
    return nc, A


def make_in_maps(inputs, extra=None):
    x = np.ascontiguousarray(np.asarray(inputs["L0_attn_x"], dtype=np.float32))
    in_maps = []
    for b in range(8):
        m = {"x_in": np.ascontiguousarray(x[b])}
        for name, shp in WEIGHTS.items():
            m[name] = np.ascontiguousarray(np.asarray(inputs[name], dtype=np.float32).reshape(shp))
        if extra is not None:
            m.update(extra[b])
        in_maps.append(m)
    return in_maps


def kernel(**inputs):
    phases = sorted(PHASE_FNS.keys())
    nc, A = build(phases)
    in_maps = make_in_maps(inputs)
    res = run_bass_kernel_spmd(nc, in_maps, core_ids=list(range(8)))
    return np.stack([np.asarray(r["out"]) for r in res.results], axis=0)


# version 06
# speedup vs baseline: 1.0067x; 1.0067x over previous
import math
import os
from contextlib import ExitStack

import numpy as np
import concourse.bass as bass
import concourse.mybir as mybir
from concourse.bass_utils import run_bass_kernel_spmd

F32 = mybir.dt.float32
BF16 = mybir.dt.bfloat16
AF = mybir.ActivationFunctionType
OP = mybir.AluOpType

T = 4096
D = 1024
NC8 = 8
LT = 512
NLT = T // LT
H = 16
DH = 64
EPS = 1e-5

ENGINES = ("sync", "tensor", "scalar", "vector", "gpsimd")


class Sem:
    def __init__(self, stack, nc, name):
        self.h = nc.alloc_semaphore(name)
        self.n = 0


class DmaQ:

    def __init__(self, S, eng, tag, n=3):
        self.S = S
        self.eng = eng
        self.sems = [S.sem(f"{tag}{i}") for i in range(n)]
        self.n = n
        self.w = 0

    def wave(self):
        s = self.sems[self.w % self.n]
        if self.w >= self.n:
            self.S.wait(self.eng, s)
        self.w += 1
        return s


AUTO_SERIAL = ("vector", "scalar", "gpsimd")


class Sched:

    def __init__(self, nc, stack, name):
        self.nc = nc
        self.stack = stack
        self.name = name
        self.ops = {e: [] for e in ENGINES}
        self._semid = 0
        self.sems = []
        self.eng = {e: self.sem(e[:2]) for e in ("tensor", "scalar", "vector", "gpsimd")}

    def sem(self, tag=""):
        self._semid += 1
        s = Sem(self.stack, self.nc, f"{self.name}_s{self._semid}{tag}")
        self.sems.append(s)
        return s

    def wait(self, eng, sem, val=None):
        v = sem.n if val is None else val
        if v > 0:
            self.ops[eng].append(lambda e, _h=sem.h, _v=v: e.wait_ge(_h, _v))

    def op(self, eng, fn):
        s = self.eng[eng]
        if eng in AUTO_SERIAL and s.n > 0:
            self.wait(eng, s)
        self.ops[eng].append(lambda e, _f=fn, _h=s.h: _f(e).then_inc(_h, 1))
        s.n += 1
        return s.n

    def dmaq(self, eng, tag, n=3):
        return DmaQ(self, eng, tag, n)

    def dma(self, eng, sem, out, in_, **kw):
        self.ops[eng].append(lambda e, _o=out, _i=in_, _h=sem.h, _kw=kw: e.dma_start(
            out=_o, in_=_i, **_kw).then_inc(_h, 16))
        sem.n += 16

    def barrier(self, engines=ENGINES):
        for e in engines:
            for s in self.sems:
                if not (e in self.eng and self.eng[e] is s):
                    self.wait(e, s)

    def flush(self):
        with self.nc.Block() as block:
            for e in ENGINES:
                if self.ops[e]:
                    def body(eng, _l=tuple(self.ops[e])):
                        for f in _l:
                            f(eng)
                    getattr(block, e)(body)
        self.ops = {e: [] for e in ENGINES}


DRAM_KINDS = {}
CROSS = {}


def declare_dram(nc, name, shape, dtype, kind):
    DRAM_KINDS[name] = kind
    return nc.dram_tensor(name, list(shape), dtype, kind=kind).ap()


WEIGHTS = {
    "L0_attn_Wqkv_w": (3072, 1024), "L0_attn_Wqkv_b": (1, 3072),
    "L0_attn_out_w": (1024, 1024), "L0_attn_out_b": (1, 1024),
    "L0_ffn_gate_W": (8, 1024), "L0_ffn_expert_W": (8, 1024, 1024),
    "L0_ffn_out_W": (1024, 1024),
    "L1_attn_ln1_g": (1, 1024), "L1_attn_ln1_b": (1, 1024),
    "L1_attn_ln2_g": (1, 1024), "L1_attn_ln2_b": (1, 1024),
    "L1_attn_tm_k": (1, 1024), "L1_attn_tm_v": (1, 1024), "L1_attn_tm_r": (1, 1024),
    "L1_attn_time_decay": (1, 1024), "L1_attn_time_first": (1, 1024),
    "L1_attn_Wk": (1024, 1024), "L1_attn_Wv": (1024, 1024),
    "L1_attn_Wr": (1024, 1024), "L1_attn_Wo": (1024, 1024),
    "L1_attn_fm_k": (1, 1024), "L1_attn_fm_r": (1, 1024),
    "L1_attn_Fk": (4096, 1024), "L1_attn_Fr": (1024, 1024), "L1_attn_Fv": (1024, 4096),
    "L1_ffn_gate_w": (16, 1024),
    "L1_ffn_w1": (16, 1024, 512), "L1_ffn_w2": (16, 512, 1024), "L1_ffn_w3": (16, 1024, 512),
    "L1_ffn_ws1": (1024, 1024), "L1_ffn_ws2": (1024, 1024), "L1_ffn_ws3": (1024, 1024),
}

INTERMEDIATES = {
    "x0": ((D, T), F32, 0), "xb0": ((D, T), BF16, 0),
    "wqkv_bf": ((3072, 1024), BF16, 0),
    "outw_bf": ((1024, 1024), BF16, 0),
    "expw0_bf": ((1024, 1024), BF16, 0),
    "expw1_bf": ((1024, 1024), BF16, 0),
    "outW_bf": ((1024, 1024), BF16, 0),
    "wk_bf": ((1024, 1024), BF16, 0), "wv_bf": ((1024, 1024), BF16, 0),
    "wr_bf": ((1024, 1024), BF16, 0), "wo_bf": ((1024, 1024), BF16, 0),
    "fk_bf": ((4096, 1024), BF16, 0), "fr_bf": ((1024, 1024), BF16, 0),
    "fv_bf": ((1024, 4096), BF16, 0),
    "x1": ((D, T), F32, 1), "xb1": ((D, T), BF16, 1),
    "x2": ((D, T), F32, 2), "xb2": ((D, T), BF16, 2),
    "kTf": ((D, T), F32, 3), "vTf": ((D, T), F32, 3), "srT": ((D, T), BF16, 4),
    "xrT": ((D, T), BF16, 3),
    "xm": ((D, T), F32, 4), "xmb": ((D, T), BF16, 4),
    "x3": ((D, T), F32, 5), "xb3": ((D, T), BF16, 5),
    "gkb": ((D, T), BF16, 5), "grb": ((D, T), BF16, 5),
    "kvp": ((D, T), F32, 5), "kvT": ((D, T), F32, 5),
    "ctr": ((16, T), BF16, 6), "ash": ((D, T), BF16, 6),
    "a_all": ((16, 512, T), BF16, 7),
    "ypart": ((D, T), F32, 8),
}

PHASE_READS = {
    1: ["x0", "xb0", "wqkv_bf", "outw_bf"],
    2: ["x1", "xb1", "expw0_bf", "expw1_bf", "outW_bf"],
    3: ["x2", "xb2", "wk_bf", "wv_bf", "wr_bf"],
    4: ["x2", "kTf", "vTf", "xrT", "srT", "wr_bf", "wo_bf"],
    5: ["x2", "xm", "fk_bf", "fr_bf", "fv_bf", "gkb", "grb", "kvp", "kvT"],
    6: ["x3", "xb3"],
    7: ["xb3", "ctr"],
    8: ["x3", "a_all", "ash", "ypart"],
}



def make_identity(S, stack, dtype, tag):
    nc = S.nc
    ones = stack.enter_context(nc.sbuf_tensor(f"{S.name}_idsrc_{tag}", [128, 128], dtype))
    ident = stack.enter_context(nc.sbuf_tensor(f"{S.name}_ident_{tag}", [128, 128], dtype))
    S.op("gpsimd", lambda g: g.memset(ones[:], 1.0))
    S.op("gpsimd", lambda g: g.affine_select(
        ident[:], ones[:], pattern=[[-1, 128]], compare_op=OP.is_equal, fill=0.0,
        base=0, channel_multiplier=1))
    return ident, S.eng["gpsimd"].n


def load_vecs_fm(S, stack, specs, identity, id_ready, psum, ldq):
    nc = S.nc
    nmax = max(n for _, _, n in specs)
    row = stack.enter_context(nc.sbuf_tensor(f"{S.name}_vrow", [1, nmax], F32))
    outs = {}
    col = 0
    S.wait("tensor", S.eng["gpsimd"], id_ready)
    S.wait("tensor", S.eng["scalar"])
    for key, ap, n in specs:
        nch = n // 128
        w = ldq.wave()
        S.wait("sync", S.eng["tensor"])
        S.dma(ldq.eng, w, row[0:1, 0:n], ap[0:1, :])
        S.wait("tensor", w)
        for c in range(nch):
            S.op("tensor", lambda t, _c=c, _col=col: t.transpose(
                psum[:, _col + _c:_col + _c + 1], row[0:1, _c * 128:(_c + 1) * 128],
                identity[0:1, 0:1]))
        outs[key] = stack.enter_context(nc.sbuf_tensor(f"{S.name}_vfm_{key}", [128, nch], F32))
        S.wait("scalar", S.eng["tensor"])
        S.op("scalar", lambda s, _k=key, _n=nch, _col=col: s.activation(
            outs[_k][:, 0:_n], psum[:, _col:_col + _n], AF.Copy))
        col += nch
    return outs


def make_ln_scratch(S, stack):
    nc = S.nc
    scr = {
        "rb": [stack.enter_context(nc.sbuf_tensor(f"{S.name}_lnrb{c}", [128, LT], BF16)) for c in range(NC8)],
        "sq": [stack.enter_context(nc.sbuf_tensor(f"{S.name}_lnsq{c}", [128, LT], BF16)) for c in range(NC8)],
        "mb": stack.enter_context(nc.sbuf_tensor(f"{S.name}_lnmb", [128, LT], F32)),
        "rs": stack.enter_context(nc.sbuf_tensor(f"{S.name}_lnrs", [128, LT], F32)),
        "rows": stack.enter_context(nc.sbuf_tensor(f"{S.name}_lnrows", [1, 4 * LT], F32)),
        "epsr": stack.enter_context(nc.sbuf_tensor(f"{S.name}_lneps", [1, 1], F32)),
        "hbm": declare_dram(nc, f"{S.name}_lnrow_hbm", (2, 2 * LT), F32, "Internal"),
        "rel": {},
    }
    S.op("vector", lambda v: v.memset(scr["epsr"][:, :], EPS))
    return scr


def emit_ln(S, scr, r_tiles, onesc, onesr, bsum, bsumsq, out_f32, out_b16, ldq,
            tix, g_vec=None, b_vec=None):
    rb, sq, mb, rs, rows = scr["rb"], scr["sq"], scr["mb"], scr["rs"], scr["rows"]
    lrel = scr["rel"]
    PE, SC, VE = S.eng["tensor"], S.eng["scalar"], S.eng["vector"]
    par = tix % 2
    for sem, cnt in lrel.get(("rbsq", 0), ()):
        S.wait("vector", sem, cnt)
        S.wait("scalar", sem, cnt)
    for c in range(NC8):
        S.op("vector", lambda v, _c=c: v.tensor_copy(rb[_c][:, :], r_tiles[_c][:, :]))
        S.op("scalar", lambda s, _c=c: s.activation(sq[_c][:, :], r_tiles[_c][:, :], AF.Square))
    S.wait("tensor", VE)
    S.wait("tensor", SC)
    for c in range(NC8):
        S.op("tensor", lambda t, _c=c: t.matmul(
            bsum[0:1, :], onesc[:, 0:1], rb[_c][:, :], start=(_c == 0), stop=(_c == NC8 - 1)))
        S.op("tensor", lambda t, _c=c: t.matmul(
            bsumsq[0:1, :], onesc[:, 0:1], sq[_c][:, :], start=(_c == 0), stop=(_c == NC8 - 1)))
    lrel[("rbsq", 0)] = ((PE, PE.n),)
    mean = rows[0:1, 0:LT]
    msq = rows[0:1, LT:2 * LT]
    var = rows[0:1, 2 * LT:3 * LT]
    rstd = rows[0:1, 3 * LT:4 * LT]
    S.wait("vector", PE)
    S.op("vector", lambda v: v.tensor_scalar(mean, bsum[0:1, :], 1.0 / D, None, OP.mult))
    S.op("vector", lambda v: v.tensor_tensor(msq, mean, mean, OP.mult))
    S.op("vector", lambda v: v.scalar_tensor_tensor(
        var, bsumsq[0:1, :], 1.0 / D, msq, OP.mult, OP.subtract))
    S.wait("scalar", VE)
    S.op("scalar", lambda s: s.activation(var, var, AF.Ln, bias=scr["epsr"][0:1, 0:1], scale=1.0))
    S.op("scalar", lambda s: s.activation(rstd, var, AF.Exp, scale=-0.5))
    w1 = ldq.wave()
    S.wait("sync", SC)
    S.dma("sync", w1, scr["hbm"][par:par + 1, 0:LT], mean)
    S.dma("sync", w1, scr["hbm"][par:par + 1, LT:2 * LT], rstd)
    w2 = ldq.wave()
    S.wait("sync", w1)
    for sem, cnt in lrel.get(("mbrs", 0), ()):
        S.wait("sync", sem, cnt)
    S.dma("sync", w2, mb[:, :], scr["hbm"][par:par + 1, 0:LT].broadcast_to((128, LT)))
    S.dma("sync", w2, rs[:, :], scr["hbm"][par:par + 1, LT:2 * LT].broadcast_to((128, LT)))
    S.wait("vector", w2)
    for c in range(NC8):
        S.op("vector", lambda v, _c=c: v.tensor_tensor(out_f32[_c][:, :], r_tiles[_c][:, :], mb[:, :], OP.subtract))
        if g_vec is not None:
            S.op("vector", lambda v, _c=c: v.tensor_tensor(out_f32[_c][:, :], out_f32[_c][:, :], rs[:, :], OP.mult))
            S.op("vector", lambda v, _c=c: v.tensor_scalar(
                out_f32[_c][:, :], out_f32[_c][:, :], g_vec[:, _c:_c + 1], b_vec[:, _c:_c + 1], OP.mult, OP.add))
        else:
            S.op("vector", lambda v, _c=c: v.tensor_tensor(out_f32[_c][:, :], out_f32[_c][:, :], rs[:, :], OP.mult))
        if out_b16 is not None:
            S.op("vector", lambda v, _c=c: v.tensor_copy(out_b16[_c][:, :], out_f32[_c][:, :]))
    lrel[("mbrs", 0)] = ((VE, VE.n),)



def phase0(nc, A):
    with ExitStack() as stack:
        S = Sched(nc, stack, "p0")
        gq = S.dmaq("gpsimd", "gq", 3)
        cq = S.dmaq("gpsimd", "cq", 1)
        cast = cq.wave()
        casts = [
            ("L0_attn_Wqkv_w", "wqkv_bf"), ("L0_attn_out_w", "outw_bf"),
            ("L0_ffn_out_W", "outW_bf"),
            ("L1_attn_Wk", "wk_bf"), ("L1_attn_Wv", "wv_bf"),
            ("L1_attn_Wr", "wr_bf"), ("L1_attn_Wo", "wo_bf"),
            ("L1_attn_Fk", "fk_bf"), ("L1_attn_Fr", "fr_bf"), ("L1_attn_Fv", "fv_bf"),
        ]
        for src, dst in casts:
            S.dma("gpsimd", cast, A[dst][:, :], A[src][:, :])
        S.dma("gpsimd", cast, A["expw0_bf"][:, :], A["L0_ffn_expert_W"][0, :, :])
        S.dma("gpsimd", cast, A["expw1_bf"][:, :], A["L0_ffn_expert_W"][1, :, :])
        CROSS["cast"] = (cast, cast.n)

        identf, idrdy = make_identity(S, stack, F32, "f")
        xin_s = [stack.enter_context(nc.sbuf_tensor(f"p0_xin{p}", [128, 1024], F32)) for p in range(2)]
        stg = [stack.enter_context(nc.sbuf_tensor(f"p0_stg{c}", [128, 512], F32)) for c in range(NC8)]
        stgb = [stack.enter_context(nc.sbuf_tensor(f"p0_stgb{c}", [128, 512], BF16)) for c in range(NC8)]
        pt = [stack.enter_context(nc.psum_tensor(f"p0_pt{i}", [128, 512], F32)) for i in range(2)]
        lq = S.dmaq("sync", "lq", 4)
        PE, SC, VE, PL = S.eng["tensor"], S.eng["scalar"], S.eng["vector"], S.eng["gpsimd"]
        S.wait("tensor", PL, idrdy)
        rel = {}

        def wait_rel(eng, key):
            for sem, cnt in rel.get(key, ()):
                S.wait(eng, sem, cnt)

        stw = None
        for grp in range(8):
            for j in range(4):
                i = grp * 4 + j
                xin = xin_s[i % 2]
                wait_rel("sync", ("xin", i % 2))
                w = lq.wave()
                S.dma("sync", w, xin[:, :], A["x_in"][i * 128:(i + 1) * 128, :])
                S.wait("tensor", w)
                if grp > 0 and j == 0:
                    S.wait("scalar", stw)
                    S.wait("scalar", VE)
                for c in range(NC8):
                    reg = ((c % 2) * 2 + (j % 2))
                    bank = pt[reg // 2]
                    r0 = (reg % 2) * 256
                    wait_rel("tensor", ("pt", reg))
                    S.wait("tensor", SC)
                    S.op("tensor", lambda t, _c=c, _b=bank, _r=r0, _x=xin: t.transpose(
                        _b[:, _r:_r + 128], _x[:, _c * 128:(_c + 1) * 128], identf[:]))
                    S.wait("scalar", PE)
                    S.op("scalar", lambda s, _c=c, _j=j, _b=bank, _r=r0: s.activation(
                        stg[_c][:, _j * 128:(_j + 1) * 128], _b[:, _r:_r + 128], AF.Copy))
                    rel[("pt", reg)] = ((SC, SC.n),)
                rel[("xin", i % 2)] = ((PE, PE.n),)
            S.wait("vector", SC)
            if stw is not None:
                S.wait("vector", stw)
            for c in range(NC8):
                S.op("vector", lambda v, _c=c: v.tensor_copy(stgb[_c][:, :], stg[_c][:, :]))
            S.wait("gpsimd", SC)
            S.wait("gpsimd", VE)
            stw = gq.wave()
            for c in range(NC8):
                S.dma("gpsimd", stw, A["x0"][c * 128:(c + 1) * 128, grp * 512:(grp + 1) * 512], stg[c][:, :])
                S.dma("gpsimd", stw, A["xb0"][c * 128:(c + 1) * 128, grp * 512:(grp + 1) * 512], stgb[c][:, :])
        for s_ in gq.sems:
            S.wait("gpsimd", s_)
        S.flush()



def phase1(nc, A):
    with ExitStack() as stack:
        S = Sched(nc, stack, "p1")
        wqkvT = stack.enter_context(nc.sbuf_tensor("p1_wqkvT", [128, NC8 * 3072], BF16))
        outwT = stack.enter_context(nc.sbuf_tensor("p1_outwT", [128, NC8 * 1024], BF16))
        xb = [stack.enter_context(nc.sbuf_tensor(f"p1_xb{c}", [128, LT], BF16)) for c in range(NC8)]
        x0t = [stack.enter_context(nc.sbuf_tensor(f"p1_x0t{c}", [128, LT], F32)) for c in range(NC8)]
        qT = stack.enter_context(nc.sbuf_tensor("p1_qT", [128, NC8 * LT], BF16))
        kT = stack.enter_context(nc.sbuf_tensor("p1_kT", [128, NC8 * LT], BF16))
        ktok = [stack.enter_context(nc.sbuf_tensor(f"p1_ktok{ts}", [128, 1024], BF16)) for ts in range(4)]
        vtok = [stack.enter_context(nc.sbuf_tensor(f"p1_vtok{ts}", [128, 1024], BF16)) for ts in range(4)]
        sT = [stack.enter_context(nc.sbuf_tensor(f"p1_sT{i}", [128, LT], BF16)) for i in range(4)]
        yT = [stack.enter_context(nc.sbuf_tensor(f"p1_yT{c}", [128, LT], BF16)) for c in range(NC8)]
        msb = stack.enter_context(nc.sbuf_tensor("p1_msb", [128, 512], BF16))
        dlt = [stack.enter_context(nc.sbuf_tensor(f"p1_dlt{c}", [128, LT], F32)) for c in range(NC8)]
        x1t = dlt
        xb1t = [stack.enter_context(nc.sbuf_tensor(f"p1_xb1t{c}", [128, LT], BF16)) for c in range(NC8)]
        trisrc = stack.enter_context(nc.sbuf_tensor("p1_trisrc", [128, 128], F32))
        tri = stack.enter_context(nc.sbuf_tensor("p1_tri", [128, 128], F32))
        onesc = stack.enter_context(nc.sbuf_tensor("p1_onesc", [128, 1], BF16))
        onesr = stack.enter_context(nc.sbuf_tensor("p1_onesr", [1, 128], BF16))
        ones_lt = stack.enter_context(nc.sbuf_tensor("p1_oneslt", [1, LT], BF16))
        lnscr = make_ln_scratch(S, stack)
        bqkv_row = stack.enter_context(nc.sbuf_tensor("p1_bqkvr", [1, 3072], BF16))
        bout_row = stack.enter_context(nc.sbuf_tensor("p1_boutr", [1, 1024], BF16))
        bk = [stack.enter_context(nc.psum_tensor(f"p1_bk{i}", [128, LT], F32)) for i in range(6)]
        py = stack.enter_context(nc.psum_tensor("p1_py", [128, LT], F32))
        pm = stack.enter_context(nc.psum_tensor("p1_pm", [128, 512], F32))

        lq = S.dmaq("sync", "lq", 4)
        gq = S.dmaq("gpsimd", "gq", 3)
        PE, SC, VE, PL = S.eng["tensor"], S.eng["scalar"], S.eng["vector"], S.eng["gpsimd"]
        rel = {}

        def wait_rel(eng, key):
            for sem, cnt in rel.get(key, ()):
                S.wait(eng, sem, cnt)

        ldp = gq.wave()
        S.dma("gpsimd", ldp, bqkv_row[0:1, :], A["L0_attn_Wqkv_b"][0:1, :])
        S.dma("gpsimd", ldp, bout_row[0:1, :], A["L0_attn_out_b"][0:1, :])
        if "cast" in CROSS:
            S.wait("sync", CROSS["cast"][0], CROSS["cast"][1])
        wld = lq.wave()
        for c in range(NC8):
            S.dma("sync", wld, wqkvT[:, c * 3072:(c + 1) * 3072], A["wqkv_bf"][:, c * 128:(c + 1) * 128], transpose=True)
            S.dma("sync", wld, outwT[:, c * 1024:(c + 1) * 1024], A["outw_bf"][:, c * 128:(c + 1) * 128], transpose=True)
        S.op("gpsimd", lambda g: g.memset(trisrc[:], 1.0))
        S.op("gpsimd", lambda g: g.affine_select(
            tri[:], trisrc[:], pattern=[[1, 128]], compare_op=OP.is_ge, fill=0.0,
            base=0, channel_multiplier=-1))
        S.op("vector", lambda v: v.memset(onesc[:], 1.0))
        S.op("vector", lambda v: v.memset(onesr[:], 1.0))
        S.op("vector", lambda v: v.memset(ones_lt[:], 1.0))
        S.op("vector", lambda v: v.memset(msb[:], 0.0))

        for lt in range(NLT):
            tw = lq.wave()
            wait_rel("sync", "xtiles")
            for c in range(NC8):
                S.dma("sync", tw, xb[c][:, :], A["xb0"][c * 128:(c + 1) * 128, lt * LT:(lt + 1) * LT])
                S.dma("sync", tw, x0t[c][:, :], A["x0"][c * 128:(c + 1) * 128, lt * LT:(lt + 1) * LT])
            S.wait("tensor", tw)
            if lt == 0:
                S.wait("tensor", wld)
                S.wait("tensor", ldp)
                S.wait("tensor", PL)
                S.wait("tensor", VE)
            for oc in range(16):
                bank = bk[oc % 3]
                wait_rel("tensor", ("bk", oc % 3))
                for kc in range(NC8):
                    S.op("tensor", lambda t, _oc=oc, _kc=kc, _b=bank: t.matmul(
                        _b[:, :], wqkvT[:, _kc * 3072 + _oc * 128:_kc * 3072 + (_oc + 1) * 128],
                        xb[_kc][:, :], start=(_kc == 0), stop=False))
                S.op("tensor", lambda t, _oc=oc, _b=bank: t.matmul(
                    _b[:, :], bqkv_row[0:1, _oc * 128:(_oc + 1) * 128], ones_lt[0:1, :],
                    start=False, stop=True))
                dst, cc, scl = (qT, oc, 1.0) if oc < 8 else (kT, oc - 8, 0.125)
                S.wait("scalar", PE)
                wait_rel("scalar", ("qkT", 0))
                S.op("scalar", lambda s, _b=bank, _d=dst, _cc=cc, _sc=scl: s.activation(
                    _d[:, _cc * LT:(_cc + 1) * LT], _b[:, :], AF.Identity, scale=_sc))
                rel[("bk", oc % 3)] = ((SC, SC.n),)
            for ts in range(4):
                for part in range(2):
                    for half in range(2):
                        i = (ts * 4 + part * 2 + half)
                        bank = bk[i % 3]
                        dout0 = 1024 + part * 1024 + half * 512
                        wait_rel("tensor", ("bk", i % 3))
                        for kc in range(NC8):
                            S.op("tensor", lambda t, _ts=ts, _kc=kc, _b=bank, _d0=dout0: t.matmul(
                                _b[:, :], xb[_kc][:, _ts * 128:(_ts + 1) * 128],
                                wqkvT[:, _kc * 3072 + _d0:_kc * 3072 + _d0 + 512],
                                start=(_kc == 0), stop=False))
                        S.op("tensor", lambda t, _b=bank, _d0=dout0: t.matmul(
                            _b[:, :], onesr[0:1, 0:128], bqkv_row[0:1, _d0:_d0 + 512],
                            start=False, stop=True))
                        dstt = ktok if part == 0 else vtok
                        scl = 0.125 if part == 0 else 1.0
                        S.wait("scalar", PE)
                        wait_rel("scalar", ("tok", part, ts))
                        S.op("scalar", lambda s, _b=bank, _d=dstt, _ts=ts, _h=half, _sc=scl: s.activation(
                            _d[_ts][:, _h * 512:(_h + 1) * 512], _b[:, :], AF.Identity, scale=_sc))
                        rel[("bk", i % 3)] = ((SC, SC.n),)
            rel["xtiles"] = ((PE, PE.n), (VE, VE.n))
            if lt > 0:
                S.wait("scalar", PE)
                S.op("scalar", lambda s: s.activation(msb[:, :], pm[:, :], AF.Copy))
            msb_ready = (SC, SC.n)
            for h in range(H):
                hc = h // 2
                hb = (h % 2) * 64
                st_ready = {}
                for ms in range(4):
                    n = LT - ms * 128
                    bi = (h * 4 + ms) % 6
                    bank = bk[bi]
                    wait_rel("tensor", ("bk", bi))
                    S.op("tensor", lambda t, _hc=hc, _hb=hb, _ms=ms, _n=n, _b=bank: t.matmul(
                        _b[:, 0:_n],
                        kT[_hb:_hb + 64, _hc * LT + _ms * 128:_hc * LT + (_ms + 1) * 128],
                        qT[_hb:_hb + 64, _hc * LT + _ms * 128:_hc * LT + _ms * 128 + _n],
                        start=True, stop=True))
                    S.wait("vector", PE)
                    S.op("vector", lambda v, _ms=ms, _b=bank: v.tensor_tensor(
                        sT[_ms][:, 0:128], _b[:, 0:128], tri[:, :], OP.mult))
                    if n > 128:
                        S.wait("scalar", PE)
                        S.wait("scalar", VE)
                        S.op("scalar", lambda s, _ms=ms, _n=n, _b=bank: s.activation(
                            sT[_ms][:, 128:_n], _b[:, 128:_n], AF.Copy))
                        st_ready[ms] = ((VE, VE.n), (SC, SC.n))
                        rel[("bk", bi)] = ((VE, VE.n), (SC, SC.n))
                    else:
                        st_ready[ms] = ((VE, VE.n),)
                        rel[("bk", bi)] = ((VE, VE.n),)
                wait_rel("tensor", "py")
                S.wait("tensor", msb_ready[0], msb_ready[1])
                S.op("tensor", lambda t, _hb=hb, _hc=hc: t.matmul(
                    py[_hb:_hb + 64, :], msb[_hb:_hb + 64, _hc * 64:_hc * 64 + 64],
                    qT[_hb:_hb + 64, _hc * LT:(_hc + 1) * LT], start=True, stop=False))
                for ms in range(4):
                    n = LT - ms * 128
                    for sem, cnt in st_ready[ms]:
                        S.wait("tensor", sem, cnt)
                    S.op("tensor", lambda t, _hb=hb, _h=h, _ms=ms, _n=n: t.matmul(
                        py[_hb:_hb + 64, _ms * 128:_ms * 128 + _n],
                        vtok[_ms][:, _h * 64:(_h + 1) * 64],
                        sT[_ms][:, 0:_n], start=False, stop=(_ms == 3)))
                if h % 2 == 1:
                    S.wait("scalar", PE)
                    wait_rel("scalar", ("yT", hc))
                    S.op("scalar", lambda s, _hc=hc: s.activation(
                        yT[_hc][:, :], py[:, :], AF.Copy))
                    rel["py"] = ((SC, SC.n),)
            for h in range(H):
                hc = h // 2
                hb = (h % 2) * 64
                for ts in range(4):
                    first = (lt == 0 and h in (0, 1) and ts == 0)
                    S.op("tensor", lambda t, _h=h, _hb=hb, _hc=hc, _ts=ts, _f=first: t.matmul(
                        pm[_hb:_hb + 64, _hc * 64:_hc * 64 + 64],
                        ktok[_ts][:, _h * 64:(_h + 1) * 64],
                        vtok[_ts][:, _h * 64:(_h + 1) * 64],
                        start=_f, stop=False, skip_group_check=True))
            for part in range(2):
                for ts in range(4):
                    rel[("tok", part, ts)] = ((PE, PE.n),)
            S.wait("tensor", SC)
            for oc in range(NC8):
                bank = bk[3 + oc % 2]
                wait_rel("tensor", ("bk", 3 + oc % 2))
                for kc in range(NC8):
                    S.op("tensor", lambda t, _oc=oc, _kc=kc, _b=bank: t.matmul(
                        _b[:, :], outwT[:, _kc * 1024 + _oc * 128:_kc * 1024 + (_oc + 1) * 128],
                        yT[_kc][:, :], start=(_kc == 0), stop=False))
                S.op("tensor", lambda t, _oc=oc, _b=bank: t.matmul(
                    _b[:, :], bout_row[0:1, _oc * 128:(_oc + 1) * 128], ones_lt[0:1, :],
                    start=False, stop=True))
                rel[("yT", oc)] = ((PE, PE.n),)
                S.wait("scalar", PE)
                wait_rel("scalar", "stores")
                wait_rel("scalar", ("dlt", oc))
                S.op("scalar", lambda s, _oc=oc, _b=bank: s.activation(
                    dlt[_oc][:, :], _b[:, :], AF.Copy))
                rel[("bk", 3 + oc % 2)] = ((SC, SC.n),)
            S.wait("vector", SC)
            for c in range(NC8):
                S.op("vector", lambda v, _c=c: v.tensor_tensor(
                    dlt[_c][:, :], dlt[_c][:, :], x0t[_c][:, :], OP.add))
            rel["xtiles"] = ((PE, rel["xtiles"][0][1]), (VE, VE.n))
            S.wait("scalar", VE)
            emit_ln(S, lnscr, dlt, onesc, onesr, bk[3], bk[4], x1t, xb1t, lq, lt)
            rel[("bk", 3)] = ((VE, VE.n),)
            rel[("bk", 4)] = ((VE, VE.n),)
            S.wait("gpsimd", VE)
            sw = gq.wave()
            for c in range(NC8):
                S.dma("gpsimd", sw, A["x1"][c * 128:(c + 1) * 128, lt * LT:(lt + 1) * LT], x1t[c][:, :])
                S.dma("gpsimd", sw, A["xb1"][c * 128:(c + 1) * 128, lt * LT:(lt + 1) * LT], xb1t[c][:, :])
            rel["stores"] = ((sw, sw.n),)
            for c in range(NC8):
                rel[("dlt", c)] = ((sw, sw.n), (VE, VE.n))
            rel[("qkT", 0)] = ((PE, PE.n),)
        for s_ in gq.sems:
            S.wait("gpsimd", s_)
        S.flush()



def phase2(nc, A):
    with ExitStack() as stack:
        S = Sched(nc, stack, "p2")
        w0T = stack.enter_context(nc.sbuf_tensor("p2_w0T", [128, NC8 * 1024], BF16))
        w1T = stack.enter_context(nc.sbuf_tensor("p2_w1T", [128, NC8 * 1024], BF16))
        outWT = stack.enter_context(nc.sbuf_tensor("p2_outWT", [128, NC8 * 1024], BF16))
        gw = stack.enter_context(nc.sbuf_tensor("p2_gw", [8, 1024], F32))
        gwT = stack.enter_context(nc.sbuf_tensor("p2_gwT", [128, NC8 * 8], F32))
        xb = [stack.enter_context(nc.sbuf_tensor(f"p2_xb{c}", [128, LT], BF16)) for c in range(NC8)]
        x1f = [stack.enter_context(nc.sbuf_tensor(f"p2_x1f{c}", [128, LT], F32)) for c in range(NC8)]
        glog_s = stack.enter_context(nc.sbuf_tensor("p2_glog", [8, LT], F32))
        ptok_s = stack.enter_context(nc.sbuf_tensor("p2_ptok", [128, 32], F32))
        e8 = stack.enter_context(nc.sbuf_tensor("p2_e8", [128, 8], F32))
        p8_ = stack.enter_context(nc.sbuf_tensor("p2_p8", [128, 8], F32))
        g8 = stack.enter_context(nc.sbuf_tensor("p2_g8", [128, 8], F32))
        gs = stack.enter_context(nc.sbuf_tensor("p2_gs", [128, 8], F32))
        s01 = stack.enter_context(nc.sbuf_tensor("p2_s01", [128, 8], F32))
        st2row = stack.enter_context(nc.sbuf_tensor("p2_st2r", [2, LT], BF16))
        sel0 = stack.enter_context(nc.sbuf_tensor("p2_sel0", [2, 128], BF16))
        sel1 = stack.enter_context(nc.sbuf_tensor("p2_sel1", [2, 128], BF16))
        ones2 = stack.enter_context(nc.sbuf_tensor("p2_ones2", [2, 128], BF16))
        h0s = [stack.enter_context(nc.sbuf_tensor(f"p2_h0s{c}", [128, LT], F32)) for c in range(NC8)]
        h1s = [stack.enter_context(nc.sbuf_tensor(f"p2_h1s{c}", [128, LT], F32)) for c in range(NC8)]
        f0 = stack.enter_context(nc.sbuf_tensor("p2_f0", [128, LT], F32))
        finT = [stack.enter_context(nc.sbuf_tensor(f"p2_fin{c}", [128, LT], BF16)) for c in range(NC8)]
        dlt = [stack.enter_context(nc.sbuf_tensor(f"p2_dlt{c}", [128, LT], F32)) for c in range(NC8)]
        x2t = dlt
        xb2t = [stack.enter_context(nc.sbuf_tensor(f"p2_xb2t{c}", [128, LT], BF16)) for c in range(NC8)]
        onesc = stack.enter_context(nc.sbuf_tensor("p2_onesc", [128, 1], BF16))
        onesr = stack.enter_context(nc.sbuf_tensor("p2_onesr", [1, 128], BF16))
        lnscr = make_ln_scratch(S, stack)
        identf, idrdyp = make_identity(S, stack, F32, "f")

        psc = [stack.enter_context(nc.psum_tensor(f"p2_psc{i}", [128, LT], F32)) for i in range(2)]
        pbc = [stack.enter_context(nc.psum_tensor(f"p2_pbc{i}", [128, LT], F32)) for i in range(2)]
        pst = [stack.enter_context(nc.psum_tensor(f"p2_pst{i}", [128, LT], F32)) for i in range(3)]
        pl = stack.enter_context(nc.psum_tensor("p2_pl", [128, LT], F32))

        lq = S.dmaq("sync", "lq", 4)
        gq = S.dmaq("gpsimd", "gq", 3)
        PE, SC, VE, PL = S.eng["tensor"], S.eng["scalar"], S.eng["vector"], S.eng["gpsimd"]
        rel = {}

        def wait_rel(eng, key):
            for sem, cnt in rel.get(key, ()):
                S.wait(eng, sem, cnt)

        s01_rel = 0
        if "cast" in CROSS:
            S.wait("sync", CROSS["cast"][0], CROSS["cast"][1])
        wld = lq.wave()
        for c in range(NC8):
            S.dma("sync", wld, w0T[:, c * 1024:(c + 1) * 1024], A["expw0_bf"][:, c * 128:(c + 1) * 128], transpose=True)
            S.dma("sync", wld, w1T[:, c * 1024:(c + 1) * 1024], A["expw1_bf"][:, c * 128:(c + 1) * 128], transpose=True)
            S.dma("sync", wld, outWT[:, c * 1024:(c + 1) * 1024], A["outW_bf"][:, c * 128:(c + 1) * 128], transpose=True)
        S.dma("sync", wld, gw[:, :], A["L0_ffn_gate_W"][:, :])
        S.wait("tensor", wld)
        S.wait("tensor", PL, idrdyp)
        for c in range(NC8):
            S.op("tensor", lambda t, _c=c: t.transpose(
                pl[:, _c * 8:(_c + 1) * 8], gw[0:8, _c * 128:(_c + 1) * 128], identf[0:8, 0:8]))
        S.wait("scalar", PE)
        S.op("scalar", lambda s: s.activation(gwT[:, 0:NC8 * 8], pl[:, 0:NC8 * 8], AF.Copy))
        S.op("gpsimd", lambda g: g.memset(ones2[:, :], 1.0))
        S.op("gpsimd", lambda g: g.affine_select(
            sel0[:, :], ones2[:, :], pattern=[[0, 128]], compare_op=OP.is_equal, fill=0.0,
            base=0, channel_multiplier=1))
        S.op("gpsimd", lambda g: g.affine_select(
            sel1[:, :], ones2[:, :], pattern=[[0, 128]], compare_op=OP.is_equal, fill=0.0,
            base=-1, channel_multiplier=1))
        S.op("vector", lambda v: v.memset(onesc[:], 1.0))
        S.op("vector", lambda v: v.memset(onesr[:], 1.0))

        for lt in range(NLT):
            tw = lq.wave()
            wait_rel("sync", "xtiles")
            for c in range(NC8):
                S.dma("sync", tw, xb[c][:, :], A["xb1"][c * 128:(c + 1) * 128, lt * LT:(lt + 1) * LT])
                S.dma("sync", tw, x1f[c][:, :], A["x1"][c * 128:(c + 1) * 128, lt * LT:(lt + 1) * LT])
            S.wait("tensor", tw)
            if lt == 0:
                S.wait("tensor", SC)
                S.wait("tensor", PL)
                S.wait("tensor", VE)
            wait_rel("tensor", "pst0")
            for kc in range(NC8):
                S.op("tensor", lambda t, _kc=kc: t.matmul(
                    pst[0][0:8, :], gwT[:, _kc * 8:(_kc + 1) * 8], x1f[_kc][:, :],
                    start=(_kc == 0), stop=(_kc == NC8 - 1)))
            S.wait("scalar", PE)
            wait_rel("scalar", "glog")
            S.op("scalar", lambda s: s.activation(glog_s[:, :], pst[0][0:8, :], AF.Copy))
            S.wait("tensor", SC)
            wait_rel("tensor", "pst1")
            for j in range(4):
                S.op("tensor", lambda t, _j=j: t.transpose(
                    pst[1][:, _j * 8:(_j + 1) * 8], glog_s[0:8, _j * 128:(_j + 1) * 128], identf[0:8, 0:8]))
            S.wait("scalar", PE)
            wait_rel("scalar", "ptok")
            S.op("scalar", lambda s: s.activation(ptok_s[:, 0:32], pst[1][:, 0:32], AF.Copy))
            rel["glog"] = ((PE, PE.n),)
            rel["pst0"] = ((SC, SC.n),)
            rel["pst1"] = ((SC, SC.n),)
            for oc in range(NC8):
                pair = (psc[0], psc[1]) if oc % 2 == 0 else (pst[0], pst[1])
                key = ("xp", oc % 2)
                wait_rel("tensor", key)
                if oc % 2 == 1:
                    wait_rel("tensor", "pst0")
                    wait_rel("tensor", "pst1")
                for kc in range(NC8):
                    S.op("tensor", lambda t, _oc=oc, _kc=kc, _b=pair[0]: t.matmul(
                        _b[:, :], w0T[:, _kc * 1024 + _oc * 128:_kc * 1024 + (_oc + 1) * 128],
                        xb[_kc][:, :], start=(_kc == 0), stop=(_kc == NC8 - 1)))
                for kc in range(NC8):
                    S.op("tensor", lambda t, _oc=oc, _kc=kc, _b=pair[1]: t.matmul(
                        _b[:, :], w1T[:, _kc * 1024 + _oc * 128:_kc * 1024 + (_oc + 1) * 128],
                        xb[_kc][:, :], start=(_kc == 0), stop=(_kc == NC8 - 1)))
                S.wait("scalar", PE)
                wait_rel("scalar", ("hs", oc))
                S.op("scalar", lambda s, _oc=oc, _b=pair[0]: s.activation(h0s[_oc][:, :], _b[:, :], AF.Copy))
                S.op("scalar", lambda s, _oc=oc, _b=pair[1]: s.activation(h1s[_oc][:, :], _b[:, :], AF.Copy))
                rel[key] = ((SC, SC.n),)
                if oc % 2 == 1:
                    rel["pst0"] = ((SC, SC.n),)
                    rel["pst1"] = ((SC, SC.n),)
            rel["xtiles"] = ((PE, PE.n), (VE, VE.n))
            for j in range(4):
                L = ptok_s[:, j * 8:(j + 1) * 8]
                S.wait("vector", SC)
                S.wait("vector", PE, s01_rel)
                S.op("vector", lambda v, _L=L: v.reduce_max(gs[:, 0:1], _L, axis=mybir.AxisListType.X))
                S.op("vector", lambda v: v.tensor_scalar(gs[:, 1:2], gs[:, 0:1], -1.0, None, OP.mult))
                S.wait("scalar", VE)
                S.op("scalar", lambda s, _L=L: s.activation(e8[:, :], _L, AF.Exp, bias=gs[:, 1:2]))
                S.wait("vector", SC)
                S.op("vector", lambda v: v.reduce_sum(gs[:, 2:3], e8[:, :], axis=mybir.AxisListType.X))
                S.op("vector", lambda v: v.reciprocal(gs[:, 3:4], gs[:, 2:3]))
                S.op("vector", lambda v: v.tensor_scalar(p8_[:, :], e8[:, :], gs[:, 3:4], None, OP.mult))
                S.op("vector", lambda v: v.tensor_scalar(g8[:, :], p8_[:, :], 0.3, None, OP.is_gt))
                S.op("vector", lambda v: v.reduce_sum(gs[:, 4:5], g8[:, :], axis=mybir.AxisListType.X))
                S.op("vector", lambda v: v.tensor_scalar(gs[:, 5:6], gs[:, 4:5], -1.0, 8.0, OP.mult, OP.add))
                S.op("vector", lambda v, _j=j: v.tensor_tensor(s01[:, 2 * _j:2 * _j + 1], gs[:, 5:6], p8_[:, 0:1], OP.mult))
                S.op("vector", lambda v, _j=j: v.tensor_tensor(s01[:, 2 * _j + 1:2 * _j + 2], gs[:, 4:5], p8_[:, 1:2], OP.mult))
            S.wait("tensor", VE)
            wait_rel("tensor", "pst2")
            for j in range(4):
                S.op("tensor", lambda t, _j=j: t.transpose(
                    pst[2][0:2, _j * 128:(_j + 1) * 128], s01[:, 2 * _j:2 * _j + 2], identf[:, :]))
            s01_rel = PE.n
            rel["ptok"] = ((VE, VE.n),)
            S.wait("scalar", PE)
            wait_rel("scalar", "st2row")
            S.op("scalar", lambda s: s.activation(st2row[:, :], pst[2][0:2, :], AF.Copy))
            S.wait("tensor", SC)
            wait_rel("tensor", "pbc")
            S.op("tensor", lambda t: t.matmul(pbc[0][:, :], sel0[:, :], st2row[:, :], start=True, stop=True))
            S.op("tensor", lambda t: t.matmul(pbc[1][:, :], sel1[:, :], st2row[:, :], start=True, stop=True))
            rel["st2row"] = ((PE, PE.n),)
            rel["pst2"] = ((PE, PE.n),)
            bc_done = PE.n
            S.wait("vector", PE, bc_done)
            S.wait("vector", SC)
            for oc in range(NC8):
                S.op("vector", lambda v, _oc=oc: v.tensor_tensor(f0[:, :], h0s[_oc][:, :], pbc[0][:, :], OP.mult))
                S.op("vector", lambda v, _oc=oc: v.tensor_tensor(finT[_oc][:, :], h1s[_oc][:, :], pbc[1][:, :], OP.mult))
                S.op("vector", lambda v, _oc=oc: v.tensor_tensor(finT[_oc][:, :], finT[_oc][:, :], f0[:, :], OP.add))
                rel[("hs", oc)] = ((VE, VE.n),)
            rel["pbc"] = ((VE, VE.n),)
            S.wait("tensor", VE)
            for oc in range(NC8):
                bank = psc[oc % 2]
                wait_rel("tensor", ("xp", 0))
                for kc in range(NC8):
                    S.op("tensor", lambda t, _oc=oc, _kc=kc, _b=bank: t.matmul(
                        _b[:, :], outWT[:, _kc * 1024 + _oc * 128:_kc * 1024 + (_oc + 1) * 128],
                        finT[_kc][:, :], start=(_kc == 0), stop=(_kc == NC8 - 1)))
                S.wait("scalar", PE)
                wait_rel("scalar", ("dlt", oc))
                S.op("scalar", lambda s, _oc=oc, _b=bank: s.activation(
                    dlt[_oc][:, :], _b[:, :], AF.Copy))
                rel[("xp", 0)] = ((SC, SC.n),)
            S.wait("vector", SC)
            for c in range(NC8):
                S.op("vector", lambda v, _c=c: v.tensor_tensor(
                    dlt[_c][:, :], dlt[_c][:, :], x1f[_c][:, :], OP.add))
            rel["xtiles"] = ((PE, rel["xtiles"][0][1]), (VE, VE.n))
            S.wait("scalar", VE)
            emit_ln(S, lnscr, dlt, onesc, onesr, pl, pbc[0], x2t, xb2t, lq, lt)
            rel["pbc"] = ((VE, VE.n),)
            S.wait("gpsimd", VE)
            sw = gq.wave()
            for c in range(NC8):
                S.dma("gpsimd", sw, A["x2"][c * 128:(c + 1) * 128, lt * LT:(lt + 1) * LT], x2t[c][:, :])
                S.dma("gpsimd", sw, A["xb2"][c * 128:(c + 1) * 128, lt * LT:(lt + 1) * LT], xb2t[c][:, :])
            for c in range(NC8):
                rel[("dlt", c)] = ((sw, sw.n), (VE, VE.n))
            rel["pst0"] = ((PE, PE.n),)
        for s_ in gq.sems:
            S.wait("gpsimd", s_)
        S.flush()



def phase3(nc, A):
    with ExitStack() as stack:
        S = Sched(nc, stack, "p3")
        wkT = stack.enter_context(nc.sbuf_tensor("p3_wkT", [128, NC8 * 1024], BF16))
        wvT = stack.enter_context(nc.sbuf_tensor("p3_wvT", [128, NC8 * 1024], BF16))
        x2f = [stack.enter_context(nc.sbuf_tensor(f"p3_x2f{c}", [128, LT], F32)) for c in range(NC8)]
        hbuf = [stack.enter_context(nc.sbuf_tensor(f"p3_h{c}", [128, LT + 1], F32)) for c in range(NC8)]
        dscr = stack.enter_context(nc.sbuf_tensor("p3_dscr", [128, LT], F32))
        xkb = [stack.enter_context(nc.sbuf_tensor(f"p3_xkb{c}", [128, LT], BF16)) for c in range(NC8)]
        xvb = [stack.enter_context(nc.sbuf_tensor(f"p3_xvb{c}", [128, LT], BF16)) for c in range(NC8)]
        xrb = [stack.enter_context(nc.sbuf_tensor(f"p3_xrb{c}", [128, LT], BF16)) for c in range(NC8)]
        kst = [stack.enter_context(nc.sbuf_tensor(f"p3_kst{c}", [128, LT], F32)) for c in range(NC8)]
        vst = [stack.enter_context(nc.sbuf_tensor(f"p3_vst{c}", [128, LT], F32)) for c in range(NC8)]
        onesc = stack.enter_context(nc.sbuf_tensor("p3_onesc", [128, 1], BF16))
        onesr = stack.enter_context(nc.sbuf_tensor("p3_onesr", [1, 128], BF16))
        lnscr = make_ln_scratch(S, stack)
        identf, idrdy = make_identity(S, stack, F32, "f")
        psc = [stack.enter_context(nc.psum_tensor(f"p3_psc{i}", [128, LT], F32)) for i in range(6)]
        pl = [stack.enter_context(nc.psum_tensor(f"p3_pl{i}", [128, LT], F32)) for i in range(2)]

        lq = S.dmaq("sync", "lq", 4)
        gq = S.dmaq("gpsimd", "gq", 3)
        PE, SC, VE, PL = S.eng["tensor"], S.eng["scalar"], S.eng["vector"], S.eng["gpsimd"]
        rel = {}

        def wait_rel(eng, key):
            for sem, cnt in rel.get(key, ()):
                S.wait(eng, sem, cnt)

        if "cast" in CROSS:
            S.wait("sync", CROSS["cast"][0], CROSS["cast"][1])
        wld = lq.wave()
        for c in range(NC8):
            S.dma("sync", wld, wkT[:, c * 1024:(c + 1) * 1024], A["wk_bf"][:, c * 128:(c + 1) * 128], transpose=True)
            S.dma("sync", wld, wvT[:, c * 1024:(c + 1) * 1024], A["wv_bf"][:, c * 128:(c + 1) * 128], transpose=True)
        vecs = load_vecs_fm(S, stack, [
            ("l1g", A["L1_attn_ln1_g"], D), ("l1b", A["L1_attn_ln1_b"], D),
            ("tmk", A["L1_attn_tm_k"], D), ("tmv", A["L1_attn_tm_v"], D),
            ("tmr", A["L1_attn_tm_r"], D)], identf, idrdy, pl[0], lq)
        ln1g, ln1b = vecs["l1g"], vecs["l1b"]
        tmk, tmv, tmr = vecs["tmk"], vecs["tmv"], vecs["tmr"]
        S.op("vector", lambda v: v.memset(onesc[:], 1.0))
        S.op("vector", lambda v: v.memset(onesr[:], 1.0))
        for c in range(NC8):
            S.op("vector", lambda v, _c=c: v.memset(hbuf[_c][:, 0:1], 0.0))

        for lt in range(NLT):
            tw = lq.wave()
            wait_rel("sync", "x2f")
            for c in range(NC8):
                S.dma("sync", tw, x2f[c][:, :], A["x2"][c * 128:(c + 1) * 128, lt * LT:(lt + 1) * LT])
            S.wait("tensor", tw)
            S.wait("scalar", tw)
            S.wait("vector", tw)
            if lt == 0:
                S.wait("tensor", wld)
                S.wait("tensor", SC)
            if lt > 0:
                for c in range(NC8):
                    S.op("vector", lambda v, _c=c: v.tensor_copy(hbuf[_c][:, 0:1], hbuf[_c][:, LT:LT + 1]))
            emit_ln(S, lnscr, x2f, onesc, onesr, pl[0], pl[1],
                    [hbuf[c][:, 1:LT + 1] for c in range(NC8)], None, lq, lt, g_vec=ln1g, b_vec=ln1b)
            rel["x2f"] = ((VE, VE.n),)
            wait_rel("vector", "mixes")
            wait_rel("vector", ("st", 0))
            for c in range(NC8):
                h_ = hbuf[c][:, 1:LT + 1]
                hh = hbuf[c][:, 0:LT]
                S.op("vector", lambda v, _h=h_, _hh=hh: v.tensor_tensor(dscr[:, :], _h, _hh, OP.subtract))
                S.op("vector", lambda v, _c=c, _hh=hh: v.scalar_tensor_tensor(
                    xkb[_c][:, :], dscr[:, :], tmk[:, _c:_c + 1], _hh, OP.mult, OP.add))
                S.op("vector", lambda v, _c=c, _hh=hh: v.scalar_tensor_tensor(
                    xvb[_c][:, :], dscr[:, :], tmv[:, _c:_c + 1], _hh, OP.mult, OP.add))
                S.op("vector", lambda v, _c=c, _hh=hh: v.scalar_tensor_tensor(
                    xrb[_c][:, :], dscr[:, :], tmr[:, _c:_c + 1], _hh, OP.mult, OP.add))
            S.wait("tensor", VE)
            for oc in range(NC8):
                bk = psc[0 + oc % 2]
                bv = psc[2 + oc % 2]
                for i in (0 + oc % 2, 2 + oc % 2):
                    wait_rel("tensor", ("psc", i))
                for kc in range(NC8):
                    S.op("tensor", lambda t, _oc=oc, _kc=kc, _b=bk: t.matmul(
                        _b[:, :], wkT[:, _kc * 1024 + _oc * 128:_kc * 1024 + (_oc + 1) * 128],
                        xkb[_kc][:, :], start=(_kc == 0), stop=(_kc == NC8 - 1)))
                for kc in range(NC8):
                    S.op("tensor", lambda t, _oc=oc, _kc=kc, _b=bv: t.matmul(
                        _b[:, :], wvT[:, _kc * 1024 + _oc * 128:_kc * 1024 + (_oc + 1) * 128],
                        xvb[_kc][:, :], start=(_kc == 0), stop=(_kc == NC8 - 1)))
                S.wait("scalar", PE)
                wait_rel("scalar", ("st", oc))
                S.op("scalar", lambda s, _oc=oc, _b=bk: s.activation(kst[_oc][:, :], _b[:, :], AF.Copy))
                rel[("psc", 0 + oc % 2)] = ((SC, SC.n),)
                S.op("scalar", lambda s, _oc=oc, _b=bv: s.activation(vst[_oc][:, :], _b[:, :], AF.Copy))
                rel[("psc", 2 + oc % 2)] = ((SC, SC.n),)
            rel["mixes"] = ((PE, PE.n), (VE, VE.n))
            S.wait("gpsimd", SC)
            S.wait("gpsimd", VE)
            sw = gq.wave()
            for c in range(NC8):
                S.dma("gpsimd", sw, A["kTf"][c * 128:(c + 1) * 128, lt * LT:(lt + 1) * LT], kst[c][:, :])
                S.dma("gpsimd", sw, A["vTf"][c * 128:(c + 1) * 128, lt * LT:(lt + 1) * LT], vst[c][:, :])
                S.dma("gpsimd", sw, A["xrT"][c * 128:(c + 1) * 128, lt * LT:(lt + 1) * LT], xrb[c][:, :])
            for c in range(NC8):
                rel[("st", c)] = ((sw, sw.n),)
        for s_ in gq.sems:
            S.wait("gpsimd", s_)
        S.flush()


H2 = T // 2


def phase4(nc, A):
    with ExitStack() as stack:
        S = Sched(nc, stack, "p4")
        woT = stack.enter_context(nc.sbuf_tensor("p4_woT", [128, NC8 * 1024], BF16))
        wrT = stack.enter_context(nc.sbuf_tensor("p4_wrT", [128, NC8 * 1024], BF16))
        yT = [stack.enter_context(nc.sbuf_tensor(f"p4_y{c}", [128, T], BF16)) for c in range(NC8)]
        kbb = [stack.enter_context(nc.sbuf_tensor(f"p4_kb{p}", [128, H2], F32)) for p in range(2)]
        vb = [stack.enter_context(nc.sbuf_tensor(f"p4_vb{p}", [128, H2], F32)) for p in range(2)]
        ekb = stack.enter_context(nc.sbuf_tensor("p4_ekb", [128, H2], F32))
        ekvb = stack.enter_context(nc.sbuf_tensor("p4_ekvb", [128, H2], F32))
        lnsc = stack.enter_context(nc.sbuf_tensor("p4_lnsc", [128, H2], F32))
        Nb = stack.enter_context(nc.sbuf_tensor("p4_Nb", [128, H2 + 1], F32))
        Db = stack.enter_context(nc.sbuf_tensor("p4_Db", [128, H2 + 1], F32))
        xrb = [stack.enter_context(nc.sbuf_tensor(f"p4_xrb{c}", [128, LT], BF16)) for c in range(NC8)]
        srt = [stack.enter_context(nc.sbuf_tensor(f"p4_srt{c}", [128, LT], BF16)) for c in range(NC8)]
        srl = xrb
        rwk = srt
        x2f = [stack.enter_context(nc.sbuf_tensor(f"p4_x2f{c}", [128, LT], F32)) for c in range(NC8)]
        ew = stack.enter_context(nc.sbuf_tensor("p4_ew", [128, NC8], F32))
        eu = stack.enter_context(nc.sbuf_tensor("p4_eu", [128, NC8], F32))
        identf, idrdy = make_identity(S, stack, F32, "f")
        pvec = stack.enter_context(nc.psum_tensor("p4_pvec", [128, LT], F32))
        psc = [stack.enter_context(nc.psum_tensor(f"p4_psc{i}", [128, LT], F32)) for i in range(4)]

        lq = S.dmaq("sync", "lq", 4)
        gq = S.dmaq("gpsimd", "gq", 3)
        PE, SC, VE, PL = S.eng["tensor"], S.eng["scalar"], S.eng["vector"], S.eng["gpsimd"]
        rel = {}

        def wait_rel(eng, key):
            for sem, cnt in rel.get(key, ()):
                S.wait(eng, sem, cnt)

        if "cast" in CROSS:
            S.wait("sync", CROSS["cast"][0], CROSS["cast"][1])
        wld = lq.wave()
        for c in range(NC8):
            S.dma("sync", wld, woT[:, c * 1024:(c + 1) * 1024], A["wo_bf"][:, c * 128:(c + 1) * 128], transpose=True)
            S.dma("sync", wld, wrT[:, c * 1024:(c + 1) * 1024], A["wr_bf"][:, c * 128:(c + 1) * 128], transpose=True)
        vecs = load_vecs_fm(S, stack, [
            ("td", A["L1_attn_time_decay"], D), ("tf", A["L1_attn_time_first"], D)],
            identf, idrdy, pvec, lq)
        td, tf = vecs["td"], vecs["tf"]
        S.op("scalar", lambda s: s.activation(ew[:, :], td[:, :], AF.Exp))
        S.op("scalar", lambda s: s.activation(ew[:, :], ew[:, :], AF.Exp, scale=-1.0))
        S.op("scalar", lambda s: s.activation(eu[:, :], tf[:, :], AF.Exp))

        sw0 = lq.wave()
        S.dma("sync", sw0, kbb[0][:, :], A["kTf"][0:128, 0:H2])
        S.dma("sync", sw0, vb[0][:, :], A["vTf"][0:128, 0:H2])
        scan_waves = {0: sw0}

        def emit_rtile(lt):
            tw = lq.wave()
            wait_rel("sync", "xrb")
            for c in range(NC8):
                S.dma("sync", tw, xrb[c][:, :], A["xrT"][c * 128:(c + 1) * 128, lt * LT:(lt + 1) * LT])
            S.wait("tensor", tw)
            if lt == 0:
                S.wait("tensor", wld)
            for oc in range(NC8):
                bank = psc[oc % 2]
                wait_rel("tensor", ("psc", oc % 2))
                for kc in range(NC8):
                    S.op("tensor", lambda t, _oc=oc, _kc=kc, _b=bank: t.matmul(
                        _b[:, :], wrT[:, _kc * 1024 + _oc * 128:_kc * 1024 + (_oc + 1) * 128],
                        xrb[_kc][:, :], start=(_kc == 0), stop=(_kc == NC8 - 1)))
                S.wait("scalar", PE)
                wait_rel("scalar", ("srt", oc))
                S.op("scalar", lambda s, _oc=oc, _b=bank: s.activation(srt[_oc][:, :], _b[:, :], AF.Sigmoid))
                rel[("psc", oc % 2)] = ((SC, SC.n),)
            rel["xrb"] = ((PE, PE.n),)
            S.wait("gpsimd", SC)
            rw = gq.wave()
            for c in range(NC8):
                S.dma("gpsimd", rw, A["srT"][c * 128:(c + 1) * 128, lt * LT:(lt + 1) * LT], srt[c][:, :])
            for c in range(NC8):
                rel[("srt", c)] = ((rw, rw.n),)

        for it in range(16):
            ch, half = it // 2, it % 2
            t0 = half * H2
            par = it % 2
            w = scan_waves[it]
            if it + 1 < 16:
                nch, nhalf = (it + 1) // 2, (it + 1) % 2
                nw = lq.wave()
                scan_waves[it + 1] = nw
                wait_rel("sync", ("kv", (it + 1) % 2))
                S.dma("sync", nw, kbb[(it + 1) % 2][:, :],
                      A["kTf"][nch * 128:(nch + 1) * 128, nhalf * H2:(nhalf + 1) * H2])
                S.dma("sync", nw, vb[(it + 1) % 2][:, :],
                      A["vTf"][nch * 128:(nch + 1) * 128, nhalf * H2:(nhalf + 1) * H2])
            if it % 2 == 0 and it // 2 < NLT:
                emit_rtile(it // 2)
            S.wait("scalar", w)
            S.wait("scalar", VE)
            S.op("scalar", lambda s, _p=par: s.activation(ekb[:, :], kbb[_p][:, :], AF.Exp))
            S.wait("vector", SC)
            S.op("vector", lambda v, _p=par: v.tensor_tensor(ekvb[:, :], ekb[:, :], vb[_p][:, :], OP.mult))
            rel[("kv", par)] = ((VE, VE.n),)
            if half == 0:
                S.op("vector", lambda v: v.memset(Nb[:, 0:1], 0.0))
                S.op("vector", lambda v: v.memset(Db[:, 0:1], 0.0))
            else:
                S.op("vector", lambda v: v.tensor_copy(Nb[:, 0:1], Nb[:, H2:H2 + 1]))
                S.op("vector", lambda v: v.tensor_copy(Db[:, 0:1], Db[:, H2:H2 + 1]))
            ewb = ew[:, ch:ch + 1].broadcast_to((128, H2))
            S.op("vector", lambda v, _e=ewb: v.tensor_tensor_scan(
                Nb[:, 1:H2 + 1], _e, ekvb[:, :], Nb[:, 0:1], OP.mult, OP.add))
            S.op("vector", lambda v, _e=ewb: v.tensor_tensor_scan(
                Db[:, 1:H2 + 1], _e, ekb[:, :], Db[:, 0:1], OP.mult, OP.add))
            S.op("vector", lambda v, _ch=ch: v.scalar_tensor_tensor(
                ekvb[:, :], ekvb[:, :], eu[:, _ch:_ch + 1], Nb[:, 0:H2], OP.mult, OP.add))
            S.op("vector", lambda v, _ch=ch: v.scalar_tensor_tensor(
                ekb[:, :], ekb[:, :], eu[:, _ch:_ch + 1], Db[:, 0:H2], OP.mult, OP.add))
            S.wait("scalar", VE)
            S.op("scalar", lambda s: s.activation(lnsc[:, :], ekb[:, :], AF.Ln))
            S.op("scalar", lambda s: s.activation(ekb[:, :], lnsc[:, :], AF.Exp, scale=-1.0))
            S.wait("vector", SC)
            S.op("vector", lambda v, _ch=ch, _t0=t0: v.tensor_tensor(
                yT[_ch][:, _t0:_t0 + H2], ekvb[:, :], ekb[:, :], OP.mult))

        for lt in range(NLT):
            tw = lq.wave()
            wait_rel("sync", "wotile")
            wait_rel("sync", "xrb")
            for c in range(NC8):
                S.dma("sync", tw, x2f[c][:, :], A["x2"][c * 128:(c + 1) * 128, lt * LT:(lt + 1) * LT])
                S.dma("sync", tw, srl[c][:, :], A["srT"][c * 128:(c + 1) * 128, lt * LT:(lt + 1) * LT])
            if lt == 0:
                for s_ in gq.sems:
                    S.wait("sync", s_)
            S.wait("vector", tw)
            S.wait("vector", VE)
            if lt == 0:
                for s_ in gq.sems:
                    S.wait("vector", s_)
            for c in range(NC8):
                S.op("vector", lambda v, _c=c, _lt=lt: v.tensor_tensor(
                    rwk[_c][:, :], yT[_c][:, _lt * LT:(_lt + 1) * LT], srl[_c][:, :], OP.mult))
            S.wait("tensor", VE)
            for oc in range(NC8):
                bank = psc[2 + oc % 2]
                wait_rel("tensor", ("pwo", oc % 2))
                for kc in range(NC8):
                    S.op("tensor", lambda t, _oc=oc, _kc=kc, _b=bank: t.matmul(
                        _b[:, :], woT[:, _kc * 1024 + _oc * 128:_kc * 1024 + (_oc + 1) * 128],
                        rwk[_kc][:, :], start=(_kc == 0), stop=(_kc == NC8 - 1)))
                S.wait("vector", PE)
                S.op("vector", lambda v, _oc=oc, _b=bank: v.tensor_tensor(
                    x2f[_oc][:, :], x2f[_oc][:, :], _b[:, :], OP.add))
                rel[("pwo", oc % 2)] = ((VE, VE.n),)
            S.wait("gpsimd", VE)
            sw = gq.wave()
            for c in range(NC8):
                S.dma("gpsimd", sw, A["xm"][c * 128:(c + 1) * 128, lt * LT:(lt + 1) * LT], x2f[c][:, :])
            rel["wotile"] = ((VE, VE.n), (PE, PE.n), (sw, sw.n))
        for s_ in gq.sems:
            S.wait("gpsimd", s_)
        S.flush()



def phase5(nc, A):
    with ExitStack() as stack:
        S = Sched(nc, stack, "p5")
        fkTh = stack.enter_context(nc.sbuf_tensor("p5_fkTh", [128, NC8 * 2048], BF16))
        fvTh = stack.enter_context(nc.sbuf_tensor("p5_fvTh", [128, 16 * 1024], BF16))
        xmf = [stack.enter_context(nc.sbuf_tensor(f"p5_xmf{c}", [128, LT], F32)) for c in range(NC8)]
        x2f = [stack.enter_context(nc.sbuf_tensor(f"p5_x2f{c}", [128, LT], F32)) for c in range(NC8)]
        gbuf = [stack.enter_context(nc.sbuf_tensor(f"p5_g{c}", [128, LT + 1], F32)) for c in range(NC8)]
        dscr = stack.enter_context(nc.sbuf_tensor("p5_dscr", [128, LT], F32))
        gkb = [stack.enter_context(nc.sbuf_tensor(f"p5_gkb{c}", [128, LT], BF16)) for c in range(NC8)]
        gkp = [gkb, [stack.enter_context(nc.sbuf_tensor(f"p5_gk2{c}", [128, LT], BF16)) for c in range(NC8)]]
        grb = [stack.enter_context(nc.sbuf_tensor(f"p5_grb{c}", [128, LT], BF16)) for c in range(NC8)]
        kfT = [stack.enter_context(nc.sbuf_tensor(f"p5_kf{i}", [128, LT], BF16)) for i in range(16)]
        relsc = stack.enter_context(nc.sbuf_tensor("p5_rel", [128, LT], F32))
        kvs = [stack.enter_context(nc.sbuf_tensor(f"p5_kvs{c}", [128, LT], F32)) for c in range(NC8)]
        kvp_t = x2f
        sig = relsc
        dlt = kvs
        x3t = dlt
        xb3t = gkb
        onesc = stack.enter_context(nc.sbuf_tensor("p5_onesc", [128, 1], BF16))
        onesr = stack.enter_context(nc.sbuf_tensor("p5_onesr", [1, 128], BF16))
        lnscr = make_ln_scratch(S, stack)
        identf, idrdy = make_identity(S, stack, F32, "f")
        pk = [stack.enter_context(nc.psum_tensor(f"p5_pk{i}", [128, LT], F32)) for i in range(2)]
        pkv = [stack.enter_context(nc.psum_tensor(f"p5_pkv{i}", [128, LT], F32)) for i in range(2)]
        pvec = stack.enter_context(nc.psum_tensor("p5_pvec", [128, LT], F32))
        pl = [stack.enter_context(nc.psum_tensor(f"p5_pl{i}", [128, LT], F32)) for i in range(2)]

        lq = S.dmaq("sync", "lq", 4)
        gq = S.dmaq("gpsimd", "gq", 4)
        PE, SC, VE, PL = S.eng["tensor"], S.eng["scalar"], S.eng["vector"], S.eng["gpsimd"]
        rel = {}

        def wait_rel(eng, key):
            for sem, cnt in rel.get(key, ()):
                S.wait(eng, sem, cnt)

        vecs = load_vecs_fm(S, stack, [
            ("l2g", A["L1_attn_ln2_g"], D), ("l2b", A["L1_attn_ln2_b"], D),
            ("fmk", A["L1_attn_fm_k"], D), ("fmr", A["L1_attn_fm_r"], D)],
            identf, idrdy, pvec, lq)
        ln2g, ln2b, fmk, fmr = vecs["l2g"], vecs["l2b"], vecs["fmk"], vecs["fmr"]
        S.op("vector", lambda v: v.memset(onesc[:], 1.0))
        S.op("vector", lambda v: v.memset(onesr[:], 1.0))
        for c in range(NC8):
            S.op("vector", lambda v, _c=c: v.memset(gbuf[_c][:, 0:1], 0.0))
        wld = {}

        def load_dfh_weights(dfh):
            if "cast" in CROSS:
                S.wait("sync", CROSS["cast"][0], CROSS["cast"][1])
            w = lq.wave()
            wld[dfh] = w
            for c in range(NC8):
                S.dma("sync", w, fkTh[:, c * 2048:(c + 1) * 2048],
                      A["fk_bf"][dfh * 2048:(dfh + 1) * 2048, c * 128:(c + 1) * 128], transpose=True)
            for d in range(16):
                S.dma("sync", w, fvTh[:, d * 1024:(d + 1) * 1024],
                      A["fv_bf"][:, dfh * 2048 + d * 128:dfh * 2048 + (d + 1) * 128], transpose=True)

        P5SEC = int(os.environ.get("P5SEC", "4"))
        kfT_rel = 0
        load_dfh_weights(0)

        def emit_a0(lt):
            tw = lq.wave()
            wait_rel("sync", "xmf")
            for c in range(NC8):
                S.dma("sync", tw, xmf[c][:, :], A["xm"][c * 128:(c + 1) * 128, lt * LT:(lt + 1) * LT])
            S.wait("vector", tw)
            S.wait("scalar", tw)
            if lt > 0:
                for c in range(NC8):
                    S.op("vector", lambda v, _c=c: v.tensor_copy(gbuf[_c][:, 0:1], gbuf[_c][:, LT:LT + 1]))
            emit_ln(S, lnscr, xmf, onesc, onesr, pl[0], pl[1],
                    [gbuf[c][:, 1:LT + 1] for c in range(NC8)], None, lq, lt, g_vec=ln2g, b_vec=ln2b)
            rel["xmf"] = ((VE, VE.n),)
            par = lt % 2
            wait_rel("vector", ("gmix", par))
            wait_rel("vector", "grst")
            for c in range(NC8):
                g_ = gbuf[c][:, 1:LT + 1]
                gg = gbuf[c][:, 0:LT]
                S.op("vector", lambda v, _g=g_, _gg=gg: v.tensor_tensor(dscr[:, :], _g, _gg, OP.subtract))
                S.op("vector", lambda v, _c=c, _gg=gg, _p=par: v.scalar_tensor_tensor(
                    gkp[_p][_c][:, :], dscr[:, :], fmk[:, _c:_c + 1], _gg, OP.mult, OP.add))
                S.op("vector", lambda v, _c=c, _gg=gg: v.scalar_tensor_tensor(
                    grb[_c][:, :], dscr[:, :], fmr[:, _c:_c + 1], _gg, OP.mult, OP.add))
            S.wait("gpsimd", VE)
            mw = gq.wave()
            for c in range(NC8):
                S.dma("gpsimd", mw, A["gkb"][c * 128:(c + 1) * 128, lt * LT:(lt + 1) * LT], gkp[par][c][:, :])
                S.dma("gpsimd", mw, A["grb"][c * 128:(c + 1) * 128, lt * LT:(lt + 1) * LT], grb[c][:, :])
            rel["grst"] = ((mw, mw.n),)
            rel[("a0", lt)] = ((VE, VE.n),)

        def emit_kfkv(lt, dfh, gk_tiles):
            nonlocal kfT_rel
            for dfc in range(16):
                bank = pk[dfc % 2]
                wait_rel("tensor", ("pk", dfc % 2))
                for kc in range(NC8):
                    S.op("tensor", lambda t, _d=dfc, _kc=kc, _b=bank, _g=gk_tiles: t.matmul(
                        _b[:, :], fkTh[:, _kc * 2048 + _d * 128:_kc * 2048 + (_d + 1) * 128],
                        _g[_kc][:, :], start=(_kc == 0), stop=(_kc == NC8 - 1)))
                S.wait("scalar", PE)
                S.wait("scalar", PE, kfT_rel)
                S.op("scalar", lambda s, _b=bank: s.activation(relsc[:, :], _b[:, :], AF.Relu))
                rel[("pk", dfc % 2)] = ((SC, SC.n),)
                S.op("scalar", lambda s, _d=dfc: s.activation(kfT[_d][:, :], relsc[:, :], AF.Square))
            for oc in range(NC8):
                bank = pkv[oc % 2]
                wait_rel("tensor", ("pkv", oc % 2))
                S.wait("tensor", SC)
                for dfc in range(16):
                    S.op("tensor", lambda t, _oc=oc, _d=dfc, _b=bank: t.matmul(
                        _b[:, :], fvTh[:, _d * 1024 + _oc * 128:_d * 1024 + (_oc + 1) * 128],
                        kfT[_d][:, :], start=(_d == 0), stop=(_d == 15)))
                S.wait("scalar", PE)
                wait_rel("scalar", ("kvs", oc))
                S.op("scalar", lambda s, _oc=oc, _b=bank: s.activation(kvs[_oc][:, :], _b[:, :], AF.Copy))
                rel[("pkv", oc % 2)] = ((SC, SC.n),)
                if dfh == 1:
                    S.wait("vector", SC)
                    S.op("vector", lambda v, _oc=oc: v.tensor_tensor(
                        kvs[_oc][:, :], kvs[_oc][:, :], kvp_t[_oc][:, :], OP.add))
            kfT_rel = PE.n
            if dfh == 1:
                rel["kvp"] = ((VE, VE.n),)
            S.wait("gpsimd", VE if dfh == 1 else SC)
            kw = gq.wave()
            dstn = "kvp" if dfh == 0 else "kvT"
            for c in range(NC8):
                S.dma("gpsimd", kw, A[dstn][c * 128:(c + 1) * 128, lt * LT:(lt + 1) * LT], kvs[c][:, :])
            for c in range(NC8):
                rel[("kvs", c)] = ((kw, kw.n),)

        for lt in range(NLT + 1):
            if lt >= 1:
                plt = lt - 1
                par = plt % 2
                for sem, cnt in rel.get(("a0", plt), ()):
                    S.wait("tensor", sem, cnt)
                if plt == 0:
                    S.wait("tensor", wld[0])
                emit_kfkv(plt, 0, gkp[par])
                rel[("gmix", par)] = ((PE, PE.n),)
            if lt < NLT:
                emit_a0(lt)
        S.wait("sync", PE)
        load_dfh_weights(1)
        for lt in range(NLT):
            tw = lq.wave()
            wait_rel("sync", "gki")
            for c in range(NC8):
                S.dma("sync", tw, gkb[c][:, :], A["gkb"][c * 128:(c + 1) * 128, lt * LT:(lt + 1) * LT])
            wait_rel("sync", "kvp")
            for c in range(NC8):
                S.dma("sync", tw, kvp_t[c][:, :], A["kvp"][c * 128:(c + 1) * 128, lt * LT:(lt + 1) * LT])
            S.wait("tensor", tw)
            S.wait("vector", tw)
            if lt == 0:
                S.wait("tensor", wld[1])
                for s_ in gq.sems:
                    S.wait("sync", s_)
            emit_kfkv(lt, 1, gkb)
            rel["gki"] = ((PE, PE.n),)
        for s_ in gq.sems:
            S.wait("sync", s_)
        frT = fkTh
        if "cast" in CROSS:
            S.wait("sync", CROSS["cast"][0], CROSS["cast"][1])
        wldc = lq.wave()
        S.wait("sync", PE)
        for c in range(NC8):
            S.dma("sync", wldc, frT[:, c * 1024:(c + 1) * 1024], A["fr_bf"][:, c * 128:(c + 1) * 128], transpose=True)
        rel.pop("gki", None)
        for lt in range(NLT if P5SEC >= 4 else 0):
            tw = lq.wave()
            wait_rel("sync", "ctile")
            wait_rel("sync", "xmf")
            for c in range(NC8):
                S.dma("sync", tw, grb[c][:, :], A["grb"][c * 128:(c + 1) * 128, lt * LT:(lt + 1) * LT])
                S.dma("sync", tw, kvs[c][:, :], A["kvT"][c * 128:(c + 1) * 128, lt * LT:(lt + 1) * LT])
                S.dma("sync", tw, xmf[c][:, :], A["xm"][c * 128:(c + 1) * 128, lt * LT:(lt + 1) * LT])
                S.dma("sync", tw, x2f[c][:, :], A["x2"][c * 128:(c + 1) * 128, lt * LT:(lt + 1) * LT])
            S.wait("tensor", tw)
            if lt == 0:
                S.wait("tensor", wldc)
            S.wait("vector", tw)
            S.wait("scalar", tw)
            for oc in range(NC8):
                bank = pk[oc % 2]
                wait_rel("tensor", ("pk", oc % 2))
                for kc in range(NC8):
                    S.op("tensor", lambda t, _oc=oc, _kc=kc, _b=bank: t.matmul(
                        _b[:, :], frT[:, _kc * 1024 + _oc * 128:_kc * 1024 + (_oc + 1) * 128],
                        grb[_kc][:, :], start=(_kc == 0), stop=(_kc == NC8 - 1)))
                S.wait("scalar", PE)
                S.wait("scalar", VE)
                S.op("scalar", lambda s, _b=bank: s.activation(sig[:, :], _b[:, :], AF.Sigmoid))
                rel[("pk", oc % 2)] = ((SC, SC.n),)
                S.wait("vector", SC)
                S.op("vector", lambda v, _oc=oc: v.tensor_tensor(sig[:, :], sig[:, :], kvs[_oc][:, :], OP.mult))
                S.op("vector", lambda v, _oc=oc: v.tensor_tensor(dlt[_oc][:, :], x2f[_oc][:, :], xmf[_oc][:, :], OP.add))
                S.op("vector", lambda v, _oc=oc: v.tensor_tensor(dlt[_oc][:, :], dlt[_oc][:, :], sig[:, :], OP.add))
            S.wait("scalar", VE)
            emit_ln(S, lnscr, dlt, onesc, onesr, pl[0], pl[1], x3t, xb3t, lq, lt)
            S.wait("gpsimd", VE)
            sw = gq.wave()
            for c in range(NC8):
                S.dma("gpsimd", sw, A["x3"][c * 128:(c + 1) * 128, lt * LT:(lt + 1) * LT], x3t[c][:, :])
                S.dma("gpsimd", sw, A["xb3"][c * 128:(c + 1) * 128, lt * LT:(lt + 1) * LT], xb3t[c][:, :])
            rel["ctile"] = ((sw, sw.n), (VE, VE.n))
        for s_ in gq.sems:
            S.wait("gpsimd", s_)
        S.flush()



def phase6(nc, A):
    with ExitStack() as stack:
        S = Sched(nc, stack, "p6")
        ws1b = stack.enter_context(nc.sbuf_tensor("p6_ws1", [128, NC8 * 1024], BF16))
        ws3b = stack.enter_context(nc.sbuf_tensor("p6_ws3", [128, NC8 * 1024], BF16))
        gw = stack.enter_context(nc.sbuf_tensor("p6_gw", [16, 1024], F32))
        gwT = stack.enter_context(nc.sbuf_tensor("p6_gwT", [128, NC8 * 16], F32))
        x3f = [stack.enter_context(nc.sbuf_tensor(f"p6_x3f{c}", [128, LT], F32)) for c in range(NC8)]
        xb3 = [stack.enter_context(nc.sbuf_tensor(f"p6_xb3{c}", [128, LT], BF16)) for c in range(NC8)]
        glog_s = stack.enter_context(nc.sbuf_tensor("p6_glog", [16, LT], F32))
        ptok = stack.enter_context(nc.sbuf_tensor("p6_ptok", [128, 64], F32))
        e16 = stack.enter_context(nc.sbuf_tensor("p6_e16", [128, 16], F32))
        p16 = stack.enter_context(nc.sbuf_tensor("p6_p16", [128, 16], F32))
        m16 = stack.enter_context(nc.sbuf_tensor("p6_m16", [128, 16], F32))
        cur = stack.enter_context(nc.sbuf_tensor("p6_cur", [128, 16], F32))
        gm16 = stack.enter_context(nc.sbuf_tensor("p6_gm16", [128, 16], F32))
        gsc = stack.enter_context(nc.sbuf_tensor("p6_gsc", [128, 4], F32))
        gs = stack.enter_context(nc.sbuf_tensor("p6_gs", [128, 8], F32))
        ctok = stack.enter_context(nc.sbuf_tensor("p6_ctok", [128, 64], F32))
        ctr_s = stack.enter_context(nc.sbuf_tensor("p6_ctr", [16, LT], BF16))
        s1f = stack.enter_context(nc.sbuf_tensor("p6_s1f", [128, LT], F32))
        ashb = [stack.enter_context(nc.sbuf_tensor(f"p6_ash{c}", [128, LT], BF16)) for c in range(NC8)]
        identf, idrdy = make_identity(S, stack, F32, "f")
        pg = [stack.enter_context(nc.psum_tensor(f"p6_pg{i}", [128, LT], F32)) for i in range(2)]
        ph = [stack.enter_context(nc.psum_tensor(f"p6_ph{i}", [128, LT], F32)) for i in range(4)]
        pt2 = stack.enter_context(nc.psum_tensor("p6_pt2", [128, LT], F32))

        lq = S.dmaq("sync", "lq", 3)
        gq = S.dmaq("gpsimd", "gq", 3)
        PE, SC, VE, PL = S.eng["tensor"], S.eng["scalar"], S.eng["vector"], S.eng["gpsimd"]

        wldp = gq.wave()
        for c in range(NC8):
            S.dma("gpsimd", wldp, ws1b[:, c * 1024:(c + 1) * 1024], A["L1_ffn_ws1"][c * 128:(c + 1) * 128, :])
            S.dma("gpsimd", wldp, ws3b[:, c * 1024:(c + 1) * 1024], A["L1_ffn_ws3"][c * 128:(c + 1) * 128, :])
        wld = lq.wave()
        S.dma("sync", wld, gw[:, :], A["L1_ffn_gate_w"][:, :])
        S.wait("tensor", wld)
        S.wait("tensor", PL, idrdy)
        for c in range(NC8):
            S.op("tensor", lambda t, _c=c: t.transpose(
                pt2[:, _c * 16:(_c + 1) * 16], gw[0:16, _c * 128:(_c + 1) * 128], identf[0:16, 0:16]))
        S.wait("scalar", PE)
        S.op("scalar", lambda s: s.activation(gwT[:, 0:NC8 * 16], pt2[:, 0:NC8 * 16], AF.Copy))
        s1f_rel = 0
        ct_rel = 0

        rel = {}

        def wait_rel(eng, key):
            for sem, cnt in rel.get(key, ()):
                S.wait(eng, sem, cnt)

        for lt in range(NLT):
            tw = lq.wave()
            wait_rel("sync", "xtiles")
            for c in range(NC8):
                S.dma("sync", tw, x3f[c][:, :], A["x3"][c * 128:(c + 1) * 128, lt * LT:(lt + 1) * LT])
                S.dma("sync", tw, xb3[c][:, :], A["xb3"][c * 128:(c + 1) * 128, lt * LT:(lt + 1) * LT])
            S.wait("tensor", tw)
            if lt == 0:
                S.wait("tensor", wldp)
                S.wait("tensor", SC)
                S.wait("tensor", VE)
            wait_rel("tensor", "pt2")
            for kc in range(NC8):
                S.op("tensor", lambda t, _kc=kc: t.matmul(
                    pt2[0:16, :], gwT[:, _kc * 16:(_kc + 1) * 16], x3f[_kc][:, :],
                    start=(_kc == 0), stop=(_kc == NC8 - 1)))
            S.wait("scalar", PE)
            wait_rel("scalar", "glog")
            S.op("scalar", lambda s: s.activation(glog_s[:, :], pt2[0:16, :], AF.Copy))
            S.wait("tensor", SC)
            for j in range(4):
                S.op("tensor", lambda t, _j=j: t.transpose(
                    pt2[:, _j * 16:(_j + 1) * 16], glog_s[0:16, _j * 128:(_j + 1) * 128], identf[0:16, 0:16]))
            S.wait("scalar", PE)
            wait_rel("scalar", "ptok")
            S.op("scalar", lambda s: s.activation(ptok[:, 0:64], pt2[:, 0:64], AF.Copy))
            rel["glog"] = ((PE, PE.n),)
            rel["pt2"] = ((SC, SC.n),)
            for oc in range(NC8):
                b1 = ph[oc % 2]
                b3 = ph[2 + oc % 2]
                wait_rel("tensor", ("ph", oc % 2))
                for kc in range(NC8):
                    S.op("tensor", lambda t, _oc=oc, _kc=kc, _b=b1: t.matmul(
                        _b[:, :], ws1b[:, _kc * 1024 + _oc * 128:_kc * 1024 + (_oc + 1) * 128],
                        xb3[_kc][:, :], start=(_kc == 0), stop=(_kc == NC8 - 1)))
                for kc in range(NC8):
                    S.op("tensor", lambda t, _oc=oc, _kc=kc, _b=b3: t.matmul(
                        _b[:, :], ws3b[:, _kc * 1024 + _oc * 128:_kc * 1024 + (_oc + 1) * 128],
                        xb3[_kc][:, :], start=(_kc == 0), stop=(_kc == NC8 - 1)))
                S.wait("scalar", PE)
                S.wait("scalar", VE, s1f_rel)
                S.op("scalar", lambda s, _b=b1: s.activation(s1f[:, :], _b[:, :], AF.Silu))
                S.wait("vector", SC)
                wait_rel("vector", ("ashb", oc))
                S.op("vector", lambda v, _oc=oc, _b=b3: v.tensor_tensor(
                    ashb[_oc][:, :], s1f[:, :], _b[:, :], OP.mult))
                s1f_rel = VE.n
                rel[("ph", oc % 2)] = ((VE, VE.n), (SC, SC.n))
            rel["xtiles"] = ((PE, PE.n),)
            for j in range(4):
                L = ptok[:, j * 16:(j + 1) * 16]
                S.wait("vector", SC)
                S.wait("vector", PE, ct_rel)
                S.op("vector", lambda v, _L=L: v.reduce_max(gs[:, 0:1], _L, axis=mybir.AxisListType.X))
                S.op("vector", lambda v: v.tensor_scalar(gs[:, 1:2], gs[:, 0:1], -1.0, None, OP.mult))
                S.wait("scalar", VE)
                S.op("scalar", lambda s, _L=L: s.activation(e16[:, :], _L, AF.Exp, bias=gs[:, 1:2]))
                S.wait("vector", SC)
                S.op("vector", lambda v: v.reduce_sum(gs[:, 2:3], e16[:, :], axis=mybir.AxisListType.X))
                S.op("vector", lambda v: v.reciprocal(gs[:, 3:4], gs[:, 2:3]))
                S.op("vector", lambda v: v.tensor_scalar(p16[:, :], e16[:, :], gs[:, 3:4], None, OP.mult))
                S.op("vector", lambda v: v.reduce_max(
                    gsc[:, :], p16[:, :].rearrange("p (g e) -> p g e", g=4), axis=mybir.AxisListType.X))
                S.op("vector", lambda v: v.reduce_max(gs[:, 4:5], gsc[:, :], axis=mybir.AxisListType.X))
                S.op("vector", lambda v: v.tensor_scalar(cur[:, 0:4], gsc[:, :], gs[:, 4:5], None, OP.is_ge))
                S.op("vector", lambda v: v.scalar_tensor_tensor(
                    cur[:, 0:4], cur[:, 0:4], -1e30, gsc[:, :], OP.mult, OP.add))
                S.op("vector", lambda v: v.reduce_max(gs[:, 5:6], cur[:, 0:4], axis=mybir.AxisListType.X))
                S.op("vector", lambda v: v.tensor_scalar(gsc[:, :], gsc[:, :], gs[:, 5:6], None, OP.is_ge))
                S.op("vector", lambda v: v.tensor_copy(
                    gm16[:, :].rearrange("p (g e) -> p g e", g=4),
                    gsc[:, :].broadcast_to((128, 4, 4))))
                S.op("vector", lambda v: v.tensor_tensor(m16[:, :], p16[:, :], gm16[:, :], OP.mult))
                S.op("vector", lambda v: v.tensor_copy(cur[:, :], m16[:, :]))
                for it in range(3):
                    S.op("vector", lambda v: v.reduce_max(gs[:, 6:7], cur[:, :], axis=mybir.AxisListType.X))
                    S.op("vector", lambda v: v.tensor_scalar(e16[:, :], cur[:, :], gs[:, 6:7], None, OP.is_ge))
                    S.op("vector", lambda v: v.scalar_tensor_tensor(
                        cur[:, :], e16[:, :], -1e30, cur[:, :], OP.mult, OP.add))
                S.op("vector", lambda v: v.reduce_max(gs[:, 6:7], cur[:, :], axis=mybir.AxisListType.X))
                S.op("vector", lambda v: v.tensor_scalar(e16[:, :], m16[:, :], gs[:, 6:7], None, OP.is_ge))
                S.op("vector", lambda v, _j=j: v.tensor_tensor(ctok[:, _j * 16:(_j + 1) * 16], p16[:, :], e16[:, :], OP.mult))
            S.wait("tensor", VE)
            wait_rel("tensor", "pt2")
            for j in range(4):
                S.op("tensor", lambda t, _j=j: t.transpose(
                    pt2[0:16, _j * 128:(_j + 1) * 128], ctok[:, _j * 16:(_j + 1) * 16], identf[:, :]))
            ct_rel = PE.n
            rel["ptok"] = ((VE, VE.n),)
            S.wait("scalar", PE)
            wait_rel("scalar", "ctr")
            S.op("scalar", lambda s: s.activation(ctr_s[:, :], pt2[0:16, :], AF.Copy))
            rel["pt2"] = ((SC, SC.n),)
            S.wait("gpsimd", SC)
            cw = gq.wave()
            S.dma("gpsimd", cw, A["ctr"][:, lt * LT:(lt + 1) * LT], ctr_s[:, :])
            rel["ctr"] = ((cw, cw.n),)
            S.wait("gpsimd", VE)
            aw = gq.wave()
            for c in range(NC8):
                S.dma("gpsimd", aw, A["ash"][c * 128:(c + 1) * 128, lt * LT:(lt + 1) * LT], ashb[c][:, :])
            for c in range(NC8):
                rel[("ashb", c)] = ((aw, aw.n),)
        for s_ in gq.sems:
            S.wait("gpsimd", s_)
        S.flush()



def phase7(nc, A):
    with ExitStack() as stack:
        S = Sched(nc, stack, "p7")
        w1b = [stack.enter_context(nc.sbuf_tensor(f"p7_w1_{p}", [128, NC8 * 512], BF16)) for p in range(2)]
        w3b = [stack.enter_context(nc.sbuf_tensor(f"p7_w3_{p}", [128, NC8 * 512], BF16)) for p in range(2)]
        xb3 = [stack.enter_context(nc.sbuf_tensor(f"p7_xb3{c}", [128, T], BF16)) for c in range(NC8)]
        ctr_s = stack.enter_context(nc.sbuf_tensor("p7_ctr", [16, T], BF16))
        sel = [stack.enter_context(nc.sbuf_tensor(f"p7_sel{e}", [16, 128], BF16)) for e in range(16)]
        ones16 = stack.enter_context(nc.sbuf_tensor("p7_ones16", [16, 128], BF16))
        s1f = stack.enter_context(nc.sbuf_tensor("p7_s1f", [128, LT], F32))
        af = stack.enter_context(nc.sbuf_tensor("p7_af", [128, LT], F32))
        ab = [[stack.enter_context(nc.sbuf_tensor(f"p7_ab{p}_{i}", [128, LT], BF16)) for i in range(4)]
              for p in range(2)]
        ph = [stack.enter_context(nc.psum_tensor(f"p7_ph{i}", [128, LT], F32)) for i in range(4)]
        pc = [stack.enter_context(nc.psum_tensor(f"p7_pc{i}", [128, LT], F32)) for i in range(2)]

        lq = S.dmaq("sync", "lq", 3)
        gq = S.dmaq("gpsimd", "gq", 4)
        PE, SC, VE, PL = S.eng["tensor"], S.eng["scalar"], S.eng["vector"], S.eng["gpsimd"]

        rel = {}

        def wait_rel(eng, key):
            for sem, cnt in rel.get(key, ()):
                S.wait(eng, sem, cnt)

        S.op("gpsimd", lambda g: g.memset(ones16[:, :], 1.0))
        for e in range(16):
            S.op("gpsimd", lambda g, _e=e: g.affine_select(
                sel[_e][:, :], ones16[:, :], pattern=[[0, 128]], compare_op=OP.is_equal,
                fill=0.0, base=-_e, channel_multiplier=1))
        cld = lq.wave()
        S.dma("sync", cld, ctr_s[:, :], A["ctr"][:, :])
        for c in range(NC8):
            S.dma("sync", cld, xb3[c][:, :], A["xb3"][c * 128:(c + 1) * 128, :])
        ew = {0: gq.wave()}
        for kc in range(NC8):
            S.dma("gpsimd", ew[0], w1b[0][:, kc * 512:(kc + 1) * 512], A["L1_ffn_w1"][0, kc * 128:(kc + 1) * 128, :])
            S.dma("gpsimd", ew[0], w3b[0][:, kc * 512:(kc + 1) * 512], A["L1_ffn_w3"][0, kc * 128:(kc + 1) * 128, :])

        wrel = {}
        s1f_rel = 0
        for e in range(16):
            if e + 1 < 16:
                nset = (e + 1) % 2
                w = gq.wave()
                ew[e + 1] = w
                if e >= 1:
                    S.wait("gpsimd", PE, wrel[e - 1])
                for kc in range(NC8):
                    S.dma("gpsimd", w, w1b[nset][:, kc * 512:(kc + 1) * 512],
                          A["L1_ffn_w1"][e + 1, kc * 128:(kc + 1) * 128, :])
                    S.dma("gpsimd", w, w3b[nset][:, kc * 512:(kc + 1) * 512],
                          A["L1_ffn_w3"][e + 1, kc * 128:(kc + 1) * 128, :])
            wset = e % 2
            for lt in range(NLT):
                glob = e * NLT + lt
                par = glob % 2
                if e == 0 and lt == 0:
                    S.wait("tensor", cld)
                    S.wait("tensor", PL)
                if lt == 0:
                    S.wait("tensor", ew[e])
                pcb = pc[par]
                wait_rel("tensor", ("pc", par))
                S.op("tensor", lambda t, _e=e, _lt=lt, _b=pcb: t.matmul(
                    _b[:, :], sel[_e][:, :], ctr_s[:, _lt * LT:(_lt + 1) * LT], start=True, stop=True))
                for ic in range(4):
                    b1 = ph[ic % 2]
                    b3 = ph[2 + ic % 2]
                    wait_rel("tensor", ("ph", ic % 2))
                    for kc in range(NC8):
                        S.op("tensor", lambda t, _ic=ic, _kc=kc, _b=b1, _lt=lt, _w=wset: t.matmul(
                            _b[:, :], w1b[_w][:, _kc * 512 + _ic * 128:_kc * 512 + (_ic + 1) * 128],
                            xb3[_kc][:, _lt * LT:(_lt + 1) * LT], start=(_kc == 0), stop=(_kc == NC8 - 1)))
                    for kc in range(NC8):
                        S.op("tensor", lambda t, _ic=ic, _kc=kc, _b=b3, _lt=lt, _w=wset: t.matmul(
                            _b[:, :], w3b[_w][:, _kc * 512 + _ic * 128:_kc * 512 + (_ic + 1) * 128],
                            xb3[_kc][:, _lt * LT:(_lt + 1) * LT], start=(_kc == 0), stop=(_kc == NC8 - 1)))
                    S.wait("scalar", PE)
                    S.wait("scalar", VE, s1f_rel)
                    S.op("scalar", lambda s, _b=b1: s.activation(s1f[:, :], _b[:, :], AF.Silu))
                    S.wait("vector", SC)
                    S.op("vector", lambda v, _b=b3: v.tensor_tensor(af[:, :], s1f[:, :], _b[:, :], OP.mult))
                    s1f_rel = VE.n
                    if ("ab", par) in rel:
                        wait_rel("vector", ("ab", par))
                        rel.pop(("ab", par))
                    S.op("vector", lambda v, _ic=ic, _p=par, _pc=pcb: v.tensor_tensor(
                        ab[_p][_ic][:, :], af[:, :], _pc[:, :], OP.mult))
                    rel[("ph", ic % 2)] = ((VE, VE.n), (SC, SC.n))
                rel[("pc", par)] = ((VE, VE.n),)
                S.wait("gpsimd", VE)
                aw = gq.wave()
                for ic in range(4):
                    S.dma("gpsimd", aw, A["a_all"][e, ic * 128:(ic + 1) * 128, lt * LT:(lt + 1) * LT],
                          ab[par][ic][:, :])
                rel[("ab", par)] = ((aw, aw.n),)
            wrel[e] = PE.n
        for s_ in gq.sems:
            S.wait("gpsimd", s_)
        S.flush()



def phase8(nc, A):
    with ExitStack() as stack:
        S = Sched(nc, stack, "p8")
        w2b = stack.enter_context(nc.sbuf_tensor("p8_w2", [128, 32 * 1024], BF16))
        ws2b = stack.enter_context(nc.sbuf_tensor("p8_ws2", [128, NC8 * 1024], BF16))
        at = [[stack.enter_context(nc.sbuf_tensor(f"p8_a{e}_{i}", [128, LT], BF16)) for i in range(4)]
              for e in range(8)]
        ashb = [stack.enter_context(nc.sbuf_tensor(f"p8_ash{c}", [128, LT], BF16)) for c in range(NC8)]
        x3f = [stack.enter_context(nc.sbuf_tensor(f"p8_x3f{c}", [128, LT], F32)) for c in range(NC8)]
        ypt = [stack.enter_context(nc.sbuf_tensor(f"p8_ypt{c}", [128, LT], F32)) for c in range(NC8)]
        dlt = [stack.enter_context(nc.sbuf_tensor(f"p8_dlt{c}", [128, LT], F32)) for c in range(NC8)]
        x4t = dlt
        stg = stack.enter_context(nc.sbuf_tensor("p8_stg", [128, 1024], F32))
        onesc = stack.enter_context(nc.sbuf_tensor("p8_onesc", [128, 1], BF16))
        onesr = stack.enter_context(nc.sbuf_tensor("p8_onesr", [1, 128], BF16))
        lnscr = make_ln_scratch(S, stack)
        identf, idrdy = make_identity(S, stack, F32, "f")
        py = [stack.enter_context(nc.psum_tensor(f"p8_py{i}", [128, LT], F32)) for i in range(3)]
        ptr = [stack.enter_context(nc.psum_tensor(f"p8_ptr{i}", [128, LT], F32)) for i in range(2)]
        pl = [stack.enter_context(nc.psum_tensor(f"p8_pl{i}", [128, LT], F32)) for i in range(3)]

        lq = S.dmaq("sync", "lq", 4)
        gq = S.dmaq("gpsimd", "gq", 4)
        PE, SC, VE, PL = S.eng["tensor"], S.eng["scalar"], S.eng["vector"], S.eng["gpsimd"]
        rel = {}

        def wait_rel(eng, key):
            for sem, cnt in rel.get(key, ()):
                S.wait(eng, sem, cnt)

        S.op("vector", lambda v: v.memset(onesc[:], 1.0))
        S.op("vector", lambda v: v.memset(onesr[:], 1.0))

        for half in range(2):
            if half == 1:
                for s_ in gq.sems:
                    S.wait("sync", s_)
            ww = gq.wave()
            if half == 1:
                S.wait("gpsimd", PE)
            for eg in range(8):
                e = half * 8 + eg
                for ic in range(4):
                    S.dma("gpsimd", ww, w2b[:, (eg * 4 + ic) * 1024:(eg * 4 + ic + 1) * 1024],
                          A["L1_ffn_w2"][e, ic * 128:(ic + 1) * 128, :])
            if half == 0:
                for c in range(NC8):
                    S.dma("gpsimd", ww, ws2b[:, c * 1024:(c + 1) * 1024],
                          A["L1_ffn_ws2"][c * 128:(c + 1) * 128, :])
            for lt in range(NLT):
                tw = lq.wave()
                wait_rel("sync", "atiles")
                for eg in range(8):
                    e = half * 8 + eg
                    for ic in range(4):
                        S.dma("sync", tw, at[eg][ic][:, :],
                              A["a_all"][e, ic * 128:(ic + 1) * 128, lt * LT:(lt + 1) * LT])
                if half == 0:
                    wait_rel("sync", "ashb")
                    for c in range(NC8):
                        S.dma("sync", tw, ashb[c][:, :], A["ash"][c * 128:(c + 1) * 128, lt * LT:(lt + 1) * LT])
                else:
                    wait_rel("sync", "yxt")
                    for c in range(NC8):
                        S.dma("sync", tw, ypt[c][:, :], A["ypart"][c * 128:(c + 1) * 128, lt * LT:(lt + 1) * LT])
                        S.dma("sync", tw, x3f[c][:, :], A["x3"][c * 128:(c + 1) * 128, lt * LT:(lt + 1) * LT])
                S.wait("tensor", tw)
                if lt == 0:
                    S.wait("tensor", ww)
                    S.wait("tensor", PL, idrdy)
                    S.wait("tensor", VE)
                for oc in range(NC8):
                    bank = py[oc % 3]
                    wait_rel("tensor", ("py", oc % 3))
                    first = True
                    for eg in range(8):
                        for ic in range(4):
                            last = (half == 1 and eg == 7 and ic == 3)
                            S.op("tensor", lambda t, _eg=eg, _ic=ic, _oc=oc, _b=bank, _f=first, _l=last: t.matmul(
                                _b[:, :], w2b[:, (_eg * 4 + _ic) * 1024 + _oc * 128:(_eg * 4 + _ic) * 1024 + (_oc + 1) * 128],
                                at[_eg][_ic][:, :], start=_f, stop=_l))
                            first = False
                    if half == 0:
                        for kc in range(NC8):
                            S.op("tensor", lambda t, _kc=kc, _oc=oc, _b=bank: t.matmul(
                                _b[:, :], ws2b[:, _kc * 1024 + _oc * 128:_kc * 1024 + (_oc + 1) * 128],
                                ashb[_kc][:, :], start=False, stop=(_kc == NC8 - 1)))
                    S.wait("scalar", PE)
                    if half == 0:
                        wait_rel("scalar", ("ypt", oc))
                        S.op("scalar", lambda s, _oc=oc, _b=bank: s.activation(ypt[_oc][:, :], _b[:, :], AF.Copy))
                    else:
                        wait_rel("scalar", ("dlt", oc))
                        S.op("scalar", lambda s, _oc=oc, _b=bank: s.activation(dlt[_oc][:, :], _b[:, :], AF.Copy))
                    rel[("py", oc % 3)] = ((SC, SC.n),)
                rel["atiles"] = ((PE, PE.n),)
                if half == 0:
                    rel["ashb"] = ((PE, PE.n),)
                    S.wait("gpsimd", SC)
                    yw = gq.wave()
                    for c in range(NC8):
                        S.dma("gpsimd", yw, A["ypart"][c * 128:(c + 1) * 128, lt * LT:(lt + 1) * LT], ypt[c][:, :])
                    for c in range(NC8):
                        rel[("ypt", c)] = ((yw, yw.n),)
                else:
                    S.wait("vector", SC)
                    for c in range(NC8):
                        S.op("vector", lambda v, _c=c: v.tensor_tensor(dlt[_c][:, :], dlt[_c][:, :], ypt[_c][:, :], OP.add))
                        S.op("vector", lambda v, _c=c: v.tensor_tensor(dlt[_c][:, :], dlt[_c][:, :], x3f[_c][:, :], OP.add))
                    rel["yxt"] = ((VE, VE.n),)
                    S.wait("scalar", VE)
                    emit_ln(S, lnscr, dlt, onesc, onesr, pl[0], pl[1], x4t, None, lq, lt)
                    S.wait("tensor", VE)
                    for ts in range(4):
                        for c in range(NC8):
                            bank = ptr[c % 2]
                            wait_rel("tensor", ("ptr", c % 2))
                            S.wait("tensor", SC)
                            S.op("tensor", lambda t, _c=c, _ts=ts, _b=bank: t.transpose(
                                _b[:, 0:128], x4t[_c][:, _ts * 128:(_ts + 1) * 128], identf[:, :]))
                            S.wait("scalar", PE)
                            wait_rel("scalar", "stg")
                            S.op("scalar", lambda s, _c=c, _b=bank: s.activation(
                                stg[:, _c * 128:(_c + 1) * 128], _b[:, 0:128], AF.Copy))
                            rel[("ptr", c % 2)] = ((SC, SC.n),)
                        S.wait("gpsimd", SC)
                        ow = gq.wave()
                        S.dma("gpsimd", ow, A["out"][lt * LT + ts * 128:lt * LT + (ts + 1) * 128, :], stg[:, :])
                        rel["stg"] = ((ow, ow.n),)
                    for c in range(NC8):
                        rel[("dlt", c)] = ((PE, PE.n),)
        for s_ in gq.sems:
            S.wait("gpsimd", s_)
        S.flush()


PHASE_FNS = {0: phase0, 1: phase1, 2: phase2, 3: phase3, 4: phase4, 5: phase5, 6: phase6, 7: phase7, 8: phase8}


def build(phases):
    CROSS.clear()
    DRAM_KINDS.clear()
    nc = bass.Bass("TRN2", target_bir_lowering=False)
    present = set(phases)
    A = {}
    A["x_in"] = declare_dram(nc, "x_in", (T, D), F32, "ExternalInput")
    for name, shp in WEIGHTS.items():
        A[name] = declare_dram(nc, name, shp, F32, "ExternalInput")
    A["out"] = declare_dram(nc, "out", (T, D), F32, "ExternalOutput")
    for name, (shp, dt, prod) in INTERMEDIATES.items():
        readers = [p for p in present if name in PHASE_READS.get(p, [])]
        written = prod in present
        if written and readers:
            kind = "Internal"
        elif written:
            kind = "ExternalOutput"
        elif readers:
            kind = "ExternalInput"
        else:
            continue
        A[name] = declare_dram(nc, name, shp, dt, kind)
    for p in phases:
        PHASE_FNS[p](nc, A)
    return nc, A


def make_in_maps(inputs, extra=None):
    x = np.ascontiguousarray(np.asarray(inputs["L0_attn_x"], dtype=np.float32))
    in_maps = []
    for b in range(8):
        m = {"x_in": np.ascontiguousarray(x[b])}
        for name, shp in WEIGHTS.items():
            m[name] = np.ascontiguousarray(np.asarray(inputs[name], dtype=np.float32).reshape(shp))
        if extra is not None:
            m.update(extra[b])
        in_maps.append(m)
    return in_maps


def kernel(**inputs):
    phases = sorted(PHASE_FNS.keys())
    nc, A = build(phases)
    in_maps = make_in_maps(inputs)
    res = run_bass_kernel_spmd(nc, in_maps, core_ids=list(range(8)))
    return np.stack([np.asarray(r["out"]) for r in res.results], axis=0)


# version 07
# speedup vs baseline: 1.0290x; 1.0221x over previous
import math
import os
from contextlib import ExitStack

import numpy as np
import concourse.bass as bass
import concourse.mybir as mybir
from concourse.bass_utils import run_bass_kernel_spmd

F32 = mybir.dt.float32
BF16 = mybir.dt.bfloat16
AF = mybir.ActivationFunctionType
OP = mybir.AluOpType

T = 4096
D = 1024
NC8 = 8
LT = 512
NLT = T // LT
H = 16
DH = 64
EPS = 1e-5

ENGINES = ("sync", "tensor", "scalar", "vector", "gpsimd")


class Sem:
    def __init__(self, stack, nc, name):
        self.h = nc.alloc_semaphore(name)
        self.n = 0


class DmaQ:

    def __init__(self, S, eng, tag, n=3):
        self.S = S
        self.eng = eng
        self.sems = [S.sem(f"{tag}{i}") for i in range(n)]
        self.n = n
        self.w = 0

    def wave(self):
        s = self.sems[self.w % self.n]
        if self.w >= self.n:
            self.S.wait(self.eng, s)
        self.w += 1
        return s


AUTO_SERIAL = ("vector", "scalar", "gpsimd")


class Sched:

    def __init__(self, nc, stack, name):
        self.nc = nc
        self.stack = stack
        self.name = name
        self.ops = {e: [] for e in ENGINES}
        self._semid = 0
        self.sems = []
        self.eng = {e: self.sem(e[:2]) for e in ("tensor", "scalar", "vector", "gpsimd")}

    def sem(self, tag=""):
        self._semid += 1
        s = Sem(self.stack, self.nc, f"{self.name}_s{self._semid}{tag}")
        self.sems.append(s)
        return s

    def wait(self, eng, sem, val=None):
        v = sem.n if val is None else val
        if v > 0:
            self.ops[eng].append(lambda e, _h=sem.h, _v=v: e.wait_ge(_h, _v))

    def op(self, eng, fn):
        s = self.eng[eng]
        if eng in AUTO_SERIAL and s.n > 0:
            self.wait(eng, s)
        self.ops[eng].append(lambda e, _f=fn, _h=s.h: _f(e).then_inc(_h, 1))
        s.n += 1
        return s.n

    def dmaq(self, eng, tag, n=3):
        return DmaQ(self, eng, tag, n)

    def dma(self, eng, sem, out, in_, **kw):
        self.ops[eng].append(lambda e, _o=out, _i=in_, _h=sem.h, _kw=kw: e.dma_start(
            out=_o, in_=_i, **_kw).then_inc(_h, 16))
        sem.n += 16

    def barrier(self, engines=ENGINES):
        for e in engines:
            for s in self.sems:
                if not (e in self.eng and self.eng[e] is s):
                    self.wait(e, s)

    def flush(self):
        with self.nc.Block() as block:
            for e in ENGINES:
                if self.ops[e]:
                    def body(eng, _l=tuple(self.ops[e])):
                        for f in _l:
                            f(eng)
                    getattr(block, e)(body)
        self.ops = {e: [] for e in ENGINES}


DRAM_KINDS = {}
CROSS = {}


def declare_dram(nc, name, shape, dtype, kind):
    DRAM_KINDS[name] = kind
    return nc.dram_tensor(name, list(shape), dtype, kind=kind).ap()


WEIGHTS = {
    "L0_attn_Wqkv_w": (3072, 1024), "L0_attn_Wqkv_b": (1, 3072),
    "L0_attn_out_w": (1024, 1024), "L0_attn_out_b": (1, 1024),
    "L0_ffn_gate_W": (8, 1024), "L0_ffn_expert_W": (8, 1024, 1024),
    "L0_ffn_out_W": (1024, 1024),
    "L1_attn_ln1_g": (1, 1024), "L1_attn_ln1_b": (1, 1024),
    "L1_attn_ln2_g": (1, 1024), "L1_attn_ln2_b": (1, 1024),
    "L1_attn_tm_k": (1, 1024), "L1_attn_tm_v": (1, 1024), "L1_attn_tm_r": (1, 1024),
    "L1_attn_time_decay": (1, 1024), "L1_attn_time_first": (1, 1024),
    "L1_attn_Wk": (1024, 1024), "L1_attn_Wv": (1024, 1024),
    "L1_attn_Wr": (1024, 1024), "L1_attn_Wo": (1024, 1024),
    "L1_attn_fm_k": (1, 1024), "L1_attn_fm_r": (1, 1024),
    "L1_attn_Fk": (4096, 1024), "L1_attn_Fr": (1024, 1024), "L1_attn_Fv": (1024, 4096),
    "L1_ffn_gate_w": (16, 1024),
    "L1_ffn_w1": (16, 1024, 512), "L1_ffn_w2": (16, 512, 1024), "L1_ffn_w3": (16, 1024, 512),
    "L1_ffn_ws1": (1024, 1024), "L1_ffn_ws2": (1024, 1024), "L1_ffn_ws3": (1024, 1024),
}

INTERMEDIATES = {
    "x0": ((D, T), F32, 0), "xb0": ((D, T), BF16, 0),
    "wqkv_bf": ((3072, 1024), BF16, 0),
    "outw_bf": ((1024, 1024), BF16, 0),
    "expw0_bf": ((1024, 1024), BF16, 0),
    "expw1_bf": ((1024, 1024), BF16, 0),
    "outW_bf": ((1024, 1024), BF16, 0),
    "wk_bf": ((1024, 1024), BF16, 0), "wv_bf": ((1024, 1024), BF16, 0),
    "wr_bf": ((1024, 1024), BF16, 0), "wo_bf": ((1024, 1024), BF16, 0),
    "fk_bf": ((4096, 1024), BF16, 0), "fr_bf": ((1024, 1024), BF16, 0),
    "fv_bf": ((1024, 4096), BF16, 0),
    "x1": ((D, T), F32, 1), "xb1": ((D, T), BF16, 1),
    "x2": ((D, T), F32, 2), "xb2": ((D, T), BF16, 2),
    "kTf": ((D, T), F32, 3), "vTf": ((D, T), F32, 3), "srT": ((D, T), BF16, 4),
    "xrT": ((D, T), BF16, 3),
    "xm": ((D, T), F32, 4), "xmb": ((D, T), BF16, 4),
    "x3": ((D, T), F32, 5), "xb3": ((D, T), BF16, 5),
    "gkb": ((D, T), BF16, 5), "grb": ((D, T), BF16, 5),
    "kvp": ((D, T), F32, 5), "kvT": ((D, T), F32, 5),
    "ctr": ((16, T), BF16, 6), "ash": ((D, T), BF16, 6),
    "a_all": ((16, 512, T), BF16, 7),
    "ypart": ((D, T), F32, 8),
}

PHASE_READS = {
    1: ["x0", "xb0", "wqkv_bf", "outw_bf"],
    2: ["x1", "xb1", "expw0_bf", "expw1_bf", "outW_bf"],
    3: ["x2", "xb2", "wk_bf", "wv_bf", "wr_bf"],
    4: ["x2", "kTf", "vTf", "xrT", "srT", "wr_bf", "wo_bf"],
    5: ["x2", "xm", "fk_bf", "fr_bf", "fv_bf", "gkb", "grb", "kvp", "kvT"],
    6: ["x3", "xb3"],
    7: ["xb3", "ctr"],
    8: ["x3", "a_all", "ash", "ypart"],
}



def make_identity(S, stack, dtype, tag):
    nc = S.nc
    ones = stack.enter_context(nc.sbuf_tensor(f"{S.name}_idsrc_{tag}", [128, 128], dtype))
    ident = stack.enter_context(nc.sbuf_tensor(f"{S.name}_ident_{tag}", [128, 128], dtype))
    S.op("gpsimd", lambda g: g.memset(ones[:], 1.0))
    S.op("gpsimd", lambda g: g.affine_select(
        ident[:], ones[:], pattern=[[-1, 128]], compare_op=OP.is_equal, fill=0.0,
        base=0, channel_multiplier=1))
    return ident, S.eng["gpsimd"].n


def load_vecs_fm(S, stack, specs, identity, id_ready, psum, ldq):
    nc = S.nc
    nmax = max(n for _, _, n in specs)
    row = stack.enter_context(nc.sbuf_tensor(f"{S.name}_vrow", [1, nmax], F32))
    outs = {}
    col = 0
    S.wait("tensor", S.eng["gpsimd"], id_ready)
    S.wait("tensor", S.eng["scalar"])
    for key, ap, n in specs:
        nch = n // 128
        w = ldq.wave()
        S.wait("sync", S.eng["tensor"])
        S.dma(ldq.eng, w, row[0:1, 0:n], ap[0:1, :])
        S.wait("tensor", w)
        for c in range(nch):
            S.op("tensor", lambda t, _c=c, _col=col: t.transpose(
                psum[:, _col + _c:_col + _c + 1], row[0:1, _c * 128:(_c + 1) * 128],
                identity[0:1, 0:1]))
        outs[key] = stack.enter_context(nc.sbuf_tensor(f"{S.name}_vfm_{key}", [128, nch], F32))
        S.wait("scalar", S.eng["tensor"])
        S.op("scalar", lambda s, _k=key, _n=nch, _col=col: s.activation(
            outs[_k][:, 0:_n], psum[:, _col:_col + _n], AF.Copy))
        col += nch
    return outs


def make_ln_scratch(S, stack):
    nc = S.nc
    scr = {
        "rb": [stack.enter_context(nc.sbuf_tensor(f"{S.name}_lnrb{c}", [128, LT], BF16)) for c in range(NC8)],
        "sq": [stack.enter_context(nc.sbuf_tensor(f"{S.name}_lnsq{c}", [128, LT], BF16)) for c in range(NC8)],
        "mb": stack.enter_context(nc.sbuf_tensor(f"{S.name}_lnmb", [1, LT], BF16)),
        "rs": stack.enter_context(nc.sbuf_tensor(f"{S.name}_lnrs", [1, LT], BF16)),
        "rows": stack.enter_context(nc.sbuf_tensor(f"{S.name}_lnrows", [1, 4 * LT], F32)),
        "epsr": stack.enter_context(nc.sbuf_tensor(f"{S.name}_lneps", [1, 1], F32)),
        "rel": {},
    }
    S.op("vector", lambda v: v.memset(scr["epsr"][:, :], EPS))
    return scr


def emit_ln(S, scr, r_tiles, onesc, onesr, bsum, bsumsq, out_f32, out_b16, ldq,
            tix, g_vec=None, b_vec=None):
    rb, sq, rows = scr["rb"], scr["sq"], scr["rows"]
    mrow = scr["mb"]
    rrow = scr["rs"]
    lrel = scr["rel"]
    PE, SC, VE = S.eng["tensor"], S.eng["scalar"], S.eng["vector"]
    for sem, cnt in lrel.get(("rbsq", 0), ()):
        S.wait("vector", sem, cnt)
        S.wait("scalar", sem, cnt)
    for c in range(NC8):
        S.op("vector", lambda v, _c=c: v.tensor_copy(rb[_c][:, :], r_tiles[_c][:, :]))
        S.op("scalar", lambda s, _c=c: s.activation(sq[_c][:, :], r_tiles[_c][:, :], AF.Square))
    S.wait("tensor", VE)
    S.wait("tensor", SC)
    for c in range(NC8):
        S.op("tensor", lambda t, _c=c: t.matmul(
            bsum[0:1, :], onesc[:, 0:1], rb[_c][:, :], start=(_c == 0), stop=(_c == NC8 - 1)))
        S.op("tensor", lambda t, _c=c: t.matmul(
            bsumsq[0:1, :], onesc[:, 0:1], sq[_c][:, :], start=(_c == 0), stop=(_c == NC8 - 1)))
    lrel[("rbsq", 0)] = ((PE, PE.n),)
    mean = rows[0:1, 0:LT]
    msq = rows[0:1, LT:2 * LT]
    var = rows[0:1, 2 * LT:3 * LT]
    rstd = rows[0:1, 3 * LT:4 * LT]
    S.wait("vector", PE)
    S.op("vector", lambda v: v.tensor_scalar(mean, bsum[0:1, :], 1.0 / D, None, OP.mult))
    S.op("vector", lambda v: v.tensor_tensor(msq, mean, mean, OP.mult))
    S.op("vector", lambda v: v.scalar_tensor_tensor(
        var, bsumsq[0:1, :], 1.0 / D, msq, OP.mult, OP.subtract))
    S.wait("scalar", VE)
    S.op("scalar", lambda s: s.activation(var, var, AF.Ln, bias=scr["epsr"][0:1, 0:1], scale=1.0))
    S.op("scalar", lambda s: s.activation(rstd, var, AF.Exp, scale=-0.5))
    S.wait("vector", SC)
    S.op("vector", lambda v: v.tensor_copy(mrow[0:1, :], mean))
    S.op("vector", lambda v: v.tensor_copy(rrow[0:1, :], rstd))
    S.wait("tensor", VE)
    S.op("tensor", lambda t: t.matmul(bsumsq[:, :], onesr[0:1, :], mrow[0:1, :], start=True, stop=True))
    S.op("tensor", lambda t: t.matmul(bsum[:, :], onesr[0:1, :], rrow[0:1, :], start=True, stop=True))
    S.wait("vector", PE)
    for c in range(NC8):
        S.op("vector", lambda v, _c=c: v.tensor_tensor(out_f32[_c][:, :], r_tiles[_c][:, :], bsumsq[:, :], OP.subtract))
        if g_vec is not None:
            S.op("vector", lambda v, _c=c: v.tensor_tensor(out_f32[_c][:, :], out_f32[_c][:, :], bsum[:, :], OP.mult))
            S.op("vector", lambda v, _c=c: v.tensor_scalar(
                out_f32[_c][:, :], out_f32[_c][:, :], g_vec[:, _c:_c + 1], b_vec[:, _c:_c + 1], OP.mult, OP.add))
        else:
            S.op("vector", lambda v, _c=c: v.tensor_tensor(out_f32[_c][:, :], out_f32[_c][:, :], bsum[:, :], OP.mult))
        if out_b16 is not None:
            S.op("vector", lambda v, _c=c: v.tensor_copy(out_b16[_c][:, :], out_f32[_c][:, :]))



def phase0(nc, A):
    with ExitStack() as stack:
        S = Sched(nc, stack, "p0")
        gq = S.dmaq("gpsimd", "gq", 3)
        cq = S.dmaq("gpsimd", "cq", 1)
        cast = cq.wave()
        casts = [
            ("L0_attn_Wqkv_w", "wqkv_bf"), ("L0_attn_out_w", "outw_bf"),
            ("L0_ffn_out_W", "outW_bf"),
            ("L1_attn_Wk", "wk_bf"), ("L1_attn_Wv", "wv_bf"),
            ("L1_attn_Wr", "wr_bf"), ("L1_attn_Wo", "wo_bf"),
            ("L1_attn_Fk", "fk_bf"), ("L1_attn_Fr", "fr_bf"), ("L1_attn_Fv", "fv_bf"),
        ]
        for src, dst in casts:
            S.dma("gpsimd", cast, A[dst][:, :], A[src][:, :])
        S.dma("gpsimd", cast, A["expw0_bf"][:, :], A["L0_ffn_expert_W"][0, :, :])
        S.dma("gpsimd", cast, A["expw1_bf"][:, :], A["L0_ffn_expert_W"][1, :, :])
        CROSS["cast"] = (cast, cast.n)

        identf, idrdy = make_identity(S, stack, F32, "f")
        xin_s = [stack.enter_context(nc.sbuf_tensor(f"p0_xin{p}", [128, 1024], F32)) for p in range(2)]
        stg = [stack.enter_context(nc.sbuf_tensor(f"p0_stg{c}", [128, 512], F32)) for c in range(NC8)]
        stgb = [stack.enter_context(nc.sbuf_tensor(f"p0_stgb{c}", [128, 512], BF16)) for c in range(NC8)]
        pt = [stack.enter_context(nc.psum_tensor(f"p0_pt{i}", [128, 512], F32)) for i in range(2)]
        lq = S.dmaq("sync", "lq", 4)
        PE, SC, VE, PL = S.eng["tensor"], S.eng["scalar"], S.eng["vector"], S.eng["gpsimd"]
        S.wait("tensor", PL, idrdy)
        rel = {}

        def wait_rel(eng, key):
            for sem, cnt in rel.get(key, ()):
                S.wait(eng, sem, cnt)

        stw = None
        for grp in range(8):
            for j in range(4):
                i = grp * 4 + j
                xin = xin_s[i % 2]
                wait_rel("sync", ("xin", i % 2))
                w = lq.wave()
                S.dma("sync", w, xin[:, :], A["x_in"][i * 128:(i + 1) * 128, :])
                S.wait("tensor", w)
                if grp > 0 and j == 0:
                    S.wait("scalar", stw)
                    S.wait("scalar", VE)
                for c in range(NC8):
                    reg = ((c % 2) * 2 + (j % 2))
                    bank = pt[reg // 2]
                    r0 = (reg % 2) * 256
                    wait_rel("tensor", ("pt", reg))
                    S.wait("tensor", SC)
                    S.op("tensor", lambda t, _c=c, _b=bank, _r=r0, _x=xin: t.transpose(
                        _b[:, _r:_r + 128], _x[:, _c * 128:(_c + 1) * 128], identf[:]))
                    S.wait("scalar", PE)
                    S.op("scalar", lambda s, _c=c, _j=j, _b=bank, _r=r0: s.activation(
                        stg[_c][:, _j * 128:(_j + 1) * 128], _b[:, _r:_r + 128], AF.Copy))
                    rel[("pt", reg)] = ((SC, SC.n),)
                rel[("xin", i % 2)] = ((PE, PE.n),)
            S.wait("vector", SC)
            if stw is not None:
                S.wait("vector", stw)
            for c in range(NC8):
                S.op("vector", lambda v, _c=c: v.tensor_copy(stgb[_c][:, :], stg[_c][:, :]))
            S.wait("gpsimd", SC)
            S.wait("gpsimd", VE)
            stw = gq.wave()
            for c in range(NC8):
                S.dma("gpsimd", stw, A["x0"][c * 128:(c + 1) * 128, grp * 512:(grp + 1) * 512], stg[c][:, :])
                S.dma("gpsimd", stw, A["xb0"][c * 128:(c + 1) * 128, grp * 512:(grp + 1) * 512], stgb[c][:, :])
        for s_ in gq.sems:
            S.wait("gpsimd", s_)
        S.flush()



def phase1(nc, A):
    with ExitStack() as stack:
        S = Sched(nc, stack, "p1")
        wqkvT = stack.enter_context(nc.sbuf_tensor("p1_wqkvT", [128, NC8 * 3072], BF16))
        outwT = stack.enter_context(nc.sbuf_tensor("p1_outwT", [128, NC8 * 1024], BF16))
        xb = [stack.enter_context(nc.sbuf_tensor(f"p1_xb{c}", [128, LT], BF16)) for c in range(NC8)]
        x0t = [stack.enter_context(nc.sbuf_tensor(f"p1_x0t{c}", [128, LT], F32)) for c in range(NC8)]
        qT = stack.enter_context(nc.sbuf_tensor("p1_qT", [128, NC8 * LT], BF16))
        kT = stack.enter_context(nc.sbuf_tensor("p1_kT", [128, NC8 * LT], BF16))
        ktok = [stack.enter_context(nc.sbuf_tensor(f"p1_ktok{ts}", [128, 1024], BF16)) for ts in range(4)]
        vtok = [stack.enter_context(nc.sbuf_tensor(f"p1_vtok{ts}", [128, 1024], BF16)) for ts in range(4)]
        sT = [stack.enter_context(nc.sbuf_tensor(f"p1_sT{i}", [128, LT], BF16)) for i in range(4)]
        yT = [stack.enter_context(nc.sbuf_tensor(f"p1_yT{c}", [128, LT], BF16)) for c in range(NC8)]
        msb = stack.enter_context(nc.sbuf_tensor("p1_msb", [128, 512], BF16))
        dlt = [stack.enter_context(nc.sbuf_tensor(f"p1_dlt{c}", [128, LT], F32)) for c in range(NC8)]
        x1t = dlt
        xb1t = [stack.enter_context(nc.sbuf_tensor(f"p1_xb1t{c}", [128, LT], BF16)) for c in range(NC8)]
        trisrc = stack.enter_context(nc.sbuf_tensor("p1_trisrc", [128, 128], F32))
        tri = stack.enter_context(nc.sbuf_tensor("p1_tri", [128, 128], F32))
        onesc = stack.enter_context(nc.sbuf_tensor("p1_onesc", [128, 1], BF16))
        onesr = stack.enter_context(nc.sbuf_tensor("p1_onesr", [1, 128], BF16))
        ones_lt = stack.enter_context(nc.sbuf_tensor("p1_oneslt", [1, LT], BF16))
        lnscr = make_ln_scratch(S, stack)
        bqkv_row = stack.enter_context(nc.sbuf_tensor("p1_bqkvr", [1, 3072], BF16))
        bout_row = stack.enter_context(nc.sbuf_tensor("p1_boutr", [1, 1024], BF16))
        bk = [stack.enter_context(nc.psum_tensor(f"p1_bk{i}", [128, LT], F32)) for i in range(6)]
        py = stack.enter_context(nc.psum_tensor("p1_py", [128, LT], F32))
        pm = stack.enter_context(nc.psum_tensor("p1_pm", [128, 512], F32))

        lq = S.dmaq("sync", "lq", 4)
        gq = S.dmaq("gpsimd", "gq", 3)
        PE, SC, VE, PL = S.eng["tensor"], S.eng["scalar"], S.eng["vector"], S.eng["gpsimd"]
        rel = {}

        def wait_rel(eng, key):
            for sem, cnt in rel.get(key, ()):
                S.wait(eng, sem, cnt)

        ldp = gq.wave()
        S.dma("gpsimd", ldp, bqkv_row[0:1, :], A["L0_attn_Wqkv_b"][0:1, :])
        S.dma("gpsimd", ldp, bout_row[0:1, :], A["L0_attn_out_b"][0:1, :])
        if "cast" in CROSS:
            S.wait("sync", CROSS["cast"][0], CROSS["cast"][1])
        wld = lq.wave()
        for c in range(NC8):
            S.dma("sync", wld, wqkvT[:, c * 3072:(c + 1) * 3072], A["wqkv_bf"][:, c * 128:(c + 1) * 128], transpose=True)
            S.dma("sync", wld, outwT[:, c * 1024:(c + 1) * 1024], A["outw_bf"][:, c * 128:(c + 1) * 128], transpose=True)
        S.op("gpsimd", lambda g: g.memset(trisrc[:], 1.0))
        S.op("gpsimd", lambda g: g.affine_select(
            tri[:], trisrc[:], pattern=[[1, 128]], compare_op=OP.is_ge, fill=0.0,
            base=0, channel_multiplier=-1))
        S.op("vector", lambda v: v.memset(onesc[:], 1.0))
        S.op("vector", lambda v: v.memset(onesr[:], 1.0))
        S.op("vector", lambda v: v.memset(ones_lt[:], 1.0))
        S.op("vector", lambda v: v.memset(msb[:], 0.0))

        for lt in range(NLT):
            tw = lq.wave()
            wait_rel("sync", "xtiles")
            for c in range(NC8):
                S.dma("sync", tw, xb[c][:, :], A["xb0"][c * 128:(c + 1) * 128, lt * LT:(lt + 1) * LT])
                S.dma("sync", tw, x0t[c][:, :], A["x0"][c * 128:(c + 1) * 128, lt * LT:(lt + 1) * LT])
            S.wait("tensor", tw)
            if lt == 0:
                S.wait("tensor", wld)
                S.wait("tensor", ldp)
                S.wait("tensor", PL)
                S.wait("tensor", VE)
            for oc in range(16):
                bank = bk[oc % 3]
                wait_rel("tensor", ("bk", oc % 3))
                for kc in range(NC8):
                    S.op("tensor", lambda t, _oc=oc, _kc=kc, _b=bank: t.matmul(
                        _b[:, :], wqkvT[:, _kc * 3072 + _oc * 128:_kc * 3072 + (_oc + 1) * 128],
                        xb[_kc][:, :], start=(_kc == 0), stop=False))
                S.op("tensor", lambda t, _oc=oc, _b=bank: t.matmul(
                    _b[:, :], bqkv_row[0:1, _oc * 128:(_oc + 1) * 128], ones_lt[0:1, :],
                    start=False, stop=True))
                dst, cc, scl = (qT, oc, 1.0) if oc < 8 else (kT, oc - 8, 0.125)
                S.wait("scalar", PE)
                wait_rel("scalar", ("qkT", 0))
                S.op("scalar", lambda s, _b=bank, _d=dst, _cc=cc, _sc=scl: s.activation(
                    _d[:, _cc * LT:(_cc + 1) * LT], _b[:, :], AF.Identity, scale=_sc))
                rel[("bk", oc % 3)] = ((SC, SC.n),)
            for ts in range(4):
                for part in range(2):
                    for half in range(2):
                        i = (ts * 4 + part * 2 + half)
                        bank = bk[i % 3]
                        dout0 = 1024 + part * 1024 + half * 512
                        wait_rel("tensor", ("bk", i % 3))
                        for kc in range(NC8):
                            S.op("tensor", lambda t, _ts=ts, _kc=kc, _b=bank, _d0=dout0: t.matmul(
                                _b[:, :], xb[_kc][:, _ts * 128:(_ts + 1) * 128],
                                wqkvT[:, _kc * 3072 + _d0:_kc * 3072 + _d0 + 512],
                                start=(_kc == 0), stop=False))
                        S.op("tensor", lambda t, _b=bank, _d0=dout0: t.matmul(
                            _b[:, :], onesr[0:1, 0:128], bqkv_row[0:1, _d0:_d0 + 512],
                            start=False, stop=True))
                        dstt = ktok if part == 0 else vtok
                        scl = 0.125 if part == 0 else 1.0
                        S.wait("scalar", PE)
                        wait_rel("scalar", ("tok", part, ts))
                        S.op("scalar", lambda s, _b=bank, _d=dstt, _ts=ts, _h=half, _sc=scl: s.activation(
                            _d[_ts][:, _h * 512:(_h + 1) * 512], _b[:, :], AF.Identity, scale=_sc))
                        rel[("bk", i % 3)] = ((SC, SC.n),)
            rel["xtiles"] = ((PE, PE.n), (VE, VE.n))
            if lt > 0:
                S.wait("scalar", PE)
                S.op("scalar", lambda s: s.activation(msb[:, :], pm[:, :], AF.Copy))
            msb_ready = (SC, SC.n)
            for h in range(H):
                hc = h // 2
                hb = (h % 2) * 64
                st_ready = {}
                for ms in range(4):
                    n = LT - ms * 128
                    bi = (h * 4 + ms) % 6
                    bank = bk[bi]
                    wait_rel("tensor", ("bk", bi))
                    S.op("tensor", lambda t, _hc=hc, _hb=hb, _ms=ms, _n=n, _b=bank: t.matmul(
                        _b[:, 0:_n],
                        kT[_hb:_hb + 64, _hc * LT + _ms * 128:_hc * LT + (_ms + 1) * 128],
                        qT[_hb:_hb + 64, _hc * LT + _ms * 128:_hc * LT + _ms * 128 + _n],
                        start=True, stop=True))
                    S.wait("vector", PE)
                    S.op("vector", lambda v, _ms=ms, _b=bank: v.tensor_tensor(
                        sT[_ms][:, 0:128], _b[:, 0:128], tri[:, :], OP.mult))
                    if n > 128:
                        S.wait("scalar", PE)
                        S.wait("scalar", VE)
                        S.op("scalar", lambda s, _ms=ms, _n=n, _b=bank: s.activation(
                            sT[_ms][:, 128:_n], _b[:, 128:_n], AF.Copy))
                        st_ready[ms] = ((VE, VE.n), (SC, SC.n))
                        rel[("bk", bi)] = ((VE, VE.n), (SC, SC.n))
                    else:
                        st_ready[ms] = ((VE, VE.n),)
                        rel[("bk", bi)] = ((VE, VE.n),)
                wait_rel("tensor", "py")
                S.wait("tensor", msb_ready[0], msb_ready[1])
                S.op("tensor", lambda t, _hb=hb, _hc=hc: t.matmul(
                    py[_hb:_hb + 64, :], msb[_hb:_hb + 64, _hc * 64:_hc * 64 + 64],
                    qT[_hb:_hb + 64, _hc * LT:(_hc + 1) * LT], start=True, stop=False))
                for ms in range(4):
                    n = LT - ms * 128
                    for sem, cnt in st_ready[ms]:
                        S.wait("tensor", sem, cnt)
                    S.op("tensor", lambda t, _hb=hb, _h=h, _ms=ms, _n=n: t.matmul(
                        py[_hb:_hb + 64, _ms * 128:_ms * 128 + _n],
                        vtok[_ms][:, _h * 64:(_h + 1) * 64],
                        sT[_ms][:, 0:_n], start=False, stop=(_ms == 3)))
                if h % 2 == 1:
                    S.wait("scalar", PE)
                    wait_rel("scalar", ("yT", hc))
                    S.op("scalar", lambda s, _hc=hc: s.activation(
                        yT[_hc][:, :], py[:, :], AF.Copy))
                    rel["py"] = ((SC, SC.n),)
            for h in range(H):
                hc = h // 2
                hb = (h % 2) * 64
                for ts in range(4):
                    first = (lt == 0 and h in (0, 1) and ts == 0)
                    S.op("tensor", lambda t, _h=h, _hb=hb, _hc=hc, _ts=ts, _f=first: t.matmul(
                        pm[_hb:_hb + 64, _hc * 64:_hc * 64 + 64],
                        ktok[_ts][:, _h * 64:(_h + 1) * 64],
                        vtok[_ts][:, _h * 64:(_h + 1) * 64],
                        start=_f, stop=False, skip_group_check=True))
            for part in range(2):
                for ts in range(4):
                    rel[("tok", part, ts)] = ((PE, PE.n),)
            S.wait("tensor", SC)
            for oc in range(NC8):
                bank = bk[3 + oc % 2]
                wait_rel("tensor", ("bk", 3 + oc % 2))
                for kc in range(NC8):
                    S.op("tensor", lambda t, _oc=oc, _kc=kc, _b=bank: t.matmul(
                        _b[:, :], outwT[:, _kc * 1024 + _oc * 128:_kc * 1024 + (_oc + 1) * 128],
                        yT[_kc][:, :], start=(_kc == 0), stop=False))
                S.op("tensor", lambda t, _oc=oc, _b=bank: t.matmul(
                    _b[:, :], bout_row[0:1, _oc * 128:(_oc + 1) * 128], ones_lt[0:1, :],
                    start=False, stop=True))
                rel[("yT", oc)] = ((PE, PE.n),)
                S.wait("scalar", PE)
                wait_rel("scalar", "stores")
                wait_rel("scalar", ("dlt", oc))
                S.op("scalar", lambda s, _oc=oc, _b=bank: s.activation(
                    dlt[_oc][:, :], _b[:, :], AF.Copy))
                rel[("bk", 3 + oc % 2)] = ((SC, SC.n),)
            S.wait("vector", SC)
            for c in range(NC8):
                S.op("vector", lambda v, _c=c: v.tensor_tensor(
                    dlt[_c][:, :], dlt[_c][:, :], x0t[_c][:, :], OP.add))
            rel["xtiles"] = ((PE, rel["xtiles"][0][1]), (VE, VE.n))
            S.wait("scalar", VE)
            emit_ln(S, lnscr, dlt, onesc, onesr, bk[3], bk[4], x1t, xb1t, lq, lt)
            rel[("bk", 3)] = ((VE, VE.n),)
            rel[("bk", 4)] = ((VE, VE.n),)
            S.wait("gpsimd", VE)
            sw = gq.wave()
            for c in range(NC8):
                S.dma("gpsimd", sw, A["x1"][c * 128:(c + 1) * 128, lt * LT:(lt + 1) * LT], x1t[c][:, :])
                S.dma("gpsimd", sw, A["xb1"][c * 128:(c + 1) * 128, lt * LT:(lt + 1) * LT], xb1t[c][:, :])
            rel["stores"] = ((sw, sw.n),)
            for c in range(NC8):
                rel[("dlt", c)] = ((sw, sw.n), (VE, VE.n))
            rel[("qkT", 0)] = ((PE, PE.n),)
        for s_ in gq.sems:
            S.wait("gpsimd", s_)
        S.flush()



def phase2(nc, A):
    with ExitStack() as stack:
        S = Sched(nc, stack, "p2")
        w0T = stack.enter_context(nc.sbuf_tensor("p2_w0T", [128, NC8 * 1024], BF16))
        w1T = stack.enter_context(nc.sbuf_tensor("p2_w1T", [128, NC8 * 1024], BF16))
        outWT = stack.enter_context(nc.sbuf_tensor("p2_outWT", [128, NC8 * 1024], BF16))
        gw = stack.enter_context(nc.sbuf_tensor("p2_gw", [8, 1024], F32))
        gwT = stack.enter_context(nc.sbuf_tensor("p2_gwT", [128, NC8 * 8], F32))
        xb = [stack.enter_context(nc.sbuf_tensor(f"p2_xb{c}", [128, LT], BF16)) for c in range(NC8)]
        x1f = [stack.enter_context(nc.sbuf_tensor(f"p2_x1f{c}", [128, LT], F32)) for c in range(NC8)]
        glog_s = stack.enter_context(nc.sbuf_tensor("p2_glog", [8, LT], F32))
        ptok_s = stack.enter_context(nc.sbuf_tensor("p2_ptok", [128, 32], F32))
        e8 = stack.enter_context(nc.sbuf_tensor("p2_e8", [128, 8], F32))
        p8_ = stack.enter_context(nc.sbuf_tensor("p2_p8", [128, 8], F32))
        g8 = stack.enter_context(nc.sbuf_tensor("p2_g8", [128, 8], F32))
        gs = stack.enter_context(nc.sbuf_tensor("p2_gs", [128, 8], F32))
        s01 = stack.enter_context(nc.sbuf_tensor("p2_s01", [128, 8], F32))
        st2row = stack.enter_context(nc.sbuf_tensor("p2_st2r", [2, LT], BF16))
        sel0 = stack.enter_context(nc.sbuf_tensor("p2_sel0", [2, 128], BF16))
        sel1 = stack.enter_context(nc.sbuf_tensor("p2_sel1", [2, 128], BF16))
        ones2 = stack.enter_context(nc.sbuf_tensor("p2_ones2", [2, 128], BF16))
        h0s = [stack.enter_context(nc.sbuf_tensor(f"p2_h0s{c}", [128, LT], F32)) for c in range(NC8)]
        h1s = [stack.enter_context(nc.sbuf_tensor(f"p2_h1s{c}", [128, LT], F32)) for c in range(NC8)]
        f0 = stack.enter_context(nc.sbuf_tensor("p2_f0", [128, LT], F32))
        finT = [stack.enter_context(nc.sbuf_tensor(f"p2_fin{c}", [128, LT], BF16)) for c in range(NC8)]
        dlt = [stack.enter_context(nc.sbuf_tensor(f"p2_dlt{c}", [128, LT], F32)) for c in range(NC8)]
        x2t = dlt
        xb2t = [stack.enter_context(nc.sbuf_tensor(f"p2_xb2t{c}", [128, LT], BF16)) for c in range(NC8)]
        onesc = stack.enter_context(nc.sbuf_tensor("p2_onesc", [128, 1], BF16))
        onesr = stack.enter_context(nc.sbuf_tensor("p2_onesr", [1, 128], BF16))
        lnscr = make_ln_scratch(S, stack)
        identf, idrdyp = make_identity(S, stack, F32, "f")

        psc = [stack.enter_context(nc.psum_tensor(f"p2_psc{i}", [128, LT], F32)) for i in range(2)]
        pbc = [stack.enter_context(nc.psum_tensor(f"p2_pbc{i}", [128, LT], F32)) for i in range(2)]
        pst = [stack.enter_context(nc.psum_tensor(f"p2_pst{i}", [128, LT], F32)) for i in range(3)]
        pl = stack.enter_context(nc.psum_tensor("p2_pl", [128, LT], F32))

        lq = S.dmaq("sync", "lq", 4)
        gq = S.dmaq("gpsimd", "gq", 3)
        PE, SC, VE, PL = S.eng["tensor"], S.eng["scalar"], S.eng["vector"], S.eng["gpsimd"]
        rel = {}

        def wait_rel(eng, key):
            for sem, cnt in rel.get(key, ()):
                S.wait(eng, sem, cnt)

        s01_rel = 0
        if "cast" in CROSS:
            S.wait("sync", CROSS["cast"][0], CROSS["cast"][1])
        wld = lq.wave()
        for c in range(NC8):
            S.dma("sync", wld, w0T[:, c * 1024:(c + 1) * 1024], A["expw0_bf"][:, c * 128:(c + 1) * 128], transpose=True)
            S.dma("sync", wld, w1T[:, c * 1024:(c + 1) * 1024], A["expw1_bf"][:, c * 128:(c + 1) * 128], transpose=True)
            S.dma("sync", wld, outWT[:, c * 1024:(c + 1) * 1024], A["outW_bf"][:, c * 128:(c + 1) * 128], transpose=True)
        S.dma("sync", wld, gw[:, :], A["L0_ffn_gate_W"][:, :])
        S.wait("tensor", wld)
        S.wait("tensor", PL, idrdyp)
        for c in range(NC8):
            S.op("tensor", lambda t, _c=c: t.transpose(
                pl[:, _c * 8:(_c + 1) * 8], gw[0:8, _c * 128:(_c + 1) * 128], identf[0:8, 0:8]))
        S.wait("scalar", PE)
        S.op("scalar", lambda s: s.activation(gwT[:, 0:NC8 * 8], pl[:, 0:NC8 * 8], AF.Copy))
        S.op("gpsimd", lambda g: g.memset(ones2[:, :], 1.0))
        S.op("gpsimd", lambda g: g.affine_select(
            sel0[:, :], ones2[:, :], pattern=[[0, 128]], compare_op=OP.is_equal, fill=0.0,
            base=0, channel_multiplier=1))
        S.op("gpsimd", lambda g: g.affine_select(
            sel1[:, :], ones2[:, :], pattern=[[0, 128]], compare_op=OP.is_equal, fill=0.0,
            base=-1, channel_multiplier=1))
        S.op("vector", lambda v: v.memset(onesc[:], 1.0))
        S.op("vector", lambda v: v.memset(onesr[:], 1.0))

        for lt in range(NLT):
            tw = lq.wave()
            wait_rel("sync", "xtiles")
            for c in range(NC8):
                S.dma("sync", tw, xb[c][:, :], A["xb1"][c * 128:(c + 1) * 128, lt * LT:(lt + 1) * LT])
                S.dma("sync", tw, x1f[c][:, :], A["x1"][c * 128:(c + 1) * 128, lt * LT:(lt + 1) * LT])
            S.wait("tensor", tw)
            if lt == 0:
                S.wait("tensor", SC)
                S.wait("tensor", PL)
                S.wait("tensor", VE)
            wait_rel("tensor", "pst0")
            for kc in range(NC8):
                S.op("tensor", lambda t, _kc=kc: t.matmul(
                    pst[0][0:8, :], gwT[:, _kc * 8:(_kc + 1) * 8], x1f[_kc][:, :],
                    start=(_kc == 0), stop=(_kc == NC8 - 1)))
            S.wait("scalar", PE)
            wait_rel("scalar", "glog")
            S.op("scalar", lambda s: s.activation(glog_s[:, :], pst[0][0:8, :], AF.Copy))
            S.wait("tensor", SC)
            wait_rel("tensor", "pst1")
            for j in range(4):
                S.op("tensor", lambda t, _j=j: t.transpose(
                    pst[1][:, _j * 8:(_j + 1) * 8], glog_s[0:8, _j * 128:(_j + 1) * 128], identf[0:8, 0:8]))
            S.wait("scalar", PE)
            wait_rel("scalar", "ptok")
            S.op("scalar", lambda s: s.activation(ptok_s[:, 0:32], pst[1][:, 0:32], AF.Copy))
            rel["glog"] = ((PE, PE.n),)
            rel["pst0"] = ((SC, SC.n),)
            rel["pst1"] = ((SC, SC.n),)
            for oc in range(NC8):
                pair = (psc[0], psc[1]) if oc % 2 == 0 else (pst[0], pst[1])
                key = ("xp", oc % 2)
                wait_rel("tensor", key)
                if oc % 2 == 1:
                    wait_rel("tensor", "pst0")
                    wait_rel("tensor", "pst1")
                for kc in range(NC8):
                    S.op("tensor", lambda t, _oc=oc, _kc=kc, _b=pair[0]: t.matmul(
                        _b[:, :], w0T[:, _kc * 1024 + _oc * 128:_kc * 1024 + (_oc + 1) * 128],
                        xb[_kc][:, :], start=(_kc == 0), stop=(_kc == NC8 - 1)))
                for kc in range(NC8):
                    S.op("tensor", lambda t, _oc=oc, _kc=kc, _b=pair[1]: t.matmul(
                        _b[:, :], w1T[:, _kc * 1024 + _oc * 128:_kc * 1024 + (_oc + 1) * 128],
                        xb[_kc][:, :], start=(_kc == 0), stop=(_kc == NC8 - 1)))
                S.wait("scalar", PE)
                wait_rel("scalar", ("hs", oc))
                S.op("scalar", lambda s, _oc=oc, _b=pair[0]: s.activation(h0s[_oc][:, :], _b[:, :], AF.Copy))
                S.op("scalar", lambda s, _oc=oc, _b=pair[1]: s.activation(h1s[_oc][:, :], _b[:, :], AF.Copy))
                rel[key] = ((SC, SC.n),)
                if oc % 2 == 1:
                    rel["pst0"] = ((SC, SC.n),)
                    rel["pst1"] = ((SC, SC.n),)
            rel["xtiles"] = ((PE, PE.n), (VE, VE.n))
            for j in range(4):
                L = ptok_s[:, j * 8:(j + 1) * 8]
                S.wait("vector", SC)
                S.wait("vector", PE, s01_rel)
                S.op("vector", lambda v, _L=L: v.reduce_max(gs[:, 0:1], _L, axis=mybir.AxisListType.X))
                S.op("vector", lambda v: v.tensor_scalar(gs[:, 1:2], gs[:, 0:1], -1.0, None, OP.mult))
                S.wait("scalar", VE)
                S.op("scalar", lambda s, _L=L: s.activation(e8[:, :], _L, AF.Exp, bias=gs[:, 1:2]))
                S.wait("vector", SC)
                S.op("vector", lambda v: v.reduce_sum(gs[:, 2:3], e8[:, :], axis=mybir.AxisListType.X))
                S.op("vector", lambda v: v.reciprocal(gs[:, 3:4], gs[:, 2:3]))
                S.op("vector", lambda v: v.tensor_scalar(p8_[:, :], e8[:, :], gs[:, 3:4], None, OP.mult))
                S.op("vector", lambda v: v.tensor_scalar(g8[:, :], p8_[:, :], 0.3, None, OP.is_gt))
                S.op("vector", lambda v: v.reduce_sum(gs[:, 4:5], g8[:, :], axis=mybir.AxisListType.X))
                S.op("vector", lambda v: v.tensor_scalar(gs[:, 5:6], gs[:, 4:5], -1.0, 8.0, OP.mult, OP.add))
                S.op("vector", lambda v, _j=j: v.tensor_tensor(s01[:, 2 * _j:2 * _j + 1], gs[:, 5:6], p8_[:, 0:1], OP.mult))
                S.op("vector", lambda v, _j=j: v.tensor_tensor(s01[:, 2 * _j + 1:2 * _j + 2], gs[:, 4:5], p8_[:, 1:2], OP.mult))
            S.wait("tensor", VE)
            wait_rel("tensor", "pst2")
            for j in range(4):
                S.op("tensor", lambda t, _j=j: t.transpose(
                    pst[2][0:2, _j * 128:(_j + 1) * 128], s01[:, 2 * _j:2 * _j + 2], identf[:, :]))
            s01_rel = PE.n
            rel["ptok"] = ((VE, VE.n),)
            S.wait("scalar", PE)
            wait_rel("scalar", "st2row")
            S.op("scalar", lambda s: s.activation(st2row[:, :], pst[2][0:2, :], AF.Copy))
            S.wait("tensor", SC)
            wait_rel("tensor", "pbc")
            S.op("tensor", lambda t: t.matmul(pbc[0][:, :], sel0[:, :], st2row[:, :], start=True, stop=True))
            S.op("tensor", lambda t: t.matmul(pbc[1][:, :], sel1[:, :], st2row[:, :], start=True, stop=True))
            rel["st2row"] = ((PE, PE.n),)
            rel["pst2"] = ((PE, PE.n),)
            bc_done = PE.n
            S.wait("vector", PE, bc_done)
            S.wait("vector", SC)
            for oc in range(NC8):
                S.op("vector", lambda v, _oc=oc: v.tensor_tensor(f0[:, :], h0s[_oc][:, :], pbc[0][:, :], OP.mult))
                S.op("vector", lambda v, _oc=oc: v.tensor_tensor(finT[_oc][:, :], h1s[_oc][:, :], pbc[1][:, :], OP.mult))
                S.op("vector", lambda v, _oc=oc: v.tensor_tensor(finT[_oc][:, :], finT[_oc][:, :], f0[:, :], OP.add))
                rel[("hs", oc)] = ((VE, VE.n),)
            rel["pbc"] = ((VE, VE.n),)
            S.wait("tensor", VE)
            for oc in range(NC8):
                bank = psc[oc % 2]
                wait_rel("tensor", ("xp", 0))
                for kc in range(NC8):
                    S.op("tensor", lambda t, _oc=oc, _kc=kc, _b=bank: t.matmul(
                        _b[:, :], outWT[:, _kc * 1024 + _oc * 128:_kc * 1024 + (_oc + 1) * 128],
                        finT[_kc][:, :], start=(_kc == 0), stop=(_kc == NC8 - 1)))
                S.wait("scalar", PE)
                wait_rel("scalar", ("dlt", oc))
                S.op("scalar", lambda s, _oc=oc, _b=bank: s.activation(
                    dlt[_oc][:, :], _b[:, :], AF.Copy))
                rel[("xp", 0)] = ((SC, SC.n),)
            S.wait("vector", SC)
            for c in range(NC8):
                S.op("vector", lambda v, _c=c: v.tensor_tensor(
                    dlt[_c][:, :], dlt[_c][:, :], x1f[_c][:, :], OP.add))
            rel["xtiles"] = ((PE, rel["xtiles"][0][1]), (VE, VE.n))
            S.wait("scalar", VE)
            emit_ln(S, lnscr, dlt, onesc, onesr, pl, pbc[0], x2t, xb2t, lq, lt)
            rel["pbc"] = ((VE, VE.n),)
            S.wait("gpsimd", VE)
            sw = gq.wave()
            for c in range(NC8):
                S.dma("gpsimd", sw, A["x2"][c * 128:(c + 1) * 128, lt * LT:(lt + 1) * LT], x2t[c][:, :])
                S.dma("gpsimd", sw, A["xb2"][c * 128:(c + 1) * 128, lt * LT:(lt + 1) * LT], xb2t[c][:, :])
            for c in range(NC8):
                rel[("dlt", c)] = ((sw, sw.n), (VE, VE.n))
            rel["pst0"] = ((PE, PE.n),)
        for s_ in gq.sems:
            S.wait("gpsimd", s_)
        S.flush()



def phase3(nc, A):
    with ExitStack() as stack:
        S = Sched(nc, stack, "p3")
        wkT = stack.enter_context(nc.sbuf_tensor("p3_wkT", [128, NC8 * 1024], BF16))
        wvT = stack.enter_context(nc.sbuf_tensor("p3_wvT", [128, NC8 * 1024], BF16))
        x2f = [stack.enter_context(nc.sbuf_tensor(f"p3_x2f{c}", [128, LT], F32)) for c in range(NC8)]
        hbuf = [stack.enter_context(nc.sbuf_tensor(f"p3_h{c}", [128, LT + 1], F32)) for c in range(NC8)]
        dscr = stack.enter_context(nc.sbuf_tensor("p3_dscr", [128, LT], F32))
        xkb = [stack.enter_context(nc.sbuf_tensor(f"p3_xkb{c}", [128, LT], BF16)) for c in range(NC8)]
        xvb = [stack.enter_context(nc.sbuf_tensor(f"p3_xvb{c}", [128, LT], BF16)) for c in range(NC8)]
        xrb = [stack.enter_context(nc.sbuf_tensor(f"p3_xrb{c}", [128, LT], BF16)) for c in range(NC8)]
        kst = [stack.enter_context(nc.sbuf_tensor(f"p3_kst{c}", [128, LT], F32)) for c in range(NC8)]
        vst = [stack.enter_context(nc.sbuf_tensor(f"p3_vst{c}", [128, LT], F32)) for c in range(NC8)]
        onesc = stack.enter_context(nc.sbuf_tensor("p3_onesc", [128, 1], BF16))
        onesr = stack.enter_context(nc.sbuf_tensor("p3_onesr", [1, 128], BF16))
        lnscr = make_ln_scratch(S, stack)
        identf, idrdy = make_identity(S, stack, F32, "f")
        psc = [stack.enter_context(nc.psum_tensor(f"p3_psc{i}", [128, LT], F32)) for i in range(6)]
        pl = [stack.enter_context(nc.psum_tensor(f"p3_pl{i}", [128, LT], F32)) for i in range(2)]

        lq = S.dmaq("sync", "lq", 4)
        gq = S.dmaq("gpsimd", "gq", 3)
        PE, SC, VE, PL = S.eng["tensor"], S.eng["scalar"], S.eng["vector"], S.eng["gpsimd"]
        rel = {}

        def wait_rel(eng, key):
            for sem, cnt in rel.get(key, ()):
                S.wait(eng, sem, cnt)

        if "cast" in CROSS:
            S.wait("sync", CROSS["cast"][0], CROSS["cast"][1])
        wld = lq.wave()
        for c in range(NC8):
            S.dma("sync", wld, wkT[:, c * 1024:(c + 1) * 1024], A["wk_bf"][:, c * 128:(c + 1) * 128], transpose=True)
            S.dma("sync", wld, wvT[:, c * 1024:(c + 1) * 1024], A["wv_bf"][:, c * 128:(c + 1) * 128], transpose=True)
        vecs = load_vecs_fm(S, stack, [
            ("l1g", A["L1_attn_ln1_g"], D), ("l1b", A["L1_attn_ln1_b"], D),
            ("tmk", A["L1_attn_tm_k"], D), ("tmv", A["L1_attn_tm_v"], D),
            ("tmr", A["L1_attn_tm_r"], D)], identf, idrdy, pl[0], lq)
        ln1g, ln1b = vecs["l1g"], vecs["l1b"]
        tmk, tmv, tmr = vecs["tmk"], vecs["tmv"], vecs["tmr"]
        S.op("vector", lambda v: v.memset(onesc[:], 1.0))
        S.op("vector", lambda v: v.memset(onesr[:], 1.0))
        for c in range(NC8):
            S.op("vector", lambda v, _c=c: v.memset(hbuf[_c][:, 0:1], 0.0))

        for lt in range(NLT):
            tw = lq.wave()
            wait_rel("sync", "x2f")
            for c in range(NC8):
                S.dma("sync", tw, x2f[c][:, :], A["x2"][c * 128:(c + 1) * 128, lt * LT:(lt + 1) * LT])
            S.wait("tensor", tw)
            S.wait("scalar", tw)
            S.wait("vector", tw)
            if lt == 0:
                S.wait("tensor", wld)
                S.wait("tensor", SC)
            if lt > 0:
                for c in range(NC8):
                    S.op("vector", lambda v, _c=c: v.tensor_copy(hbuf[_c][:, 0:1], hbuf[_c][:, LT:LT + 1]))
            emit_ln(S, lnscr, x2f, onesc, onesr, pl[0], pl[1],
                    [hbuf[c][:, 1:LT + 1] for c in range(NC8)], None, lq, lt, g_vec=ln1g, b_vec=ln1b)
            rel["x2f"] = ((VE, VE.n),)
            wait_rel("vector", "mixes")
            wait_rel("vector", ("st", 0))
            for c in range(NC8):
                h_ = hbuf[c][:, 1:LT + 1]
                hh = hbuf[c][:, 0:LT]
                S.op("vector", lambda v, _h=h_, _hh=hh: v.tensor_tensor(dscr[:, :], _h, _hh, OP.subtract))
                S.op("vector", lambda v, _c=c, _hh=hh: v.scalar_tensor_tensor(
                    xkb[_c][:, :], dscr[:, :], tmk[:, _c:_c + 1], _hh, OP.mult, OP.add))
                S.op("vector", lambda v, _c=c, _hh=hh: v.scalar_tensor_tensor(
                    xvb[_c][:, :], dscr[:, :], tmv[:, _c:_c + 1], _hh, OP.mult, OP.add))
                S.op("vector", lambda v, _c=c, _hh=hh: v.scalar_tensor_tensor(
                    xrb[_c][:, :], dscr[:, :], tmr[:, _c:_c + 1], _hh, OP.mult, OP.add))
            S.wait("tensor", VE)
            for oc in range(NC8):
                bk = psc[0 + oc % 2]
                bv = psc[2 + oc % 2]
                for i in (0 + oc % 2, 2 + oc % 2):
                    wait_rel("tensor", ("psc", i))
                for kc in range(NC8):
                    S.op("tensor", lambda t, _oc=oc, _kc=kc, _b=bk: t.matmul(
                        _b[:, :], wkT[:, _kc * 1024 + _oc * 128:_kc * 1024 + (_oc + 1) * 128],
                        xkb[_kc][:, :], start=(_kc == 0), stop=(_kc == NC8 - 1)))
                for kc in range(NC8):
                    S.op("tensor", lambda t, _oc=oc, _kc=kc, _b=bv: t.matmul(
                        _b[:, :], wvT[:, _kc * 1024 + _oc * 128:_kc * 1024 + (_oc + 1) * 128],
                        xvb[_kc][:, :], start=(_kc == 0), stop=(_kc == NC8 - 1)))
                S.wait("scalar", PE)
                wait_rel("scalar", ("st", oc))
                S.op("scalar", lambda s, _oc=oc, _b=bk: s.activation(kst[_oc][:, :], _b[:, :], AF.Copy))
                rel[("psc", 0 + oc % 2)] = ((SC, SC.n),)
                S.op("scalar", lambda s, _oc=oc, _b=bv: s.activation(vst[_oc][:, :], _b[:, :], AF.Copy))
                rel[("psc", 2 + oc % 2)] = ((SC, SC.n),)
            rel["mixes"] = ((PE, PE.n), (VE, VE.n))
            S.wait("gpsimd", SC)
            S.wait("gpsimd", VE)
            sw = gq.wave()
            for c in range(NC8):
                S.dma("gpsimd", sw, A["kTf"][c * 128:(c + 1) * 128, lt * LT:(lt + 1) * LT], kst[c][:, :])
                S.dma("gpsimd", sw, A["vTf"][c * 128:(c + 1) * 128, lt * LT:(lt + 1) * LT], vst[c][:, :])
                S.dma("gpsimd", sw, A["xrT"][c * 128:(c + 1) * 128, lt * LT:(lt + 1) * LT], xrb[c][:, :])
            for c in range(NC8):
                rel[("st", c)] = ((sw, sw.n),)
        for s_ in gq.sems:
            S.wait("gpsimd", s_)
        S.flush()


H2 = T // 2


def phase4(nc, A):
    with ExitStack() as stack:
        S = Sched(nc, stack, "p4")
        woT = stack.enter_context(nc.sbuf_tensor("p4_woT", [128, NC8 * 1024], BF16))
        wrT = stack.enter_context(nc.sbuf_tensor("p4_wrT", [128, NC8 * 1024], BF16))
        yT = [stack.enter_context(nc.sbuf_tensor(f"p4_y{c}", [128, T], BF16)) for c in range(NC8)]
        kbb = [stack.enter_context(nc.sbuf_tensor(f"p4_kb{p}", [128, H2], F32)) for p in range(2)]
        vb = [stack.enter_context(nc.sbuf_tensor(f"p4_vb{p}", [128, H2], F32)) for p in range(2)]
        ekb = stack.enter_context(nc.sbuf_tensor("p4_ekb", [128, H2], F32))
        ekvb = stack.enter_context(nc.sbuf_tensor("p4_ekvb", [128, H2], F32))
        lnsc = stack.enter_context(nc.sbuf_tensor("p4_lnsc", [128, H2], F32))
        Nb = stack.enter_context(nc.sbuf_tensor("p4_Nb", [128, H2 + 1], F32))
        Db = stack.enter_context(nc.sbuf_tensor("p4_Db", [128, H2 + 1], F32))
        xrb = [stack.enter_context(nc.sbuf_tensor(f"p4_xrb{c}", [128, LT], BF16)) for c in range(NC8)]
        srt = [stack.enter_context(nc.sbuf_tensor(f"p4_srt{c}", [128, LT], BF16)) for c in range(NC8)]
        srl = xrb
        rwk = srt
        x2f = [stack.enter_context(nc.sbuf_tensor(f"p4_x2f{c}", [128, LT], F32)) for c in range(NC8)]
        ew = stack.enter_context(nc.sbuf_tensor("p4_ew", [128, NC8], F32))
        eu = stack.enter_context(nc.sbuf_tensor("p4_eu", [128, NC8], F32))
        identf, idrdy = make_identity(S, stack, F32, "f")
        pvec = stack.enter_context(nc.psum_tensor("p4_pvec", [128, LT], F32))
        psc = [stack.enter_context(nc.psum_tensor(f"p4_psc{i}", [128, LT], F32)) for i in range(4)]

        lq = S.dmaq("sync", "lq", 4)
        gq = S.dmaq("gpsimd", "gq", 3)
        PE, SC, VE, PL = S.eng["tensor"], S.eng["scalar"], S.eng["vector"], S.eng["gpsimd"]
        rel = {}

        def wait_rel(eng, key):
            for sem, cnt in rel.get(key, ()):
                S.wait(eng, sem, cnt)

        if "cast" in CROSS:
            S.wait("sync", CROSS["cast"][0], CROSS["cast"][1])
        wld = lq.wave()
        for c in range(NC8):
            S.dma("sync", wld, woT[:, c * 1024:(c + 1) * 1024], A["wo_bf"][:, c * 128:(c + 1) * 128], transpose=True)
            S.dma("sync", wld, wrT[:, c * 1024:(c + 1) * 1024], A["wr_bf"][:, c * 128:(c + 1) * 128], transpose=True)
        vecs = load_vecs_fm(S, stack, [
            ("td", A["L1_attn_time_decay"], D), ("tf", A["L1_attn_time_first"], D)],
            identf, idrdy, pvec, lq)
        td, tf = vecs["td"], vecs["tf"]
        S.op("scalar", lambda s: s.activation(ew[:, :], td[:, :], AF.Exp))
        S.op("scalar", lambda s: s.activation(ew[:, :], ew[:, :], AF.Exp, scale=-1.0))
        S.op("scalar", lambda s: s.activation(eu[:, :], tf[:, :], AF.Exp))

        sw0 = lq.wave()
        S.dma("sync", sw0, kbb[0][:, :], A["kTf"][0:128, 0:H2])
        S.dma("sync", sw0, vb[0][:, :], A["vTf"][0:128, 0:H2])
        scan_waves = {0: sw0}

        def emit_rtile(lt):
            tw = lq.wave()
            wait_rel("sync", "xrb")
            for c in range(NC8):
                S.dma("sync", tw, xrb[c][:, :], A["xrT"][c * 128:(c + 1) * 128, lt * LT:(lt + 1) * LT])
            S.wait("tensor", tw)
            if lt == 0:
                S.wait("tensor", wld)
            for oc in range(NC8):
                bank = psc[oc % 2]
                wait_rel("tensor", ("psc", oc % 2))
                for kc in range(NC8):
                    S.op("tensor", lambda t, _oc=oc, _kc=kc, _b=bank: t.matmul(
                        _b[:, :], wrT[:, _kc * 1024 + _oc * 128:_kc * 1024 + (_oc + 1) * 128],
                        xrb[_kc][:, :], start=(_kc == 0), stop=(_kc == NC8 - 1)))
                S.wait("scalar", PE)
                wait_rel("scalar", ("srt", oc))
                S.op("scalar", lambda s, _oc=oc, _b=bank: s.activation(srt[_oc][:, :], _b[:, :], AF.Sigmoid))
                rel[("psc", oc % 2)] = ((SC, SC.n),)
            rel["xrb"] = ((PE, PE.n),)
            S.wait("gpsimd", SC)
            rw = gq.wave()
            for c in range(NC8):
                S.dma("gpsimd", rw, A["srT"][c * 128:(c + 1) * 128, lt * LT:(lt + 1) * LT], srt[c][:, :])
            for c in range(NC8):
                rel[("srt", c)] = ((rw, rw.n),)

        for it in range(16):
            ch, half = it // 2, it % 2
            t0 = half * H2
            par = it % 2
            w = scan_waves[it]
            if it + 1 < 16:
                nch, nhalf = (it + 1) // 2, (it + 1) % 2
                nw = lq.wave()
                scan_waves[it + 1] = nw
                wait_rel("sync", ("kv", (it + 1) % 2))
                S.dma("sync", nw, kbb[(it + 1) % 2][:, :],
                      A["kTf"][nch * 128:(nch + 1) * 128, nhalf * H2:(nhalf + 1) * H2])
                S.dma("sync", nw, vb[(it + 1) % 2][:, :],
                      A["vTf"][nch * 128:(nch + 1) * 128, nhalf * H2:(nhalf + 1) * H2])
            if it % 2 == 0 and it // 2 < NLT:
                emit_rtile(it // 2)
            S.wait("scalar", w)
            S.wait("scalar", VE)
            S.op("scalar", lambda s, _p=par: s.activation(ekb[:, :], kbb[_p][:, :], AF.Exp))
            S.wait("vector", SC)
            S.op("vector", lambda v, _p=par: v.tensor_tensor(ekvb[:, :], ekb[:, :], vb[_p][:, :], OP.mult))
            rel[("kv", par)] = ((VE, VE.n),)
            if half == 0:
                S.op("vector", lambda v: v.memset(Nb[:, 0:1], 0.0))
                S.op("vector", lambda v: v.memset(Db[:, 0:1], 0.0))
            else:
                S.op("vector", lambda v: v.tensor_copy(Nb[:, 0:1], Nb[:, H2:H2 + 1]))
                S.op("vector", lambda v: v.tensor_copy(Db[:, 0:1], Db[:, H2:H2 + 1]))
            ewb = ew[:, ch:ch + 1].broadcast_to((128, H2))
            S.op("vector", lambda v, _e=ewb: v.tensor_tensor_scan(
                Nb[:, 1:H2 + 1], _e, ekvb[:, :], Nb[:, 0:1], OP.mult, OP.add))
            S.op("vector", lambda v, _e=ewb: v.tensor_tensor_scan(
                Db[:, 1:H2 + 1], _e, ekb[:, :], Db[:, 0:1], OP.mult, OP.add))
            S.op("vector", lambda v, _ch=ch: v.scalar_tensor_tensor(
                ekvb[:, :], ekvb[:, :], eu[:, _ch:_ch + 1], Nb[:, 0:H2], OP.mult, OP.add))
            S.op("vector", lambda v, _ch=ch: v.scalar_tensor_tensor(
                ekb[:, :], ekb[:, :], eu[:, _ch:_ch + 1], Db[:, 0:H2], OP.mult, OP.add))
            S.wait("scalar", VE)
            S.op("scalar", lambda s: s.activation(lnsc[:, :], ekb[:, :], AF.Ln))
            S.op("scalar", lambda s: s.activation(ekb[:, :], lnsc[:, :], AF.Exp, scale=-1.0))
            S.wait("vector", SC)
            S.op("vector", lambda v, _ch=ch, _t0=t0: v.tensor_tensor(
                yT[_ch][:, _t0:_t0 + H2], ekvb[:, :], ekb[:, :], OP.mult))

        for lt in range(NLT):
            tw = lq.wave()
            wait_rel("sync", "wotile")
            wait_rel("sync", "xrb")
            for c in range(NC8):
                S.dma("sync", tw, x2f[c][:, :], A["x2"][c * 128:(c + 1) * 128, lt * LT:(lt + 1) * LT])
                S.dma("sync", tw, srl[c][:, :], A["srT"][c * 128:(c + 1) * 128, lt * LT:(lt + 1) * LT])
            if lt == 0:
                for s_ in gq.sems:
                    S.wait("sync", s_)
            S.wait("vector", tw)
            S.wait("vector", VE)
            if lt == 0:
                for s_ in gq.sems:
                    S.wait("vector", s_)
            for c in range(NC8):
                S.op("vector", lambda v, _c=c, _lt=lt: v.tensor_tensor(
                    rwk[_c][:, :], yT[_c][:, _lt * LT:(_lt + 1) * LT], srl[_c][:, :], OP.mult))
            S.wait("tensor", VE)
            for oc in range(NC8):
                bank = psc[2 + oc % 2]
                wait_rel("tensor", ("pwo", oc % 2))
                for kc in range(NC8):
                    S.op("tensor", lambda t, _oc=oc, _kc=kc, _b=bank: t.matmul(
                        _b[:, :], woT[:, _kc * 1024 + _oc * 128:_kc * 1024 + (_oc + 1) * 128],
                        rwk[_kc][:, :], start=(_kc == 0), stop=(_kc == NC8 - 1)))
                S.wait("vector", PE)
                S.op("vector", lambda v, _oc=oc, _b=bank: v.tensor_tensor(
                    x2f[_oc][:, :], x2f[_oc][:, :], _b[:, :], OP.add))
                rel[("pwo", oc % 2)] = ((VE, VE.n),)
            S.wait("gpsimd", VE)
            sw = gq.wave()
            for c in range(NC8):
                S.dma("gpsimd", sw, A["xm"][c * 128:(c + 1) * 128, lt * LT:(lt + 1) * LT], x2f[c][:, :])
            rel["wotile"] = ((VE, VE.n), (PE, PE.n), (sw, sw.n))
        for s_ in gq.sems:
            S.wait("gpsimd", s_)
        S.flush()



def phase5(nc, A):
    with ExitStack() as stack:
        S = Sched(nc, stack, "p5")
        fkTh = stack.enter_context(nc.sbuf_tensor("p5_fkTh", [128, NC8 * 2048], BF16))
        fvTh = stack.enter_context(nc.sbuf_tensor("p5_fvTh", [128, 16 * 1024], BF16))
        xmf = [stack.enter_context(nc.sbuf_tensor(f"p5_xmf{c}", [128, LT], F32)) for c in range(NC8)]
        x2f = [stack.enter_context(nc.sbuf_tensor(f"p5_x2f{c}", [128, LT], F32)) for c in range(NC8)]
        gbuf = [stack.enter_context(nc.sbuf_tensor(f"p5_g{c}", [128, LT + 1], F32)) for c in range(NC8)]
        dscr = stack.enter_context(nc.sbuf_tensor("p5_dscr", [128, LT], F32))
        gkb = [stack.enter_context(nc.sbuf_tensor(f"p5_gkb{c}", [128, LT], BF16)) for c in range(NC8)]
        gkp = [gkb, [stack.enter_context(nc.sbuf_tensor(f"p5_gk2{c}", [128, LT], BF16)) for c in range(NC8)]]
        grb = [stack.enter_context(nc.sbuf_tensor(f"p5_grb{c}", [128, LT], BF16)) for c in range(NC8)]
        kfT = [stack.enter_context(nc.sbuf_tensor(f"p5_kf{i}", [128, LT], BF16)) for i in range(16)]
        relsc = stack.enter_context(nc.sbuf_tensor("p5_rel", [128, LT], F32))
        kvs = [stack.enter_context(nc.sbuf_tensor(f"p5_kvs{c}", [128, LT], F32)) for c in range(NC8)]
        kvp_t = x2f
        sig = relsc
        dlt = kvs
        x3t = dlt
        xb3t = gkb
        onesc = stack.enter_context(nc.sbuf_tensor("p5_onesc", [128, 1], BF16))
        onesr = stack.enter_context(nc.sbuf_tensor("p5_onesr", [1, 128], BF16))
        lnscr = make_ln_scratch(S, stack)
        identf, idrdy = make_identity(S, stack, F32, "f")
        pk = [stack.enter_context(nc.psum_tensor(f"p5_pk{i}", [128, LT], F32)) for i in range(2)]
        pkv = [stack.enter_context(nc.psum_tensor(f"p5_pkv{i}", [128, LT], F32)) for i in range(2)]
        pvec = stack.enter_context(nc.psum_tensor("p5_pvec", [128, LT], F32))
        pl = [stack.enter_context(nc.psum_tensor(f"p5_pl{i}", [128, LT], F32)) for i in range(2)]

        lq = S.dmaq("sync", "lq", 4)
        gq = S.dmaq("gpsimd", "gq", 4)
        PE, SC, VE, PL = S.eng["tensor"], S.eng["scalar"], S.eng["vector"], S.eng["gpsimd"]
        rel = {}

        def wait_rel(eng, key):
            for sem, cnt in rel.get(key, ()):
                S.wait(eng, sem, cnt)

        vecs = load_vecs_fm(S, stack, [
            ("l2g", A["L1_attn_ln2_g"], D), ("l2b", A["L1_attn_ln2_b"], D),
            ("fmk", A["L1_attn_fm_k"], D), ("fmr", A["L1_attn_fm_r"], D)],
            identf, idrdy, pvec, lq)
        ln2g, ln2b, fmk, fmr = vecs["l2g"], vecs["l2b"], vecs["fmk"], vecs["fmr"]
        S.op("vector", lambda v: v.memset(onesc[:], 1.0))
        S.op("vector", lambda v: v.memset(onesr[:], 1.0))
        for c in range(NC8):
            S.op("vector", lambda v, _c=c: v.memset(gbuf[_c][:, 0:1], 0.0))
        wld = {}

        def load_dfh_weights(dfh):
            if "cast" in CROSS:
                S.wait("sync", CROSS["cast"][0], CROSS["cast"][1])
            w = lq.wave()
            wld[dfh] = w
            for c in range(NC8):
                S.dma("sync", w, fkTh[:, c * 2048:(c + 1) * 2048],
                      A["fk_bf"][dfh * 2048:(dfh + 1) * 2048, c * 128:(c + 1) * 128], transpose=True)
            for d in range(16):
                S.dma("sync", w, fvTh[:, d * 1024:(d + 1) * 1024],
                      A["fv_bf"][:, dfh * 2048 + d * 128:dfh * 2048 + (d + 1) * 128], transpose=True)

        P5SEC = int(os.environ.get("P5SEC", "4"))
        kfT_rel = 0
        load_dfh_weights(0)

        def emit_a0(lt):
            tw = lq.wave()
            wait_rel("sync", "xmf")
            for c in range(NC8):
                S.dma("sync", tw, xmf[c][:, :], A["xm"][c * 128:(c + 1) * 128, lt * LT:(lt + 1) * LT])
            S.wait("vector", tw)
            S.wait("scalar", tw)
            if lt > 0:
                for c in range(NC8):
                    S.op("vector", lambda v, _c=c: v.tensor_copy(gbuf[_c][:, 0:1], gbuf[_c][:, LT:LT + 1]))
            emit_ln(S, lnscr, xmf, onesc, onesr, pl[0], pl[1],
                    [gbuf[c][:, 1:LT + 1] for c in range(NC8)], None, lq, lt, g_vec=ln2g, b_vec=ln2b)
            rel["xmf"] = ((VE, VE.n),)
            par = lt % 2
            wait_rel("vector", ("gmix", par))
            wait_rel("vector", "grst")
            for c in range(NC8):
                g_ = gbuf[c][:, 1:LT + 1]
                gg = gbuf[c][:, 0:LT]
                S.op("vector", lambda v, _g=g_, _gg=gg: v.tensor_tensor(dscr[:, :], _g, _gg, OP.subtract))
                S.op("vector", lambda v, _c=c, _gg=gg, _p=par: v.scalar_tensor_tensor(
                    gkp[_p][_c][:, :], dscr[:, :], fmk[:, _c:_c + 1], _gg, OP.mult, OP.add))
                S.op("vector", lambda v, _c=c, _gg=gg: v.scalar_tensor_tensor(
                    grb[_c][:, :], dscr[:, :], fmr[:, _c:_c + 1], _gg, OP.mult, OP.add))
            S.wait("gpsimd", VE)
            mw = gq.wave()
            for c in range(NC8):
                S.dma("gpsimd", mw, A["gkb"][c * 128:(c + 1) * 128, lt * LT:(lt + 1) * LT], gkp[par][c][:, :])
                S.dma("gpsimd", mw, A["grb"][c * 128:(c + 1) * 128, lt * LT:(lt + 1) * LT], grb[c][:, :])
            rel["grst"] = ((mw, mw.n),)
            rel[("a0", lt)] = ((VE, VE.n),)

        def emit_kfkv(lt, dfh, gk_tiles):
            nonlocal kfT_rel
            for dfc in range(16):
                bank = pk[dfc % 2]
                wait_rel("tensor", ("pk", dfc % 2))
                for kc in range(NC8):
                    S.op("tensor", lambda t, _d=dfc, _kc=kc, _b=bank, _g=gk_tiles: t.matmul(
                        _b[:, :], fkTh[:, _kc * 2048 + _d * 128:_kc * 2048 + (_d + 1) * 128],
                        _g[_kc][:, :], start=(_kc == 0), stop=(_kc == NC8 - 1)))
                S.wait("scalar", PE)
                S.wait("scalar", PE, kfT_rel)
                S.op("scalar", lambda s, _b=bank: s.activation(relsc[:, :], _b[:, :], AF.Relu))
                rel[("pk", dfc % 2)] = ((SC, SC.n),)
                S.op("scalar", lambda s, _d=dfc: s.activation(kfT[_d][:, :], relsc[:, :], AF.Square))
            for oc in range(NC8):
                bank = pkv[oc % 2]
                wait_rel("tensor", ("pkv", oc % 2))
                S.wait("tensor", SC)
                for dfc in range(16):
                    S.op("tensor", lambda t, _oc=oc, _d=dfc, _b=bank: t.matmul(
                        _b[:, :], fvTh[:, _d * 1024 + _oc * 128:_d * 1024 + (_oc + 1) * 128],
                        kfT[_d][:, :], start=(_d == 0), stop=(_d == 15)))
                S.wait("scalar", PE)
                wait_rel("scalar", ("kvs", oc))
                S.op("scalar", lambda s, _oc=oc, _b=bank: s.activation(kvs[_oc][:, :], _b[:, :], AF.Copy))
                rel[("pkv", oc % 2)] = ((SC, SC.n),)
                if dfh == 1:
                    S.wait("vector", SC)
                    S.op("vector", lambda v, _oc=oc: v.tensor_tensor(
                        kvs[_oc][:, :], kvs[_oc][:, :], kvp_t[_oc][:, :], OP.add))
            kfT_rel = PE.n
            if dfh == 1:
                rel["kvp"] = ((VE, VE.n),)
            S.wait("gpsimd", VE if dfh == 1 else SC)
            kw = gq.wave()
            dstn = "kvp" if dfh == 0 else "kvT"
            for c in range(NC8):
                S.dma("gpsimd", kw, A[dstn][c * 128:(c + 1) * 128, lt * LT:(lt + 1) * LT], kvs[c][:, :])
            for c in range(NC8):
                rel[("kvs", c)] = ((kw, kw.n),)

        for lt in range(NLT + 1):
            if lt >= 1:
                plt = lt - 1
                par = plt % 2
                for sem, cnt in rel.get(("a0", plt), ()):
                    S.wait("tensor", sem, cnt)
                if plt == 0:
                    S.wait("tensor", wld[0])
                emit_kfkv(plt, 0, gkp[par])
                rel[("gmix", par)] = ((PE, PE.n),)
            if lt < NLT:
                emit_a0(lt)
        S.wait("sync", PE)
        load_dfh_weights(1)
        for lt in range(NLT):
            tw = lq.wave()
            wait_rel("sync", "gki")
            for c in range(NC8):
                S.dma("sync", tw, gkb[c][:, :], A["gkb"][c * 128:(c + 1) * 128, lt * LT:(lt + 1) * LT])
            wait_rel("sync", "kvp")
            for c in range(NC8):
                S.dma("sync", tw, kvp_t[c][:, :], A["kvp"][c * 128:(c + 1) * 128, lt * LT:(lt + 1) * LT])
            S.wait("tensor", tw)
            S.wait("vector", tw)
            if lt == 0:
                S.wait("tensor", wld[1])
                for s_ in gq.sems:
                    S.wait("sync", s_)
            emit_kfkv(lt, 1, gkb)
            rel["gki"] = ((PE, PE.n),)
        for s_ in gq.sems:
            S.wait("sync", s_)
        frT = fkTh
        if "cast" in CROSS:
            S.wait("sync", CROSS["cast"][0], CROSS["cast"][1])
        wldc = lq.wave()
        S.wait("sync", PE)
        for c in range(NC8):
            S.dma("sync", wldc, frT[:, c * 1024:(c + 1) * 1024], A["fr_bf"][:, c * 128:(c + 1) * 128], transpose=True)
        rel.pop("gki", None)
        for lt in range(NLT if P5SEC >= 4 else 0):
            tw = lq.wave()
            wait_rel("sync", "ctile")
            wait_rel("sync", "xmf")
            for c in range(NC8):
                S.dma("sync", tw, grb[c][:, :], A["grb"][c * 128:(c + 1) * 128, lt * LT:(lt + 1) * LT])
                S.dma("sync", tw, kvs[c][:, :], A["kvT"][c * 128:(c + 1) * 128, lt * LT:(lt + 1) * LT])
                S.dma("sync", tw, xmf[c][:, :], A["xm"][c * 128:(c + 1) * 128, lt * LT:(lt + 1) * LT])
                S.dma("sync", tw, x2f[c][:, :], A["x2"][c * 128:(c + 1) * 128, lt * LT:(lt + 1) * LT])
            S.wait("tensor", tw)
            if lt == 0:
                S.wait("tensor", wldc)
            S.wait("vector", tw)
            S.wait("scalar", tw)
            for oc in range(NC8):
                bank = pk[oc % 2]
                wait_rel("tensor", ("pk", oc % 2))
                for kc in range(NC8):
                    S.op("tensor", lambda t, _oc=oc, _kc=kc, _b=bank: t.matmul(
                        _b[:, :], frT[:, _kc * 1024 + _oc * 128:_kc * 1024 + (_oc + 1) * 128],
                        grb[_kc][:, :], start=(_kc == 0), stop=(_kc == NC8 - 1)))
                S.wait("scalar", PE)
                S.wait("scalar", VE)
                S.op("scalar", lambda s, _b=bank: s.activation(sig[:, :], _b[:, :], AF.Sigmoid))
                rel[("pk", oc % 2)] = ((SC, SC.n),)
                S.wait("vector", SC)
                S.op("vector", lambda v, _oc=oc: v.tensor_tensor(sig[:, :], sig[:, :], kvs[_oc][:, :], OP.mult))
                S.op("vector", lambda v, _oc=oc: v.tensor_tensor(dlt[_oc][:, :], x2f[_oc][:, :], xmf[_oc][:, :], OP.add))
                S.op("vector", lambda v, _oc=oc: v.tensor_tensor(dlt[_oc][:, :], dlt[_oc][:, :], sig[:, :], OP.add))
            S.wait("scalar", VE)
            emit_ln(S, lnscr, dlt, onesc, onesr, pl[0], pl[1], x3t, xb3t, lq, lt)
            S.wait("gpsimd", VE)
            sw = gq.wave()
            for c in range(NC8):
                S.dma("gpsimd", sw, A["x3"][c * 128:(c + 1) * 128, lt * LT:(lt + 1) * LT], x3t[c][:, :])
                S.dma("gpsimd", sw, A["xb3"][c * 128:(c + 1) * 128, lt * LT:(lt + 1) * LT], xb3t[c][:, :])
            rel["ctile"] = ((sw, sw.n), (VE, VE.n))
        for s_ in gq.sems:
            S.wait("gpsimd", s_)
        S.flush()



def phase6(nc, A):
    with ExitStack() as stack:
        S = Sched(nc, stack, "p6")
        ws1b = stack.enter_context(nc.sbuf_tensor("p6_ws1", [128, NC8 * 1024], BF16))
        ws3b = stack.enter_context(nc.sbuf_tensor("p6_ws3", [128, NC8 * 1024], BF16))
        gw = stack.enter_context(nc.sbuf_tensor("p6_gw", [16, 1024], F32))
        gwT = stack.enter_context(nc.sbuf_tensor("p6_gwT", [128, NC8 * 16], F32))
        x3f = [stack.enter_context(nc.sbuf_tensor(f"p6_x3f{c}", [128, LT], F32)) for c in range(NC8)]
        xb3 = [stack.enter_context(nc.sbuf_tensor(f"p6_xb3{c}", [128, LT], BF16)) for c in range(NC8)]
        glog_s = stack.enter_context(nc.sbuf_tensor("p6_glog", [16, LT], F32))
        ptok = stack.enter_context(nc.sbuf_tensor("p6_ptok", [128, 64], F32))
        e16 = stack.enter_context(nc.sbuf_tensor("p6_e16", [128, 16], F32))
        p16 = stack.enter_context(nc.sbuf_tensor("p6_p16", [128, 16], F32))
        m16 = stack.enter_context(nc.sbuf_tensor("p6_m16", [128, 16], F32))
        cur = stack.enter_context(nc.sbuf_tensor("p6_cur", [128, 16], F32))
        gm16 = stack.enter_context(nc.sbuf_tensor("p6_gm16", [128, 16], F32))
        gsc = stack.enter_context(nc.sbuf_tensor("p6_gsc", [128, 4], F32))
        gs = stack.enter_context(nc.sbuf_tensor("p6_gs", [128, 8], F32))
        ctok = stack.enter_context(nc.sbuf_tensor("p6_ctok", [128, 64], F32))
        ctr_s = stack.enter_context(nc.sbuf_tensor("p6_ctr", [16, LT], BF16))
        s1f = stack.enter_context(nc.sbuf_tensor("p6_s1f", [128, LT], F32))
        ashb = [stack.enter_context(nc.sbuf_tensor(f"p6_ash{c}", [128, LT], BF16)) for c in range(NC8)]
        identf, idrdy = make_identity(S, stack, F32, "f")
        pg = [stack.enter_context(nc.psum_tensor(f"p6_pg{i}", [128, LT], F32)) for i in range(2)]
        ph = [stack.enter_context(nc.psum_tensor(f"p6_ph{i}", [128, LT], F32)) for i in range(4)]
        pt2 = stack.enter_context(nc.psum_tensor("p6_pt2", [128, LT], F32))

        lq = S.dmaq("sync", "lq", 3)
        gq = S.dmaq("gpsimd", "gq", 3)
        PE, SC, VE, PL = S.eng["tensor"], S.eng["scalar"], S.eng["vector"], S.eng["gpsimd"]

        wldp = gq.wave()
        for c in range(NC8):
            S.dma("gpsimd", wldp, ws1b[:, c * 1024:(c + 1) * 1024], A["L1_ffn_ws1"][c * 128:(c + 1) * 128, :])
            S.dma("gpsimd", wldp, ws3b[:, c * 1024:(c + 1) * 1024], A["L1_ffn_ws3"][c * 128:(c + 1) * 128, :])
        wld = lq.wave()
        S.dma("sync", wld, gw[:, :], A["L1_ffn_gate_w"][:, :])
        S.wait("tensor", wld)
        S.wait("tensor", PL, idrdy)
        for c in range(NC8):
            S.op("tensor", lambda t, _c=c: t.transpose(
                pt2[:, _c * 16:(_c + 1) * 16], gw[0:16, _c * 128:(_c + 1) * 128], identf[0:16, 0:16]))
        S.wait("scalar", PE)
        S.op("scalar", lambda s: s.activation(gwT[:, 0:NC8 * 16], pt2[:, 0:NC8 * 16], AF.Copy))
        s1f_rel = 0
        ct_rel = 0

        rel = {}

        def wait_rel(eng, key):
            for sem, cnt in rel.get(key, ()):
                S.wait(eng, sem, cnt)

        for lt in range(NLT):
            tw = lq.wave()
            wait_rel("sync", "xtiles")
            for c in range(NC8):
                S.dma("sync", tw, x3f[c][:, :], A["x3"][c * 128:(c + 1) * 128, lt * LT:(lt + 1) * LT])
                S.dma("sync", tw, xb3[c][:, :], A["xb3"][c * 128:(c + 1) * 128, lt * LT:(lt + 1) * LT])
            S.wait("tensor", tw)
            if lt == 0:
                S.wait("tensor", wldp)
                S.wait("tensor", SC)
                S.wait("tensor", VE)
            wait_rel("tensor", "pt2")
            for kc in range(NC8):
                S.op("tensor", lambda t, _kc=kc: t.matmul(
                    pt2[0:16, :], gwT[:, _kc * 16:(_kc + 1) * 16], x3f[_kc][:, :],
                    start=(_kc == 0), stop=(_kc == NC8 - 1)))
            S.wait("scalar", PE)
            wait_rel("scalar", "glog")
            S.op("scalar", lambda s: s.activation(glog_s[:, :], pt2[0:16, :], AF.Copy))
            S.wait("tensor", SC)
            for j in range(4):
                S.op("tensor", lambda t, _j=j: t.transpose(
                    pt2[:, _j * 16:(_j + 1) * 16], glog_s[0:16, _j * 128:(_j + 1) * 128], identf[0:16, 0:16]))
            S.wait("scalar", PE)
            wait_rel("scalar", "ptok")
            S.op("scalar", lambda s: s.activation(ptok[:, 0:64], pt2[:, 0:64], AF.Copy))
            rel["glog"] = ((PE, PE.n),)
            rel["pt2"] = ((SC, SC.n),)
            for oc in range(NC8):
                b1 = ph[oc % 2]
                b3 = ph[2 + oc % 2]
                wait_rel("tensor", ("ph", oc % 2))
                for kc in range(NC8):
                    S.op("tensor", lambda t, _oc=oc, _kc=kc, _b=b1: t.matmul(
                        _b[:, :], ws1b[:, _kc * 1024 + _oc * 128:_kc * 1024 + (_oc + 1) * 128],
                        xb3[_kc][:, :], start=(_kc == 0), stop=(_kc == NC8 - 1)))
                for kc in range(NC8):
                    S.op("tensor", lambda t, _oc=oc, _kc=kc, _b=b3: t.matmul(
                        _b[:, :], ws3b[:, _kc * 1024 + _oc * 128:_kc * 1024 + (_oc + 1) * 128],
                        xb3[_kc][:, :], start=(_kc == 0), stop=(_kc == NC8 - 1)))
                S.wait("scalar", PE)
                S.wait("scalar", VE, s1f_rel)
                S.op("scalar", lambda s, _b=b1: s.activation(s1f[:, :], _b[:, :], AF.Silu))
                S.wait("vector", SC)
                wait_rel("vector", ("ashb", oc))
                S.op("vector", lambda v, _oc=oc, _b=b3: v.tensor_tensor(
                    ashb[_oc][:, :], s1f[:, :], _b[:, :], OP.mult))
                s1f_rel = VE.n
                rel[("ph", oc % 2)] = ((VE, VE.n), (SC, SC.n))
            rel["xtiles"] = ((PE, PE.n),)
            for j in range(4):
                L = ptok[:, j * 16:(j + 1) * 16]
                S.wait("vector", SC)
                S.wait("vector", PE, ct_rel)
                S.op("vector", lambda v, _L=L: v.reduce_max(gs[:, 0:1], _L, axis=mybir.AxisListType.X))
                S.op("vector", lambda v: v.tensor_scalar(gs[:, 1:2], gs[:, 0:1], -1.0, None, OP.mult))
                S.wait("scalar", VE)
                S.op("scalar", lambda s, _L=L: s.activation(e16[:, :], _L, AF.Exp, bias=gs[:, 1:2]))
                S.wait("vector", SC)
                S.op("vector", lambda v: v.reduce_sum(gs[:, 2:3], e16[:, :], axis=mybir.AxisListType.X))
                S.op("vector", lambda v: v.reciprocal(gs[:, 3:4], gs[:, 2:3]))
                S.op("vector", lambda v: v.tensor_scalar(p16[:, :], e16[:, :], gs[:, 3:4], None, OP.mult))
                S.op("vector", lambda v: v.reduce_max(
                    gsc[:, :], p16[:, :].rearrange("p (g e) -> p g e", g=4), axis=mybir.AxisListType.X))
                S.op("vector", lambda v: v.reduce_max(gs[:, 4:5], gsc[:, :], axis=mybir.AxisListType.X))
                S.op("vector", lambda v: v.tensor_scalar(cur[:, 0:4], gsc[:, :], gs[:, 4:5], None, OP.is_ge))
                S.op("vector", lambda v: v.scalar_tensor_tensor(
                    cur[:, 0:4], cur[:, 0:4], -1e30, gsc[:, :], OP.mult, OP.add))
                S.op("vector", lambda v: v.reduce_max(gs[:, 5:6], cur[:, 0:4], axis=mybir.AxisListType.X))
                S.op("vector", lambda v: v.tensor_scalar(gsc[:, :], gsc[:, :], gs[:, 5:6], None, OP.is_ge))
                S.op("vector", lambda v: v.tensor_copy(
                    gm16[:, :].rearrange("p (g e) -> p g e", g=4),
                    gsc[:, :].broadcast_to((128, 4, 4))))
                S.op("vector", lambda v: v.tensor_tensor(m16[:, :], p16[:, :], gm16[:, :], OP.mult))
                S.op("vector", lambda v: v.tensor_copy(cur[:, :], m16[:, :]))
                for it in range(3):
                    S.op("vector", lambda v: v.reduce_max(gs[:, 6:7], cur[:, :], axis=mybir.AxisListType.X))
                    S.op("vector", lambda v: v.tensor_scalar(e16[:, :], cur[:, :], gs[:, 6:7], None, OP.is_ge))
                    S.op("vector", lambda v: v.scalar_tensor_tensor(
                        cur[:, :], e16[:, :], -1e30, cur[:, :], OP.mult, OP.add))
                S.op("vector", lambda v: v.reduce_max(gs[:, 6:7], cur[:, :], axis=mybir.AxisListType.X))
                S.op("vector", lambda v: v.tensor_scalar(e16[:, :], m16[:, :], gs[:, 6:7], None, OP.is_ge))
                S.op("vector", lambda v, _j=j: v.tensor_tensor(ctok[:, _j * 16:(_j + 1) * 16], p16[:, :], e16[:, :], OP.mult))
            S.wait("tensor", VE)
            wait_rel("tensor", "pt2")
            for j in range(4):
                S.op("tensor", lambda t, _j=j: t.transpose(
                    pt2[0:16, _j * 128:(_j + 1) * 128], ctok[:, _j * 16:(_j + 1) * 16], identf[:, :]))
            ct_rel = PE.n
            rel["ptok"] = ((VE, VE.n),)
            S.wait("scalar", PE)
            wait_rel("scalar", "ctr")
            S.op("scalar", lambda s: s.activation(ctr_s[:, :], pt2[0:16, :], AF.Copy))
            rel["pt2"] = ((SC, SC.n),)
            S.wait("gpsimd", SC)
            cw = gq.wave()
            S.dma("gpsimd", cw, A["ctr"][:, lt * LT:(lt + 1) * LT], ctr_s[:, :])
            rel["ctr"] = ((cw, cw.n),)
            S.wait("gpsimd", VE)
            aw = gq.wave()
            for c in range(NC8):
                S.dma("gpsimd", aw, A["ash"][c * 128:(c + 1) * 128, lt * LT:(lt + 1) * LT], ashb[c][:, :])
            for c in range(NC8):
                rel[("ashb", c)] = ((aw, aw.n),)
        for s_ in gq.sems:
            S.wait("gpsimd", s_)
        S.flush()



def phase7(nc, A):
    with ExitStack() as stack:
        S = Sched(nc, stack, "p7")
        w1b = [stack.enter_context(nc.sbuf_tensor(f"p7_w1_{p}", [128, NC8 * 512], BF16)) for p in range(2)]
        w3b = [stack.enter_context(nc.sbuf_tensor(f"p7_w3_{p}", [128, NC8 * 512], BF16)) for p in range(2)]
        xb3 = [stack.enter_context(nc.sbuf_tensor(f"p7_xb3{c}", [128, T], BF16)) for c in range(NC8)]
        ctr_s = stack.enter_context(nc.sbuf_tensor("p7_ctr", [16, T], BF16))
        sel = [stack.enter_context(nc.sbuf_tensor(f"p7_sel{e}", [16, 128], BF16)) for e in range(16)]
        ones16 = stack.enter_context(nc.sbuf_tensor("p7_ones16", [16, 128], BF16))
        s1f = stack.enter_context(nc.sbuf_tensor("p7_s1f", [128, LT], F32))
        af = stack.enter_context(nc.sbuf_tensor("p7_af", [128, LT], F32))
        ab = [[stack.enter_context(nc.sbuf_tensor(f"p7_ab{p}_{i}", [128, LT], BF16)) for i in range(4)]
              for p in range(2)]
        ph = [stack.enter_context(nc.psum_tensor(f"p7_ph{i}", [128, LT], F32)) for i in range(4)]
        pc = [stack.enter_context(nc.psum_tensor(f"p7_pc{i}", [128, LT], F32)) for i in range(2)]

        lq = S.dmaq("sync", "lq", 3)
        gq = S.dmaq("gpsimd", "gq", 4)
        PE, SC, VE, PL = S.eng["tensor"], S.eng["scalar"], S.eng["vector"], S.eng["gpsimd"]

        rel = {}

        def wait_rel(eng, key):
            for sem, cnt in rel.get(key, ()):
                S.wait(eng, sem, cnt)

        S.op("gpsimd", lambda g: g.memset(ones16[:, :], 1.0))
        for e in range(16):
            S.op("gpsimd", lambda g, _e=e: g.affine_select(
                sel[_e][:, :], ones16[:, :], pattern=[[0, 128]], compare_op=OP.is_equal,
                fill=0.0, base=-_e, channel_multiplier=1))
        cld = lq.wave()
        S.dma("sync", cld, ctr_s[:, :], A["ctr"][:, :])
        for c in range(NC8):
            S.dma("sync", cld, xb3[c][:, :], A["xb3"][c * 128:(c + 1) * 128, :])
        ew = {0: gq.wave()}
        for kc in range(NC8):
            S.dma("gpsimd", ew[0], w1b[0][:, kc * 512:(kc + 1) * 512], A["L1_ffn_w1"][0, kc * 128:(kc + 1) * 128, :])
            S.dma("gpsimd", ew[0], w3b[0][:, kc * 512:(kc + 1) * 512], A["L1_ffn_w3"][0, kc * 128:(kc + 1) * 128, :])

        wrel = {}
        s1f_rel = 0
        for e in range(16):
            if e + 1 < 16:
                nset = (e + 1) % 2
                w = gq.wave()
                ew[e + 1] = w
                if e >= 1:
                    S.wait("gpsimd", PE, wrel[e - 1])
                for kc in range(NC8):
                    S.dma("gpsimd", w, w1b[nset][:, kc * 512:(kc + 1) * 512],
                          A["L1_ffn_w1"][e + 1, kc * 128:(kc + 1) * 128, :])
                    S.dma("gpsimd", w, w3b[nset][:, kc * 512:(kc + 1) * 512],
                          A["L1_ffn_w3"][e + 1, kc * 128:(kc + 1) * 128, :])
            wset = e % 2
            for lt in range(NLT):
                glob = e * NLT + lt
                par = glob % 2
                if e == 0 and lt == 0:
                    S.wait("tensor", cld)
                    S.wait("tensor", PL)
                if lt == 0:
                    S.wait("tensor", ew[e])
                pcb = pc[par]
                wait_rel("tensor", ("pc", par))
                S.op("tensor", lambda t, _e=e, _lt=lt, _b=pcb: t.matmul(
                    _b[:, :], sel[_e][:, :], ctr_s[:, _lt * LT:(_lt + 1) * LT], start=True, stop=True))
                for ic in range(4):
                    b1 = ph[ic % 2]
                    b3 = ph[2 + ic % 2]
                    wait_rel("tensor", ("ph", ic % 2))
                    for kc in range(NC8):
                        S.op("tensor", lambda t, _ic=ic, _kc=kc, _b=b1, _lt=lt, _w=wset: t.matmul(
                            _b[:, :], w1b[_w][:, _kc * 512 + _ic * 128:_kc * 512 + (_ic + 1) * 128],
                            xb3[_kc][:, _lt * LT:(_lt + 1) * LT], start=(_kc == 0), stop=(_kc == NC8 - 1)))
                    for kc in range(NC8):
                        S.op("tensor", lambda t, _ic=ic, _kc=kc, _b=b3, _lt=lt, _w=wset: t.matmul(
                            _b[:, :], w3b[_w][:, _kc * 512 + _ic * 128:_kc * 512 + (_ic + 1) * 128],
                            xb3[_kc][:, _lt * LT:(_lt + 1) * LT], start=(_kc == 0), stop=(_kc == NC8 - 1)))
                    S.wait("scalar", PE)
                    S.wait("scalar", VE, s1f_rel)
                    S.op("scalar", lambda s, _b=b1: s.activation(s1f[:, :], _b[:, :], AF.Silu))
                    S.wait("vector", SC)
                    S.op("vector", lambda v, _b=b3: v.tensor_tensor(af[:, :], s1f[:, :], _b[:, :], OP.mult))
                    s1f_rel = VE.n
                    if ("ab", par) in rel:
                        wait_rel("vector", ("ab", par))
                        rel.pop(("ab", par))
                    S.op("vector", lambda v, _ic=ic, _p=par, _pc=pcb: v.tensor_tensor(
                        ab[_p][_ic][:, :], af[:, :], _pc[:, :], OP.mult))
                    rel[("ph", ic % 2)] = ((VE, VE.n), (SC, SC.n))
                rel[("pc", par)] = ((VE, VE.n),)
                S.wait("gpsimd", VE)
                aw = gq.wave()
                for ic in range(4):
                    S.dma("gpsimd", aw, A["a_all"][e, ic * 128:(ic + 1) * 128, lt * LT:(lt + 1) * LT],
                          ab[par][ic][:, :])
                rel[("ab", par)] = ((aw, aw.n),)
            wrel[e] = PE.n
        for s_ in gq.sems:
            S.wait("gpsimd", s_)
        S.flush()



def phase8(nc, A):
    with ExitStack() as stack:
        S = Sched(nc, stack, "p8")
        w2b = stack.enter_context(nc.sbuf_tensor("p8_w2", [128, 32 * 1024], BF16))
        ws2b = stack.enter_context(nc.sbuf_tensor("p8_ws2", [128, NC8 * 1024], BF16))
        at = [[stack.enter_context(nc.sbuf_tensor(f"p8_a{e}_{i}", [128, LT], BF16)) for i in range(4)]
              for e in range(8)]
        ashb = [stack.enter_context(nc.sbuf_tensor(f"p8_ash{c}", [128, LT], BF16)) for c in range(NC8)]
        x3f = [stack.enter_context(nc.sbuf_tensor(f"p8_x3f{c}", [128, LT], F32)) for c in range(NC8)]
        ypt = [stack.enter_context(nc.sbuf_tensor(f"p8_ypt{c}", [128, LT], F32)) for c in range(NC8)]
        dlt = [stack.enter_context(nc.sbuf_tensor(f"p8_dlt{c}", [128, LT], F32)) for c in range(NC8)]
        x4t = dlt
        stg = stack.enter_context(nc.sbuf_tensor("p8_stg", [128, 1024], F32))
        onesc = stack.enter_context(nc.sbuf_tensor("p8_onesc", [128, 1], BF16))
        onesr = stack.enter_context(nc.sbuf_tensor("p8_onesr", [1, 128], BF16))
        lnscr = make_ln_scratch(S, stack)
        identf, idrdy = make_identity(S, stack, F32, "f")
        py = [stack.enter_context(nc.psum_tensor(f"p8_py{i}", [128, LT], F32)) for i in range(3)]
        ptr = [stack.enter_context(nc.psum_tensor(f"p8_ptr{i}", [128, LT], F32)) for i in range(2)]
        pl = [stack.enter_context(nc.psum_tensor(f"p8_pl{i}", [128, LT], F32)) for i in range(3)]

        lq = S.dmaq("sync", "lq", 4)
        gq = S.dmaq("gpsimd", "gq", 4)
        PE, SC, VE, PL = S.eng["tensor"], S.eng["scalar"], S.eng["vector"], S.eng["gpsimd"]
        rel = {}

        def wait_rel(eng, key):
            for sem, cnt in rel.get(key, ()):
                S.wait(eng, sem, cnt)

        S.op("vector", lambda v: v.memset(onesc[:], 1.0))
        S.op("vector", lambda v: v.memset(onesr[:], 1.0))

        for half in range(2):
            if half == 1:
                for s_ in gq.sems:
                    S.wait("sync", s_)
            ww = gq.wave()
            if half == 1:
                S.wait("gpsimd", PE)
            for eg in range(8):
                e = half * 8 + eg
                for ic in range(4):
                    S.dma("gpsimd", ww, w2b[:, (eg * 4 + ic) * 1024:(eg * 4 + ic + 1) * 1024],
                          A["L1_ffn_w2"][e, ic * 128:(ic + 1) * 128, :])
            if half == 0:
                for c in range(NC8):
                    S.dma("gpsimd", ww, ws2b[:, c * 1024:(c + 1) * 1024],
                          A["L1_ffn_ws2"][c * 128:(c + 1) * 128, :])
            for lt in range(NLT):
                tw = lq.wave()
                wait_rel("sync", "atiles")
                for eg in range(8):
                    e = half * 8 + eg
                    for ic in range(4):
                        S.dma("sync", tw, at[eg][ic][:, :],
                              A["a_all"][e, ic * 128:(ic + 1) * 128, lt * LT:(lt + 1) * LT])
                if half == 0:
                    wait_rel("sync", "ashb")
                    for c in range(NC8):
                        S.dma("sync", tw, ashb[c][:, :], A["ash"][c * 128:(c + 1) * 128, lt * LT:(lt + 1) * LT])
                else:
                    wait_rel("sync", "yxt")
                    for c in range(NC8):
                        S.dma("sync", tw, ypt[c][:, :], A["ypart"][c * 128:(c + 1) * 128, lt * LT:(lt + 1) * LT])
                        S.dma("sync", tw, x3f[c][:, :], A["x3"][c * 128:(c + 1) * 128, lt * LT:(lt + 1) * LT])
                S.wait("tensor", tw)
                if lt == 0:
                    S.wait("tensor", ww)
                    S.wait("tensor", PL, idrdy)
                    S.wait("tensor", VE)
                for oc in range(NC8):
                    bank = py[oc % 3]
                    wait_rel("tensor", ("py", oc % 3))
                    first = True
                    for eg in range(8):
                        for ic in range(4):
                            last = (half == 1 and eg == 7 and ic == 3)
                            S.op("tensor", lambda t, _eg=eg, _ic=ic, _oc=oc, _b=bank, _f=first, _l=last: t.matmul(
                                _b[:, :], w2b[:, (_eg * 4 + _ic) * 1024 + _oc * 128:(_eg * 4 + _ic) * 1024 + (_oc + 1) * 128],
                                at[_eg][_ic][:, :], start=_f, stop=_l))
                            first = False
                    if half == 0:
                        for kc in range(NC8):
                            S.op("tensor", lambda t, _kc=kc, _oc=oc, _b=bank: t.matmul(
                                _b[:, :], ws2b[:, _kc * 1024 + _oc * 128:_kc * 1024 + (_oc + 1) * 128],
                                ashb[_kc][:, :], start=False, stop=(_kc == NC8 - 1)))
                    S.wait("scalar", PE)
                    if half == 0:
                        wait_rel("scalar", ("ypt", oc))
                        S.op("scalar", lambda s, _oc=oc, _b=bank: s.activation(ypt[_oc][:, :], _b[:, :], AF.Copy))
                    else:
                        wait_rel("scalar", ("dlt", oc))
                        S.op("scalar", lambda s, _oc=oc, _b=bank: s.activation(dlt[_oc][:, :], _b[:, :], AF.Copy))
                    rel[("py", oc % 3)] = ((SC, SC.n),)
                rel["atiles"] = ((PE, PE.n),)
                if half == 0:
                    rel["ashb"] = ((PE, PE.n),)
                    S.wait("gpsimd", SC)
                    yw = gq.wave()
                    for c in range(NC8):
                        S.dma("gpsimd", yw, A["ypart"][c * 128:(c + 1) * 128, lt * LT:(lt + 1) * LT], ypt[c][:, :])
                    for c in range(NC8):
                        rel[("ypt", c)] = ((yw, yw.n),)
                else:
                    S.wait("vector", SC)
                    for c in range(NC8):
                        S.op("vector", lambda v, _c=c: v.tensor_tensor(dlt[_c][:, :], dlt[_c][:, :], ypt[_c][:, :], OP.add))
                        S.op("vector", lambda v, _c=c: v.tensor_tensor(dlt[_c][:, :], dlt[_c][:, :], x3f[_c][:, :], OP.add))
                    rel["yxt"] = ((VE, VE.n),)
                    S.wait("scalar", VE)
                    emit_ln(S, lnscr, dlt, onesc, onesr, pl[0], pl[1], x4t, None, lq, lt)
                    S.wait("tensor", VE)
                    for ts in range(4):
                        for c in range(NC8):
                            bank = ptr[c % 2]
                            wait_rel("tensor", ("ptr", c % 2))
                            S.wait("tensor", SC)
                            S.op("tensor", lambda t, _c=c, _ts=ts, _b=bank: t.transpose(
                                _b[:, 0:128], x4t[_c][:, _ts * 128:(_ts + 1) * 128], identf[:, :]))
                            S.wait("scalar", PE)
                            wait_rel("scalar", "stg")
                            S.op("scalar", lambda s, _c=c, _b=bank: s.activation(
                                stg[:, _c * 128:(_c + 1) * 128], _b[:, 0:128], AF.Copy))
                            rel[("ptr", c % 2)] = ((SC, SC.n),)
                        S.wait("gpsimd", SC)
                        ow = gq.wave()
                        S.dma("gpsimd", ow, A["out"][lt * LT + ts * 128:lt * LT + (ts + 1) * 128, :], stg[:, :])
                        rel["stg"] = ((ow, ow.n),)
                    for c in range(NC8):
                        rel[("dlt", c)] = ((PE, PE.n),)
        for s_ in gq.sems:
            S.wait("gpsimd", s_)
        S.flush()


PHASE_FNS = {0: phase0, 1: phase1, 2: phase2, 3: phase3, 4: phase4, 5: phase5, 6: phase6, 7: phase7, 8: phase8}


def build(phases):
    CROSS.clear()
    DRAM_KINDS.clear()
    nc = bass.Bass("TRN2", target_bir_lowering=False)
    present = set(phases)
    A = {}
    A["x_in"] = declare_dram(nc, "x_in", (T, D), F32, "ExternalInput")
    for name, shp in WEIGHTS.items():
        A[name] = declare_dram(nc, name, shp, F32, "ExternalInput")
    A["out"] = declare_dram(nc, "out", (T, D), F32, "ExternalOutput")
    for name, (shp, dt, prod) in INTERMEDIATES.items():
        readers = [p for p in present if name in PHASE_READS.get(p, [])]
        written = prod in present
        if written and readers:
            kind = "Internal"
        elif written:
            kind = "ExternalOutput"
        elif readers:
            kind = "ExternalInput"
        else:
            continue
        A[name] = declare_dram(nc, name, shp, dt, kind)
    for p in phases:
        PHASE_FNS[p](nc, A)
    return nc, A


def make_in_maps(inputs, extra=None):
    x = np.ascontiguousarray(np.asarray(inputs["L0_attn_x"], dtype=np.float32))
    in_maps = []
    for b in range(8):
        m = {"x_in": np.ascontiguousarray(x[b])}
        for name, shp in WEIGHTS.items():
            m[name] = np.ascontiguousarray(np.asarray(inputs[name], dtype=np.float32).reshape(shp))
        if extra is not None:
            m.update(extra[b])
        in_maps.append(m)
    return in_maps


def kernel(**inputs):
    phases = sorted(PHASE_FNS.keys())
    nc, A = build(phases)
    in_maps = make_in_maps(inputs)
    res = run_bass_kernel_spmd(nc, in_maps, core_ids=list(range(8)))
    return np.stack([np.asarray(r["out"]) for r in res.results], axis=0)


# version 08
# speedup vs baseline: 1.0648x; 1.0349x over previous
import math
import os
from contextlib import ExitStack

import numpy as np
import concourse.bass as bass
import concourse.mybir as mybir
from concourse.bass_utils import run_bass_kernel_spmd

F32 = mybir.dt.float32
BF16 = mybir.dt.bfloat16
AF = mybir.ActivationFunctionType
OP = mybir.AluOpType

T = 4096
D = 1024
NC8 = 8
LT = 512
NLT = T // LT
H = 16
DH = 64
EPS = 1e-5

ENGINES = ("sync", "tensor", "scalar", "vector", "gpsimd")


class Sem:
    def __init__(self, stack, nc, name):
        self.h = nc.alloc_semaphore(name)
        self.n = 0


class DmaQ:

    def __init__(self, S, eng, tag, n=3):
        self.S = S
        self.eng = eng
        self.sems = [S.sem(f"{tag}{i}") for i in range(n)]
        self.n = n
        self.w = 0

    def wave(self):
        s = self.sems[self.w % self.n]
        if self.w >= self.n:
            self.S.wait(self.eng, s)
        self.w += 1
        return s


AUTO_SERIAL = ("vector", "scalar", "gpsimd")


class Sched:

    def __init__(self, nc, stack, name):
        self.nc = nc
        self.stack = stack
        self.name = name
        self.ops = {e: [] for e in ENGINES}
        self._semid = 0
        self.sems = []
        self.eng = {e: self.sem(e[:2]) for e in ("tensor", "scalar", "vector", "gpsimd")}

    def sem(self, tag=""):
        self._semid += 1
        s = Sem(self.stack, self.nc, f"{self.name}_s{self._semid}{tag}")
        self.sems.append(s)
        return s

    def wait(self, eng, sem, val=None):
        v = sem.n if val is None else val
        if v > 0:
            self.ops[eng].append(lambda e, _h=sem.h, _v=v: e.wait_ge(_h, _v))

    def op(self, eng, fn):
        s = self.eng[eng]
        if eng in AUTO_SERIAL and s.n > 0:
            self.wait(eng, s)
        self.ops[eng].append(lambda e, _f=fn, _h=s.h: _f(e).then_inc(_h, 1))
        s.n += 1
        return s.n

    def dmaq(self, eng, tag, n=3):
        return DmaQ(self, eng, tag, n)

    def dma(self, eng, sem, out, in_, **kw):
        self.ops[eng].append(lambda e, _o=out, _i=in_, _h=sem.h, _kw=kw: e.dma_start(
            out=_o, in_=_i, **_kw).then_inc(_h, 16))
        sem.n += 16

    def barrier(self, engines=ENGINES):
        for e in engines:
            for s in self.sems:
                if not (e in self.eng and self.eng[e] is s):
                    self.wait(e, s)

    def flush(self):
        with self.nc.Block() as block:
            for e in ENGINES:
                if self.ops[e]:
                    def body(eng, _l=tuple(self.ops[e])):
                        for f in _l:
                            f(eng)
                    getattr(block, e)(body)
        self.ops = {e: [] for e in ENGINES}


DRAM_KINDS = {}
CROSS = {}


def declare_dram(nc, name, shape, dtype, kind):
    DRAM_KINDS[name] = kind
    return nc.dram_tensor(name, list(shape), dtype, kind=kind).ap()


WEIGHTS = {
    "L0_attn_Wqkv_w": (3072, 1024), "L0_attn_Wqkv_b": (1, 3072),
    "L0_attn_out_w": (1024, 1024), "L0_attn_out_b": (1, 1024),
    "L0_ffn_gate_W": (8, 1024), "L0_ffn_expert_W": (8, 1024, 1024),
    "L0_ffn_out_W": (1024, 1024),
    "L1_attn_ln1_g": (1, 1024), "L1_attn_ln1_b": (1, 1024),
    "L1_attn_ln2_g": (1, 1024), "L1_attn_ln2_b": (1, 1024),
    "L1_attn_tm_k": (1, 1024), "L1_attn_tm_v": (1, 1024), "L1_attn_tm_r": (1, 1024),
    "L1_attn_time_decay": (1, 1024), "L1_attn_time_first": (1, 1024),
    "L1_attn_Wk": (1024, 1024), "L1_attn_Wv": (1024, 1024),
    "L1_attn_Wr": (1024, 1024), "L1_attn_Wo": (1024, 1024),
    "L1_attn_fm_k": (1, 1024), "L1_attn_fm_r": (1, 1024),
    "L1_attn_Fk": (4096, 1024), "L1_attn_Fr": (1024, 1024), "L1_attn_Fv": (1024, 4096),
    "L1_ffn_gate_w": (16, 1024),
    "L1_ffn_w1": (16, 1024, 512), "L1_ffn_w2": (16, 512, 1024), "L1_ffn_w3": (16, 1024, 512),
    "L1_ffn_ws1": (1024, 1024), "L1_ffn_ws2": (1024, 1024), "L1_ffn_ws3": (1024, 1024),
}

INTERMEDIATES = {
    "x0": ((D, T), F32, 0), "xb0": ((D, T), BF16, 0),
    "wqkv_bf": ((3072, 1024), BF16, 0),
    "outw_bf": ((1024, 1024), BF16, 0),
    "expw0_bf": ((1024, 1024), BF16, 0),
    "expw1_bf": ((1024, 1024), BF16, 0),
    "outW_bf": ((1024, 1024), BF16, 0),
    "wk_bf": ((1024, 1024), BF16, 0), "wv_bf": ((1024, 1024), BF16, 0),
    "wr_bf": ((1024, 1024), BF16, 0), "wo_bf": ((1024, 1024), BF16, 0),
    "fk_bf": ((4096, 1024), BF16, 0), "fr_bf": ((1024, 1024), BF16, 0),
    "fv_bf": ((1024, 4096), BF16, 0),
    "x1": ((D, T), F32, 1), "xb1": ((D, T), BF16, 1),
    "x2": ((D, T), F32, 2), "xb2": ((D, T), BF16, 2),
    "kTf": ((D, T), F32, 3), "vTf": ((D, T), F32, 3), "srT": ((D, T), BF16, 4),
    "xrT": ((D, T), BF16, 3),
    "xm": ((D, T), F32, 4), "xmb": ((D, T), BF16, 4),
    "x3": ((D, T), F32, 5), "xb3": ((D, T), BF16, 5),
    "gkb": ((D, T), BF16, 5), "grb": ((D, T), BF16, 5),
    "kvp": ((D, T), F32, 5), "kvT": ((D, T), F32, 5),
    "ctr": ((16, T), BF16, 6), "ash": ((D, T), BF16, 6),
    "a_all": ((16, 512, T), BF16, 7),
    "ypart": ((D, T), F32, 8),
}

PHASE_READS = {
    1: ["x0", "xb0", "wqkv_bf", "outw_bf"],
    2: ["x1", "xb1", "expw0_bf", "expw1_bf", "outW_bf"],
    3: ["x2", "xb2", "wk_bf", "wv_bf", "wr_bf"],
    4: ["x2", "kTf", "vTf", "xrT", "srT", "wr_bf", "wo_bf"],
    5: ["x2", "xm", "fk_bf", "fr_bf", "fv_bf", "gkb", "grb", "kvp", "kvT"],
    6: ["x3", "xb3"],
    7: ["xb3", "ctr"],
    8: ["x3", "a_all", "ash", "ypart"],
}



def make_identity(S, stack, dtype, tag):
    nc = S.nc
    ones = stack.enter_context(nc.sbuf_tensor(f"{S.name}_idsrc_{tag}", [128, 128], dtype))
    ident = stack.enter_context(nc.sbuf_tensor(f"{S.name}_ident_{tag}", [128, 128], dtype))
    S.op("gpsimd", lambda g: g.memset(ones[:], 1.0))
    S.op("gpsimd", lambda g: g.affine_select(
        ident[:], ones[:], pattern=[[-1, 128]], compare_op=OP.is_equal, fill=0.0,
        base=0, channel_multiplier=1))
    return ident, S.eng["gpsimd"].n


def load_vecs_fm(S, stack, specs, identity, id_ready, psum, ldq):
    nc = S.nc
    nmax = max(n for _, _, n in specs)
    row = stack.enter_context(nc.sbuf_tensor(f"{S.name}_vrow", [1, nmax], F32))
    outs = {}
    col = 0
    S.wait("tensor", S.eng["gpsimd"], id_ready)
    S.wait("tensor", S.eng["scalar"])
    for key, ap, n in specs:
        nch = n // 128
        w = ldq.wave()
        S.wait("sync", S.eng["tensor"])
        S.dma(ldq.eng, w, row[0:1, 0:n], ap[0:1, :])
        S.wait("tensor", w)
        for c in range(nch):
            S.op("tensor", lambda t, _c=c, _col=col: t.transpose(
                psum[:, _col + _c:_col + _c + 1], row[0:1, _c * 128:(_c + 1) * 128],
                identity[0:1, 0:1]))
        outs[key] = stack.enter_context(nc.sbuf_tensor(f"{S.name}_vfm_{key}", [128, nch], F32))
        S.wait("scalar", S.eng["tensor"])
        S.op("scalar", lambda s, _k=key, _n=nch, _col=col: s.activation(
            outs[_k][:, 0:_n], psum[:, _col:_col + _n], AF.Copy))
        col += nch
    return outs


def make_ln_scratch(S, stack):
    nc = S.nc
    scr = {
        "rb": [stack.enter_context(nc.sbuf_tensor(f"{S.name}_lnrb{c}", [128, LT], BF16)) for c in range(4)],
        "sq": [stack.enter_context(nc.sbuf_tensor(f"{S.name}_lnsq{c}", [128, LT], BF16)) for c in range(4)],
        "mb": stack.enter_context(nc.sbuf_tensor(f"{S.name}_lnmb", [1, LT], BF16)),
        "rs": stack.enter_context(nc.sbuf_tensor(f"{S.name}_lnrs", [1, LT], BF16)),
        "rows": stack.enter_context(nc.sbuf_tensor(f"{S.name}_lnrows", [1, 4 * LT], F32)),
        "epsr": stack.enter_context(nc.sbuf_tensor(f"{S.name}_lneps", [1, 1], F32)),
        "rel": {},
    }
    S.op("vector", lambda v: v.memset(scr["epsr"][:, :], EPS))
    return scr


def emit_ln(S, scr, r_tiles, onesc, onesr, bsum, bsumsq, out_f32, out_b16, ldq,
            tix, g_vec=None, b_vec=None):
    rb, sq, rows = scr["rb"], scr["sq"], scr["rows"]
    mrow = scr["mb"]
    rrow = scr["rs"]
    lrel = scr["rel"]
    PE, SC, VE = S.eng["tensor"], S.eng["scalar"], S.eng["vector"]
    for half in range(2):
        for sem, cnt in lrel.get(("rbsq", 1 - half), ()):
            S.wait("vector", sem, cnt)
            S.wait("scalar", sem, cnt)
        for c4 in range(4):
            c = half * 4 + c4
            S.op("vector", lambda v, _c=c, _c4=c4: v.tensor_copy(rb[_c4][:, :], r_tiles[_c][:, :]))
            S.op("scalar", lambda s, _c=c, _c4=c4: s.activation(sq[_c4][:, :], r_tiles[_c][:, :], AF.Square))
        S.wait("tensor", VE)
        S.wait("tensor", SC)
        for c4 in range(4):
            c = half * 4 + c4
            S.op("tensor", lambda t, _c4=c4, _f=(half == 0 and c4 == 0), _l=(half == 1 and c4 == 3): t.matmul(
                bsum[0:1, :], onesc[:, 0:1], rb[_c4][:, :], start=_f, stop=_l))
            S.op("tensor", lambda t, _c4=c4, _f=(half == 0 and c4 == 0), _l=(half == 1 and c4 == 3): t.matmul(
                bsumsq[0:1, :], onesc[:, 0:1], sq[_c4][:, :], start=_f, stop=_l))
        lrel[("rbsq", half)] = ((PE, PE.n),)
    mean = rows[0:1, 0:LT]
    msq = rows[0:1, LT:2 * LT]
    var = rows[0:1, 2 * LT:3 * LT]
    rstd = rows[0:1, 3 * LT:4 * LT]
    S.wait("vector", PE)
    S.op("vector", lambda v: v.tensor_scalar(mean, bsum[0:1, :], 1.0 / D, None, OP.mult))
    S.op("vector", lambda v: v.tensor_tensor(msq, mean, mean, OP.mult))
    S.op("vector", lambda v: v.scalar_tensor_tensor(
        var, bsumsq[0:1, :], 1.0 / D, msq, OP.mult, OP.subtract))
    S.wait("scalar", VE)
    S.op("scalar", lambda s: s.activation(var, var, AF.Ln, bias=scr["epsr"][0:1, 0:1], scale=1.0))
    S.op("scalar", lambda s: s.activation(rstd, var, AF.Exp, scale=-0.5))
    S.wait("vector", SC)
    S.op("vector", lambda v: v.tensor_copy(mrow[0:1, :], mean))
    S.op("vector", lambda v: v.tensor_copy(rrow[0:1, :], rstd))
    S.wait("tensor", VE)
    S.op("tensor", lambda t: t.matmul(bsumsq[:, :], onesr[0:1, :], mrow[0:1, :], start=True, stop=True))
    S.op("tensor", lambda t: t.matmul(bsum[:, :], onesr[0:1, :], rrow[0:1, :], start=True, stop=True))
    S.wait("vector", PE)
    for c in range(NC8):
        S.op("vector", lambda v, _c=c: v.tensor_tensor(out_f32[_c][:, :], r_tiles[_c][:, :], bsumsq[:, :], OP.subtract))
        if g_vec is not None:
            S.op("vector", lambda v, _c=c: v.tensor_tensor(out_f32[_c][:, :], out_f32[_c][:, :], bsum[:, :], OP.mult))
            S.op("vector", lambda v, _c=c: v.tensor_scalar(
                out_f32[_c][:, :], out_f32[_c][:, :], g_vec[:, _c:_c + 1], b_vec[:, _c:_c + 1], OP.mult, OP.add))
        else:
            S.op("vector", lambda v, _c=c: v.tensor_tensor(out_f32[_c][:, :], out_f32[_c][:, :], bsum[:, :], OP.mult))
        if out_b16 is not None:
            S.op("vector", lambda v, _c=c: v.tensor_copy(out_b16[_c][:, :], out_f32[_c][:, :]))



def phase0(nc, A):
    with ExitStack() as stack:
        S = Sched(nc, stack, "p0")
        gq = S.dmaq("gpsimd", "gq", 3)
        cq = S.dmaq("gpsimd", "cq", 1)
        cast = cq.wave()
        casts = [
            ("L0_attn_Wqkv_w", "wqkv_bf"), ("L0_attn_out_w", "outw_bf"),
            ("L0_ffn_out_W", "outW_bf"),
            ("L1_attn_Wk", "wk_bf"), ("L1_attn_Wv", "wv_bf"),
            ("L1_attn_Wr", "wr_bf"), ("L1_attn_Wo", "wo_bf"),
            ("L1_attn_Fk", "fk_bf"), ("L1_attn_Fr", "fr_bf"), ("L1_attn_Fv", "fv_bf"),
        ]
        for src, dst in casts:
            S.dma("gpsimd", cast, A[dst][:, :], A[src][:, :])
        S.dma("gpsimd", cast, A["expw0_bf"][:, :], A["L0_ffn_expert_W"][0, :, :])
        S.dma("gpsimd", cast, A["expw1_bf"][:, :], A["L0_ffn_expert_W"][1, :, :])
        CROSS["cast"] = (cast, cast.n)

        identf, idrdy = make_identity(S, stack, F32, "f")
        xin_s = [stack.enter_context(nc.sbuf_tensor(f"p0_xin{p}", [128, 1024], F32)) for p in range(2)]
        stg = [stack.enter_context(nc.sbuf_tensor(f"p0_stg{c}", [128, 512], F32)) for c in range(NC8)]
        stgb = [stack.enter_context(nc.sbuf_tensor(f"p0_stgb{c}", [128, 512], BF16)) for c in range(NC8)]
        pt = [stack.enter_context(nc.psum_tensor(f"p0_pt{i}", [128, 512], F32)) for i in range(2)]
        lq = S.dmaq("sync", "lq", 4)
        PE, SC, VE, PL = S.eng["tensor"], S.eng["scalar"], S.eng["vector"], S.eng["gpsimd"]
        S.wait("tensor", PL, idrdy)
        rel = {}

        def wait_rel(eng, key):
            for sem, cnt in rel.get(key, ()):
                S.wait(eng, sem, cnt)

        stw = None
        for grp in range(8):
            for j in range(4):
                i = grp * 4 + j
                xin = xin_s[i % 2]
                wait_rel("sync", ("xin", i % 2))
                w = lq.wave()
                S.dma("sync", w, xin[:, :], A["x_in"][i * 128:(i + 1) * 128, :])
                S.wait("tensor", w)
                if grp > 0 and j == 0:
                    S.wait("scalar", stw)
                    S.wait("scalar", VE)
                for c in range(NC8):
                    reg = ((c % 2) * 2 + (j % 2))
                    bank = pt[reg // 2]
                    r0 = (reg % 2) * 256
                    wait_rel("tensor", ("pt", reg))
                    S.wait("tensor", SC)
                    S.op("tensor", lambda t, _c=c, _b=bank, _r=r0, _x=xin: t.transpose(
                        _b[:, _r:_r + 128], _x[:, _c * 128:(_c + 1) * 128], identf[:]))
                    S.wait("scalar", PE)
                    S.op("scalar", lambda s, _c=c, _j=j, _b=bank, _r=r0: s.activation(
                        stg[_c][:, _j * 128:(_j + 1) * 128], _b[:, _r:_r + 128], AF.Copy))
                    rel[("pt", reg)] = ((SC, SC.n),)
                rel[("xin", i % 2)] = ((PE, PE.n),)
            S.wait("vector", SC)
            if stw is not None:
                S.wait("vector", stw)
            for c in range(NC8):
                S.op("vector", lambda v, _c=c: v.tensor_copy(stgb[_c][:, :], stg[_c][:, :]))
            S.wait("gpsimd", SC)
            S.wait("gpsimd", VE)
            stw = gq.wave()
            for c in range(NC8):
                S.dma("gpsimd", stw, A["x0"][c * 128:(c + 1) * 128, grp * 512:(grp + 1) * 512], stg[c][:, :])
                S.dma("gpsimd", stw, A["xb0"][c * 128:(c + 1) * 128, grp * 512:(grp + 1) * 512], stgb[c][:, :])
        for s_ in gq.sems:
            S.wait("gpsimd", s_)
        S.flush()



def phase1(nc, A):
    with ExitStack() as stack:
        S = Sched(nc, stack, "p1")
        wqkvT = stack.enter_context(nc.sbuf_tensor("p1_wqkvT", [128, NC8 * 3072], BF16))
        outwT = stack.enter_context(nc.sbuf_tensor("p1_outwT", [128, NC8 * 1024], BF16))
        xb = [stack.enter_context(nc.sbuf_tensor(f"p1_xb{c}", [128, LT], BF16)) for c in range(NC8)]
        x0t = [stack.enter_context(nc.sbuf_tensor(f"p1_x0t{c}", [128, LT], F32)) for c in range(NC8)]
        qT = stack.enter_context(nc.sbuf_tensor("p1_qT", [128, NC8 * LT], BF16))
        kT = stack.enter_context(nc.sbuf_tensor("p1_kT", [128, NC8 * LT], BF16))
        ktok = [stack.enter_context(nc.sbuf_tensor(f"p1_ktok{ts}", [128, 1024], BF16)) for ts in range(4)]
        vtok = [stack.enter_context(nc.sbuf_tensor(f"p1_vtok{ts}", [128, 1024], BF16)) for ts in range(4)]
        sT = [stack.enter_context(nc.sbuf_tensor(f"p1_sT{i}", [128, LT], BF16)) for i in range(4)]
        yT = [stack.enter_context(nc.sbuf_tensor(f"p1_yT{c}", [128, LT], BF16)) for c in range(NC8)]
        msb = stack.enter_context(nc.sbuf_tensor("p1_msb", [128, 512], BF16))
        dlt = [stack.enter_context(nc.sbuf_tensor(f"p1_dlt{c}", [128, LT], F32)) for c in range(NC8)]
        x1t = dlt
        xb1t = [stack.enter_context(nc.sbuf_tensor(f"p1_xb1t{c}", [128, LT], BF16)) for c in range(NC8)]
        trisrc = stack.enter_context(nc.sbuf_tensor("p1_trisrc", [128, 128], F32))
        tri = stack.enter_context(nc.sbuf_tensor("p1_tri", [128, 128], F32))
        onesc = stack.enter_context(nc.sbuf_tensor("p1_onesc", [128, 1], BF16))
        onesr = stack.enter_context(nc.sbuf_tensor("p1_onesr", [1, 128], BF16))
        ones_lt = stack.enter_context(nc.sbuf_tensor("p1_oneslt", [1, LT], BF16))
        lnscr = make_ln_scratch(S, stack)
        bqkv_row = stack.enter_context(nc.sbuf_tensor("p1_bqkvr", [1, 3072], BF16))
        bout_row = stack.enter_context(nc.sbuf_tensor("p1_boutr", [1, 1024], BF16))
        bk = [stack.enter_context(nc.psum_tensor(f"p1_bk{i}", [128, LT], F32)) for i in range(6)]
        py = stack.enter_context(nc.psum_tensor("p1_py", [128, LT], F32))
        pm = stack.enter_context(nc.psum_tensor("p1_pm", [128, 512], F32))

        lq = S.dmaq("sync", "lq", 4)
        gq = S.dmaq("gpsimd", "gq", 3)
        PE, SC, VE, PL = S.eng["tensor"], S.eng["scalar"], S.eng["vector"], S.eng["gpsimd"]
        rel = {}

        def wait_rel(eng, key):
            for sem, cnt in rel.get(key, ()):
                S.wait(eng, sem, cnt)

        ldp = gq.wave()
        S.dma("gpsimd", ldp, bqkv_row[0:1, :], A["L0_attn_Wqkv_b"][0:1, :])
        S.dma("gpsimd", ldp, bout_row[0:1, :], A["L0_attn_out_b"][0:1, :])
        if "cast" in CROSS:
            S.wait("sync", CROSS["cast"][0], CROSS["cast"][1])
        wld = lq.wave()
        for c in range(NC8):
            S.dma("sync", wld, wqkvT[:, c * 3072:(c + 1) * 3072], A["wqkv_bf"][:, c * 128:(c + 1) * 128], transpose=True)
            S.dma("sync", wld, outwT[:, c * 1024:(c + 1) * 1024], A["outw_bf"][:, c * 128:(c + 1) * 128], transpose=True)
        S.op("gpsimd", lambda g: g.memset(trisrc[:], 1.0))
        S.op("gpsimd", lambda g: g.affine_select(
            tri[:], trisrc[:], pattern=[[1, 128]], compare_op=OP.is_ge, fill=0.0,
            base=0, channel_multiplier=-1))
        S.op("vector", lambda v: v.memset(onesc[:], 1.0))
        S.op("vector", lambda v: v.memset(onesr[:], 1.0))
        S.op("vector", lambda v: v.memset(ones_lt[:], 1.0))
        S.op("vector", lambda v: v.memset(msb[:], 0.0))

        for lt in range(NLT):
            tw = lq.wave()
            wait_rel("sync", "xtiles")
            for c in range(NC8):
                S.dma("sync", tw, xb[c][:, :], A["xb0"][c * 128:(c + 1) * 128, lt * LT:(lt + 1) * LT])
                S.dma("sync", tw, x0t[c][:, :], A["x0"][c * 128:(c + 1) * 128, lt * LT:(lt + 1) * LT])
            S.wait("tensor", tw)
            if lt == 0:
                S.wait("tensor", wld)
                S.wait("tensor", ldp)
                S.wait("tensor", PL)
                S.wait("tensor", VE)
            for oc in range(16):
                bank = bk[oc % 3]
                wait_rel("tensor", ("bk", oc % 3))
                for kc in range(NC8):
                    S.op("tensor", lambda t, _oc=oc, _kc=kc, _b=bank: t.matmul(
                        _b[:, :], wqkvT[:, _kc * 3072 + _oc * 128:_kc * 3072 + (_oc + 1) * 128],
                        xb[_kc][:, :], start=(_kc == 0), stop=False))
                S.op("tensor", lambda t, _oc=oc, _b=bank: t.matmul(
                    _b[:, :], bqkv_row[0:1, _oc * 128:(_oc + 1) * 128], ones_lt[0:1, :],
                    start=False, stop=True))
                dst, cc, scl = (qT, oc, 1.0) if oc < 8 else (kT, oc - 8, 0.125)
                S.wait("scalar", PE)
                wait_rel("scalar", ("qkT", 0))
                S.op("scalar", lambda s, _b=bank, _d=dst, _cc=cc, _sc=scl: s.activation(
                    _d[:, _cc * LT:(_cc + 1) * LT], _b[:, :], AF.Identity, scale=_sc))
                rel[("bk", oc % 3)] = ((SC, SC.n),)
            for ts in range(4):
                for part in range(2):
                    for half in range(2):
                        i = (ts * 4 + part * 2 + half)
                        bank = bk[i % 3]
                        dout0 = 1024 + part * 1024 + half * 512
                        wait_rel("tensor", ("bk", i % 3))
                        for kc in range(NC8):
                            S.op("tensor", lambda t, _ts=ts, _kc=kc, _b=bank, _d0=dout0: t.matmul(
                                _b[:, :], xb[_kc][:, _ts * 128:(_ts + 1) * 128],
                                wqkvT[:, _kc * 3072 + _d0:_kc * 3072 + _d0 + 512],
                                start=(_kc == 0), stop=False))
                        S.op("tensor", lambda t, _b=bank, _d0=dout0: t.matmul(
                            _b[:, :], onesr[0:1, 0:128], bqkv_row[0:1, _d0:_d0 + 512],
                            start=False, stop=True))
                        dstt = ktok if part == 0 else vtok
                        scl = 0.125 if part == 0 else 1.0
                        S.wait("scalar", PE)
                        wait_rel("scalar", ("tok", part, ts))
                        S.op("scalar", lambda s, _b=bank, _d=dstt, _ts=ts, _h=half, _sc=scl: s.activation(
                            _d[_ts][:, _h * 512:(_h + 1) * 512], _b[:, :], AF.Identity, scale=_sc))
                        rel[("bk", i % 3)] = ((SC, SC.n),)
            rel["xtiles"] = ((PE, PE.n), (VE, VE.n))
            if lt > 0:
                S.wait("scalar", PE)
                S.op("scalar", lambda s: s.activation(msb[:, :], pm[:, :], AF.Copy))
            msb_ready = (SC, SC.n)
            for h in range(H):
                hc = h // 2
                hb = (h % 2) * 64
                st_ready = {}
                for ms in range(4):
                    n = LT - ms * 128
                    bi = (h * 4 + ms) % 6
                    bank = bk[bi]
                    wait_rel("tensor", ("bk", bi))
                    S.op("tensor", lambda t, _hc=hc, _hb=hb, _ms=ms, _n=n, _b=bank: t.matmul(
                        _b[:, 0:_n],
                        kT[_hb:_hb + 64, _hc * LT + _ms * 128:_hc * LT + (_ms + 1) * 128],
                        qT[_hb:_hb + 64, _hc * LT + _ms * 128:_hc * LT + _ms * 128 + _n],
                        start=True, stop=True))
                    S.wait("vector", PE)
                    S.op("vector", lambda v, _ms=ms, _b=bank: v.tensor_tensor(
                        sT[_ms][:, 0:128], _b[:, 0:128], tri[:, :], OP.mult))
                    if n > 128:
                        S.wait("scalar", PE)
                        S.wait("scalar", VE)
                        S.op("scalar", lambda s, _ms=ms, _n=n, _b=bank: s.activation(
                            sT[_ms][:, 128:_n], _b[:, 128:_n], AF.Copy))
                        st_ready[ms] = ((VE, VE.n), (SC, SC.n))
                        rel[("bk", bi)] = ((VE, VE.n), (SC, SC.n))
                    else:
                        st_ready[ms] = ((VE, VE.n),)
                        rel[("bk", bi)] = ((VE, VE.n),)
                wait_rel("tensor", "py")
                S.wait("tensor", msb_ready[0], msb_ready[1])
                S.op("tensor", lambda t, _hb=hb, _hc=hc: t.matmul(
                    py[_hb:_hb + 64, :], msb[_hb:_hb + 64, _hc * 64:_hc * 64 + 64],
                    qT[_hb:_hb + 64, _hc * LT:(_hc + 1) * LT], start=True, stop=False))
                for ms in range(4):
                    n = LT - ms * 128
                    for sem, cnt in st_ready[ms]:
                        S.wait("tensor", sem, cnt)
                    S.op("tensor", lambda t, _hb=hb, _h=h, _ms=ms, _n=n: t.matmul(
                        py[_hb:_hb + 64, _ms * 128:_ms * 128 + _n],
                        vtok[_ms][:, _h * 64:(_h + 1) * 64],
                        sT[_ms][:, 0:_n], start=False, stop=(_ms == 3)))
                if h % 2 == 1:
                    S.wait("scalar", PE)
                    wait_rel("scalar", ("yT", hc))
                    S.op("scalar", lambda s, _hc=hc: s.activation(
                        yT[_hc][:, :], py[:, :], AF.Copy))
                    rel["py"] = ((SC, SC.n),)
            for h in range(H):
                hc = h // 2
                hb = (h % 2) * 64
                for ts in range(4):
                    first = (lt == 0 and h in (0, 1) and ts == 0)
                    S.op("tensor", lambda t, _h=h, _hb=hb, _hc=hc, _ts=ts, _f=first: t.matmul(
                        pm[_hb:_hb + 64, _hc * 64:_hc * 64 + 64],
                        ktok[_ts][:, _h * 64:(_h + 1) * 64],
                        vtok[_ts][:, _h * 64:(_h + 1) * 64],
                        start=_f, stop=False, skip_group_check=True))
            for part in range(2):
                for ts in range(4):
                    rel[("tok", part, ts)] = ((PE, PE.n),)
            S.wait("tensor", SC)
            for oc in range(NC8):
                bank = bk[3 + oc % 2]
                wait_rel("tensor", ("bk", 3 + oc % 2))
                for kc in range(NC8):
                    S.op("tensor", lambda t, _oc=oc, _kc=kc, _b=bank: t.matmul(
                        _b[:, :], outwT[:, _kc * 1024 + _oc * 128:_kc * 1024 + (_oc + 1) * 128],
                        yT[_kc][:, :], start=(_kc == 0), stop=False))
                S.op("tensor", lambda t, _oc=oc, _b=bank: t.matmul(
                    _b[:, :], bout_row[0:1, _oc * 128:(_oc + 1) * 128], ones_lt[0:1, :],
                    start=False, stop=True))
                rel[("yT", oc)] = ((PE, PE.n),)
                S.wait("scalar", PE)
                wait_rel("scalar", "stores")
                wait_rel("scalar", ("dlt", oc))
                S.op("scalar", lambda s, _oc=oc, _b=bank: s.activation(
                    dlt[_oc][:, :], _b[:, :], AF.Copy))
                rel[("bk", 3 + oc % 2)] = ((SC, SC.n),)
            S.wait("vector", SC)
            for c in range(NC8):
                S.op("vector", lambda v, _c=c: v.tensor_tensor(
                    dlt[_c][:, :], dlt[_c][:, :], x0t[_c][:, :], OP.add))
            rel["xtiles"] = ((PE, rel["xtiles"][0][1]), (VE, VE.n))
            S.wait("scalar", VE)
            emit_ln(S, lnscr, dlt, onesc, onesr, bk[3], bk[4], x1t, xb1t, lq, lt)
            rel[("bk", 3)] = ((VE, VE.n),)
            rel[("bk", 4)] = ((VE, VE.n),)
            S.wait("gpsimd", VE)
            sw = gq.wave()
            for c in range(NC8):
                S.dma("gpsimd", sw, A["x1"][c * 128:(c + 1) * 128, lt * LT:(lt + 1) * LT], x1t[c][:, :])
                S.dma("gpsimd", sw, A["xb1"][c * 128:(c + 1) * 128, lt * LT:(lt + 1) * LT], xb1t[c][:, :])
            rel["stores"] = ((sw, sw.n),)
            for c in range(NC8):
                rel[("dlt", c)] = ((sw, sw.n), (VE, VE.n))
            rel[("qkT", 0)] = ((PE, PE.n),)
        for s_ in gq.sems:
            S.wait("gpsimd", s_)
        S.flush()



def phase2(nc, A):
    with ExitStack() as stack:
        S = Sched(nc, stack, "p2")
        w0T = stack.enter_context(nc.sbuf_tensor("p2_w0T", [128, NC8 * 1024], BF16))
        w1T = stack.enter_context(nc.sbuf_tensor("p2_w1T", [128, NC8 * 1024], BF16))
        outWT = stack.enter_context(nc.sbuf_tensor("p2_outWT", [128, NC8 * 1024], BF16))
        gw = stack.enter_context(nc.sbuf_tensor("p2_gw", [8, 1024], F32))
        gwT = stack.enter_context(nc.sbuf_tensor("p2_gwT", [128, NC8 * 8], F32))
        xbs = [[stack.enter_context(nc.sbuf_tensor(f"p2_xb{p}_{c}", [128, LT], BF16)) for c in range(NC8)] for p in range(2)]
        x1fs = [[stack.enter_context(nc.sbuf_tensor(f"p2_x1f{p}_{c}", [128, LT], F32)) for c in range(NC8)] for p in range(2)]
        glog_s = stack.enter_context(nc.sbuf_tensor("p2_glog", [8, LT], F32))
        ptok_s = stack.enter_context(nc.sbuf_tensor("p2_ptok", [128, 32], F32))
        e8 = stack.enter_context(nc.sbuf_tensor("p2_e8", [128, 8], F32))
        p8_ = stack.enter_context(nc.sbuf_tensor("p2_p8", [128, 8], F32))
        g8 = stack.enter_context(nc.sbuf_tensor("p2_g8", [128, 8], F32))
        gs = stack.enter_context(nc.sbuf_tensor("p2_gs", [128, 8], F32))
        s01 = stack.enter_context(nc.sbuf_tensor("p2_s01", [128, 8], F32))
        st2row = stack.enter_context(nc.sbuf_tensor("p2_st2r", [2, LT], BF16))
        sel0 = stack.enter_context(nc.sbuf_tensor("p2_sel0", [2, 128], BF16))
        sel1 = stack.enter_context(nc.sbuf_tensor("p2_sel1", [2, 128], BF16))
        ones2 = stack.enter_context(nc.sbuf_tensor("p2_ones2", [2, 128], BF16))
        h0s = [stack.enter_context(nc.sbuf_tensor(f"p2_h0s{c}", [128, LT], F32)) for c in range(NC8)]
        h1s = [stack.enter_context(nc.sbuf_tensor(f"p2_h1s{c}", [128, LT], F32)) for c in range(NC8)]
        f0 = stack.enter_context(nc.sbuf_tensor("p2_f0", [128, LT], F32))
        finT = [stack.enter_context(nc.sbuf_tensor(f"p2_fin{c}", [128, LT], BF16)) for c in range(NC8)]
        dlt = [stack.enter_context(nc.sbuf_tensor(f"p2_dlt{c}", [128, LT], F32)) for c in range(NC8)]
        x2t = dlt
        xb2t = [stack.enter_context(nc.sbuf_tensor(f"p2_xb2t{c}", [128, LT], BF16)) for c in range(NC8)]
        onesc = stack.enter_context(nc.sbuf_tensor("p2_onesc", [128, 1], BF16))
        onesr = stack.enter_context(nc.sbuf_tensor("p2_onesr", [1, 128], BF16))
        lnscr = make_ln_scratch(S, stack)
        identf, idrdyp = make_identity(S, stack, F32, "f")

        psc = [stack.enter_context(nc.psum_tensor(f"p2_psc{i}", [128, LT], F32)) for i in range(2)]
        pbc = [stack.enter_context(nc.psum_tensor(f"p2_pbc{i}", [128, LT], F32)) for i in range(2)]
        pst = [stack.enter_context(nc.psum_tensor(f"p2_pst{i}", [128, LT], F32)) for i in range(3)]
        pl = stack.enter_context(nc.psum_tensor("p2_pl", [128, LT], F32))

        lq = S.dmaq("sync", "lq", 4)
        gq = S.dmaq("gpsimd", "gq", 3)
        PE, SC, VE, PL = S.eng["tensor"], S.eng["scalar"], S.eng["vector"], S.eng["gpsimd"]
        rel = {}

        def wait_rel(eng, key):
            for sem, cnt in rel.get(key, ()):
                S.wait(eng, sem, cnt)

        s01_rel = 0
        if "cast" in CROSS:
            S.wait("sync", CROSS["cast"][0], CROSS["cast"][1])
        wld = lq.wave()
        for c in range(NC8):
            S.dma("sync", wld, w0T[:, c * 1024:(c + 1) * 1024], A["expw0_bf"][:, c * 128:(c + 1) * 128], transpose=True)
            S.dma("sync", wld, w1T[:, c * 1024:(c + 1) * 1024], A["expw1_bf"][:, c * 128:(c + 1) * 128], transpose=True)
            S.dma("sync", wld, outWT[:, c * 1024:(c + 1) * 1024], A["outW_bf"][:, c * 128:(c + 1) * 128], transpose=True)
        S.dma("sync", wld, gw[:, :], A["L0_ffn_gate_W"][:, :])
        S.wait("tensor", wld)
        S.wait("tensor", PL, idrdyp)
        for c in range(NC8):
            S.op("tensor", lambda t, _c=c: t.transpose(
                pl[:, _c * 8:(_c + 1) * 8], gw[0:8, _c * 128:(_c + 1) * 128], identf[0:8, 0:8]))
        S.wait("scalar", PE)
        S.op("scalar", lambda s: s.activation(gwT[:, 0:NC8 * 8], pl[:, 0:NC8 * 8], AF.Copy))
        S.op("gpsimd", lambda g: g.memset(ones2[:, :], 1.0))
        S.op("gpsimd", lambda g: g.affine_select(
            sel0[:, :], ones2[:, :], pattern=[[0, 128]], compare_op=OP.is_equal, fill=0.0,
            base=0, channel_multiplier=1))
        S.op("gpsimd", lambda g: g.affine_select(
            sel1[:, :], ones2[:, :], pattern=[[0, 128]], compare_op=OP.is_equal, fill=0.0,
            base=-1, channel_multiplier=1))
        S.op("vector", lambda v: v.memset(onesc[:], 1.0))
        S.op("vector", lambda v: v.memset(onesr[:], 1.0))

        twv = {}

        def issue_loads2(lt):
            w = lq.wave()
            twv[lt] = w
            wait_rel("sync", ("xtiles", lt % 2))
            for c in range(NC8):
                S.dma("sync", w, xbs[lt % 2][c][:, :], A["xb1"][c * 128:(c + 1) * 128, lt * LT:(lt + 1) * LT])
                S.dma("sync", w, x1fs[lt % 2][c][:, :], A["x1"][c * 128:(c + 1) * 128, lt * LT:(lt + 1) * LT])

        issue_loads2(0)
        for lt in range(NLT):
            if lt + 1 < NLT:
                issue_loads2(lt + 1)
            xb = xbs[lt % 2]
            x1f = x1fs[lt % 2]
            S.wait("tensor", twv[lt])
            if lt == 0:
                S.wait("tensor", SC)
                S.wait("tensor", PL)
                S.wait("tensor", VE)
            wait_rel("tensor", "pst0")
            for kc in range(NC8):
                S.op("tensor", lambda t, _kc=kc, _x=x1f: t.matmul(
                    pst[0][0:8, :], gwT[:, _kc * 8:(_kc + 1) * 8], _x[_kc][:, :],
                    start=(_kc == 0), stop=(_kc == NC8 - 1)))
            S.wait("scalar", PE)
            wait_rel("scalar", "glog")
            S.op("scalar", lambda s: s.activation(glog_s[:, :], pst[0][0:8, :], AF.Copy))
            S.wait("tensor", SC)
            wait_rel("tensor", "pst1")
            for j in range(4):
                S.op("tensor", lambda t, _j=j: t.transpose(
                    pst[1][:, _j * 8:(_j + 1) * 8], glog_s[0:8, _j * 128:(_j + 1) * 128], identf[0:8, 0:8]))
            S.wait("scalar", PE)
            wait_rel("scalar", "ptok")
            S.op("scalar", lambda s: s.activation(ptok_s[:, 0:32], pst[1][:, 0:32], AF.Copy))
            rel["glog"] = ((PE, PE.n),)
            rel["pst0"] = ((SC, SC.n),)
            rel["pst1"] = ((SC, SC.n),)
            for oc in range(NC8):
                pair = (psc[0], psc[1]) if oc % 2 == 0 else (pst[0], pst[1])
                key = ("xp", oc % 2)
                wait_rel("tensor", key)
                if oc % 2 == 1:
                    wait_rel("tensor", "pst0")
                    wait_rel("tensor", "pst1")
                for kc in range(NC8):
                    S.op("tensor", lambda t, _oc=oc, _kc=kc, _b=pair[0], _xb=xb: t.matmul(
                        _b[:, :], w0T[:, _kc * 1024 + _oc * 128:_kc * 1024 + (_oc + 1) * 128],
                        _xb[_kc][:, :], start=(_kc == 0), stop=(_kc == NC8 - 1)))
                for kc in range(NC8):
                    S.op("tensor", lambda t, _oc=oc, _kc=kc, _b=pair[1], _xb=xb: t.matmul(
                        _b[:, :], w1T[:, _kc * 1024 + _oc * 128:_kc * 1024 + (_oc + 1) * 128],
                        _xb[_kc][:, :], start=(_kc == 0), stop=(_kc == NC8 - 1)))
                S.wait("scalar", PE)
                wait_rel("scalar", ("hs", oc))
                S.op("scalar", lambda s, _oc=oc, _b=pair[0]: s.activation(h0s[_oc][:, :], _b[:, :], AF.Copy))
                S.op("scalar", lambda s, _oc=oc, _b=pair[1]: s.activation(h1s[_oc][:, :], _b[:, :], AF.Copy))
                rel[key] = ((SC, SC.n),)
                if oc % 2 == 1:
                    rel["pst0"] = ((SC, SC.n),)
                    rel["pst1"] = ((SC, SC.n),)
            for j in range(4):
                L = ptok_s[:, j * 8:(j + 1) * 8]
                S.wait("vector", SC)
                S.wait("vector", PE, s01_rel)
                S.op("vector", lambda v, _L=L: v.reduce_max(gs[:, 0:1], _L, axis=mybir.AxisListType.X))
                S.op("vector", lambda v: v.tensor_scalar(gs[:, 1:2], gs[:, 0:1], -1.0, None, OP.mult))
                S.wait("scalar", VE)
                S.op("scalar", lambda s, _L=L: s.activation(e8[:, :], _L, AF.Exp, bias=gs[:, 1:2]))
                S.wait("vector", SC)
                S.op("vector", lambda v: v.reduce_sum(gs[:, 2:3], e8[:, :], axis=mybir.AxisListType.X))
                S.op("vector", lambda v: v.reciprocal(gs[:, 3:4], gs[:, 2:3]))
                S.op("vector", lambda v: v.tensor_scalar(p8_[:, :], e8[:, :], gs[:, 3:4], None, OP.mult))
                S.op("vector", lambda v: v.tensor_scalar(g8[:, :], p8_[:, :], 0.3, None, OP.is_gt))
                S.op("vector", lambda v: v.reduce_sum(gs[:, 4:5], g8[:, :], axis=mybir.AxisListType.X))
                S.op("vector", lambda v: v.tensor_scalar(gs[:, 5:6], gs[:, 4:5], -1.0, 8.0, OP.mult, OP.add))
                S.op("vector", lambda v, _j=j: v.tensor_tensor(s01[:, 2 * _j:2 * _j + 1], gs[:, 5:6], p8_[:, 0:1], OP.mult))
                S.op("vector", lambda v, _j=j: v.tensor_tensor(s01[:, 2 * _j + 1:2 * _j + 2], gs[:, 4:5], p8_[:, 1:2], OP.mult))
            S.wait("tensor", VE)
            wait_rel("tensor", "pst2")
            for j in range(4):
                S.op("tensor", lambda t, _j=j: t.transpose(
                    pst[2][0:2, _j * 128:(_j + 1) * 128], s01[:, 2 * _j:2 * _j + 2], identf[:, :]))
            s01_rel = PE.n
            rel["ptok"] = ((VE, VE.n),)
            S.wait("scalar", PE)
            wait_rel("scalar", "st2row")
            S.op("scalar", lambda s: s.activation(st2row[:, :], pst[2][0:2, :], AF.Copy))
            S.wait("tensor", SC)
            wait_rel("tensor", "pbc")
            S.op("tensor", lambda t: t.matmul(pbc[0][:, :], sel0[:, :], st2row[:, :], start=True, stop=True))
            S.op("tensor", lambda t: t.matmul(pbc[1][:, :], sel1[:, :], st2row[:, :], start=True, stop=True))
            rel["st2row"] = ((PE, PE.n),)
            rel["pst2"] = ((PE, PE.n),)
            bc_done = PE.n
            S.wait("vector", PE, bc_done)
            S.wait("vector", SC)
            for oc in range(NC8):
                S.op("vector", lambda v, _oc=oc: v.tensor_tensor(f0[:, :], h0s[_oc][:, :], pbc[0][:, :], OP.mult))
                S.op("vector", lambda v, _oc=oc: v.tensor_tensor(finT[_oc][:, :], h1s[_oc][:, :], pbc[1][:, :], OP.mult))
                S.op("vector", lambda v, _oc=oc: v.tensor_tensor(finT[_oc][:, :], finT[_oc][:, :], f0[:, :], OP.add))
                rel[("hs", oc)] = ((VE, VE.n),)
            rel["pbc"] = ((VE, VE.n),)
            S.wait("tensor", VE)
            for oc in range(NC8):
                bank = psc[oc % 2]
                wait_rel("tensor", ("xp", 0))
                for kc in range(NC8):
                    S.op("tensor", lambda t, _oc=oc, _kc=kc, _b=bank: t.matmul(
                        _b[:, :], outWT[:, _kc * 1024 + _oc * 128:_kc * 1024 + (_oc + 1) * 128],
                        finT[_kc][:, :], start=(_kc == 0), stop=(_kc == NC8 - 1)))
                S.wait("scalar", PE)
                wait_rel("scalar", ("dlt", oc))
                S.op("scalar", lambda s, _oc=oc, _b=bank: s.activation(
                    dlt[_oc][:, :], _b[:, :], AF.Copy))
                rel[("xp", 0)] = ((SC, SC.n),)
            S.wait("vector", SC)
            for c in range(NC8):
                S.op("vector", lambda v, _c=c, _x=x1f: v.tensor_tensor(
                    dlt[_c][:, :], dlt[_c][:, :], _x[_c][:, :], OP.add))
            rel[("xtiles", lt % 2)] = ((PE, PE.n), (VE, VE.n))
            S.wait("scalar", VE)
            emit_ln(S, lnscr, dlt, onesc, onesr, pl, pbc[0], x2t, xb2t, lq, lt)
            rel["pbc"] = ((VE, VE.n),)
            S.wait("gpsimd", VE)
            sw = gq.wave()
            for c in range(NC8):
                S.dma("gpsimd", sw, A["x2"][c * 128:(c + 1) * 128, lt * LT:(lt + 1) * LT], x2t[c][:, :])
                S.dma("gpsimd", sw, A["xb2"][c * 128:(c + 1) * 128, lt * LT:(lt + 1) * LT], xb2t[c][:, :])
            for c in range(NC8):
                rel[("dlt", c)] = ((sw, sw.n), (VE, VE.n))
            rel["pst0"] = ((PE, PE.n),)
        for s_ in gq.sems:
            S.wait("gpsimd", s_)
        S.flush()



def phase3(nc, A):
    with ExitStack() as stack:
        S = Sched(nc, stack, "p3")
        wkT = stack.enter_context(nc.sbuf_tensor("p3_wkT", [128, NC8 * 1024], BF16))
        wvT = stack.enter_context(nc.sbuf_tensor("p3_wvT", [128, NC8 * 1024], BF16))
        x2fs = [[stack.enter_context(nc.sbuf_tensor(f"p3_x2f{p}_{c}", [128, LT], F32)) for c in range(NC8)] for p in range(2)]
        hbuf = [stack.enter_context(nc.sbuf_tensor(f"p3_h{c}", [128, LT + 1], F32)) for c in range(NC8)]
        dscr = stack.enter_context(nc.sbuf_tensor("p3_dscr", [128, LT], F32))
        xkb = [stack.enter_context(nc.sbuf_tensor(f"p3_xkb{c}", [128, LT], BF16)) for c in range(NC8)]
        xvb = [stack.enter_context(nc.sbuf_tensor(f"p3_xvb{c}", [128, LT], BF16)) for c in range(NC8)]
        xrb = [stack.enter_context(nc.sbuf_tensor(f"p3_xrb{c}", [128, LT], BF16)) for c in range(NC8)]
        kst = [stack.enter_context(nc.sbuf_tensor(f"p3_kst{c}", [128, LT], F32)) for c in range(NC8)]
        vst = [stack.enter_context(nc.sbuf_tensor(f"p3_vst{c}", [128, LT], F32)) for c in range(NC8)]
        onesc = stack.enter_context(nc.sbuf_tensor("p3_onesc", [128, 1], BF16))
        onesr = stack.enter_context(nc.sbuf_tensor("p3_onesr", [1, 128], BF16))
        lnscr = make_ln_scratch(S, stack)
        identf, idrdy = make_identity(S, stack, F32, "f")
        psc = [stack.enter_context(nc.psum_tensor(f"p3_psc{i}", [128, LT], F32)) for i in range(6)]
        pl = [stack.enter_context(nc.psum_tensor(f"p3_pl{i}", [128, LT], F32)) for i in range(2)]

        lq = S.dmaq("sync", "lq", 4)
        gq = S.dmaq("gpsimd", "gq", 3)
        PE, SC, VE, PL = S.eng["tensor"], S.eng["scalar"], S.eng["vector"], S.eng["gpsimd"]
        rel = {}

        def wait_rel(eng, key):
            for sem, cnt in rel.get(key, ()):
                S.wait(eng, sem, cnt)

        if "cast" in CROSS:
            S.wait("sync", CROSS["cast"][0], CROSS["cast"][1])
        wld = lq.wave()
        for c in range(NC8):
            S.dma("sync", wld, wkT[:, c * 1024:(c + 1) * 1024], A["wk_bf"][:, c * 128:(c + 1) * 128], transpose=True)
            S.dma("sync", wld, wvT[:, c * 1024:(c + 1) * 1024], A["wv_bf"][:, c * 128:(c + 1) * 128], transpose=True)
        vecs = load_vecs_fm(S, stack, [
            ("l1g", A["L1_attn_ln1_g"], D), ("l1b", A["L1_attn_ln1_b"], D),
            ("tmk", A["L1_attn_tm_k"], D), ("tmv", A["L1_attn_tm_v"], D),
            ("tmr", A["L1_attn_tm_r"], D)], identf, idrdy, pl[0], lq)
        ln1g, ln1b = vecs["l1g"], vecs["l1b"]
        tmk, tmv, tmr = vecs["tmk"], vecs["tmv"], vecs["tmr"]
        S.op("vector", lambda v: v.memset(onesc[:], 1.0))
        S.op("vector", lambda v: v.memset(onesr[:], 1.0))
        for c in range(NC8):
            S.op("vector", lambda v, _c=c: v.memset(hbuf[_c][:, 0:1], 0.0))

        twv = {}

        def issue_loads3(lt):
            w = lq.wave()
            twv[lt] = w
            wait_rel("sync", ("x2f", lt % 2))
            for c in range(NC8):
                S.dma("sync", w, x2fs[lt % 2][c][:, :], A["x2"][c * 128:(c + 1) * 128, lt * LT:(lt + 1) * LT])

        issue_loads3(0)
        for lt in range(NLT):
            if lt + 1 < NLT:
                issue_loads3(lt + 1)
            x2f = x2fs[lt % 2]
            tw = twv[lt]
            S.wait("tensor", tw)
            S.wait("scalar", tw)
            S.wait("vector", tw)
            if lt == 0:
                S.wait("tensor", wld)
                S.wait("tensor", SC)
            if lt > 0:
                for c in range(NC8):
                    S.op("vector", lambda v, _c=c: v.tensor_copy(hbuf[_c][:, 0:1], hbuf[_c][:, LT:LT + 1]))
            emit_ln(S, lnscr, x2f, onesc, onesr, pl[0], pl[1],
                    [hbuf[c][:, 1:LT + 1] for c in range(NC8)], None, lq, lt, g_vec=ln1g, b_vec=ln1b)
            rel[("x2f", lt % 2)] = ((VE, VE.n),)
            wait_rel("vector", "mixes")
            wait_rel("vector", ("st", 0))
            for c in range(NC8):
                h_ = hbuf[c][:, 1:LT + 1]
                hh = hbuf[c][:, 0:LT]
                S.op("vector", lambda v, _h=h_, _hh=hh: v.tensor_tensor(dscr[:, :], _h, _hh, OP.subtract))
                S.op("vector", lambda v, _c=c, _hh=hh: v.scalar_tensor_tensor(
                    xkb[_c][:, :], dscr[:, :], tmk[:, _c:_c + 1], _hh, OP.mult, OP.add))
                S.op("vector", lambda v, _c=c, _hh=hh: v.scalar_tensor_tensor(
                    xvb[_c][:, :], dscr[:, :], tmv[:, _c:_c + 1], _hh, OP.mult, OP.add))
                S.op("vector", lambda v, _c=c, _hh=hh: v.scalar_tensor_tensor(
                    xrb[_c][:, :], dscr[:, :], tmr[:, _c:_c + 1], _hh, OP.mult, OP.add))
            S.wait("tensor", VE)
            for oc in range(NC8):
                bk = psc[0 + oc % 2]
                bv = psc[2 + oc % 2]
                for i in (0 + oc % 2, 2 + oc % 2):
                    wait_rel("tensor", ("psc", i))
                for kc in range(NC8):
                    S.op("tensor", lambda t, _oc=oc, _kc=kc, _b=bk: t.matmul(
                        _b[:, :], wkT[:, _kc * 1024 + _oc * 128:_kc * 1024 + (_oc + 1) * 128],
                        xkb[_kc][:, :], start=(_kc == 0), stop=(_kc == NC8 - 1)))
                for kc in range(NC8):
                    S.op("tensor", lambda t, _oc=oc, _kc=kc, _b=bv: t.matmul(
                        _b[:, :], wvT[:, _kc * 1024 + _oc * 128:_kc * 1024 + (_oc + 1) * 128],
                        xvb[_kc][:, :], start=(_kc == 0), stop=(_kc == NC8 - 1)))
                S.wait("scalar", PE)
                wait_rel("scalar", ("st", oc))
                S.op("scalar", lambda s, _oc=oc, _b=bk: s.activation(kst[_oc][:, :], _b[:, :], AF.Copy))
                rel[("psc", 0 + oc % 2)] = ((SC, SC.n),)
                S.op("scalar", lambda s, _oc=oc, _b=bv: s.activation(vst[_oc][:, :], _b[:, :], AF.Copy))
                rel[("psc", 2 + oc % 2)] = ((SC, SC.n),)
            rel["mixes"] = ((PE, PE.n), (VE, VE.n))
            S.wait("gpsimd", SC)
            S.wait("gpsimd", VE)
            sw = gq.wave()
            for c in range(NC8):
                S.dma("gpsimd", sw, A["kTf"][c * 128:(c + 1) * 128, lt * LT:(lt + 1) * LT], kst[c][:, :])
                S.dma("gpsimd", sw, A["vTf"][c * 128:(c + 1) * 128, lt * LT:(lt + 1) * LT], vst[c][:, :])
                S.dma("gpsimd", sw, A["xrT"][c * 128:(c + 1) * 128, lt * LT:(lt + 1) * LT], xrb[c][:, :])
            for c in range(NC8):
                rel[("st", c)] = ((sw, sw.n),)
        for s_ in gq.sems:
            S.wait("gpsimd", s_)
        S.flush()


H2 = T // 2


def phase4(nc, A):
    with ExitStack() as stack:
        S = Sched(nc, stack, "p4")
        woT = stack.enter_context(nc.sbuf_tensor("p4_woT", [128, NC8 * 1024], BF16))
        wrT = stack.enter_context(nc.sbuf_tensor("p4_wrT", [128, NC8 * 1024], BF16))
        yT = [stack.enter_context(nc.sbuf_tensor(f"p4_y{c}", [128, T], BF16)) for c in range(NC8)]
        kbb = [stack.enter_context(nc.sbuf_tensor(f"p4_kb{p}", [128, H2], F32)) for p in range(2)]
        vb = [stack.enter_context(nc.sbuf_tensor(f"p4_vb{p}", [128, H2], F32)) for p in range(2)]
        ekb = stack.enter_context(nc.sbuf_tensor("p4_ekb", [128, H2], F32))
        ekvb = stack.enter_context(nc.sbuf_tensor("p4_ekvb", [128, H2], F32))
        lnsc = stack.enter_context(nc.sbuf_tensor("p4_lnsc", [128, H2], F32))
        Nb = stack.enter_context(nc.sbuf_tensor("p4_Nb", [128, H2 + 1], F32))
        Db = stack.enter_context(nc.sbuf_tensor("p4_Db", [128, H2 + 1], F32))
        xrb = [stack.enter_context(nc.sbuf_tensor(f"p4_xrb{c}", [128, LT], BF16)) for c in range(NC8)]
        srt = [stack.enter_context(nc.sbuf_tensor(f"p4_srt{c}", [128, LT], BF16)) for c in range(NC8)]
        srl = xrb
        rwk = srt
        x2f = [stack.enter_context(nc.sbuf_tensor(f"p4_x2f{c}", [128, LT], F32)) for c in range(NC8)]
        ew = stack.enter_context(nc.sbuf_tensor("p4_ew", [128, NC8], F32))
        eu = stack.enter_context(nc.sbuf_tensor("p4_eu", [128, NC8], F32))
        identf, idrdy = make_identity(S, stack, F32, "f")
        pvec = stack.enter_context(nc.psum_tensor("p4_pvec", [128, LT], F32))
        psc = [stack.enter_context(nc.psum_tensor(f"p4_psc{i}", [128, LT], F32)) for i in range(4)]

        lq = S.dmaq("sync", "lq", 4)
        gq = S.dmaq("gpsimd", "gq", 3)
        PE, SC, VE, PL = S.eng["tensor"], S.eng["scalar"], S.eng["vector"], S.eng["gpsimd"]
        rel = {}

        def wait_rel(eng, key):
            for sem, cnt in rel.get(key, ()):
                S.wait(eng, sem, cnt)

        if "cast" in CROSS:
            S.wait("sync", CROSS["cast"][0], CROSS["cast"][1])
        wld = lq.wave()
        for c in range(NC8):
            S.dma("sync", wld, woT[:, c * 1024:(c + 1) * 1024], A["wo_bf"][:, c * 128:(c + 1) * 128], transpose=True)
            S.dma("sync", wld, wrT[:, c * 1024:(c + 1) * 1024], A["wr_bf"][:, c * 128:(c + 1) * 128], transpose=True)
        vecs = load_vecs_fm(S, stack, [
            ("td", A["L1_attn_time_decay"], D), ("tf", A["L1_attn_time_first"], D)],
            identf, idrdy, pvec, lq)
        td, tf = vecs["td"], vecs["tf"]
        S.op("scalar", lambda s: s.activation(ew[:, :], td[:, :], AF.Exp))
        S.op("scalar", lambda s: s.activation(ew[:, :], ew[:, :], AF.Exp, scale=-1.0))
        S.op("scalar", lambda s: s.activation(eu[:, :], tf[:, :], AF.Exp))

        sw0 = lq.wave()
        S.dma("sync", sw0, kbb[0][:, :], A["kTf"][0:128, 0:H2])
        S.dma("sync", sw0, vb[0][:, :], A["vTf"][0:128, 0:H2])
        scan_waves = {0: sw0}

        def emit_rtile(lt):
            tw = lq.wave()
            wait_rel("sync", "xrb")
            for c in range(NC8):
                S.dma("sync", tw, xrb[c][:, :], A["xrT"][c * 128:(c + 1) * 128, lt * LT:(lt + 1) * LT])
            S.wait("tensor", tw)
            if lt == 0:
                S.wait("tensor", wld)
            for oc in range(NC8):
                bank = psc[oc % 2]
                wait_rel("tensor", ("psc", oc % 2))
                for kc in range(NC8):
                    S.op("tensor", lambda t, _oc=oc, _kc=kc, _b=bank: t.matmul(
                        _b[:, :], wrT[:, _kc * 1024 + _oc * 128:_kc * 1024 + (_oc + 1) * 128],
                        xrb[_kc][:, :], start=(_kc == 0), stop=(_kc == NC8 - 1)))
                S.wait("scalar", PE)
                wait_rel("scalar", ("srt", oc))
                S.op("scalar", lambda s, _oc=oc, _b=bank: s.activation(srt[_oc][:, :], _b[:, :], AF.Sigmoid))
                rel[("psc", oc % 2)] = ((SC, SC.n),)
            rel["xrb"] = ((PE, PE.n),)
            S.wait("gpsimd", SC)
            rw = gq.wave()
            for c in range(NC8):
                S.dma("gpsimd", rw, A["srT"][c * 128:(c + 1) * 128, lt * LT:(lt + 1) * LT], srt[c][:, :])
            for c in range(NC8):
                rel[("srt", c)] = ((rw, rw.n),)

        for it in range(16):
            ch, half = it // 2, it % 2
            t0 = half * H2
            par = it % 2
            w = scan_waves[it]
            if it + 1 < 16:
                nch, nhalf = (it + 1) // 2, (it + 1) % 2
                nw = lq.wave()
                scan_waves[it + 1] = nw
                wait_rel("sync", ("kv", (it + 1) % 2))
                S.dma("sync", nw, kbb[(it + 1) % 2][:, :],
                      A["kTf"][nch * 128:(nch + 1) * 128, nhalf * H2:(nhalf + 1) * H2])
                S.dma("sync", nw, vb[(it + 1) % 2][:, :],
                      A["vTf"][nch * 128:(nch + 1) * 128, nhalf * H2:(nhalf + 1) * H2])
            if it % 2 == 0 and it // 2 < NLT:
                emit_rtile(it // 2)
            S.wait("scalar", w)
            S.wait("scalar", VE)
            S.op("scalar", lambda s, _p=par: s.activation(ekb[:, :], kbb[_p][:, :], AF.Exp))
            S.wait("vector", SC)
            S.op("vector", lambda v, _p=par: v.tensor_tensor(ekvb[:, :], ekb[:, :], vb[_p][:, :], OP.mult))
            rel[("kv", par)] = ((VE, VE.n),)
            if half == 0:
                S.op("vector", lambda v: v.memset(Nb[:, 0:1], 0.0))
                S.op("vector", lambda v: v.memset(Db[:, 0:1], 0.0))
            else:
                S.op("vector", lambda v: v.tensor_copy(Nb[:, 0:1], Nb[:, H2:H2 + 1]))
                S.op("vector", lambda v: v.tensor_copy(Db[:, 0:1], Db[:, H2:H2 + 1]))
            ewb = ew[:, ch:ch + 1].broadcast_to((128, H2))
            S.op("vector", lambda v, _e=ewb: v.tensor_tensor_scan(
                Nb[:, 1:H2 + 1], _e, ekvb[:, :], Nb[:, 0:1], OP.mult, OP.add))
            S.op("vector", lambda v, _e=ewb: v.tensor_tensor_scan(
                Db[:, 1:H2 + 1], _e, ekb[:, :], Db[:, 0:1], OP.mult, OP.add))
            S.op("vector", lambda v, _ch=ch: v.scalar_tensor_tensor(
                ekvb[:, :], ekvb[:, :], eu[:, _ch:_ch + 1], Nb[:, 0:H2], OP.mult, OP.add))
            S.op("vector", lambda v, _ch=ch: v.scalar_tensor_tensor(
                ekb[:, :], ekb[:, :], eu[:, _ch:_ch + 1], Db[:, 0:H2], OP.mult, OP.add))
            S.wait("scalar", VE)
            S.op("scalar", lambda s: s.activation(lnsc[:, :], ekb[:, :], AF.Ln))
            S.op("scalar", lambda s: s.activation(ekb[:, :], lnsc[:, :], AF.Exp, scale=-1.0))
            S.wait("vector", SC)
            S.op("vector", lambda v, _ch=ch, _t0=t0: v.tensor_tensor(
                yT[_ch][:, _t0:_t0 + H2], ekvb[:, :], ekb[:, :], OP.mult))

        for lt in range(NLT):
            tw = lq.wave()
            wait_rel("sync", "wotile")
            wait_rel("sync", "xrb")
            for c in range(NC8):
                S.dma("sync", tw, x2f[c][:, :], A["x2"][c * 128:(c + 1) * 128, lt * LT:(lt + 1) * LT])
                S.dma("sync", tw, srl[c][:, :], A["srT"][c * 128:(c + 1) * 128, lt * LT:(lt + 1) * LT])
            if lt == 0:
                for s_ in gq.sems:
                    S.wait("sync", s_)
            S.wait("vector", tw)
            S.wait("vector", VE)
            if lt == 0:
                for s_ in gq.sems:
                    S.wait("vector", s_)
            for c in range(NC8):
                S.op("vector", lambda v, _c=c, _lt=lt: v.tensor_tensor(
                    rwk[_c][:, :], yT[_c][:, _lt * LT:(_lt + 1) * LT], srl[_c][:, :], OP.mult))
            S.wait("tensor", VE)
            for oc in range(NC8):
                bank = psc[2 + oc % 2]
                wait_rel("tensor", ("pwo", oc % 2))
                for kc in range(NC8):
                    S.op("tensor", lambda t, _oc=oc, _kc=kc, _b=bank: t.matmul(
                        _b[:, :], woT[:, _kc * 1024 + _oc * 128:_kc * 1024 + (_oc + 1) * 128],
                        rwk[_kc][:, :], start=(_kc == 0), stop=(_kc == NC8 - 1)))
                S.wait("vector", PE)
                S.op("vector", lambda v, _oc=oc, _b=bank: v.tensor_tensor(
                    x2f[_oc][:, :], x2f[_oc][:, :], _b[:, :], OP.add))
                rel[("pwo", oc % 2)] = ((VE, VE.n),)
            S.wait("gpsimd", VE)
            sw = gq.wave()
            for c in range(NC8):
                S.dma("gpsimd", sw, A["xm"][c * 128:(c + 1) * 128, lt * LT:(lt + 1) * LT], x2f[c][:, :])
            rel["wotile"] = ((VE, VE.n), (PE, PE.n), (sw, sw.n))
        for s_ in gq.sems:
            S.wait("gpsimd", s_)
        S.flush()



def phase5(nc, A):
    with ExitStack() as stack:
        S = Sched(nc, stack, "p5")
        fkTh = stack.enter_context(nc.sbuf_tensor("p5_fkTh", [128, NC8 * 2048], BF16))
        fvTh = stack.enter_context(nc.sbuf_tensor("p5_fvTh", [128, 16 * 1024], BF16))
        xmf = [stack.enter_context(nc.sbuf_tensor(f"p5_xmf{c}", [128, LT], F32)) for c in range(NC8)]
        x2f = [stack.enter_context(nc.sbuf_tensor(f"p5_x2f{c}", [128, LT], F32)) for c in range(NC8)]
        gbuf = [stack.enter_context(nc.sbuf_tensor(f"p5_g{c}", [128, LT + 1], BF16)) for c in range(NC8)]
        dscr = stack.enter_context(nc.sbuf_tensor("p5_dscr", [128, LT], F32))
        gkb = [stack.enter_context(nc.sbuf_tensor(f"p5_gkb{c}", [128, LT], BF16)) for c in range(NC8)]
        gkp = [gkb, [stack.enter_context(nc.sbuf_tensor(f"p5_gk2{c}", [128, LT], BF16)) for c in range(NC8)]]
        grb = [stack.enter_context(nc.sbuf_tensor(f"p5_grb{c}", [128, LT], BF16)) for c in range(NC8)]
        kfT = [stack.enter_context(nc.sbuf_tensor(f"p5_kf{i}", [128, LT], BF16)) for i in range(16)]
        relsc = stack.enter_context(nc.sbuf_tensor("p5_rel", [128, LT], F32))
        kvs = [stack.enter_context(nc.sbuf_tensor(f"p5_kvs{c}", [128, LT], F32)) for c in range(NC8)]
        kvp2 = [stack.enter_context(nc.sbuf_tensor(f"p5_kvp2_{c}", [128, LT], F32)) for c in range(NC8)]
        kvp_sets = None
        sig = relsc
        dlt = kvs
        x3t = dlt
        xb3t = gkb
        onesc = stack.enter_context(nc.sbuf_tensor("p5_onesc", [128, 1], BF16))
        onesr = stack.enter_context(nc.sbuf_tensor("p5_onesr", [1, 128], BF16))
        lnscr = make_ln_scratch(S, stack)
        identf, idrdy = make_identity(S, stack, F32, "f")
        pk = [stack.enter_context(nc.psum_tensor(f"p5_pk{i}", [128, LT], F32)) for i in range(2)]
        pkv = [stack.enter_context(nc.psum_tensor(f"p5_pkv{i}", [128, LT], F32)) for i in range(2)]
        pvec = stack.enter_context(nc.psum_tensor("p5_pvec", [128, LT], F32))
        pl = [stack.enter_context(nc.psum_tensor(f"p5_pl{i}", [128, LT], F32)) for i in range(2)]

        lq = S.dmaq("sync", "lq", 4)
        gq = S.dmaq("gpsimd", "gq", 4)
        PE, SC, VE, PL = S.eng["tensor"], S.eng["scalar"], S.eng["vector"], S.eng["gpsimd"]
        rel = {}

        def wait_rel(eng, key):
            for sem, cnt in rel.get(key, ()):
                S.wait(eng, sem, cnt)

        vecs = load_vecs_fm(S, stack, [
            ("l2g", A["L1_attn_ln2_g"], D), ("l2b", A["L1_attn_ln2_b"], D),
            ("fmk", A["L1_attn_fm_k"], D), ("fmr", A["L1_attn_fm_r"], D)],
            identf, idrdy, pvec, lq)
        ln2g, ln2b, fmk, fmr = vecs["l2g"], vecs["l2b"], vecs["fmk"], vecs["fmr"]
        S.op("vector", lambda v: v.memset(onesc[:], 1.0))
        S.op("vector", lambda v: v.memset(onesr[:], 1.0))
        for c in range(NC8):
            S.op("vector", lambda v, _c=c: v.memset(gbuf[_c][:, 0:1], 0.0))
        wld = {}

        def load_dfh_weights(dfh):
            if "cast" in CROSS:
                S.wait("sync", CROSS["cast"][0], CROSS["cast"][1])
            w = lq.wave()
            wld[dfh] = w
            for c in range(NC8):
                S.dma("sync", w, fkTh[:, c * 2048:(c + 1) * 2048],
                      A["fk_bf"][dfh * 2048:(dfh + 1) * 2048, c * 128:(c + 1) * 128], transpose=True)
            for d in range(16):
                S.dma("sync", w, fvTh[:, d * 1024:(d + 1) * 1024],
                      A["fv_bf"][:, dfh * 2048 + d * 128:dfh * 2048 + (d + 1) * 128], transpose=True)

        P5SEC = int(os.environ.get("P5SEC", "4"))
        kfT_rel = 0
        load_dfh_weights(0)

        def emit_a0(lt):
            tw = lq.wave()
            wait_rel("sync", "xmf")
            for c in range(NC8):
                S.dma("sync", tw, xmf[c][:, :], A["xm"][c * 128:(c + 1) * 128, lt * LT:(lt + 1) * LT])
            S.wait("vector", tw)
            S.wait("scalar", tw)
            if lt > 0:
                for c in range(NC8):
                    S.op("vector", lambda v, _c=c: v.tensor_copy(gbuf[_c][:, 0:1], gbuf[_c][:, LT:LT + 1]))
            emit_ln(S, lnscr, xmf, onesc, onesr, pl[0], pl[1],
                    [gbuf[c][:, 1:LT + 1] for c in range(NC8)], None, lq, lt, g_vec=ln2g, b_vec=ln2b)
            rel["xmf"] = ((VE, VE.n),)
            par = lt % 2
            wait_rel("vector", ("gmix", par))
            wait_rel("vector", "grst")
            for c in range(NC8):
                g_ = gbuf[c][:, 1:LT + 1]
                gg = gbuf[c][:, 0:LT]
                S.op("vector", lambda v, _g=g_, _gg=gg: v.tensor_tensor(dscr[:, :], _g, _gg, OP.subtract))
                S.op("vector", lambda v, _c=c, _gg=gg, _p=par: v.scalar_tensor_tensor(
                    gkp[_p][_c][:, :], dscr[:, :], fmk[:, _c:_c + 1], _gg, OP.mult, OP.add))
                S.op("vector", lambda v, _c=c, _gg=gg: v.scalar_tensor_tensor(
                    grb[_c][:, :], dscr[:, :], fmr[:, _c:_c + 1], _gg, OP.mult, OP.add))
            S.wait("gpsimd", VE)
            mw = gq.wave()
            for c in range(NC8):
                S.dma("gpsimd", mw, A["gkb"][c * 128:(c + 1) * 128, lt * LT:(lt + 1) * LT], gkp[par][c][:, :])
                S.dma("gpsimd", mw, A["grb"][c * 128:(c + 1) * 128, lt * LT:(lt + 1) * LT], grb[c][:, :])
            rel["grst"] = ((mw, mw.n),)
            rel[("a0", lt)] = ((VE, VE.n),)

        def emit_kfkv(lt, dfh, gk_tiles, kvp_t=None):
            nonlocal kfT_rel
            for dfc in range(16):
                bank = pk[dfc % 2]
                wait_rel("tensor", ("pk", dfc % 2))
                for kc in range(NC8):
                    S.op("tensor", lambda t, _d=dfc, _kc=kc, _b=bank, _g=gk_tiles: t.matmul(
                        _b[:, :], fkTh[:, _kc * 2048 + _d * 128:_kc * 2048 + (_d + 1) * 128],
                        _g[_kc][:, :], start=(_kc == 0), stop=(_kc == NC8 - 1)))
                S.wait("scalar", PE)
                S.wait("scalar", PE, kfT_rel)
                S.op("scalar", lambda s, _b=bank: s.activation(relsc[:, :], _b[:, :], AF.Relu))
                rel[("pk", dfc % 2)] = ((SC, SC.n),)
                S.op("scalar", lambda s, _d=dfc: s.activation(kfT[_d][:, :], relsc[:, :], AF.Square))
            rel["gki"] = ((PE, PE.n),)
            for oc in range(NC8):
                bank = pkv[oc % 2]
                wait_rel("tensor", ("pkv", oc % 2))
                S.wait("tensor", SC)
                for dfc in range(16):
                    S.op("tensor", lambda t, _oc=oc, _d=dfc, _b=bank: t.matmul(
                        _b[:, :], fvTh[:, _d * 1024 + _oc * 128:_d * 1024 + (_oc + 1) * 128],
                        kfT[_d][:, :], start=(_d == 0), stop=(_d == 15)))
                S.wait("scalar", PE)
                wait_rel("scalar", ("kvs", oc))
                S.op("scalar", lambda s, _oc=oc, _b=bank: s.activation(kvs[_oc][:, :], _b[:, :], AF.Copy))
                rel[("pkv", oc % 2)] = ((SC, SC.n),)
                if dfh == 1:
                    S.wait("vector", SC)
                    S.op("vector", lambda v, _oc=oc: v.tensor_tensor(
                        kvs[_oc][:, :], kvs[_oc][:, :], kvp_t[_oc][:, :], OP.add))
            kfT_rel = PE.n
            S.wait("gpsimd", VE if dfh == 1 else SC)
            kw = gq.wave()
            dstn = "kvp" if dfh == 0 else "kvT"
            for c in range(NC8):
                S.dma("gpsimd", kw, A[dstn][c * 128:(c + 1) * 128, lt * LT:(lt + 1) * LT], kvs[c][:, :])
            for c in range(NC8):
                rel[("kvs", c)] = ((kw, kw.n),)

        for lt in range(NLT + 1):
            if lt >= 1:
                plt = lt - 1
                par = plt % 2
                for sem, cnt in rel.get(("a0", plt), ()):
                    S.wait("tensor", sem, cnt)
                if plt == 0:
                    S.wait("tensor", wld[0])
                emit_kfkv(plt, 0, gkp[par])
                rel[("gmix", par)] = ((PE, PE.n),)
            if lt < NLT:
                emit_a0(lt)
        S.wait("sync", PE)
        load_dfh_weights(1)
        kvp_sets = [x2f, kvp2]
        for lt in range(NLT):
            tw = lq.wave()
            wait_rel("sync", "gki")
            for c in range(NC8):
                S.dma("sync", tw, gkb[c][:, :], A["gkb"][c * 128:(c + 1) * 128, lt * LT:(lt + 1) * LT])
            wait_rel("sync", ("kvpb", lt % 2))
            for c in range(NC8):
                S.dma("sync", tw, kvp_sets[lt % 2][c][:, :], A["kvp"][c * 128:(c + 1) * 128, lt * LT:(lt + 1) * LT])
            S.wait("tensor", tw)
            S.wait("vector", tw)
            if lt == 0:
                S.wait("tensor", wld[1])
                for s_ in gq.sems:
                    S.wait("sync", s_)
            emit_kfkv(lt, 1, gkb, kvp_sets[lt % 2])
            rel[("kvpb", lt % 2)] = ((VE, VE.n),)
        for s_ in gq.sems:
            S.wait("sync", s_)
        frT = fkTh
        if "cast" in CROSS:
            S.wait("sync", CROSS["cast"][0], CROSS["cast"][1])
        wldc = lq.wave()
        S.wait("sync", PE)
        for c in range(NC8):
            S.dma("sync", wldc, frT[:, c * 1024:(c + 1) * 1024], A["fr_bf"][:, c * 128:(c + 1) * 128], transpose=True)
        rel.pop("gki", None)
        for lt in range(NLT if P5SEC >= 4 else 0):
            tw = lq.wave()
            wait_rel("sync", "ctile")
            wait_rel("sync", "xmf")
            for c in range(NC8):
                S.dma("sync", tw, grb[c][:, :], A["grb"][c * 128:(c + 1) * 128, lt * LT:(lt + 1) * LT])
                S.dma("sync", tw, kvs[c][:, :], A["kvT"][c * 128:(c + 1) * 128, lt * LT:(lt + 1) * LT])
                S.dma("sync", tw, xmf[c][:, :], A["xm"][c * 128:(c + 1) * 128, lt * LT:(lt + 1) * LT])
                S.dma("sync", tw, x2f[c][:, :], A["x2"][c * 128:(c + 1) * 128, lt * LT:(lt + 1) * LT])
            S.wait("tensor", tw)
            if lt == 0:
                S.wait("tensor", wldc)
            S.wait("vector", tw)
            S.wait("scalar", tw)
            for oc in range(NC8):
                bank = pk[oc % 2]
                wait_rel("tensor", ("pk", oc % 2))
                for kc in range(NC8):
                    S.op("tensor", lambda t, _oc=oc, _kc=kc, _b=bank: t.matmul(
                        _b[:, :], frT[:, _kc * 1024 + _oc * 128:_kc * 1024 + (_oc + 1) * 128],
                        grb[_kc][:, :], start=(_kc == 0), stop=(_kc == NC8 - 1)))
                S.wait("scalar", PE)
                S.wait("scalar", VE)
                S.op("scalar", lambda s, _b=bank: s.activation(sig[:, :], _b[:, :], AF.Sigmoid))
                rel[("pk", oc % 2)] = ((SC, SC.n),)
                S.wait("vector", SC)
                S.op("vector", lambda v, _oc=oc: v.tensor_tensor(sig[:, :], sig[:, :], kvs[_oc][:, :], OP.mult))
                S.op("vector", lambda v, _oc=oc: v.tensor_tensor(dlt[_oc][:, :], x2f[_oc][:, :], xmf[_oc][:, :], OP.add))
                S.op("vector", lambda v, _oc=oc: v.tensor_tensor(dlt[_oc][:, :], dlt[_oc][:, :], sig[:, :], OP.add))
            S.wait("scalar", VE)
            emit_ln(S, lnscr, dlt, onesc, onesr, pl[0], pl[1], x3t, xb3t, lq, lt)
            S.wait("gpsimd", VE)
            sw = gq.wave()
            for c in range(NC8):
                S.dma("gpsimd", sw, A["x3"][c * 128:(c + 1) * 128, lt * LT:(lt + 1) * LT], x3t[c][:, :])
                S.dma("gpsimd", sw, A["xb3"][c * 128:(c + 1) * 128, lt * LT:(lt + 1) * LT], xb3t[c][:, :])
            rel["ctile"] = ((sw, sw.n), (VE, VE.n))
        for s_ in gq.sems:
            S.wait("gpsimd", s_)
        S.flush()



def phase6(nc, A):
    with ExitStack() as stack:
        S = Sched(nc, stack, "p6")
        ws1b = stack.enter_context(nc.sbuf_tensor("p6_ws1", [128, NC8 * 1024], BF16))
        ws3b = stack.enter_context(nc.sbuf_tensor("p6_ws3", [128, NC8 * 1024], BF16))
        gw = stack.enter_context(nc.sbuf_tensor("p6_gw", [16, 1024], F32))
        gwT = stack.enter_context(nc.sbuf_tensor("p6_gwT", [128, NC8 * 16], F32))
        x3f = [stack.enter_context(nc.sbuf_tensor(f"p6_x3f{c}", [128, LT], F32)) for c in range(NC8)]
        xb3 = [stack.enter_context(nc.sbuf_tensor(f"p6_xb3{c}", [128, LT], BF16)) for c in range(NC8)]
        glog_s = stack.enter_context(nc.sbuf_tensor("p6_glog", [16, LT], F32))
        ptok = stack.enter_context(nc.sbuf_tensor("p6_ptok", [128, 64], F32))
        e16 = stack.enter_context(nc.sbuf_tensor("p6_e16", [128, 16], F32))
        p16 = stack.enter_context(nc.sbuf_tensor("p6_p16", [128, 16], F32))
        m16 = stack.enter_context(nc.sbuf_tensor("p6_m16", [128, 16], F32))
        cur = stack.enter_context(nc.sbuf_tensor("p6_cur", [128, 16], F32))
        gm16 = stack.enter_context(nc.sbuf_tensor("p6_gm16", [128, 16], F32))
        gsc = stack.enter_context(nc.sbuf_tensor("p6_gsc", [128, 4], F32))
        gs = stack.enter_context(nc.sbuf_tensor("p6_gs", [128, 8], F32))
        ctok = stack.enter_context(nc.sbuf_tensor("p6_ctok", [128, 64], F32))
        ctr_s = stack.enter_context(nc.sbuf_tensor("p6_ctr", [16, LT], BF16))
        s1f = stack.enter_context(nc.sbuf_tensor("p6_s1f", [128, LT], F32))
        ashb = [stack.enter_context(nc.sbuf_tensor(f"p6_ash{c}", [128, LT], BF16)) for c in range(NC8)]
        identf, idrdy = make_identity(S, stack, F32, "f")
        pg = [stack.enter_context(nc.psum_tensor(f"p6_pg{i}", [128, LT], F32)) for i in range(2)]
        ph = [stack.enter_context(nc.psum_tensor(f"p6_ph{i}", [128, LT], F32)) for i in range(4)]
        pt2 = stack.enter_context(nc.psum_tensor("p6_pt2", [128, LT], F32))

        lq = S.dmaq("sync", "lq", 3)
        gq = S.dmaq("gpsimd", "gq", 3)
        PE, SC, VE, PL = S.eng["tensor"], S.eng["scalar"], S.eng["vector"], S.eng["gpsimd"]

        wldp = gq.wave()
        for c in range(NC8):
            S.dma("gpsimd", wldp, ws1b[:, c * 1024:(c + 1) * 1024], A["L1_ffn_ws1"][c * 128:(c + 1) * 128, :])
            S.dma("gpsimd", wldp, ws3b[:, c * 1024:(c + 1) * 1024], A["L1_ffn_ws3"][c * 128:(c + 1) * 128, :])
        wld = lq.wave()
        S.dma("sync", wld, gw[:, :], A["L1_ffn_gate_w"][:, :])
        S.wait("tensor", wld)
        S.wait("tensor", PL, idrdy)
        for c in range(NC8):
            S.op("tensor", lambda t, _c=c: t.transpose(
                pt2[:, _c * 16:(_c + 1) * 16], gw[0:16, _c * 128:(_c + 1) * 128], identf[0:16, 0:16]))
        S.wait("scalar", PE)
        S.op("scalar", lambda s: s.activation(gwT[:, 0:NC8 * 16], pt2[:, 0:NC8 * 16], AF.Copy))
        s1f_rel = 0
        ct_rel = 0

        rel = {}

        def wait_rel(eng, key):
            for sem, cnt in rel.get(key, ()):
                S.wait(eng, sem, cnt)

        for lt in range(NLT):
            tw = lq.wave()
            wait_rel("sync", "xtiles")
            for c in range(NC8):
                S.dma("sync", tw, x3f[c][:, :], A["x3"][c * 128:(c + 1) * 128, lt * LT:(lt + 1) * LT])
                S.dma("sync", tw, xb3[c][:, :], A["xb3"][c * 128:(c + 1) * 128, lt * LT:(lt + 1) * LT])
            S.wait("tensor", tw)
            if lt == 0:
                S.wait("tensor", wldp)
                S.wait("tensor", SC)
                S.wait("tensor", VE)
            wait_rel("tensor", "pt2")
            for kc in range(NC8):
                S.op("tensor", lambda t, _kc=kc: t.matmul(
                    pt2[0:16, :], gwT[:, _kc * 16:(_kc + 1) * 16], x3f[_kc][:, :],
                    start=(_kc == 0), stop=(_kc == NC8 - 1)))
            S.wait("scalar", PE)
            wait_rel("scalar", "glog")
            S.op("scalar", lambda s: s.activation(glog_s[:, :], pt2[0:16, :], AF.Copy))
            S.wait("tensor", SC)
            for j in range(4):
                S.op("tensor", lambda t, _j=j: t.transpose(
                    pt2[:, _j * 16:(_j + 1) * 16], glog_s[0:16, _j * 128:(_j + 1) * 128], identf[0:16, 0:16]))
            S.wait("scalar", PE)
            wait_rel("scalar", "ptok")
            S.op("scalar", lambda s: s.activation(ptok[:, 0:64], pt2[:, 0:64], AF.Copy))
            rel["glog"] = ((PE, PE.n),)
            rel["pt2"] = ((SC, SC.n),)
            for oc in range(NC8):
                b1 = ph[oc % 2]
                b3 = ph[2 + oc % 2]
                wait_rel("tensor", ("ph", oc % 2))
                for kc in range(NC8):
                    S.op("tensor", lambda t, _oc=oc, _kc=kc, _b=b1: t.matmul(
                        _b[:, :], ws1b[:, _kc * 1024 + _oc * 128:_kc * 1024 + (_oc + 1) * 128],
                        xb3[_kc][:, :], start=(_kc == 0), stop=(_kc == NC8 - 1)))
                for kc in range(NC8):
                    S.op("tensor", lambda t, _oc=oc, _kc=kc, _b=b3: t.matmul(
                        _b[:, :], ws3b[:, _kc * 1024 + _oc * 128:_kc * 1024 + (_oc + 1) * 128],
                        xb3[_kc][:, :], start=(_kc == 0), stop=(_kc == NC8 - 1)))
                S.wait("scalar", PE)
                S.wait("scalar", VE, s1f_rel)
                S.op("scalar", lambda s, _b=b1: s.activation(s1f[:, :], _b[:, :], AF.Silu))
                S.wait("vector", SC)
                wait_rel("vector", ("ashb", oc))
                S.op("vector", lambda v, _oc=oc, _b=b3: v.tensor_tensor(
                    ashb[_oc][:, :], s1f[:, :], _b[:, :], OP.mult))
                s1f_rel = VE.n
                rel[("ph", oc % 2)] = ((VE, VE.n), (SC, SC.n))
            rel["xtiles"] = ((PE, PE.n),)
            for j in range(4):
                L = ptok[:, j * 16:(j + 1) * 16]
                S.wait("vector", SC)
                S.wait("vector", PE, ct_rel)
                S.op("vector", lambda v, _L=L: v.reduce_max(gs[:, 0:1], _L, axis=mybir.AxisListType.X))
                S.op("vector", lambda v: v.tensor_scalar(gs[:, 1:2], gs[:, 0:1], -1.0, None, OP.mult))
                S.wait("scalar", VE)
                S.op("scalar", lambda s, _L=L: s.activation(e16[:, :], _L, AF.Exp, bias=gs[:, 1:2]))
                S.wait("vector", SC)
                S.op("vector", lambda v: v.reduce_sum(gs[:, 2:3], e16[:, :], axis=mybir.AxisListType.X))
                S.op("vector", lambda v: v.reciprocal(gs[:, 3:4], gs[:, 2:3]))
                S.op("vector", lambda v: v.tensor_scalar(p16[:, :], e16[:, :], gs[:, 3:4], None, OP.mult))
                S.op("vector", lambda v: v.reduce_max(
                    gsc[:, :], p16[:, :].rearrange("p (g e) -> p g e", g=4), axis=mybir.AxisListType.X))
                S.op("vector", lambda v: v.reduce_max(gs[:, 4:5], gsc[:, :], axis=mybir.AxisListType.X))
                S.op("vector", lambda v: v.tensor_scalar(cur[:, 0:4], gsc[:, :], gs[:, 4:5], None, OP.is_ge))
                S.op("vector", lambda v: v.scalar_tensor_tensor(
                    cur[:, 0:4], cur[:, 0:4], -1e30, gsc[:, :], OP.mult, OP.add))
                S.op("vector", lambda v: v.reduce_max(gs[:, 5:6], cur[:, 0:4], axis=mybir.AxisListType.X))
                S.op("vector", lambda v: v.tensor_scalar(gsc[:, :], gsc[:, :], gs[:, 5:6], None, OP.is_ge))
                S.op("vector", lambda v: v.tensor_copy(
                    gm16[:, :].rearrange("p (g e) -> p g e", g=4),
                    gsc[:, :].broadcast_to((128, 4, 4))))
                S.op("vector", lambda v: v.tensor_tensor(m16[:, :], p16[:, :], gm16[:, :], OP.mult))
                S.op("vector", lambda v: v.tensor_copy(cur[:, :], m16[:, :]))
                for it in range(3):
                    S.op("vector", lambda v: v.reduce_max(gs[:, 6:7], cur[:, :], axis=mybir.AxisListType.X))
                    S.op("vector", lambda v: v.tensor_scalar(e16[:, :], cur[:, :], gs[:, 6:7], None, OP.is_ge))
                    S.op("vector", lambda v: v.scalar_tensor_tensor(
                        cur[:, :], e16[:, :], -1e30, cur[:, :], OP.mult, OP.add))
                S.op("vector", lambda v: v.reduce_max(gs[:, 6:7], cur[:, :], axis=mybir.AxisListType.X))
                S.op("vector", lambda v: v.tensor_scalar(e16[:, :], m16[:, :], gs[:, 6:7], None, OP.is_ge))
                S.op("vector", lambda v, _j=j: v.tensor_tensor(ctok[:, _j * 16:(_j + 1) * 16], p16[:, :], e16[:, :], OP.mult))
            S.wait("tensor", VE)
            wait_rel("tensor", "pt2")
            for j in range(4):
                S.op("tensor", lambda t, _j=j: t.transpose(
                    pt2[0:16, _j * 128:(_j + 1) * 128], ctok[:, _j * 16:(_j + 1) * 16], identf[:, :]))
            ct_rel = PE.n
            rel["ptok"] = ((VE, VE.n),)
            S.wait("scalar", PE)
            wait_rel("scalar", "ctr")
            S.op("scalar", lambda s: s.activation(ctr_s[:, :], pt2[0:16, :], AF.Copy))
            rel["pt2"] = ((SC, SC.n),)
            S.wait("gpsimd", SC)
            cw = gq.wave()
            S.dma("gpsimd", cw, A["ctr"][:, lt * LT:(lt + 1) * LT], ctr_s[:, :])
            rel["ctr"] = ((cw, cw.n),)
            S.wait("gpsimd", VE)
            aw = gq.wave()
            for c in range(NC8):
                S.dma("gpsimd", aw, A["ash"][c * 128:(c + 1) * 128, lt * LT:(lt + 1) * LT], ashb[c][:, :])
            for c in range(NC8):
                rel[("ashb", c)] = ((aw, aw.n),)
        for s_ in gq.sems:
            S.wait("gpsimd", s_)
        S.flush()



def phase7(nc, A):
    with ExitStack() as stack:
        S = Sched(nc, stack, "p7")
        w1b = [stack.enter_context(nc.sbuf_tensor(f"p7_w1_{p}", [128, NC8 * 512], BF16)) for p in range(2)]
        w3b = [stack.enter_context(nc.sbuf_tensor(f"p7_w3_{p}", [128, NC8 * 512], BF16)) for p in range(2)]
        xb3 = [stack.enter_context(nc.sbuf_tensor(f"p7_xb3{c}", [128, T], BF16)) for c in range(NC8)]
        ctr_s = stack.enter_context(nc.sbuf_tensor("p7_ctr", [16, T], BF16))
        sel = [stack.enter_context(nc.sbuf_tensor(f"p7_sel{e}", [16, 128], BF16)) for e in range(16)]
        ones16 = stack.enter_context(nc.sbuf_tensor("p7_ones16", [16, 128], BF16))
        s1f = stack.enter_context(nc.sbuf_tensor("p7_s1f", [128, LT], F32))
        af = stack.enter_context(nc.sbuf_tensor("p7_af", [128, LT], F32))
        ab = [[stack.enter_context(nc.sbuf_tensor(f"p7_ab{p}_{i}", [128, LT], BF16)) for i in range(4)]
              for p in range(2)]
        ph = [stack.enter_context(nc.psum_tensor(f"p7_ph{i}", [128, LT], F32)) for i in range(4)]
        pc = [stack.enter_context(nc.psum_tensor(f"p7_pc{i}", [128, LT], F32)) for i in range(2)]

        lq = S.dmaq("sync", "lq", 3)
        gq = S.dmaq("gpsimd", "gq", 4)
        PE, SC, VE, PL = S.eng["tensor"], S.eng["scalar"], S.eng["vector"], S.eng["gpsimd"]

        rel = {}

        def wait_rel(eng, key):
            for sem, cnt in rel.get(key, ()):
                S.wait(eng, sem, cnt)

        S.op("gpsimd", lambda g: g.memset(ones16[:, :], 1.0))
        for e in range(16):
            S.op("gpsimd", lambda g, _e=e: g.affine_select(
                sel[_e][:, :], ones16[:, :], pattern=[[0, 128]], compare_op=OP.is_equal,
                fill=0.0, base=-_e, channel_multiplier=1))
        cld = lq.wave()
        S.dma("sync", cld, ctr_s[:, :], A["ctr"][:, :])
        for c in range(NC8):
            S.dma("sync", cld, xb3[c][:, :], A["xb3"][c * 128:(c + 1) * 128, :])
        ew = {0: gq.wave()}
        for kc in range(NC8):
            S.dma("gpsimd", ew[0], w1b[0][:, kc * 512:(kc + 1) * 512], A["L1_ffn_w1"][0, kc * 128:(kc + 1) * 128, :])
            S.dma("gpsimd", ew[0], w3b[0][:, kc * 512:(kc + 1) * 512], A["L1_ffn_w3"][0, kc * 128:(kc + 1) * 128, :])

        wrel = {}
        s1f_rel = 0
        for e in range(16):
            if e + 1 < 16:
                nset = (e + 1) % 2
                w = gq.wave()
                ew[e + 1] = w
                if e >= 1:
                    S.wait("gpsimd", PE, wrel[e - 1])
                for kc in range(NC8):
                    S.dma("gpsimd", w, w1b[nset][:, kc * 512:(kc + 1) * 512],
                          A["L1_ffn_w1"][e + 1, kc * 128:(kc + 1) * 128, :])
                    S.dma("gpsimd", w, w3b[nset][:, kc * 512:(kc + 1) * 512],
                          A["L1_ffn_w3"][e + 1, kc * 128:(kc + 1) * 128, :])
            wset = e % 2
            for lt in range(NLT):
                glob = e * NLT + lt
                par = glob % 2
                if e == 0 and lt == 0:
                    S.wait("tensor", cld)
                    S.wait("tensor", PL)
                if lt == 0:
                    S.wait("tensor", ew[e])
                pcb = pc[par]
                wait_rel("tensor", ("pc", par))
                S.op("tensor", lambda t, _e=e, _lt=lt, _b=pcb: t.matmul(
                    _b[:, :], sel[_e][:, :], ctr_s[:, _lt * LT:(_lt + 1) * LT], start=True, stop=True))
                for ic in range(4):
                    b1 = ph[ic % 2]
                    b3 = ph[2 + ic % 2]
                    wait_rel("tensor", ("ph", ic % 2))
                    for kc in range(NC8):
                        S.op("tensor", lambda t, _ic=ic, _kc=kc, _b=b1, _lt=lt, _w=wset: t.matmul(
                            _b[:, :], w1b[_w][:, _kc * 512 + _ic * 128:_kc * 512 + (_ic + 1) * 128],
                            xb3[_kc][:, _lt * LT:(_lt + 1) * LT], start=(_kc == 0), stop=(_kc == NC8 - 1)))
                    for kc in range(NC8):
                        S.op("tensor", lambda t, _ic=ic, _kc=kc, _b=b3, _lt=lt, _w=wset: t.matmul(
                            _b[:, :], w3b[_w][:, _kc * 512 + _ic * 128:_kc * 512 + (_ic + 1) * 128],
                            xb3[_kc][:, _lt * LT:(_lt + 1) * LT], start=(_kc == 0), stop=(_kc == NC8 - 1)))
                    S.wait("scalar", PE)
                    S.wait("scalar", VE, s1f_rel)
                    S.op("scalar", lambda s, _b=b1: s.activation(s1f[:, :], _b[:, :], AF.Silu))
                    S.wait("vector", SC)
                    S.op("vector", lambda v, _b=b3: v.tensor_tensor(af[:, :], s1f[:, :], _b[:, :], OP.mult))
                    s1f_rel = VE.n
                    if ("ab", par) in rel:
                        wait_rel("vector", ("ab", par))
                        rel.pop(("ab", par))
                    S.op("vector", lambda v, _ic=ic, _p=par, _pc=pcb: v.tensor_tensor(
                        ab[_p][_ic][:, :], af[:, :], _pc[:, :], OP.mult))
                    rel[("ph", ic % 2)] = ((VE, VE.n), (SC, SC.n))
                rel[("pc", par)] = ((VE, VE.n),)
                S.wait("gpsimd", VE)
                aw = gq.wave()
                for ic in range(4):
                    S.dma("gpsimd", aw, A["a_all"][e, ic * 128:(ic + 1) * 128, lt * LT:(lt + 1) * LT],
                          ab[par][ic][:, :])
                rel[("ab", par)] = ((aw, aw.n),)
            wrel[e] = PE.n
        for s_ in gq.sems:
            S.wait("gpsimd", s_)
        S.flush()



def phase8(nc, A):
    with ExitStack() as stack:
        S = Sched(nc, stack, "p8")
        w2b = stack.enter_context(nc.sbuf_tensor("p8_w2", [128, 32 * 1024], BF16))
        ws2b = stack.enter_context(nc.sbuf_tensor("p8_ws2", [128, NC8 * 1024], BF16))
        at = [[stack.enter_context(nc.sbuf_tensor(f"p8_a{e}_{i}", [128, LT], BF16)) for i in range(4)]
              for e in range(8)]
        ashb = [stack.enter_context(nc.sbuf_tensor(f"p8_ash{c}", [128, LT], BF16)) for c in range(NC8)]
        x3f = [stack.enter_context(nc.sbuf_tensor(f"p8_x3f{c}", [128, LT], F32)) for c in range(NC8)]
        ypt = [stack.enter_context(nc.sbuf_tensor(f"p8_ypt{c}", [128, LT], F32)) for c in range(NC8)]
        dlt = [stack.enter_context(nc.sbuf_tensor(f"p8_dlt{c}", [128, LT], F32)) for c in range(NC8)]
        x4t = dlt
        stg = stack.enter_context(nc.sbuf_tensor("p8_stg", [128, 1024], F32))
        onesc = stack.enter_context(nc.sbuf_tensor("p8_onesc", [128, 1], BF16))
        onesr = stack.enter_context(nc.sbuf_tensor("p8_onesr", [1, 128], BF16))
        lnscr = make_ln_scratch(S, stack)
        identf, idrdy = make_identity(S, stack, F32, "f")
        py = [stack.enter_context(nc.psum_tensor(f"p8_py{i}", [128, LT], F32)) for i in range(3)]
        ptr = [stack.enter_context(nc.psum_tensor(f"p8_ptr{i}", [128, LT], F32)) for i in range(2)]
        pl = [stack.enter_context(nc.psum_tensor(f"p8_pl{i}", [128, LT], F32)) for i in range(3)]

        lq = S.dmaq("sync", "lq", 4)
        gq = S.dmaq("gpsimd", "gq", 4)
        PE, SC, VE, PL = S.eng["tensor"], S.eng["scalar"], S.eng["vector"], S.eng["gpsimd"]
        rel = {}

        def wait_rel(eng, key):
            for sem, cnt in rel.get(key, ()):
                S.wait(eng, sem, cnt)

        S.op("vector", lambda v: v.memset(onesc[:], 1.0))
        S.op("vector", lambda v: v.memset(onesr[:], 1.0))

        for half in range(2):
            if half == 1:
                for s_ in gq.sems:
                    S.wait("sync", s_)
            ww = gq.wave()
            if half == 1:
                S.wait("gpsimd", PE)
            for eg in range(8):
                e = half * 8 + eg
                for ic in range(4):
                    S.dma("gpsimd", ww, w2b[:, (eg * 4 + ic) * 1024:(eg * 4 + ic + 1) * 1024],
                          A["L1_ffn_w2"][e, ic * 128:(ic + 1) * 128, :])
            if half == 0:
                for c in range(NC8):
                    S.dma("gpsimd", ww, ws2b[:, c * 1024:(c + 1) * 1024],
                          A["L1_ffn_ws2"][c * 128:(c + 1) * 128, :])
            for lt in range(NLT):
                tw = lq.wave()
                wait_rel("sync", "atiles")
                for eg in range(8):
                    e = half * 8 + eg
                    for ic in range(4):
                        S.dma("sync", tw, at[eg][ic][:, :],
                              A["a_all"][e, ic * 128:(ic + 1) * 128, lt * LT:(lt + 1) * LT])
                if half == 0:
                    wait_rel("sync", "ashb")
                    for c in range(NC8):
                        S.dma("sync", tw, ashb[c][:, :], A["ash"][c * 128:(c + 1) * 128, lt * LT:(lt + 1) * LT])
                else:
                    wait_rel("sync", "yxt")
                    for c in range(NC8):
                        S.dma("sync", tw, ypt[c][:, :], A["ypart"][c * 128:(c + 1) * 128, lt * LT:(lt + 1) * LT])
                        S.dma("sync", tw, x3f[c][:, :], A["x3"][c * 128:(c + 1) * 128, lt * LT:(lt + 1) * LT])
                S.wait("tensor", tw)
                if lt == 0:
                    S.wait("tensor", ww)
                    S.wait("tensor", PL, idrdy)
                    S.wait("tensor", VE)
                for oc in range(NC8):
                    bank = py[oc % 3]
                    wait_rel("tensor", ("py", oc % 3))
                    first = True
                    for eg in range(8):
                        for ic in range(4):
                            last = (half == 1 and eg == 7 and ic == 3)
                            S.op("tensor", lambda t, _eg=eg, _ic=ic, _oc=oc, _b=bank, _f=first, _l=last: t.matmul(
                                _b[:, :], w2b[:, (_eg * 4 + _ic) * 1024 + _oc * 128:(_eg * 4 + _ic) * 1024 + (_oc + 1) * 128],
                                at[_eg][_ic][:, :], start=_f, stop=_l))
                            first = False
                    if half == 0:
                        for kc in range(NC8):
                            S.op("tensor", lambda t, _kc=kc, _oc=oc, _b=bank: t.matmul(
                                _b[:, :], ws2b[:, _kc * 1024 + _oc * 128:_kc * 1024 + (_oc + 1) * 128],
                                ashb[_kc][:, :], start=False, stop=(_kc == NC8 - 1)))
                    S.wait("scalar", PE)
                    if half == 0:
                        wait_rel("scalar", ("ypt", oc))
                        S.op("scalar", lambda s, _oc=oc, _b=bank: s.activation(ypt[_oc][:, :], _b[:, :], AF.Copy))
                    else:
                        wait_rel("scalar", ("dlt", oc))
                        S.op("scalar", lambda s, _oc=oc, _b=bank: s.activation(dlt[_oc][:, :], _b[:, :], AF.Copy))
                    rel[("py", oc % 3)] = ((SC, SC.n),)
                rel["atiles"] = ((PE, PE.n),)
                if half == 0:
                    rel["ashb"] = ((PE, PE.n),)
                    S.wait("gpsimd", SC)
                    yw = gq.wave()
                    for c in range(NC8):
                        S.dma("gpsimd", yw, A["ypart"][c * 128:(c + 1) * 128, lt * LT:(lt + 1) * LT], ypt[c][:, :])
                    for c in range(NC8):
                        rel[("ypt", c)] = ((yw, yw.n),)
                else:
                    S.wait("vector", SC)
                    for c in range(NC8):
                        S.op("vector", lambda v, _c=c: v.tensor_tensor(dlt[_c][:, :], dlt[_c][:, :], ypt[_c][:, :], OP.add))
                        S.op("vector", lambda v, _c=c: v.tensor_tensor(dlt[_c][:, :], dlt[_c][:, :], x3f[_c][:, :], OP.add))
                    rel["yxt"] = ((VE, VE.n),)
                    S.wait("scalar", VE)
                    emit_ln(S, lnscr, dlt, onesc, onesr, pl[0], pl[1], x4t, None, lq, lt)
                    S.wait("tensor", VE)
                    for ts in range(4):
                        for c in range(NC8):
                            bank = ptr[c % 2]
                            wait_rel("tensor", ("ptr", c % 2))
                            S.wait("tensor", SC)
                            S.op("tensor", lambda t, _c=c, _ts=ts, _b=bank: t.transpose(
                                _b[:, 0:128], x4t[_c][:, _ts * 128:(_ts + 1) * 128], identf[:, :]))
                            S.wait("scalar", PE)
                            wait_rel("scalar", "stg")
                            S.op("scalar", lambda s, _c=c, _b=bank: s.activation(
                                stg[:, _c * 128:(_c + 1) * 128], _b[:, 0:128], AF.Copy))
                            rel[("ptr", c % 2)] = ((SC, SC.n),)
                        S.wait("gpsimd", SC)
                        ow = gq.wave()
                        S.dma("gpsimd", ow, A["out"][lt * LT + ts * 128:lt * LT + (ts + 1) * 128, :], stg[:, :])
                        rel["stg"] = ((ow, ow.n),)
                    for c in range(NC8):
                        rel[("dlt", c)] = ((PE, PE.n),)
        for s_ in gq.sems:
            S.wait("gpsimd", s_)
        S.flush()


PHASE_FNS = {0: phase0, 1: phase1, 2: phase2, 3: phase3, 4: phase4, 5: phase5, 6: phase6, 7: phase7, 8: phase8}


def build(phases):
    CROSS.clear()
    DRAM_KINDS.clear()
    nc = bass.Bass("TRN2", target_bir_lowering=False)
    present = set(phases)
    A = {}
    A["x_in"] = declare_dram(nc, "x_in", (T, D), F32, "ExternalInput")
    for name, shp in WEIGHTS.items():
        A[name] = declare_dram(nc, name, shp, F32, "ExternalInput")
    A["out"] = declare_dram(nc, "out", (T, D), F32, "ExternalOutput")
    for name, (shp, dt, prod) in INTERMEDIATES.items():
        readers = [p for p in present if name in PHASE_READS.get(p, [])]
        written = prod in present
        if written and readers:
            kind = "Internal"
        elif written:
            kind = "ExternalOutput"
        elif readers:
            kind = "ExternalInput"
        else:
            continue
        A[name] = declare_dram(nc, name, shp, dt, kind)
    for p in phases:
        PHASE_FNS[p](nc, A)
    return nc, A


def make_in_maps(inputs, extra=None):
    x = np.ascontiguousarray(np.asarray(inputs["L0_attn_x"], dtype=np.float32))
    in_maps = []
    for b in range(8):
        m = {"x_in": np.ascontiguousarray(x[b])}
        for name, shp in WEIGHTS.items():
            m[name] = np.ascontiguousarray(np.asarray(inputs[name], dtype=np.float32).reshape(shp))
        if extra is not None:
            m.update(extra[b])
        in_maps.append(m)
    return in_maps


def kernel(**inputs):
    phases = sorted(PHASE_FNS.keys())
    nc, A = build(phases)
    in_maps = make_in_maps(inputs)
    res = run_bass_kernel_spmd(nc, in_maps, core_ids=list(range(8)))
    return np.stack([np.asarray(r["out"]) for r in res.results], axis=0)
